# version 05
# baseline (speedup 1.0000x reference)
import math
import contextlib
import numpy as np
import concourse.bass as bass
import concourse.mybir as mybir
from concourse.bass_utils import run_bass_kernel_spmd

F32 = mybir.dt.float32
BF16 = mybir.dt.bfloat16
U32 = mybir.dt.uint32
AF = mybir.ActivationFunctionType
ALU = mybir.AluOpType
AX = mybir.AxisListType

MAGIC = float(np.float32(3.0 * 2.0 ** 22))
NCORES = 8
TOK = 4096
SEQ = 2048
D = 1024
DI = 3072
NT = 32
NSL = 8
NCH = 8
NCHI = 24
QMAX0 = 32767.0
LN_EPS = 1e-5
RMS_EPS = 1.1920929e-07
HEADROOM = 30.0
SCALE0 = 1.0 / math.sqrt(128.0)
SCALE1 = 64.0 ** (-0.5)
EXPMASK = float(np.float32(np.frombuffer(np.uint32(0x7F800000).tobytes(), dtype=np.float32)[0]))


class Sem:
    __slots__ = ("h", "n")

    def __init__(self, h):
        self.h = h
        self.n = 0


class Eng:
    def __init__(self, name):
        self.name = name
        self.items = []
        self.auto_sem = None

    def wait(self, sem, val=None):
        if isinstance(sem, tuple):
            sem, val = sem
        h = sem.h if isinstance(sem, Sem) else sem
        self.items.append(("w", h, int(val)))

    def op(self, method, *args, inc=None, **kwargs):
        pairs = []
        if inc:
            for s, k in inc:
                s.n += k
                pairs.append((s.h, k))
        if (self.auto_sem is not None and not pairs
                and not method.startswith("dma")
                and method not in ("collective_compute",)):
            self.auto_sem.n += 1
            pairs.append((self.auto_sem.h, 1))
        self.items.append(("o", method, args, kwargs, pairs))

    def replay(self, eng):
        for it in self.items:
            if it[0] == "w":
                eng.wait_ge(it[1], it[2])
            else:
                _, method, args, kwargs, pairs = it
                ins = getattr(eng, method)(*args, **kwargs)
                for h, k in pairs:
                    ins = ins.then_inc(h, k)


class B:
    def __init__(self, nc):
        self.nc = nc
        self.es = contextlib.ExitStack()
        self.sp = Eng("sync")
        self.gp = Eng("gpsimd")
        self.ve = Eng("vector")
        self.sc = Eng("scalar")
        self.te = Eng("tensor")
        self.engines = [self.sp, self.gp, self.ve, self.sc, self.te]
        self.S = {}
        for name in ("sp16", "gp16", "gp1", "cc1", "ve1", "sc1", "te1", "bar"):
            self.S[name] = Sem(self.es.enter_context(nc.semaphore(name)))
        self.esem = {id(self.sp): self.S["sp16"], id(self.gp): self.S["gp1"],
                     id(self.ve): self.S["ve1"], id(self.sc): self.S["sc1"],
                     id(self.te): self.S["te1"]}
        self.gp.auto_sem = self.S["gp1"]
        self.dma_sems = []

    def sb(self, name, shape, dtype):
        return self.es.enter_context(self.nc.sbuf_tensor(name, shape, dtype))

    def ps(self, name, shape, dtype):
        return self.es.enter_context(self.nc.psum_tensor(name, shape, dtype))

    def mark(self, E):
        sem = self.esem[id(E)]
        last = E.items[-1]
        assert last[0] == "o", "mark needs an op"
        if last[4]:
            h, k = last[4][0]
            assert h is sem.h or h == sem.h, "op already marked with another sem"
            return (sem, sem.n)
        sem.n += 1
        last[4].append((sem.h, 1))
        return (sem, sem.n)

    def chain(self, E):
        tok = self.mark(E)
        E.wait(*tok)
        return tok

    def dma_sem(self, name):
        sem = Sem(self.es.enter_context(self.nc.semaphore(name)))
        self.dma_sems.append((name, sem))
        return sem

    def dma_on(self, sem, out, in_, eng=None):
        E = eng or self.sp
        E.op("dma_start", out=out, in_=in_, inc=[(sem, 16)])
        return (sem, sem.n)

    def dma(self, out, in_):
        s = self.S["sp16"]
        self.sp.op("dma_start", out=out, in_=in_, inc=[(s, 16)])
        return (s, s.n)

    def dma_gp(self, out, in_):
        s = self.S["gp16"]
        self.gp.op("dma_start", out=out, in_=in_, inc=[(s, 16)])
        return (s, s.n)

    def barrier(self):
        bar = self.S["bar"]
        self.sp.wait(self.S["sp16"], self.S["sp16"].n)
        self.gp.wait(self.S["gp16"], self.S["gp16"].n)
        self.gp.wait(self.S["cc1"], self.S["cc1"].n)
        self.gp.wait(self.S["gp1"], self.S["gp1"].n)
        for _, sem in self.dma_sems:
            self.sp.wait(sem, sem.n)
        for e in self.engines:
            e.op("nop", inc=[(bar, 1)])
        tgt = bar.n
        for e in self.engines:
            e.wait(bar, tgt)


def host_prep(inputs):
    ins = {k: np.asarray(v) for k, v in inputs.items()}
    f = np.float32
    W = {}
    W["x"] = np.ascontiguousarray(ins["L0_attn_x"].astype(f, copy=False))
    W["wqkT"] = np.ascontiguousarray(ins["L0_attn_proj_qk_w"].T.astype(f))
    W["wvT"] = np.ascontiguousarray(ins["L0_attn_proj_v_w"].T.astype(f))
    wq, wk, wv = np.split(ins["L0_attn_in_proj_w"].astype(f), 3, axis=0)
    W["wqT"] = np.ascontiguousarray(wq.T)
    W["wkT"] = np.ascontiguousarray(wk.T)
    W["wvvT"] = np.ascontiguousarray(wv.T)
    W["woT"] = np.ascontiguousarray(ins["L0_attn_out_proj_w"].T.astype(f))
    bq, bk, bv = np.split(ins["L0_attn_in_proj_b"].astype(f), 3)

    def chunked(v):
        return np.ascontiguousarray(np.asarray(v, f).reshape(8, 128).T)

    W["b_qk"] = chunked(ins["L0_attn_proj_qk_b"])
    W["b_v"] = chunked(ins["L0_attn_proj_v_b"])
    W["b_q"] = chunked(bq)
    W["b_k"] = chunked(bk)
    W["b_vv_row"] = np.ascontiguousarray(bv.reshape(1, D))
    W["b_o_row"] = np.ascontiguousarray(ins["L0_attn_out_proj_b"].astype(f).reshape(1, D))
    W["w_gate"] = np.ascontiguousarray(ins["L0_ffn_w_gate"].astype(f))
    W["w_up"] = np.ascontiguousarray(ins["L0_ffn_w_up"].astype(f))
    W["w_down"] = np.ascontiguousarray(ins["L0_ffn_w_down"].astype(f))
    def pchunk(a):
        c = a.shape[0] // 128
        return np.ascontiguousarray(
            a.reshape(c, 128, a.shape[1]).transpose(1, 0, 2).reshape(128, -1).astype(f))

    W["W_DQ"] = pchunk(ins["L1_attn_W_DQ"].astype(f))
    W["W_UQ_T"] = pchunk(ins["L1_attn_W_UQ"].T.astype(f))
    W["W_DKV"] = pchunk(ins["L1_attn_W_DKV"].astype(f))
    W["W_UKV_T"] = pchunk(np.ascontiguousarray(ins["L1_attn_W_UKV"].T.astype(f)))
    W["W_QR_T"] = np.ascontiguousarray(ins["L1_attn_W_QR"].T.astype(f))
    W["W_KR_T"] = np.ascontiguousarray(ins["L1_attn_W_KR"].T.astype(f))
    W["W_O_T"] = np.ascontiguousarray(ins["L1_attn_W_O"].T.astype(f))
    W["qnw"] = np.ascontiguousarray(np.broadcast_to(ins["L1_attn_q_norm_w"].astype(f), (128, 64)))
    W["knw"] = np.ascontiguousarray(np.broadcast_to(ins["L1_attn_k_norm_w"].astype(f), (128, 64)))
    W["w1"] = np.ascontiguousarray(ins["L1_ffn_weight"].astype(f))
    W["b1_row"] = np.ascontiguousarray(ins["L1_ffn_bias"].astype(f).reshape(1, D))
    W["dw"] = np.ascontiguousarray(ins["L1_ffn_delta_w"].astype(f))
    W["dact"] = np.ascontiguousarray(ins["L1_ffn_delta_act"].astype(f).reshape(1, 1))
    W["ident"] = np.eye(128, dtype=f)
    k_ = np.arange(128)[:, None]
    q_ = np.arange(128)[None, :]
    seg = ((k_ // 4) == (q_ // 4)) & (k_ != q_)
    W["seg_mask"] = np.where(seg, f(0.0), f(-1e9)).astype(f)
    cm = np.zeros((4, 128, 512), dtype=f)
    qq = np.arange(512)[None, :]
    for j in range(4):
        cm[j] = np.where((k_ + 128 * j) <= qq, f(0.0), f(-1e9))
    W["causal_masks"] = np.ascontiguousarray(cm.transpose(1, 0, 2).reshape(128, 4 * 512))
    W["ones_row"] = np.ones((1, 128), dtype=f)
    return W



def emit_program(stop_after=None):
    nc = bass.Bass("TRN2", target_bir_lowering=False)
    dram = {}

    def din(name, shape, dtype=F32):
        dram[name] = nc.dram_tensor(name, list(shape), dtype, kind="ExternalInput")
        return dram[name]

    def dint(name, shape, dtype=BF16):
        dram[name] = nc.dram_tensor(name, list(shape), dtype)
        return dram[name]

    x_in = din("x", (TOK, D))
    out_t = nc.dram_tensor("out", [TOK, D], F32, kind="ExternalOutput")
    for nm, shp in [("wqkT", (D, D)), ("wvT", (D, D)), ("wqT", (D, D)),
                    ("wkT", (D, D)), ("wvvT", (D, D)), ("woT", (D, D)),
                    ("b_qk", (128, 8)), ("b_v", (128, 8)), ("b_q", (128, 8)),
                    ("b_k", (128, 8)), ("b_vv_row", (1, D)), ("b_o_row", (1, D)),
                    ("w_gate", (DI, D)), ("w_up", (DI, D)), ("w_down", (D, DI)),
                    ("W_DQ", (128, 256)), ("W_UQ_T", (128, 256)), ("W_DKV", (128, 512)),
                    ("W_UKV_T", (128, 1024)), ("W_QR_T", (64, 64)), ("W_KR_T", (64, 64)),
                    ("W_O_T", (64, 64)), ("qnw", (128, 64)), ("knw", (128, 64)),
                    ("w1", (D, D)), ("b1_row", (1, D)), ("dw", (D, 32)),
                    ("dact", (1, 1)), ("ident", (128, 128)), ("seg_mask", (128, 128)),
                    ("causal_masks", (128, 4 * 512)), ("ones_row", (1, 128))]:
        din(nm, shp)
    xnfm_sp = dint("xnfm_sp", (NCH, NSL, 128, 512))
    u_sp = dint("u_sp", (NCHI, NSL, 128, 512))
    g_sp = dint("g_sp", (NCHI, NSL, 128, 512))
    p_sp = dint("p_sp", (NCHI, NSL, 128, 512))
    o_sp = dint("o_sp", (NCHI, NSL, 128, 512))
    wdnT_sp = dint("wdnT_sp", (NCHI, 128, D))
    qk_dr = dint("qk_dr", (NCH, NSL, 128, 512))
    vin_dr = dint("vin_dr", (NCH, NSL, 128, 512))
    q_dr = dint("q_dr", (NCH, NSL, 128, 512))
    k_dr = dint("k_dr", (NCH, NSL, 128, 512))
    vtok_dr = dint("vtok_dr", (NSL, 128, 4 * D))
    ccin = [nc.dram_tensor(f"ccin{i}", [1, 64], F32) for i in range(4)]
    scr_row = nc.dram_tensor("scr_row", [1, 16], F32)
    ccout = [nc.dram_tensor(f"ccout{i}", [1, 64], F32) for i in range(4)]

    b = B(nc)
    S = b.S
    sem_xnsp = b.dma_sem("xnsp")
    sem_wload = b.dma_sem("wload")
    sem_wslot = b.dma_sem("wslotld")
    sem_st = [b.dma_sem("st0"), b.dma_sem("st1")]
    sem_ld = [b.dma_sem("ld0"), b.dma_sem("ld1")]
    sem_aux = b.dma_sem("auxd")
    sem_ccd = b.dma_sem("ccd")
    sem_atld = b.dma_sem("atld")
    sem_out = b.dma_sem("outd")
    sem_wdnsl = [b.dma_sem(f"wdnsl{i}") for i in range(8)]
    sp16, gp16, gp1, cc1, ve1, sc1, te1 = (S["sp16"], S["gp16"], S["gp1"],
                                           S["cc1"], S["ve1"], S["sc1"], S["te1"])

    x_sb = b.sb("x_sb", [128, NT * D], BF16)
    arena = b.sb("arena", [128, NT * D], BF16)
    wslot = b.sb("wslot", [128, 2 * 8 * D], BF16)
    xnslab = b.sb("xnslab", [128, NCH * 512], BF16)
    stats_sb = b.sb("stats_sb", [128, NT * 2], F32)
    t_a = b.sb("t_a", [128, 1024], F32)
    t_b = b.sb("t_b", [128, 1024], F32)
    t_d = b.sb("t_d", [128, 512], F32)
    t_e = b.sb("t_e", [128, 512], F32)
    t_f = b.sb("t_f", [128, 512], F32)
    t_bf = b.sb("t_bf", [128, 1024], BF16)
    stat12 = b.sb("stat12", [128, 12], F32)
    mv = b.sb("mv", [128, 8], F32)
    maxacc = b.sb("maxacc", [128, 8], F32)
    maxqk = b.sb("maxqk", [128, 128], F32)
    rowbuf = b.sb("rowbuf", [1, 256], F32)
    scal = b.sb("scal", [128, 32], F32)
    consts = b.sb("consts", [128, 8], F32)
    ident_sb = b.sb("ident_sb", [128, 128], BF16)
    segm_sb = b.sb("segm_sb", [128, 128], F32)
    cmask_sb = b.sb("cmask_sb", [128, 4 * 512], BF16)
    onesrow_sb = b.sb("onesrow_sb", [1, 128], F32)
    onesrow_bf = b.sb("onesrow_bf", [1, 128], BF16)
    onescol_bf = b.sb("onescol_bf", [128, 1], BF16)
    bias_sb = b.sb("bias_sb", [128, 4 * 8], F32)
    brow_sb = b.sb("brow_sb", [1, 3 * D], BF16)
    qnw_sb = b.sb("qnw_sb", [128, 64], F32)
    knw_sb = b.sb("knw_sb", [128, 64], F32)
    bd_sb = b.sb("bd_sb", [128, 6 * 128], BF16)
    sc_blk = b.sb("sc_blk", [128, 4 * 32], F32)
    spst = b.sb("spst", [128, 2 * 512], BF16)
    ldst = b.sb("ldst", [128, 4 * 512], BF16)
    psf = [b.ps(f"psf{i}", [128, 512], F32) for i in range(6)]
    psb = [b.ps(f"psb{i}", [128, 1024], BF16) for i in range(2)]

    DBG = {}

    def dbg_out(tag, ap, shape, dtype):
        t = nc.dram_tensor("dbg_" + tag, list(shape), dtype, kind="ExternalOutput")
        b.barrier()
        idx = tuple(slice(None) for _ in shape)
        b.dma(out=t[idx], in_=ap)
        b.barrier()
        DBG[tag] = True

    sp, gp, ve, sc, te = b.sp, b.gp, b.ve, b.sc, b.te

    for t in range(NT):
        b.dma_gp(out=x_sb[:, t * D:(t + 1) * D], in_=x_in[t * 128:(t + 1) * 128, :])
    b.dma_gp(out=ident_sb[:, :], in_=dram["ident"][:, :])
    b.dma(out=segm_sb[:, :], in_=dram["seg_mask"][:, :])
    b.dma_gp(out=cmask_sb[:, :], in_=dram["causal_masks"][:, :])
    b.dma(out=onesrow_sb[:, :], in_=dram["ones_row"][:, :])
    b.dma_gp(out=onesrow_bf[:, :], in_=dram["ones_row"][:, :])
    for i, nm in enumerate(("b_qk", "b_v", "b_q", "b_k")):
        b.dma(out=bias_sb[:, i * 8:(i + 1) * 8], in_=dram[nm][:, :])
    for i, nm in enumerate(("b_vv_row", "b_o_row", "b1_row")):
        b.dma_gp(out=brow_sb[:, i * D:(i + 1) * D], in_=dram[nm][:, :])
    b.dma(out=qnw_sb[:, :], in_=dram["qnw"][:, :])
    b.dma(out=knw_sb[:, :], in_=dram["knw"][:, :])
    gp.op("memset", onescol_bf[:, :], 1.0)
    gp.op("memset", rowbuf[0:1, :], 0.0)
    gp.op("memset", consts[:, 0:1], LN_EPS)
    gp.op("memset", consts[:, 1:2], MAGIC)
    gp.op("memset", consts[:, 2:3], RMS_EPS)
    gp.op("memset", consts[:, 3:4], -HEADROOM)
    gp.op("memset", consts[:, 4:5], float(np.log(0.25)))
    gp.op("memset", maxacc[:, :], 0.0)
    gp.op("memset", maxqk[:, :], 0.0)
    b.barrier()

    if stop_after == "prologue":
        dbg_out("x0", x_sb[:, 0:D], (128, D), BF16)
        b.es.pop_all()
        return nc, b, DBG


    def ln_stats(tag):
        for t in range(NT):
            xt = x_sb[:, t * D:(t + 1) * D]
            ve.op("bn_stats", stat12[:, 0:6], xt[:, 0:512])
            ve.op("bn_stats", stat12[:, 6:12], xt[:, 512:1024])
            b.chain(ve)
            ve.op("bn_aggr", mv[:, 0:2], stat12[:, 0:12])
            tok = b.mark(ve)
            sc.wait(*tok)
            sc.op("activation", mv[:, 2:3], mv[:, 1:2], AF.Sqrt,
                  bias=consts[:, 0:1], scale=1.0)
            tok = b.mark(sc)
            ve.wait(*tok)
            ve.op("reciprocal", stats_sb[:, 2 * t:2 * t + 1], mv[:, 2:3])
            b.chain(ve)
            ve.op("tensor_scalar", stats_sb[:, 2 * t + 1:2 * t + 2], mv[:, 0:1],
                  stats_sb[:, 2 * t:2 * t + 1], -1.0, ALU.mult, ALU.mult)
            b.chain(ve)

    def xn_tile(t, dst):
        sc.op("activation", dst, x_sb[:, t * D:(t + 1) * D], AF.Identity,
              bias=stats_sb[:, 2 * t + 1:2 * t + 2],
              scale=stats_sb[:, 2 * t:2 * t + 1])
        return b.mark(sc)

    ln_stats("L0")
    b.barrier()

    for s in range(NSL):
        for j in range(4):
            t = s * 4 + j
            tok = xn_tile(t, t_bf[:, :])
            ve.wait(*tok)
            ve.op("tensor_reduce", mv[:, 3:4], t_bf[:, :], AX.X, ALU.max,
                  apply_absolute_value=True)
            b.chain(ve)
            ve.op("tensor_max", maxacc[:, 0:1], maxacc[:, 0:1], mv[:, 3:4])
            b.chain(ve)
            te.wait(*tok)
            ttok = None
            for c in range(NCH):
                bank = psb[c % 2]
                te.op("transpose", bank[:, 0:128], t_bf[:, c * 128:(c + 1) * 128],
                      ident_sb[:, :])
                ttok = b.mark(te)
                ve.wait(*ttok)
                ve.op("tensor_copy", xnslab[:, c * 512 + j * 128: c * 512 + (j + 1) * 128],
                      bank[:, 0:128])
                etok = b.mark(ve)
                te.wait(*etok)
            sc.wait(*ttok)
            sc.wait(*etok)
        cptok = b.mark(ve)
        sp.wait(*cptok)
        ve.wait(*cptok)
        sptoks = []
        for c in range(NCH):
            ve.op("tensor_copy", arena[:, c * NSL * 512 + s * 512:
                                       c * NSL * 512 + (s + 1) * 512],
                  xnslab[:, c * 512:(c + 1) * 512])
            sptoks.append(b.dma_on(sem_xnsp, out=xnfm_sp[c, s, :, :],
                                   in_=xnslab[:, c * 512:(c + 1) * 512]))
        b.chain(ve)
        te.wait(*sptoks[-1])
    b.barrier()

    if stop_after == "xn":
        dbg_out("xnfm0", arena[:, 0:512], (128, 512), BF16)
        dbg_out("stats", stats_sb[:, :], (128, NT * 2), F32)
        dbg_out("maxacc", maxacc[:, 0:1], (128, 1), F32)
        b.es.pop_all()
        return nc, b, DBG


    prr_guard = [None]

    def part_reduce_to_row(col_ap, slot):
        if prr_guard[0] is not None:
            sp.wait(*prr_guard[0])
        tok = b.dma_on(sem_aux, out=rowbuf[0:1, 0:128], in_=col_ap)
        ve.wait(*tok)
        ve.op("tensor_reduce", rowbuf[0:1, slot:slot + 1], rowbuf[0:1, 0:128],
              AX.X, ALU.max)
        prr_guard[0] = b.mark(ve)
        return prr_guard[0]

    def bcast_row(cols):
        srcs = [c[0] for c in cols]
        dsts = [c[1] for c in cols]
        k = len(cols)
        assert dsts == list(range(dsts[0], dsts[0] + k))
        for j, src in enumerate(srcs):
            ve.op("tensor_copy", rowbuf[0:1, 200 + j:201 + j],
                  rowbuf[0:1, src:src + 1])
        vtok = b.mark(ve)
        gp.wait(*vtok)
        t1 = b.dma_on(sem_ccd, out=scr_row[0:1, 0:k],
                      in_=rowbuf[0:1, 200:200 + k], eng=gp)
        gp.wait(*t1)
        t2 = b.dma_on(sem_ccd, out=scal[:, dsts[0]:dsts[0] + k],
                      in_=scr_row[0:1, 0:k].broadcast_to([128, k]), eng=gp)
        return t2

    def collective(round_idx, n):
        tok = b.dma_on(sem_ccd, out=ccin[round_idx][0:1, 0:64],
                       in_=rowbuf[0:1, 134:134 + 64], eng=gp)
        gp.wait(*tok)
        gp.op("collective_compute", "AllReduce", ALU.max,
              replica_groups=[list(range(NCORES))],
              ins=[ccin[round_idx][:, :].opt()],
              outs=[ccout[round_idx][:, :].opt()], inc=[(cc1, 1)])
        gp.wait(cc1, cc1.n)
        tok = b.dma_on(sem_ccd, out=rowbuf[0:1, 144:144 + n],
                       in_=ccout[round_idx][0:1, 0:n], eng=gp)
        return tok

    def row_ts(dst, src, s1, s2, op0, op1=ALU.bypass):
        ve.op("tensor_scalar", rowbuf[0:1, dst:dst + 1], rowbuf[0:1, src:src + 1],
              s1, s2, op0, op1)
        b.chain(ve)

    def row_recip(dst, src):
        ve.op("reciprocal", rowbuf[0:1, dst:dst + 1], rowbuf[0:1, src:src + 1])
        b.chain(ve)

    def row_mul(dst, a, bcol):
        ve.op("tensor_tensor", rowbuf[0:1, dst:dst + 1], rowbuf[0:1, a:a + 1],
              rowbuf[0:1, bcol:bcol + 1], ALU.mult)
        b.chain(ve)

    tok = part_reduce_to_row(maxacc[:, 0:1], 134)
    ve.wait(*tok)
    gp.wait(ve1, ve1.n)
    tok = collective(0, 1)
    ve.wait(*tok)
    row_ts(128, 144, 1.0 / QMAX0, 1e-12, ALU.mult, ALU.max)
    row_recip(133, 128)
    bcast_row([(133, 0)])
    b.barrier()

    if stop_after == "ar1":
        dbg_out("scal0", scal[:, 0:8], (128, 8), F32)
        dbg_out("rowb", rowbuf[0:1, :], (1, 256), F32)
        b.es.pop_all()
        return nc, b, DBG

    for c in range(NCH):
        for s in range(NSL):
            seg = arena[:, c * NSL * 512 + s * 512: c * NSL * 512 + (s + 1) * 512]
            ve.op("tensor_scalar", t_d[:, :], seg, scal[:, 0:1], MAGIC,
                  ALU.mult, ALU.add)
            b.chain(ve)
            ve.op("tensor_scalar", seg, t_d[:, :], MAGIC, None, ALU.subtract)
            b.chain(ve)
    b.barrier()

    if stop_after == "xq":
        dbg_out("xq0", arena[:, 0:512], (128, 512), BF16)
        b.es.pop_all()
        return nc, b, DBG

    def dequant_lpbq_tile(src_dram, row0, dst_bf, guard, dst_guard=None):
        if guard is not None:
            sp.wait(*guard)
        tok = b.dma_on(sem_wload, out=t_a[:, :], in_=src_dram[row0:row0 + 128, :])
        ve.wait(*tok)
        if dst_guard is not None:
            ve.wait(*dst_guard)
        ve.op("tensor_reduce",
              sc_blk[:, 0:32].rearrange("p (g o) -> p g o", o=1),
              t_a[:, :].rearrange("p (g k) -> p g k", k=32),
              AX.X, ALU.max, apply_absolute_value=True)
        b.chain(ve)
        ve.op("tensor_scalar", sc_blk[:, 32:64], sc_blk[:, 0:32], 1.0 / 7.0,
              1e-12, ALU.mult, ALU.max)
        b.chain(ve)
        ve.op("reciprocal", sc_blk[:, 64:96], sc_blk[:, 32:64])
        b.chain(ve)
        ve.op("tensor_tensor", t_b[:, :].rearrange("p (g k) -> p g k", k=32),
              t_a[:, :].rearrange("p (g k) -> p g k", k=32),
              sc_blk[:, 64:96].rearrange("p (g o) -> p g o", o=1).broadcast_to([128, 32, 32]),
              ALU.mult)
        b.chain(ve)
        ve.op("tensor_scalar", t_a[:, :], t_b[:, :], MAGIC, MAGIC, ALU.add, ALU.subtract)
        b.chain(ve)
        ve.op("tensor_tensor", dst_bf.rearrange("p (g k) -> p g k", k=32),
              t_a[:, :].rearrange("p (g k) -> p g k", k=32),
              sc_blk[:, 32:64].rearrange("p (g o) -> p g o", o=1).broadcast_to([128, 32, 32]),
              ALU.mult)
        return b.mark(ve)

    def transpose_into_wslot(src_bf_tok, slot_base, out_col, wtok):
        last = None
        ttok = None
        for k in range(NCH):
            bank = psb[k % 2]
            te.wait(*wtok)
            te.op("transpose", bank[:, 0:128], src_bf_tok[:, k * 128:(k + 1) * 128],
                  ident_sb[:, :])
            ttok = b.mark(te)
            ve.wait(*ttok)
            ve.op("tensor_copy",
                  wslot[:, slot_base + k * 512 + out_col: slot_base + k * 512 + out_col + 128],
                  bank[:, 0:128])
            last = b.mark(ve)
            te.wait(*last)
        return last, ttok

    UP_SLOT, GATE_SLOT = 0, 8 * 512
    pst_state = [None, None]
    red_state = [None, None]

    deq_guard = None
    wslot_guard = None
    tbf_guard = None
    NCB = 1 if stop_after in ("upgate1", "dequant") else 6
    for cb in range(NCB):
        for which, (src, base) in enumerate([(dram["w_up"], UP_SLOT),
                                             (dram["w_gate"], GATE_SLOT)]):
            for rt in range(4):
                row0 = cb * 512 + rt * 128
                if wslot_guard is not None:
                    ve.wait(*wslot_guard)
                    wslot_guard = None
                dtok = dequant_lpbq_tile(src, row0, t_bf[:, :], deq_guard,
                                          dst_guard=tbf_guard)
                _, tbf_guard = transpose_into_wslot(t_bf[:, :], base, rt * 128, dtok)
                deq_guard = dtok
        wtok = b.mark(ve)
        te.wait(*wtok)
        if stop_after == "dequant":
            break
        for which, (base, spdst, mcol) in enumerate([(UP_SLOT, u_sp, 2),
                                                     (GATE_SLOT, g_sp, 3)]):
            for cc in range(4):
                ch = cb * 4 + cc
                for s in range(NSL):
                    bank = psf[(s + cc) % 2]
                    for k in range(NCH):
                        te.op("matmul", bank[:, :],
                              wslot[:, base + k * 512 + cc * 128: base + k * 512 + (cc + 1) * 128],
                              arena[:, k * NSL * 512 + s * 512: k * NSL * 512 + (s + 1) * 512],
                              start=(k == 0), stop=(k == NCH - 1))
                    mtok = b.mark(te)
                    half = s % 2
                    stg = spst[:, half * 512:(half + 1) * 512]
                    sc.wait(*mtok)
                    if pst_state[half] is not None:
                        sc.wait(*pst_state[half])
                    if red_state[half] is not None:
                        sc.wait(*red_state[half])
                    sc.op("activation", stg, bank[:, :], AF.Identity, bias=0.0, scale=1.0)
                    etok = b.mark(sc)
                    te.wait(*etok)
                    ve.wait(*etok)
                    ve.op("tensor_reduce", mv[:, 3:4], stg, AX.X, ALU.max,
                          apply_absolute_value=True)
                    redtok = b.chain(ve)
                    ve.op("tensor_max", maxacc[:, mcol:mcol + 1],
                          maxacc[:, mcol:mcol + 1], mv[:, 3:4])
                    b.chain(ve)
                    sp.wait(*etok)
                    dtok = b.dma_on(sem_st[half], out=spdst[ch, s, :, :], in_=stg)
                    pst_state[half] = dtok
                    red_state[half] = redtok
                    wslot_guard = mtok
    b.barrier()

    if stop_after == "dequant":
        dbg_out("wslot0", wslot[:, 0:512], (128, 512), BF16)
        b.es.pop_all()
        return nc, b, DBG

    if stop_after in ("upgate", "upgate1"):
        dbg_out("maxug", maxacc[:, 2:4], (128, 2), F32)
        b.es.pop_all()
        return nc, b, DBG

    tok = part_reduce_to_row(maxacc[:, 2:3], 134)
    ve.wait(*tok)
    tok = part_reduce_to_row(maxacc[:, 3:4], 135)
    ve.wait(*tok)
    gp.wait(ve1, ve1.n)
    tok = collective(1, 2)
    ve.wait(*tok)
    row_mul(145, 144, 128)
    row_ts(129, 145, 1.0 / QMAX0, 1e-12, ALU.mult, ALU.max)
    row_mul(146, 144 + 1, 128)
    row_ts(130, 146, 1.0 / QMAX0, 1e-12, ALU.mult, ALU.max)
    row_recip(147, 129)
    row_mul(148, 147, 128)
    row_recip(149, 130)
    row_mul(150, 149, 128)
    row_ts(151, 130, -MAGIC, None, ALU.mult)
    bcast_row([(150, 1), (130, 2), (151, 3), (148, 4)])
    b.barrier()

    ld_guard = [None, None]
    for ch in range(NCHI):
        for s in range(NSL):
            half = s % 2
            stg = ldst[:, half * 512:(half + 1) * 512]
            if ld_guard[half] is not None:
                sp.wait(*ld_guard[half])
            ltok = b.dma_on(sem_ld[half], out=stg, in_=g_sp[ch, s, :, :])
            ve.wait(*ltok)
            ve.op("tensor_scalar", t_d[:, :], stg, scal[:, 1:2], MAGIC,
                  ALU.mult, ALU.add)
            ttok = b.mark(ve)
            ld_guard[half] = ttok
            sc.wait(*ttok)
            sc.op("activation", t_e[:, :], t_d[:, :], AF.Sigmoid,
                  bias=scal[:, 3:4], scale=scal[:, 2:3])
            stok = b.mark(sc)
            ve.wait(*stok)
            ve.op("tensor_scalar", t_f[:, :], t_e[:, :], 65536.0, 65535.0,
                  ALU.mult, ALU.min)
            b.chain(ve)
            ve.op("tensor_scalar", t_f[:, :], t_f[:, :], MAGIC, MAGIC,
                  ALU.add, ALU.subtract)
            b.chain(ve)
            pstg = spst[:, half * 512:(half + 1) * 512]
            if pst_state[half] is not None:
                ve.wait(*pst_state[half])
            ve.op("scalar_tensor_tensor", pstg, t_d[:, :], MAGIC, t_f[:, :],
                  ALU.subtract, ALU.mult)
            ptok = b.mark(ve)
            ve.wait(*ptok)
            ve.op("tensor_reduce", mv[:, 3:4], pstg, AX.X, ALU.max,
                  apply_absolute_value=True)
            red_state[half] = b.chain(ve)
            ve.op("tensor_max", maxacc[:, 4:5], maxacc[:, 4:5], mv[:, 3:4])
            b.chain(ve)
            sp.wait(*ptok)
            dtok = b.dma_on(sem_st[half], out=p_sp[ch, s, :, :], in_=pstg)
            pst_state[half] = dtok
    b.barrier()

    tok = part_reduce_to_row(maxacc[:, 4:5], 134)
    ve.wait(*tok)
    gp.wait(ve1, ve1.n)
    tok = collective(2, 1)
    ve.wait(*tok)
    row_ts(152, 130, 1.0 / 65536.0, None, ALU.mult)
    row_mul(153, 144, 152)
    row_ts(131, 153, 1.0 / QMAX0, 1e-12, ALU.mult, ALU.max)
    row_recip(154, 131)
    row_mul(155, 152, 154)
    bcast_row([(155, 5)])
    b.barrier()

    ld_guard = [None, None]
    for ch in range(NCHI):
        for s in range(NSL):
            half = s % 2
            stgp = ldst[:, half * 1024:half * 1024 + 512]
            stgu = ldst[:, half * 1024 + 512:half * 1024 + 1024]
            if ld_guard[half] is not None:
                sp.wait(*ld_guard[half])
            ltok1 = b.dma_on(sem_ld[half], out=stgp, in_=p_sp[ch, s, :, :])
            ltok2 = b.dma_on(sem_ld[half], out=stgu, in_=u_sp[ch, s, :, :])
            ve.wait(*ltok2)
            ve.op("tensor_scalar", t_d[:, :], stgp, scal[:, 5:6], MAGIC,
                  ALU.mult, ALU.add)
            b.chain(ve)
            sc.wait(*ltok2)
            sc.op("activation", t_e[:, :], stgu, AF.Identity,
                  bias=consts[:, 1:2], scale=scal[:, 4:5])
            stok = b.mark(sc)
            ld_guard[half] = stok
            ve.wait(*stok)
            ve.op("tensor_scalar", t_f[:, :], t_e[:, :], MAGIC, None, ALU.subtract)
            b.chain(ve)
            pstg = spst[:, half * 512:(half + 1) * 512]
            if pst_state[half] is not None:
                ve.wait(*pst_state[half])
            ve.op("scalar_tensor_tensor", pstg, t_d[:, :], MAGIC, t_f[:, :],
                  ALU.subtract, ALU.mult)
            ptok = b.mark(ve)
            ve.wait(*ptok)
            ve.op("tensor_reduce", mv[:, 3:4], pstg, AX.X, ALU.max,
                  apply_absolute_value=True)
            red_state[half] = b.chain(ve)
            ve.op("tensor_max", maxacc[:, 5:6], maxacc[:, 5:6], mv[:, 3:4])
            b.chain(ve)
            sp.wait(*ptok)
            dtok = b.dma_on(sem_st[half], out=o_sp[ch, s, :, :], in_=pstg)
            pst_state[half] = dtok
    b.barrier()

    tok = part_reduce_to_row(maxacc[:, 5:6], 134)
    ve.wait(*tok)
    gp.wait(ve1, ve1.n)
    tok = collective(3, 1)
    ve.wait(*tok)
    row_mul(156, 131, 129)
    row_mul(157, 144, 156)
    row_ts(132, 157, 1.0 / QMAX0, 1e-12, ALU.mult, ALU.max)
    row_recip(158, 132)
    row_mul(159, 156, 158)
    bcast_row([(159, 6), (132, 7)])
    b.barrier()

    wdn_guard = deq_guard
    wdn_sl_guard = [None] * 8
    for rt in range(8):
        if wdn_guard is not None:
            sp.wait(*wdn_guard)
        tok = b.dma_on(sem_wload, out=t_a[:, :],
                       in_=dram["w_down"][rt * 128:(rt + 1) * 128, 0:1024])
        for part in range(3):
            if part > 0:
                sp.wait(*wdn_guard)
                tok = b.dma_on(sem_wload, out=t_a[:, :],
                               in_=dram["w_down"][rt * 128:(rt + 1) * 128,
                                                  part * 1024:(part + 1) * 1024])
            ve.wait(*tok)
            ve.op("tensor_reduce",
                  sc_blk[:, 0:32].rearrange("p (g o) -> p g o", o=1),
                  t_a[:, :].rearrange("p (g k) -> p g k", k=32),
                  AX.X, ALU.max, apply_absolute_value=True)
            b.chain(ve)
            ve.op("tensor_scalar", sc_blk[:, 32:64], sc_blk[:, 0:32], 1.0 / 7.0,
                  1e-12, ALU.mult, ALU.max)
            b.chain(ve)
            ve.op("reciprocal", sc_blk[:, 64:96], sc_blk[:, 32:64])
            b.chain(ve)
            ve.op("tensor_tensor", t_b[:, :].rearrange("p (g k) -> p g k", k=32),
                  t_a[:, :].rearrange("p (g k) -> p g k", k=32),
                  sc_blk[:, 64:96].rearrange("p (g o) -> p g o", o=1).broadcast_to([128, 32, 32]),
                  ALU.mult)
            b.chain(ve)
            ve.op("tensor_scalar", t_a[:, :], t_b[:, :], MAGIC, MAGIC,
                  ALU.add, ALU.subtract)
            b.chain(ve)
            ve.op("tensor_tensor", t_bf[:, :].rearrange("p (g k) -> p g k", k=32),
                  t_a[:, :].rearrange("p (g k) -> p g k", k=32),
                  sc_blk[:, 32:64].rearrange("p (g o) -> p g o", o=1).broadcast_to([128, 32, 32]),
                  ALU.mult)
            wtok = b.mark(ve)
            wdn_guard = wtok
            for k in range(NCH):
                bank = psb[k % 2]
                te.wait(*wtok)
                te.op("transpose", bank[:, 0:128], t_bf[:, k * 128:(k + 1) * 128],
                      ident_sb[:, :])
                ttok = b.mark(te)
                ve.wait(*ttok)
                if wdn_sl_guard[k] is not None:
                    ve.wait(*wdn_sl_guard[k])
                ve.op("tensor_copy",
                      spst[:, (k % 8) * 128:(k % 8 + 1) * 128], bank[:, 0:128])
                etok = b.mark(ve)
                te.wait(*etok)
                sp.wait(*etok)
                wsp_tok = b.dma_on(sem_wdnsl[k],
                                   out=wdnT_sp[part * 8 + k, :, rt * 128:(rt + 1) * 128],
                                   in_=spst[:, (k % 8) * 128:(k % 8 + 1) * 128])
                wdn_sl_guard[k] = wsp_tok
    b.barrier()

    ld_guard = [None, None]
    for s in range(NSL):
        for ch in range(NCHI):
            half = ch % 2
            stg = ldst[:, half * 512:(half + 1) * 512]
            if ld_guard[half] is not None:
                sp.wait(*ld_guard[half])
            ltok = b.dma_on(sem_ld[half], out=stg, in_=o_sp[ch, s, :, :])
            ve.wait(*ltok)
            ve.op("tensor_scalar", t_d[:, :], stg, scal[:, 6:7], MAGIC,
                  ALU.mult, ALU.add)
            ld_guard[half] = b.mark(ve)
            ve.wait(*ld_guard[half])
            ve.op("tensor_scalar", arena[:, ch * 512:(ch + 1) * 512], t_d[:, :],
                  MAGIC, None, ALU.subtract)
            oqtok = b.chain(ve)
        te.wait(*oqtok)
        for hf in range(2):
            for k in range(NCHI):
                half = k % 2
                wst = wslot[:, half * 4096: half * 4096 + 512]
                wtok = b.dma_on(sem_wslot, out=wst,
                                in_=wdnT_sp[k, :, hf * 512:(hf + 1) * 512])
                te.wait(*wtok)
                for sub in range(4):
                    te.op("matmul", psf[sub][:, :],
                          arena[:, k * 512 + sub * 128: k * 512 + (sub + 1) * 128],
                          wst, start=(k == 0), stop=(k == NCHI - 1))
                mtok = b.mark(te)
                sp.wait(*mtok)
            for sub in range(4):
                t = s * 4 + sub
                xsl = x_sb[:, t * D + hf * 512: t * D + (hf + 1) * 512]
                ve.wait(*mtok)
                ve.op("scalar_tensor_tensor", xsl, psf[sub][:, :], scal[:, 7:8],
                      xsl, ALU.mult, ALU.add)
                etok = b.mark(ve)
                te.wait(*etok)
    b.barrier()

    if stop_after == "ffn0":
        dbg_out("x_after_ffn0", x_sb[:, 0:D], (128, D), BF16)
        b.es.pop_all()
        return nc, b, DBG

    sem_wgp = b.dma_sem("wgp")
    wslot_rd_guard = [None, None]

    def load_wT(name, slot):
        if wslot_rd_guard[slot] is not None:
            gp.wait(*wslot_rd_guard[slot])
        base = slot * 8 * D
        tok = None
        for k in range(NCH):
            tok = b.dma_on(sem_wgp, out=wslot[:, base + k * D: base + (k + 1) * D],
                           in_=dram[name][k * 128:(k + 1) * 128, :], eng=gp)
        return tok

    arena_guard = {}

    def stream_src_slab(sp_t, sidx, base, sem):
        toks = []
        if arena_guard.get(base) is not None:
            sp.wait(*arena_guard[base])
        for k in range(NCH):
            toks.append(b.dma_on(sem, out=arena[:, base + k * 512: base + (k + 1) * 512],
                                 in_=sp_t[k, sidx, :, :]))
        return toks[-1]

    def proj_stage_from_spill(wname, slot, src_sp, dst_sp, bias_col, maxcol=None):
        wtok = load_wT(wname, slot)
        te.wait(*wtok)
        wb = slot * 8 * D
        lastmm = None
        for sidx in range(NSL):
            base = (sidx % 2) * 4096
            ltok = stream_src_slab(src_sp, sidx, base, sem_atld)
            te.wait(*ltok)
            for c in range(NCH):
                bank = psf[c % 2]
                for k in range(NCH):
                    te.op("matmul", bank[:, :],
                          wslot[:, wb + k * D + c * 128: wb + k * D + (c + 1) * 128],
                          arena[:, base + k * 512: base + (k + 1) * 512],
                          start=(k == 0), stop=(k == NCH - 1))
                mtok = b.mark(te)
                lastmm = mtok
                half = (sidx * NCH + c) % 2
                stg = spst[:, half * 512:(half + 1) * 512]
                sc.wait(*mtok)
                if pst_state[half] is not None:
                    sc.wait(*pst_state[half])
                if red_state[half] is not None:
                    sc.wait(*red_state[half])
                sc.op("activation", stg, bank[:, :], AF.Identity,
                      bias=bias_sb[:, bias_col * 8 + c: bias_col * 8 + c + 1], scale=1.0)
                etok = b.mark(sc)
                te.wait(*etok)
                if maxcol is not None:
                    ve.wait(*etok)
                    ve.op("tensor_reduce", mv[:, 3:4], stg, AX.X, ALU.max,
                          apply_absolute_value=True)
                    red_state[half] = b.chain(ve)
                    ve.op("tensor_copy", maxqk[:, maxcol * 64 + sidx * 8 + c:
                                               maxcol * 64 + sidx * 8 + c + 1], mv[:, 3:4])
                    b.chain(ve)
                sp.wait(*etok)
                dtok = b.dma_on(sem_st[half], out=dst_sp[c, sidx, :, :], in_=stg)
                pst_state[half] = dtok
            arena_guard[base] = mtok
        wslot_rd_guard[slot] = lastmm

    proj_stage_from_spill("wqkT", 0, xnfm_sp, qk_dr, 0)
    proj_stage_from_spill("wvT", 0, xnfm_sp, vin_dr, 1)
    proj_stage_from_spill("wqT", 0, qk_dr, q_dr, 2, maxcol=0)
    proj_stage_from_spill("wkT", 0, qk_dr, k_dr, 3, maxcol=1)

    wtok = load_wT("wvvT", 0)
    te.wait(*wtok)
    lastmm = None
    for sidx in range(NSL):
        base = (sidx % 2) * 4096
        ltok = stream_src_slab(vin_dr, sidx, base, sem_atld)
        te.wait(*ltok)
        for sub in range(4):
            for hf in range(2):
                bank = psf[(sub * 2 + hf) % 2]
                for k in range(NCH):
                    te.op("matmul", bank[:, :],
                          arena[:, base + k * 512 + sub * 128: base + k * 512 + (sub + 1) * 128],
                          wslot[:, k * D + hf * 512: k * D + (hf + 1) * 512],
                          start=(k == 0), stop=False)
                te.op("matmul", bank[:, :], onesrow_bf[0:1, :],
                      brow_sb[0:1, hf * 512:(hf + 1) * 512],
                      start=False, stop=True)
                mtok = b.mark(te)
                lastmm = mtok
                half = (sub * 2 + hf) % 2
                stg = spst[:, half * 512:(half + 1) * 512]
                sc.wait(*mtok)
                if pst_state[half] is not None:
                    sc.wait(*pst_state[half])
                if red_state[half] is not None:
                    sc.wait(*red_state[half])
                sc.op("activation", stg, bank[:, :], AF.Identity, bias=0.0, scale=1.0)
                etok = b.mark(sc)
                te.wait(*etok)
                sp.wait(*etok)
                dtok = b.dma_on(sem_st[half],
                                out=vtok_dr[sidx, :, sub * D + hf * 512: sub * D + (hf + 1) * 512],
                                in_=stg)
                pst_state[half] = dtok
        arena_guard[base] = mtok
    wslot_rd_guard[0] = lastmm
    b.barrier()

    if stop_after == "l0proj":
        dbg_out("maxqk", maxqk[:, :], (128, 128), F32)
        b.es.pop_all()
        return nc, b, DBG

    wtok = load_wT("woT", 1)
    te.wait(*wtok)
    negsh = mv
    ns_guard = [None, None]
    rc_guard = [None, None]
    tbf_rd = None
    obase, ofmb = 12288, 16384
    o_wr_guard = None
    for sidx in range(NSL):
        qbase, kbase, vbase = 0, 4096, 8192
        ltq = stream_src_slab(q_dr, sidx, qbase, sem_atld)
        ltk = stream_src_slab(k_dr, sidx, kbase, sem_atld)
        if arena_guard.get(vbase) is not None:
            sp.wait(*arena_guard[vbase])
        ltv = b.dma_on(sem_atld, out=arena[:, vbase:vbase + 4 * D],
                       in_=vtok_dr[sidx, :, :])
        te.wait(*ltv)
        last_ctx = None
        last_oev = None
        for h in range(NCH):
            te.op("matmul", psf[4][0:1, 0:1],
                  maxqk[:, 0 * 64 + sidx * 8 + h: 0 * 64 + sidx * 8 + h + 1],
                  maxqk[:, 1 * 64 + sidx * 8 + h: 1 * 64 + sidx * 8 + h + 1],
                  start=True, stop=True)
            dtok = b.mark(te)
            ve.wait(*dtok)
            ve.op("tensor_copy", rowbuf[0:1, 164:165], psf[4][0:1, 0:1])
            rtok = b.mark(ve)
            te.wait(*rtok)
            sc.wait(*rtok)
            sc.op("activation", rowbuf[0:1, 165:166], rowbuf[0:1, 164:165], AF.Relu,
                  bias=consts[0:1, 3:4], scale=SCALE0)
            htok = b.mark(sc)
            gp.wait(*htok)
            t1 = b.dma_on(sem_ccd, out=scr_row[0:1, 0:1], in_=rowbuf[0:1, 165:166], eng=gp)
            gp.wait(*t1)
            nscol = 2 + h % 2
            if ns_guard[h % 2] is not None:
                gp.wait(*ns_guard[h % 2])
            t2 = b.dma_on(sem_ccd, out=negsh[:, nscol:nscol + 1],
                          in_=scr_row[0:1, 0:1].broadcast_to([128, 1]), eng=gp)
            ve.wait(*t2)
            ve.op("tensor_scalar", negsh[:, nscol:nscol + 1], negsh[:, nscol:nscol + 1],
                  -1.0, None, ALU.mult)
            nstok = b.chain(ve)
            for sub in range(4):
                sbank = psf[sub % 2]
                te.op("matmul", sbank[:, 0:128],
                      arena[:, kbase + h * 512 + sub * 128: kbase + h * 512 + (sub + 1) * 128],
                      arena[:, qbase + h * 512 + sub * 128: qbase + h * 512 + (sub + 1) * 128],
                      start=True, stop=True)
                mtok = b.mark(te)
                ve.wait(*mtok)
                ve.op("tensor_tensor", sbank[:, 0:128], sbank[:, 0:128], segm_sb[:, :],
                      ALU.add)
                atok = b.mark(ve)
                sc.wait(*atok)
                sc.wait(*nstok)
                if tbf_rd is not None:
                    sc.wait(*tbf_rd)
                sc.op("activation", t_bf[:, 0:128], sbank[:, 0:128], AF.Exp,
                      bias=negsh[:, nscol:nscol + 1], scale=SCALE0)
                ptok = b.mark(sc)
                ns_guard[h % 2] = ptok
                te.wait(*ptok)
                cbank = psf[2 + sub % 2]
                te.op("matmul", cbank[:, 0:128], t_bf[:, 0:128],
                      arena[:, vbase + sub * D + h * 128: vbase + sub * D + (h + 1) * 128],
                      start=True, stop=False)
                te.op("matmul", cbank[:, 128:129], t_bf[:, 0:128], onescol_bf[:, 0:1],
                      start=False, stop=True)
                ctok = b.mark(te)
                tbf_rd = ctok
                last_ctx = ctok
                rcol = sub % 2
                ve.wait(*ctok)
                if rc_guard[rcol] is not None:
                    ve.wait(*rc_guard[rcol])
                ve.op("reciprocal", negsh[:, rcol:rcol + 1], cbank[:, 128:129])
                rctok = b.chain(ve)
                sc.wait(*rctok)
                if o_wr_guard is not None:
                    sc.wait(*o_wr_guard)
                sc.op("activation",
                      arena[:, obase + sub * D + h * 128: obase + sub * D + (h + 1) * 128],
                      cbank[:, 0:128], AF.Identity, bias=0.0, scale=negsh[:, rcol:rcol + 1])
                otok = b.mark(sc)
                rc_guard[rcol] = otok
                last_oev = otok
                te.wait(*otok)
        te.wait(*last_oev)
        if arena_guard.get(ofmb) is not None:
            ve.wait(*arena_guard[ofmb])
        lasttr = None
        for sub in range(4):
            for c in range(NCH):
                bank = psb[c % 2]
                te.op("transpose", bank[:, 0:128],
                      arena[:, obase + sub * D + c * 128: obase + sub * D + (c + 1) * 128],
                      ident_sb[:, :])
                ttok = b.mark(te)
                lasttr = ttok
                ve.wait(*ttok)
                ve.op("tensor_copy",
                      arena[:, ofmb + c * 512 + sub * 128: ofmb + c * 512 + (sub + 1) * 128],
                      bank[:, 0:128])
                etok = b.mark(ve)
                te.wait(*etok)
        o_wr_guard = lasttr
        te.wait(*etok)
        lastmm2 = None
        for sub in range(4):
            for hf in range(2):
                bank = psf[4 + (sub * 2 + hf) % 2]
                for k in range(NCH):
                    te.op("matmul", bank[:, :],
                          arena[:, ofmb + k * 512 + sub * 128: ofmb + k * 512 + (sub + 1) * 128],
                          wslot[:, 8 * D + k * D + hf * 512: 8 * D + k * D + (hf + 1) * 512],
                          start=(k == 0), stop=False)
                te.op("matmul", bank[:, :], onesrow_bf[0:1, :],
                      brow_sb[0:1, D + hf * 512: D + (hf + 1) * 512],
                      start=False, stop=True)
                mtok = b.mark(te)
                lastmm2 = mtok
                t = sidx * 4 + sub
                xsl = x_sb[:, t * D + hf * 512: t * D + (hf + 1) * 512]
                ve.wait(*mtok)
                ve.op("tensor_tensor", xsl, xsl, bank[:, :], ALU.add)
                etok2 = b.chain(ve)
                te.wait(*etok2)
        arena_guard[qbase] = last_ctx
        arena_guard[kbase] = last_ctx
        arena_guard[vbase] = last_ctx
        wslot_rd_guard[1] = lastmm2
        arena_guard[ofmb] = lastmm2
    b.barrier()

    if stop_after == "layer0":
        dbg_out("x_after_l0", x_sb[:, :], (128, NT * D), BF16)
        b.es.pop_all()
        return nc, b, DBG

    ln_stats("L1")
    b.barrier()

    tok = b.dma_on(sem_aux, out=rowbuf[0:1, 166:167], in_=dram["dact"][0:1, 0:1])
    sc.wait(*tok)
    sc.op("activation", rowbuf[0:1, 167:168], rowbuf[0:1, 166:167], AF.Tanh,
          bias=0.0, scale=4.0)
    b.chain(sc)
    sc.op("activation", rowbuf[0:1, 168:169], rowbuf[0:1, 167:168], AF.Exp,
          bias=consts[0:1, 4:5], scale=float(0.5 * math.log(2.0)))
    fatok = b.mark(sc)
    ve.wait(*fatok)
    bcast_row([(168, 8)])
    b.barrier()

    def mxfp4_tile(src_f32, dst_bf, fa_ap, n):
        nblk = n // 32
        ve.op("tensor_reduce",
              sc_blk[:, 0:nblk].rearrange("p (g o) -> p g o", o=1),
              src_f32[:, 0:n].rearrange("p (g k) -> p g k", k=32),
              AX.X, ALU.max, apply_absolute_value=True)
        b.chain(ve)
        ve.op("tensor_scalar", sc_blk[:, 32:32 + nblk].bitcast(U32),
              sc_blk[:, 0:nblk].bitcast(U32), 0x7F800000, None, ALU.bitwise_and)
        b.chain(ve)
        if fa_ap.shape[1] == 1:
            ve.op("tensor_scalar", sc_blk[:, 64:64 + nblk],
                  sc_blk[:, 32:32 + nblk], fa_ap, 1e-30, ALU.mult, ALU.max)
        else:
            ve.op("tensor_tensor", sc_blk[:, 64:64 + nblk],
                  sc_blk[:, 32:32 + nblk], fa_ap, ALU.mult)
            b.chain(ve)
            ve.op("tensor_scalar", sc_blk[:, 64:64 + nblk],
                  sc_blk[:, 64:64 + nblk], 1e-30, None, ALU.max)
        b.chain(ve)
        ve.op("reciprocal", sc_blk[:, 96:96 + nblk], sc_blk[:, 64:64 + nblk])
        b.chain(ve)
        ve.op("tensor_tensor", t_a[:, 0:n].rearrange("p (g k) -> p g k", k=32),
              src_f32[:, 0:n].rearrange("p (g k) -> p g k", k=32),
              sc_blk[:, 96:96 + nblk].rearrange("p (g o) -> p g o", o=1).broadcast_to([128, nblk, 32]),
              ALU.mult)
        b.chain(ve)
        ve.op("tensor_scalar", t_b[:, 0:n], t_a[:, 0:n], 6.0, -6.0, ALU.min, ALU.max)
        b.chain(ve)
        ve.op("tensor_scalar", t_a[:, 0:n].bitcast(U32), t_b[:, 0:n].bitcast(U32),
              0x7F800000, None, ALU.bitwise_and)
        b.chain(ve)
        ve.op("tensor_scalar", t_bf[:, 0:n], t_a[:, 0:n], 1.0, 4.0, ALU.max, ALU.min)
        b.chain(ve)
        ve.op("scalar_tensor_tensor", t_a[:, 0:n], t_bf[:, 0:n], MAGIC / 2.0,
              t_b[:, 0:n], ALU.mult, ALU.add)
        b.chain(ve)
        ve.op("scalar_tensor_tensor", t_b[:, 0:n], t_bf[:, 0:n], -MAGIC / 2.0,
              t_a[:, 0:n], ALU.mult, ALU.add)
        b.chain(ve)
        ve.op("tensor_tensor", dst_bf[:, 0:n].rearrange("p (g k) -> p g k", k=32),
              t_b[:, 0:n].rearrange("p (g k) -> p g k", k=32),
              sc_blk[:, 64:64 + nblk].rearrange("p (g o) -> p g o", o=1).broadcast_to([128, nblk, 32]),
              ALU.mult)
        return b.chain(ve)

    w1_guard = None
    for rt in range(NCH):
        if w1_guard is not None:
            sp.wait(*w1_guard)
        tok = b.dma_on(sem_wload, out=t_d[:, :],
                       in_=dram["w1"][rt * 128:(rt + 1) * 128, 0:512])
        tok2 = b.dma_on(sem_wload, out=t_e[:, :],
                        in_=dram["w1"][rt * 128:(rt + 1) * 128, 512:1024])
        dwt = b.dma_on(sem_wload, out=maxqk[:, 96:128],
                       in_=dram["dw"][rt * 128:(rt + 1) * 128, :])
        sc.wait(*dwt)
        sc.op("activation", maxqk[:, 64:96], maxqk[:, 96:128], AF.Tanh,
              bias=0.0, scale=4.0)
        b.chain(sc)
        sc.op("activation", maxqk[:, 96:128], maxqk[:, 64:96], AF.Exp,
              bias=consts[:, 4:5], scale=float(0.5 * math.log(2.0)))
        fwtok = b.mark(sc)
        ve.wait(*dwt)
        ve.wait(*fwtok)
        for hh, stg in enumerate((t_d, t_e)):
            qtok = mxfp4_tile(stg, t_bf, maxqk[:, 96 + hh * 16:112 + hh * 16], 512)
            for k in range(4):
                kk = hh * 4 + k
                bank = psb[kk % 2]
                te.wait(*qtok)
                te.op("transpose", bank[:, 0:128], t_bf[:, k * 128:(k + 1) * 128],
                      ident_sb[:, :])
                ttok = b.mark(te)
                ve.wait(*ttok)
                ve.op("tensor_copy",
                      wslot[:, kk * D + rt * 128: kk * D + (rt + 1) * 128],
                      bank[:, 0:128])
                etok = b.mark(ve)
                te.wait(*etok)
            w1_guard = etok
    b.barrier()

    ytr_guard = None
    for sidx in range(NSL):
        for j in range(4):
            t = sidx * 4 + j
            tok = xn_tile(t, t_bf[:, :])
            te.wait(*tok)
            for c in range(NCH):
                bank = psb[c % 2]
                te.op("transpose", bank[:, 0:128], t_bf[:, c * 128:(c + 1) * 128],
                      ident_sb[:, :])
                ttok = b.mark(te)
                ve.wait(*ttok)
                ve.op("tensor_copy", xnslab[:, c * 512 + j * 128: c * 512 + (j + 1) * 128],
                      bank[:, 0:128])
                etok = b.mark(ve)
                te.wait(*etok)
            ve.wait(*tok)
            if ytr_guard is not None:
                ve.wait(*ytr_guard)
            ytok = mxfp4_tile(t_bf, spst, scal[:, 8:9], 1024)
            sc.wait(*ytok)
            sc.wait(*ttok)
            for c in range(NCH):
                bank = psb[c % 2]
                te.wait(*ytok)
                te.op("transpose", bank[:, 0:128], spst[:, c * 128:(c + 1) * 128],
                      ident_sb[:, :])
                ttok2 = b.mark(te)
                ytr_guard = ttok2
                ve.wait(*ttok2)
                ve.op("tensor_copy",
                      arena[:, c * NSL * 512 + sidx * 512 + j * 128:
                            c * NSL * 512 + sidx * 512 + (j + 1) * 128],
                      bank[:, 0:128])
                etok2 = b.mark(ve)
                te.wait(*etok2)
        cptok = b.mark(ve)
        sp.wait(*cptok)
        ve.wait(*cptok)
        sptoks = []
        for c in range(NCH):
            sptoks.append(b.dma_on(sem_xnsp, out=xnfm_sp[c, sidx, :, :],
                                   in_=xnslab[:, c * 512:(c + 1) * 512]))
        te.wait(*sptoks[-1])
    b.barrier()

    if stop_after == "xq1":
        dbg_out("xq1", arena[:, 0:512], (128, 512), BF16)
        dbg_out("x_l1", x_sb[:, :], (128, NT * D), BF16)
        b.es.pop_all()
        return nc, b, DBG
    if stop_after == "full":
        dbg_out("x_l1", x_sb[:, :], (128, NT * D), BF16)

    for sidx in range(NSL):
        for sub in range(4):
            for hf in range(2):
                bank = psf[(sub * 2 + hf) % 4]
                for k in range(NCH):
                    te.op("matmul", bank[:, :],
                          arena[:, k * NSL * 512 + sidx * 512 + sub * 128:
                                k * NSL * 512 + sidx * 512 + (sub + 1) * 128],
                          wslot[:, k * D + hf * 512: k * D + (hf + 1) * 512],
                          start=(k == 0), stop=False)
                te.op("matmul", bank[:, :], onesrow_bf[0:1, :],
                      brow_sb[0:1, 2 * D + hf * 512: 2 * D + (hf + 1) * 512],
                      start=False, stop=True)
                mtok = b.mark(te)
                t = sidx * 4 + sub
                xsl = x_sb[:, t * D + hf * 512: t * D + (hf + 1) * 512]
                ve.wait(*mtok)
                ve.op("tensor_tensor", xsl, xsl, bank[:, :], ALU.add)
                etok = b.chain(ve)
                te.wait(*etok)
    b.barrier()

    if stop_after == "ffn1":
        dbg_out("x_after_ffn1", x_sb[:, 0:D], (128, D), BF16)
        b.es.pop_all()
        return nc, b, DBG

    out_guard = None
    for t in range(NT):
        if out_guard is not None:
            ve.wait(*out_guard)
        ve.op("tensor_copy", t_a[:, :], x_sb[:, t * D:(t + 1) * D])
        otok = b.chain(ve)
        sp.wait(*otok)
        out_guard = b.dma_on(sem_out, out=out_t[t * 128:(t + 1) * 128, :],
                             in_=t_a[:, :])
    b.barrier()

    tok1 = b.dma_on(sem_wload, out=t_a[:, 0:256], in_=dram["W_DQ"][:, :])
    tok2 = b.dma_on(sem_wload, out=t_a[:, 256:512], in_=dram["W_UQ_T"][:, :])
    tok3 = b.dma_on(sem_wload, out=t_b[:, 0:512], in_=dram["W_DKV"][:, :])
    tok4 = b.dma_on(sem_wload, out=t_d[:, :], in_=dram["W_UKV_T"][:, 0:512])
    tok4b = b.dma_on(sem_wload, out=t_e[:, :], in_=dram["W_UKV_T"][:, 512:1024])
    tok5 = b.dma_on(sem_wload, out=t_b[0:64, 512:576], in_=dram["W_QR_T"][:, :])
    tok6 = b.dma_on(sem_wload, out=t_b[0:64, 576:640], in_=dram["W_KR_T"][:, :])
    tok7 = b.dma_on(sem_wload, out=t_b[0:64, 640:704], in_=dram["W_O_T"][:, :])
    te.wait(*tok7)
    for c in range(4):
        te.op("matmul", psf[0][0:64, 0:64], t_a[:, c * 64:(c + 1) * 64],
              t_a[:, 256 + c * 64:256 + (c + 1) * 64],
              start=(c == 0), stop=(c == 3))
    mq_tok = b.mark(te)
    for c in range(NCH):
        src = t_d if c < 4 else t_e
        te.op("matmul", psf[1][0:64, 0:128], t_b[:, c * 64:(c + 1) * 64],
              src[:, (c % 4) * 128:(c % 4 + 1) * 128],
              start=(c == 0), stop=(c == 7))
    mkv_tok = b.mark(te)
    gp.op("memset", bd_sb[:, :], 0.0)
    gtok = (b.S["gp1"], b.S["gp1"].n)
    ve.wait(*gtok)
    ve.wait(*mq_tok)
    ve.op("tensor_copy", bd_sb[0:64, 0:64], psf[0][0:64, 0:64])
    b.chain(ve)
    ve.wait(*mkv_tok)
    ve.op("tensor_copy", bd_sb[0:64, 128:192], psf[1][0:64, 0:64])
    b.chain(ve)
    ve.op("tensor_copy", bd_sb[0:64, 256:320], psf[1][0:64, 64:128])
    b.chain(ve)
    ve.wait(*tok7)
    ve.op("tensor_copy", bd_sb[0:64, 384:448], t_b[0:64, 512:576])
    b.chain(ve)
    ve.op("tensor_copy", bd_sb[0:64, 512:576], t_b[0:64, 576:640])
    b.chain(ve)
    ve.op("tensor_copy", bd_sb[0:64, 640:704], t_b[0:64, 640:704])
    cptok = b.chain(ve)
    sp.wait(*cptok)
    dtoks = []
    for slot in range(5):
        dtoks.append(b.dma_on(sem_aux, out=bd_sb[64:128, slot * 128 + 64: slot * 128 + 128],
                              in_=bd_sb[0:64, slot * 128: slot * 128 + 64]))
    te.wait(*dtoks[-1])
    ve.wait(*dtoks[-1])
    gp.op("memset", bd_sb[64:65, 704:705], 1.0)
    bdtok = (b.S["gp1"], b.S["gp1"].n)
    te.wait(*bdtok)
    ve.op("tensor_scalar", qnw_sb[:, :], qnw_sb[:, :], SCALE1, None, ALU.mult)
    qnwtok = b.chain(ve)
    b.barrier()

    AXH, AQC, AQR, AKC, AKR, AVT, ACT_ = 0, 2048, 4096, 6144, 8192, 10240, 12288
    attnbuf = arena[:, 13312:13312 + 4096].bitcast(F32)
    a_guard = {}

    if stop_after == "attn1":
        dbg_out("xpre", x_sb[:, :], (128, NT * D), BF16)
    UNITS = 1 if stop_after == "attn1" else 16
    for unit in range(UNITS):
        seq2, pp = divmod(unit, 8)
        if a_guard.get("qtr") is not None:
            ve.wait(*a_guard["qtr"])
            sc.wait(*a_guard["qtr"])
        if a_guard.get(AXH) is not None:
            sp.wait(*a_guard[AXH])
        xtoks = []
        for half in range(4):
            sl = seq2 * 4 + half
            xtoks.append(b.dma_on(sem_atld,
                                  out=arena[:, AXH + half * 512: AXH + (half + 1) * 512],
                                  in_=xnfm_sp[pp, sl, :, :]))
        xh_tok = xtoks[-1]
        te.wait(*xh_tok)
        last_ctx = None
        prev_qtr = a_guard.get("qtr")
        for sub in range(16):
            xh_sl = arena[:, AXH + sub * 128: AXH + (sub + 1) * 128]
            bank = psf[sub % 2]
            bank2 = psf[2 + sub % 2]
            te.op("matmul", bank[:, 0:128], xh_sl, bd_sb[:, 0:128],
                  start=True, stop=True)
            qctok = b.mark(te)
            te.op("matmul", bank2[:, 0:128], xh_sl, bd_sb[:, 128:256],
                  start=True, stop=True)
            kctok = b.mark(te)
            for which, (bk, tok_, nw, dst_base) in enumerate(
                    [(bank, qctok, qnw_sb, AQC), (bank2, kctok, knw_sb, AKC)]):
                sc.wait(*tok_)
                sc.op("activation", t_d[:, 0:128], bk[:, 0:128], AF.Square,
                      bias=0.0, scale=1.0)
                sqk = b.mark(sc)
                ve.wait(*sqk)
                ve.op("tensor_reduce", mv[:, 0:2].rearrange("p (g o) -> p g o", o=1),
                      t_d[:, 0:128].rearrange("p (g k) -> p g k", k=64),
                      AX.X, ALU.add)
                b.chain(ve)
                ve.op("tensor_scalar", mv[:, 0:2], mv[:, 0:2], 1.0 / 64.0, None,
                      ALU.mult)
                mstok = b.chain(ve)
                sc.wait(*mstok)
                sc.op("activation", mv[:, 2:4], mv[:, 0:2], AF.Sqrt,
                      bias=consts[:, 2:3], scale=1.0)
                sqtok = b.mark(sc)
                ve.wait(*sqtok)
                ve.op("reciprocal", mv[:, 0:2], mv[:, 2:4])
                b.chain(ve)
                ve.op("tensor_tensor", t_d[:, 0:128].rearrange("p (g k) -> p g k", k=64),
                      bk[:, 0:128].rearrange("p (g k) -> p g k", k=64),
                      mv[:, 0:2].rearrange("p (g o) -> p g o", o=1).broadcast_to([128, 2, 64]),
                      ALU.mult)
                b.chain(ve)
                ve.op("tensor_tensor", t_bf[:, 0:128].rearrange("p (g k) -> p g k", k=64),
                      t_d[:, 0:128].rearrange("p (g k) -> p g k", k=64),
                      nw[:, :].rearrange("p (g k) -> p g k", g=1).broadcast_to([128, 2, 64]),
                      ALU.mult)
                ntok = b.chain(ve)
                te.wait(*ntok)
                tbank = psb[which]
                te.op("transpose", tbank[:, 0:128], t_bf[:, 0:128], ident_sb[:, :])
                trtok = b.mark(te)
                ve.wait(*trtok)
                if prev_qtr is not None:
                    ve.wait(*prev_qtr)
                    prev_qtr = None
                ve.op("tensor_copy",
                      arena[:, dst_base + sub * 128: dst_base + (sub + 1) * 128],
                      tbank[:, 0:128])
                evtok = b.chain(ve)
                te.wait(*evtok)
                sc.wait(*evtok)
            te.op("matmul", bank[:, 0:128], xh_sl, bd_sb[:, 256:384],
                  start=True, stop=True)
            vtok_ = b.mark(te)
            sc.wait(*vtok_)
            sc.op("activation", arena[:, AVT + sub * 128: AVT + (sub + 1) * 128],
                  bank[:, 0:128], AF.Identity, bias=0.0, scale=1.0)
            vevtok = b.mark(sc)
            te.wait(*vevtok)

        for sl4 in range(4):
            for which, (srcb, bdofs, dstb) in enumerate(
                    [(AQC, 384, AQR), (AXH, 512, AKR)]):
                bank = psf[4 + which]
                te.wait(*evtok)
                te.op("matmul", bank[:, :], bd_sb[:, bdofs:bdofs + 128],
                      arena[:, srcb + sl4 * 512: srcb + (sl4 + 1) * 512],
                      start=True, stop=True)
                mtok = b.mark(te)
                sc.wait(*mtok)
                sc.op("activation", arena[:, dstb + sl4 * 512: dstb + (sl4 + 1) * 512],
                      bank[:, :], AF.Identity, bias=0.0, scale=1.0)
                rev = b.mark(sc)
                te.wait(*rev)
        proj_done = rev
        ve.wait(*proj_done)
        ve.op("tensor_scalar", rowbuf[0:1, 176:184], rowbuf[0:1, 176:184],
              0.0, None, ALU.mult)
        b.chain(ve)
        for pi, (b1, b2) in enumerate(((AQC, AQR), (AKC, AKR))):
            for half in range(2):
                lastmm3 = None
                for si, src in enumerate((b1, b2)):
                    if lastmm3 is not None:
                        ve.wait(*lastmm3)
                    ve.op("tensor_tensor", t_bf[:, :],
                          arena[:, src + half * 1024: src + (half + 1) * 1024],
                          arena[:, src + half * 1024: src + (half + 1) * 1024],
                          ALU.mult)
                    sqk = b.chain(ve)
                    te.wait(*sqk)
                    for h in range(2):
                        for sl5 in range(2):
                            te.op("matmul", psf[4 + h][0:1, sl5 * 256:sl5 * 256 + 256]
                                  if False else psf[4 + h][0:1, sl5 * 512 - sl5 * 512: 512]
                                  if False else psf[4 + h][0:1, 0:512],
                                  onescol_bf[64 * h:64 * h + 64, 0:1],
                                  t_bf[64 * h:64 * h + 64, sl5 * 512:(sl5 + 1) * 512],
                                  start=(si == 0 and sl5 == 0),
                                  stop=(si == 1 and sl5 == 1),
                                  skip_group_check=True)
                    lastmm3 = b.mark(te)
                for h in range(2):
                    ve.wait(*lastmm3)
                    ve.op("tensor_reduce", rowbuf[0:1, 190:191],
                          psf[4 + h][0:1, 0:512], AX.X, ALU.max)
                    b.chain(ve)
                    ve.op("tensor_max",
                          rowbuf[0:1, 176 + 2 * h + pi:177 + 2 * h + pi],
                          rowbuf[0:1, 176 + 2 * h + pi:177 + 2 * h + pi],
                          rowbuf[0:1, 190:191])
                    etok2 = b.chain(ve)
                    te.wait(*etok2)
        for h in range(2):
            ve.op("tensor_tensor", rowbuf[0:1, 170 + h:171 + h],
                  rowbuf[0:1, 176 + 2 * h:177 + 2 * h],
                  rowbuf[0:1, 177 + 2 * h:178 + 2 * h], ALU.mult)
            b.chain(ve)
        vtk = (b.S["ve1"], b.S["ve1"].n)
        sc.wait(*vtk)
        sc.op("activation", rowbuf[0:1, 174:176], rowbuf[0:1, 170:172], AF.Sqrt,
              bias=0.0, scale=1.0)
        b.chain(sc)
        sc.op("activation", rowbuf[0:1, 172:174], rowbuf[0:1, 174:176], AF.Relu,
              bias=consts[0:1, 3:4], scale=1.0)
        shtok = b.mark(sc)
        gp.wait(*shtok)
        t1 = b.dma_on(sem_ccd, out=scr_row[0:1, 0:2], in_=rowbuf[0:1, 172:174], eng=gp)
        gp.wait(*t1)
        if a_guard.get("negsh") is not None:
            gp.wait(*a_guard["negsh"])
        t2 = b.dma_on(sem_ccd, out=mv[:, 6:8],
                      in_=scr_row[0:1, 0:2].broadcast_to([128, 2]), eng=gp)
        ve.wait(*t2)
        ve.op("tensor_scalar", mv[:, 4:6], mv[:, 6:8], -1.0, None, ALU.mult)
        nstok = b.chain(ve)
        exp_rd = None
        for h in range(2):
            qcn_h = arena[:, AQC:AQC + 2048]
            for qb in range(4):
                cb = psf[5]
                nkt = 4 * qb + 4
                for kt in range(nkt):
                    sbank = psf[kt % 2]
                    te.op("matmul", sbank[:, :],
                          arena[64 * h:64 * h + 64, AKC + kt * 128: AKC + (kt + 1) * 128],
                          arena[64 * h:64 * h + 64, AQC + qb * 512: AQC + (qb + 1) * 512],
                          start=True, stop=False)
                    te.op("matmul", sbank[:, :],
                          arena[64 * h:64 * h + 64, AKR + kt * 128: AKR + (kt + 1) * 128],
                          arena[64 * h:64 * h + 64, AQR + qb * 512: AQR + (qb + 1) * 512],
                          start=False, stop=True)
                    smtok = b.mark(te)
                    j = kt - 4 * qb
                    if j >= 0:
                        ve.wait(*smtok)
                        ve.op("tensor_tensor", sbank[:, :], sbank[:, :],
                              cmask_sb[:, j * 512:(j + 1) * 512], ALU.add)
                        amtok = b.chain(ve)
                    else:
                        amtok = smtok
                    sc.wait(*amtok)
                    if exp_rd is not None:
                        sc.wait(*exp_rd)
                    sc.op("activation", t_bf[:, 0:512], sbank[:, :], AF.Exp,
                          bias=mv[:, 4 + h:5 + h], scale=1.0)
                    ptok = b.mark(sc)
                    te.wait(*ptok)
                    te.op("matmul", psf[4][0:1, :], onescol_bf[:, 0:1], t_bf[:, 0:512],
                          start=(kt == 0), stop=(kt == nkt - 1))
                    te.op("matmul", cb[0:64, :],
                          arena[:, AVT + kt * 128 + 64 * h: AVT + kt * 128 + 64 * h + 64],
                          t_bf[:, 0:512], start=(kt == 0), stop=(kt == nkt - 1))
                    exp_rd = b.mark(te)
                last_ctx = exp_rd
                sc.wait(*last_ctx)
                if a_guard.get("ctxT") is not None:
                    sc.wait(*a_guard["ctxT"])
                sc.op("activation", arena[0:64, ACT_:ACT_ + 512], cb[0:64, :],
                      AF.Identity, bias=0.0, scale=1.0)
                b.chain(sc)
                sc.op("activation", arena[64:65, ACT_:ACT_ + 512], psf[4][0:1, :],
                      AF.Identity, bias=0.0, scale=1.0)
                cevtok = b.mark(sc)
                te.wait(*cevtok)
                for sub in range(4):
                    ob = psf[2 + sub % 2]
                    te.op("matmul", ob[:, 0:65],
                          arena[0:65, ACT_ + sub * 128: ACT_ + (sub + 1) * 128],
                          bd_sb[0:65, 640:705], start=True, stop=True)
                    omtok = b.mark(te)
                    ve.wait(*omtok)
                    if a_guard.get("rcp2") is not None:
                        ve.wait(*a_guard["rcp2"])
                    ve.op("reciprocal", t_d[:, 1:2], ob[:, 64:65])
                    rctok = b.chain(ve)
                    tl = qb * 4 + sub
                    asl = attnbuf[:, tl * 128 + h * 64: tl * 128 + h * 64 + 64]
                    ve.op("tensor_scalar", asl, ob[:, 0:64], t_d[:, 1:2], None,
                          ALU.mult)
                    fintok = b.chain(ve)
                    te.wait(*fintok)
                    a_guard["rcp2"] = fintok
                a_guard["ctxT"] = omtok
        flv = fintok
        gp.wait(*flv)
        if a_guard.get("flush") is not None:
            gp.wait(*a_guard["flush"])
        gp.op("dma_start",
              out=out_t[seq2 * 2048:(seq2 + 1) * 2048, pp * 128:(pp + 1) * 128]
              .rearrange("(t p) m -> p t m", p=128),
              in_=attnbuf[:, :].rearrange("p (t m) -> p t m", m=128),
              accum_op=ALU.add, inc=[(sem_out, 16)])
        ftok = (sem_out, sem_out.n)
        a_guard["flush"] = ftok
        ve.wait(*ftok)
        a_guard[AXH] = last_ctx
        a_guard["qtr"] = last_ctx
        a_guard["negsh"] = ptok
    b.barrier()
    if stop_after == "attn1":
        dbg_out("mvd", mv[:, :], (128, 8), F32)
        dbg_out("rowb2", rowbuf[0:1, :], (1, 256), F32)
        dbg_out("tdd", t_d[:, 0:2], (128, 2), F32)
        dbg_out("qcn", arena[:, AQC:AQC + 2048], (128, 2048), BF16)
        dbg_out("qr", arena[:, AQR:AQR + 2048], (128, 2048), BF16)
        dbg_out("kcn", arena[:, AKC:AKC + 2048], (128, 2048), BF16)
        dbg_out("kr", arena[:, AKR:AKR + 2048], (128, 2048), BF16)
        dbg_out("vt", arena[:, AVT:AVT + 2048], (128, 2048), BF16)
        dbg_out("ctxT", arena[:, ACT_:ACT_ + 512], (128, 512), BF16)
        dbg_out("xout", x_sb[:, :], (128, NT * D), BF16)
        dbg_out("xl1", xnfm_sp[0, 0, :, :], (128, 512), BF16)
        b.es.pop_all()
        return nc, b, DBG

    b.barrier()
    b.es.pop_all()
    return nc, b, DBG


def _layer1(nc, b, dram, L):
    pass



def kernel(**inputs):
    W = host_prep(inputs)
    nc, b, DBG = emit_program(stop_after=None)
    _finalize(nc, b)
    in_maps = []
    for c in range(NCORES):
        m = {k: v for k, v in W.items() if k != "x"}
        m["x"] = np.ascontiguousarray(W["x"][2 * c:2 * c + 2].reshape(TOK, D))
        in_maps.append(m)
    res = run_bass_kernel_spmd(nc, in_maps, core_ids=list(range(NCORES)))
    outs = [r["out"] for r in res.results]
    full = np.concatenate([np.asarray(o).reshape(2, SEQ, D) for o in outs], axis=0)
    return full.astype(np.float32, copy=False)


def _finalize(nc, b):
    with nc.Block() as block:
        @block.sync
        def _(e):
            b.sp.replay(e)

        @block.gpsimd
        def _(e):
            b.gp.replay(e)

        @block.vector
        def _(e):
            b.ve.replay(e)

        @block.scalar
        def _(e):
            b.sc.replay(e)

        @block.tensor
        def _(e):
            b.te.replay(e)


# version 06
# speedup vs baseline: 1.1648x; 1.1648x over previous
import math
import contextlib
import numpy as np
import concourse.bass as bass
import concourse.mybir as mybir
from concourse.bass_utils import run_bass_kernel_spmd

F32 = mybir.dt.float32
BF16 = mybir.dt.bfloat16
U32 = mybir.dt.uint32
AF = mybir.ActivationFunctionType
ALU = mybir.AluOpType
AX = mybir.AxisListType

MAGIC = float(np.float32(3.0 * 2.0 ** 22))
NCORES = 8
TOK = 4096
SEQ = 2048
D = 1024
DI = 3072
NT = 32
NSL = 8
NCH = 8
NCHI = 24
QMAX0 = 32767.0
LN_EPS = 1e-5
RMS_EPS = 1.1920929e-07
HEADROOM = 30.0
SCALE0 = 1.0 / math.sqrt(128.0)
SCALE1 = 64.0 ** (-0.5)
EXPMASK = float(np.float32(np.frombuffer(np.uint32(0x7F800000).tobytes(), dtype=np.float32)[0]))


class Sem:
    __slots__ = ("h", "n")

    def __init__(self, h):
        self.h = h
        self.n = 0


class Eng:
    def __init__(self, name):
        self.name = name
        self.items = []
        self.auto_sem = None

    def wait(self, sem, val=None):
        if isinstance(sem, tuple):
            sem, val = sem
        h = sem.h if isinstance(sem, Sem) else sem
        self.items.append(("w", h, int(val)))

    def op(self, method, *args, inc=None, **kwargs):
        pairs = []
        if inc:
            for s, k in inc:
                s.n += k
                pairs.append((s.h, k))
        if (self.auto_sem is not None and not pairs
                and not method.startswith("dma")
                and method not in ("collective_compute",)):
            self.auto_sem.n += 1
            pairs.append((self.auto_sem.h, 1))
        self.items.append(("o", method, args, kwargs, pairs))

    def replay(self, eng):
        for it in self.items:
            if it[0] == "w":
                eng.wait_ge(it[1], it[2])
            else:
                _, method, args, kwargs, pairs = it
                ins = getattr(eng, method)(*args, **kwargs)
                for h, k in pairs:
                    ins = ins.then_inc(h, k)


class B:
    def __init__(self, nc):
        self.nc = nc
        self.es = contextlib.ExitStack()
        self.sp = Eng("sync")
        self.gp = Eng("gpsimd")
        self.ve = Eng("vector")
        self.sc = Eng("scalar")
        self.te = Eng("tensor")
        self.engines = [self.sp, self.gp, self.ve, self.sc, self.te]
        self.S = {}
        for name in ("sp16", "gp16", "gp1", "cc1", "ve1", "sc1", "te1", "bar"):
            self.S[name] = Sem(self.es.enter_context(nc.semaphore(name)))
        self.esem = {id(self.sp): self.S["sp16"], id(self.gp): self.S["gp1"],
                     id(self.ve): self.S["ve1"], id(self.sc): self.S["sc1"],
                     id(self.te): self.S["te1"]}
        self.gp.auto_sem = self.S["gp1"]
        self.dma_sems = []

    def sb(self, name, shape, dtype):
        return self.es.enter_context(self.nc.sbuf_tensor(name, shape, dtype))

    def ps(self, name, shape, dtype):
        return self.es.enter_context(self.nc.psum_tensor(name, shape, dtype))

    def mark(self, E):
        sem = self.esem[id(E)]
        last = E.items[-1]
        assert last[0] == "o", "mark needs an op"
        if last[4]:
            h, k = last[4][0]
            assert h is sem.h or h == sem.h, "op already marked with another sem"
            return (sem, sem.n)
        sem.n += 1
        last[4].append((sem.h, 1))
        return (sem, sem.n)

    def chain(self, E):
        tok = self.mark(E)
        E.wait(*tok)
        return tok

    def dma_sem(self, name):
        sem = Sem(self.es.enter_context(self.nc.semaphore(name)))
        self.dma_sems.append((name, sem))
        return sem

    def dma_on(self, sem, out, in_, eng=None):
        E = eng or self.sp
        E.op("dma_start", out=out, in_=in_, inc=[(sem, 16)])
        return (sem, sem.n)

    def dma(self, out, in_):
        s = self.S["sp16"]
        self.sp.op("dma_start", out=out, in_=in_, inc=[(s, 16)])
        return (s, s.n)

    def dma_gp(self, out, in_):
        s = self.S["gp16"]
        self.gp.op("dma_start", out=out, in_=in_, inc=[(s, 16)])
        return (s, s.n)

    def barrier(self):
        bar = self.S["bar"]
        self.sp.wait(self.S["sp16"], self.S["sp16"].n)
        self.gp.wait(self.S["gp16"], self.S["gp16"].n)
        self.gp.wait(self.S["cc1"], self.S["cc1"].n)
        self.gp.wait(self.S["gp1"], self.S["gp1"].n)
        for _, sem in self.dma_sems:
            self.sp.wait(sem, sem.n)
        for e in self.engines:
            e.op("nop", inc=[(bar, 1)])
        tgt = bar.n
        for e in self.engines:
            e.wait(bar, tgt)


def host_prep(inputs):
    ins = {k: np.asarray(v) for k, v in inputs.items()}
    f = np.float32
    W = {}
    W["x"] = np.ascontiguousarray(ins["L0_attn_x"].astype(f, copy=False))
    W["wqkT"] = np.ascontiguousarray(ins["L0_attn_proj_qk_w"].T.astype(f))
    W["wvT"] = np.ascontiguousarray(ins["L0_attn_proj_v_w"].T.astype(f))
    wq, wk, wv = np.split(ins["L0_attn_in_proj_w"].astype(f), 3, axis=0)
    W["wqT"] = np.ascontiguousarray(wq.T)
    W["wkT"] = np.ascontiguousarray(wk.T)
    W["wvvT"] = np.ascontiguousarray(wv.T)
    W["woT"] = np.ascontiguousarray(ins["L0_attn_out_proj_w"].T.astype(f))
    bq, bk, bv = np.split(ins["L0_attn_in_proj_b"].astype(f), 3)

    def chunked(v):
        return np.ascontiguousarray(np.asarray(v, f).reshape(8, 128).T)

    W["b_qk"] = chunked(ins["L0_attn_proj_qk_b"])
    W["b_v"] = chunked(ins["L0_attn_proj_v_b"])
    W["b_q"] = chunked(bq)
    W["b_k"] = chunked(bk)
    W["b_vv_row"] = np.ascontiguousarray(bv.reshape(1, D))
    W["b_o_row"] = np.ascontiguousarray(ins["L0_attn_out_proj_b"].astype(f).reshape(1, D))
    W["w_gate"] = np.ascontiguousarray(ins["L0_ffn_w_gate"].astype(f))
    W["w_up"] = np.ascontiguousarray(ins["L0_ffn_w_up"].astype(f))
    W["w_down"] = np.ascontiguousarray(ins["L0_ffn_w_down"].astype(f))
    def pchunk(a):
        c = a.shape[0] // 128
        return np.ascontiguousarray(
            a.reshape(c, 128, a.shape[1]).transpose(1, 0, 2).reshape(128, -1).astype(f))

    W["W_DQ"] = pchunk(ins["L1_attn_W_DQ"].astype(f))
    W["W_UQ_T"] = pchunk(ins["L1_attn_W_UQ"].T.astype(f))
    W["W_DKV"] = pchunk(ins["L1_attn_W_DKV"].astype(f))
    W["W_UKV_T"] = pchunk(np.ascontiguousarray(ins["L1_attn_W_UKV"].T.astype(f)))
    W["W_QR_T"] = np.ascontiguousarray(ins["L1_attn_W_QR"].T.astype(f))
    W["W_KR_T"] = np.ascontiguousarray(ins["L1_attn_W_KR"].T.astype(f))
    W["W_O_T"] = np.ascontiguousarray(ins["L1_attn_W_O"].T.astype(f))
    W["qnw"] = np.ascontiguousarray(np.broadcast_to(ins["L1_attn_q_norm_w"].astype(f), (128, 64)))
    W["knw"] = np.ascontiguousarray(np.broadcast_to(ins["L1_attn_k_norm_w"].astype(f), (128, 64)))
    W["w1"] = np.ascontiguousarray(ins["L1_ffn_weight"].astype(f))
    W["b1_row"] = np.ascontiguousarray(ins["L1_ffn_bias"].astype(f).reshape(1, D))
    W["dw"] = np.ascontiguousarray(ins["L1_ffn_delta_w"].astype(f))
    W["dact"] = np.ascontiguousarray(ins["L1_ffn_delta_act"].astype(f).reshape(1, 1))
    W["ident"] = np.eye(128, dtype=f)
    k_ = np.arange(128)[:, None]
    q_ = np.arange(128)[None, :]
    seg = ((k_ // 4) == (q_ // 4)) & (k_ != q_)
    W["seg_mask"] = np.where(seg, f(0.0), f(-1e9)).astype(f)
    cm = np.zeros((4, 128, 512), dtype=f)
    qq = np.arange(512)[None, :]
    for j in range(4):
        cm[j] = np.where((k_ + 128 * j) <= qq, f(0.0), f(-1e9))
    W["causal_masks"] = np.ascontiguousarray(cm.transpose(1, 0, 2).reshape(128, 4 * 512))
    W["ones_row"] = np.ones((1, 128), dtype=f)
    return W



def emit_program(stop_after=None):
    nc = bass.Bass("TRN2", target_bir_lowering=False)
    dram = {}

    def din(name, shape, dtype=F32):
        dram[name] = nc.dram_tensor(name, list(shape), dtype, kind="ExternalInput")
        return dram[name]

    def dint(name, shape, dtype=BF16):
        dram[name] = nc.dram_tensor(name, list(shape), dtype)
        return dram[name]

    x_in = din("x", (TOK, D))
    out_t = nc.dram_tensor("out", [TOK, D], F32, kind="ExternalOutput")
    for nm, shp in [("wqkT", (D, D)), ("wvT", (D, D)), ("wqT", (D, D)),
                    ("wkT", (D, D)), ("wvvT", (D, D)), ("woT", (D, D)),
                    ("b_qk", (128, 8)), ("b_v", (128, 8)), ("b_q", (128, 8)),
                    ("b_k", (128, 8)), ("b_vv_row", (1, D)), ("b_o_row", (1, D)),
                    ("w_gate", (DI, D)), ("w_up", (DI, D)), ("w_down", (D, DI)),
                    ("W_DQ", (128, 256)), ("W_UQ_T", (128, 256)), ("W_DKV", (128, 512)),
                    ("W_UKV_T", (128, 1024)), ("W_QR_T", (64, 64)), ("W_KR_T", (64, 64)),
                    ("W_O_T", (64, 64)), ("qnw", (128, 64)), ("knw", (128, 64)),
                    ("w1", (D, D)), ("b1_row", (1, D)), ("dw", (D, 32)),
                    ("dact", (1, 1)), ("ident", (128, 128)), ("seg_mask", (128, 128)),
                    ("causal_masks", (128, 4 * 512)), ("ones_row", (1, 128))]:
        din(nm, shp)
    xnfm_sp = dint("xnfm_sp", (NCH, NSL, 128, 512))
    u_sp = dint("u_sp", (NCHI, NSL, 128, 512))
    g_sp = dint("g_sp", (NCHI, NSL, 128, 512))
    p_sp = dint("p_sp", (NCHI, NSL, 128, 512))
    o_sp = dint("o_sp", (NCHI, NSL, 128, 512))
    wdnT_sp = dint("wdnT_sp", (NCHI, 128, D))
    qk_dr = dint("qk_dr", (NCH, NSL, 128, 512))
    vin_dr = dint("vin_dr", (NCH, NSL, 128, 512))
    q_dr = dint("q_dr", (NCH, NSL, 128, 512))
    k_dr = dint("k_dr", (NCH, NSL, 128, 512))
    vtok_dr = dint("vtok_dr", (NSL, 128, 4 * D))
    ccin = [nc.dram_tensor(f"ccin{i}", [1, 64], F32) for i in range(4)]
    scr_row = nc.dram_tensor("scr_row", [1, 16], F32)
    ccout = [nc.dram_tensor(f"ccout{i}", [1, 64], F32) for i in range(4)]

    b = B(nc)
    S = b.S
    sem_xnsp = b.dma_sem("xnsp")
    sem_wload = b.dma_sem("wload")
    sem_wslot = b.dma_sem("wslotld")
    sem_st = [b.dma_sem("st0"), b.dma_sem("st1")]
    sem_ld = [b.dma_sem("ld0"), b.dma_sem("ld1")]
    sem_aux = b.dma_sem("auxd")
    sem_ccd = b.dma_sem("ccd")
    sem_atld = b.dma_sem("atld")
    sem_out = b.dma_sem("outd")
    sem_wdnsl = [b.dma_sem(f"wdnsl{i}") for i in range(8)]
    sp16, gp16, gp1, cc1, ve1, sc1, te1 = (S["sp16"], S["gp16"], S["gp1"],
                                           S["cc1"], S["ve1"], S["sc1"], S["te1"])

    x_sb = b.sb("x_sb", [128, NT * D], BF16)
    arena = b.sb("arena", [128, NT * D], BF16)
    wslot = b.sb("wslot", [128, 2 * 8 * D], BF16)
    xnslab = b.sb("xnslab", [128, NCH * 512], BF16)
    stats_sb = b.sb("stats_sb", [128, NT * 2], F32)
    t_a = b.sb("t_a", [128, 1024], F32)
    t_b = b.sb("t_b", [128, 1024], F32)
    t_d = b.sb("t_d", [128, 512], F32)
    t_e = b.sb("t_e", [128, 512], F32)
    t_f = b.sb("t_f", [128, 512], F32)
    t_bf = b.sb("t_bf", [128, 1024], BF16)
    stat12 = b.sb("stat12", [128, 12], F32)
    mv = b.sb("mv", [128, 8], F32)
    maxacc = b.sb("maxacc", [128, 8], F32)
    maxqk = b.sb("maxqk", [128, 128], F32)
    rowbuf = b.sb("rowbuf", [1, 256], F32)
    scal = b.sb("scal", [128, 32], F32)
    consts = b.sb("consts", [128, 8], F32)
    ident_sb = b.sb("ident_sb", [128, 128], BF16)
    segm_sb = b.sb("segm_sb", [128, 128], F32)
    cmask_sb = b.sb("cmask_sb", [128, 4 * 512], BF16)
    onesrow_sb = b.sb("onesrow_sb", [1, 128], F32)
    onesrow_bf = b.sb("onesrow_bf", [1, 128], BF16)
    onescol_bf = b.sb("onescol_bf", [128, 1], BF16)
    onescol_f32 = b.sb("onescol_f32", [128, 1], F32)
    negsh_sb = b.sb("negsh_sb", [128, 64], F32)
    bias_sb = b.sb("bias_sb", [128, 4 * 8], F32)
    brow_sb = b.sb("brow_sb", [1, 3 * D], BF16)
    qnw_sb = b.sb("qnw_sb", [128, 64], F32)
    knw_sb = b.sb("knw_sb", [128, 64], F32)
    bd_sb = b.sb("bd_sb", [128, 6 * 128], BF16)
    sc_blk = b.sb("sc_blk", [128, 4 * 32], F32)
    spst = b.sb("spst", [128, 2 * 512], BF16)
    ldst = b.sb("ldst", [128, 4 * 512], BF16)
    psf = [b.ps(f"psf{i}", [128, 512], F32) for i in range(6)]
    psb = [b.ps(f"psb{i}", [128, 1024], BF16) for i in range(2)]

    DBG = {}

    def dbg_out(tag, ap, shape, dtype):
        t = nc.dram_tensor("dbg_" + tag, list(shape), dtype, kind="ExternalOutput")
        b.barrier()
        idx = tuple(slice(None) for _ in shape)
        b.dma(out=t[idx], in_=ap)
        b.barrier()
        DBG[tag] = True

    sp, gp, ve, sc, te = b.sp, b.gp, b.ve, b.sc, b.te

    for t in range(NT):
        b.dma_gp(out=x_sb[:, t * D:(t + 1) * D], in_=x_in[t * 128:(t + 1) * 128, :])
    b.dma_gp(out=ident_sb[:, :], in_=dram["ident"][:, :])
    b.dma(out=segm_sb[:, :], in_=dram["seg_mask"][:, :])
    b.dma_gp(out=cmask_sb[:, :], in_=dram["causal_masks"][:, :])
    b.dma(out=onesrow_sb[:, :], in_=dram["ones_row"][:, :])
    b.dma_gp(out=onesrow_bf[:, :], in_=dram["ones_row"][:, :])
    for i, nm in enumerate(("b_qk", "b_v", "b_q", "b_k")):
        b.dma(out=bias_sb[:, i * 8:(i + 1) * 8], in_=dram[nm][:, :])
    for i, nm in enumerate(("b_vv_row", "b_o_row", "b1_row")):
        b.dma_gp(out=brow_sb[:, i * D:(i + 1) * D], in_=dram[nm][:, :])
    b.dma(out=qnw_sb[:, :], in_=dram["qnw"][:, :])
    b.dma(out=knw_sb[:, :], in_=dram["knw"][:, :])
    gp.op("memset", onescol_bf[:, :], 1.0)
    gp.op("memset", onescol_f32[:, :], 1.0)
    gp.op("memset", rowbuf[0:1, :], 0.0)
    gp.op("memset", consts[:, 0:1], LN_EPS)
    gp.op("memset", consts[:, 1:2], MAGIC)
    gp.op("memset", consts[:, 2:3], RMS_EPS)
    gp.op("memset", consts[:, 3:4], -HEADROOM)
    gp.op("memset", consts[:, 4:5], float(np.log(0.25)))
    gp.op("memset", maxacc[:, :], 0.0)
    gp.op("memset", maxqk[:, :], 0.0)
    b.barrier()

    if stop_after == "prologue":
        dbg_out("x0", x_sb[:, 0:D], (128, D), BF16)
        b.es.pop_all()
        return nc, b, DBG


    def ln_stats(tag):
        for t in range(NT):
            xt = x_sb[:, t * D:(t + 1) * D]
            ve.op("bn_stats", stat12[:, 0:6], xt[:, 0:512])
            ve.op("bn_stats", stat12[:, 6:12], xt[:, 512:1024])
            b.chain(ve)
            ve.op("bn_aggr", mv[:, 0:2], stat12[:, 0:12])
            tok = b.mark(ve)
            sc.wait(*tok)
            sc.op("activation", mv[:, 2:3], mv[:, 1:2], AF.Sqrt,
                  bias=consts[:, 0:1], scale=1.0)
            tok = b.mark(sc)
            ve.wait(*tok)
            ve.op("reciprocal", stats_sb[:, 2 * t:2 * t + 1], mv[:, 2:3])
            b.chain(ve)
            ve.op("tensor_scalar", stats_sb[:, 2 * t + 1:2 * t + 2], mv[:, 0:1],
                  stats_sb[:, 2 * t:2 * t + 1], -1.0, ALU.mult, ALU.mult)
            b.chain(ve)

    def xn_tile(t, dst):
        sc.op("activation", dst, x_sb[:, t * D:(t + 1) * D], AF.Identity,
              bias=stats_sb[:, 2 * t + 1:2 * t + 2],
              scale=stats_sb[:, 2 * t:2 * t + 1])
        return b.mark(sc)

    ln_stats("L0")
    b.barrier()

    for s in range(NSL):
        for j in range(4):
            t = s * 4 + j
            tok = xn_tile(t, t_bf[:, :])
            ve.wait(*tok)
            ve.op("tensor_reduce", mv[:, 3:4], t_bf[:, :], AX.X, ALU.max,
                  apply_absolute_value=True)
            b.chain(ve)
            ve.op("tensor_max", maxacc[:, 0:1], maxacc[:, 0:1], mv[:, 3:4])
            b.chain(ve)
            te.wait(*tok)
            ttok = None
            for c in range(NCH):
                bank = psb[c % 2]
                te.op("transpose", bank[:, 0:128], t_bf[:, c * 128:(c + 1) * 128],
                      ident_sb[:, :])
                ttok = b.mark(te)
                ve.wait(*ttok)
                ve.op("tensor_copy", xnslab[:, c * 512 + j * 128: c * 512 + (j + 1) * 128],
                      bank[:, 0:128])
                etok = b.mark(ve)
                te.wait(*etok)
            sc.wait(*ttok)
            sc.wait(*etok)
        cptok = b.mark(ve)
        sp.wait(*cptok)
        ve.wait(*cptok)
        sptoks = []
        for c in range(NCH):
            ve.op("tensor_copy", arena[:, c * NSL * 512 + s * 512:
                                       c * NSL * 512 + (s + 1) * 512],
                  xnslab[:, c * 512:(c + 1) * 512])
            sptoks.append(b.dma_on(sem_xnsp, out=xnfm_sp[c, s, :, :],
                                   in_=xnslab[:, c * 512:(c + 1) * 512]))
        b.chain(ve)
        te.wait(*sptoks[-1])
    b.barrier()

    if stop_after == "xn":
        dbg_out("xnfm0", arena[:, 0:512], (128, 512), BF16)
        dbg_out("stats", stats_sb[:, :], (128, NT * 2), F32)
        dbg_out("maxacc", maxacc[:, 0:1], (128, 1), F32)
        b.es.pop_all()
        return nc, b, DBG


    prr_guard = [None]

    def part_reduce_to_row(col_ap, slot):
        if prr_guard[0] is not None:
            sp.wait(*prr_guard[0])
        tok = b.dma_on(sem_aux, out=rowbuf[0:1, 0:128], in_=col_ap)
        ve.wait(*tok)
        ve.op("tensor_reduce", rowbuf[0:1, slot:slot + 1], rowbuf[0:1, 0:128],
              AX.X, ALU.max)
        prr_guard[0] = b.mark(ve)
        return prr_guard[0]

    def bcast_row(cols):
        srcs = [c[0] for c in cols]
        dsts = [c[1] for c in cols]
        k = len(cols)
        assert dsts == list(range(dsts[0], dsts[0] + k))
        for j, src in enumerate(srcs):
            ve.op("tensor_copy", rowbuf[0:1, 200 + j:201 + j],
                  rowbuf[0:1, src:src + 1])
        vtok = b.mark(ve)
        gp.wait(*vtok)
        t1 = b.dma_on(sem_ccd, out=scr_row[0:1, 0:k],
                      in_=rowbuf[0:1, 200:200 + k], eng=gp)
        gp.wait(*t1)
        t2 = b.dma_on(sem_ccd, out=scal[:, dsts[0]:dsts[0] + k],
                      in_=scr_row[0:1, 0:k].broadcast_to([128, k]), eng=gp)
        return t2

    def collective(round_idx, n):
        tok = b.dma_on(sem_ccd, out=ccin[round_idx][0:1, 0:64],
                       in_=rowbuf[0:1, 134:134 + 64], eng=gp)
        gp.wait(*tok)
        gp.op("collective_compute", "AllReduce", ALU.max,
              replica_groups=[list(range(NCORES))],
              ins=[ccin[round_idx][:, :].opt()],
              outs=[ccout[round_idx][:, :].opt()], inc=[(cc1, 1)])
        gp.wait(cc1, cc1.n)
        tok = b.dma_on(sem_ccd, out=rowbuf[0:1, 144:144 + n],
                       in_=ccout[round_idx][0:1, 0:n], eng=gp)
        return tok

    def row_ts(dst, src, s1, s2, op0, op1=ALU.bypass):
        ve.op("tensor_scalar", rowbuf[0:1, dst:dst + 1], rowbuf[0:1, src:src + 1],
              s1, s2, op0, op1)
        b.chain(ve)

    def row_recip(dst, src):
        ve.op("reciprocal", rowbuf[0:1, dst:dst + 1], rowbuf[0:1, src:src + 1])
        b.chain(ve)

    def row_mul(dst, a, bcol):
        ve.op("tensor_tensor", rowbuf[0:1, dst:dst + 1], rowbuf[0:1, a:a + 1],
              rowbuf[0:1, bcol:bcol + 1], ALU.mult)
        b.chain(ve)

    tok = part_reduce_to_row(maxacc[:, 0:1], 134)
    ve.wait(*tok)
    gp.wait(ve1, ve1.n)
    tok = collective(0, 1)
    ve.wait(*tok)
    row_ts(128, 144, 1.0 / QMAX0, 1e-12, ALU.mult, ALU.max)
    row_recip(133, 128)
    bcast_row([(133, 0)])
    b.barrier()

    if stop_after == "ar1":
        dbg_out("scal0", scal[:, 0:8], (128, 8), F32)
        dbg_out("rowb", rowbuf[0:1, :], (1, 256), F32)
        b.es.pop_all()
        return nc, b, DBG

    for c in range(NCH):
        for s in range(NSL):
            seg = arena[:, c * NSL * 512 + s * 512: c * NSL * 512 + (s + 1) * 512]
            ve.op("tensor_scalar", t_d[:, :], seg, scal[:, 0:1], MAGIC,
                  ALU.mult, ALU.add)
            b.chain(ve)
            ve.op("tensor_scalar", seg, t_d[:, :], MAGIC, None, ALU.subtract)
            b.chain(ve)
    b.barrier()

    if stop_after == "xq":
        dbg_out("xq0", arena[:, 0:512], (128, 512), BF16)
        b.es.pop_all()
        return nc, b, DBG

    def dequant_lpbq_tile(src_dram, row0, dst_bf, guard, dst_guard=None):
        if guard is not None:
            sp.wait(*guard)
        tok = b.dma_on(sem_wload, out=t_a[:, :], in_=src_dram[row0:row0 + 128, :])
        ve.wait(*tok)
        if dst_guard is not None:
            ve.wait(*dst_guard)
        ve.op("tensor_reduce",
              sc_blk[:, 0:32].rearrange("p (g o) -> p g o", o=1),
              t_a[:, :].rearrange("p (g k) -> p g k", k=32),
              AX.X, ALU.max, apply_absolute_value=True)
        b.chain(ve)
        ve.op("tensor_scalar", sc_blk[:, 32:64], sc_blk[:, 0:32], 1.0 / 7.0,
              1e-12, ALU.mult, ALU.max)
        b.chain(ve)
        ve.op("reciprocal", sc_blk[:, 64:96], sc_blk[:, 32:64])
        b.chain(ve)
        ve.op("tensor_tensor", t_b[:, :].rearrange("p (g k) -> p g k", k=32),
              t_a[:, :].rearrange("p (g k) -> p g k", k=32),
              sc_blk[:, 64:96].rearrange("p (g o) -> p g o", o=1).broadcast_to([128, 32, 32]),
              ALU.mult)
        b.chain(ve)
        ve.op("tensor_scalar", t_a[:, :], t_b[:, :], MAGIC, MAGIC, ALU.add, ALU.subtract)
        b.chain(ve)
        ve.op("tensor_tensor", dst_bf.rearrange("p (g k) -> p g k", k=32),
              t_a[:, :].rearrange("p (g k) -> p g k", k=32),
              sc_blk[:, 32:64].rearrange("p (g o) -> p g o", o=1).broadcast_to([128, 32, 32]),
              ALU.mult)
        return b.mark(ve)

    def transpose_into_wslot(src_bf_tok, slot_base, out_col, wtok):
        last = None
        ttok = None
        for k in range(NCH):
            bank = psb[k % 2]
            te.wait(*wtok)
            te.op("transpose", bank[:, 0:128], src_bf_tok[:, k * 128:(k + 1) * 128],
                  ident_sb[:, :])
            ttok = b.mark(te)
            ve.wait(*ttok)
            ve.op("tensor_copy",
                  wslot[:, slot_base + k * 512 + out_col: slot_base + k * 512 + out_col + 128],
                  bank[:, 0:128])
            last = b.mark(ve)
            te.wait(*last)
        return last, ttok

    UP_SLOT, GATE_SLOT = 0, 8 * 512
    pst_state = [None, None]
    red_state = [None, None]

    deq_guard = None
    wslot_guard = None
    tbf_guard = None
    NCB = 1 if stop_after in ("upgate1", "dequant") else 6
    for cb in range(NCB):
        for which, (src, base) in enumerate([(dram["w_up"], UP_SLOT),
                                             (dram["w_gate"], GATE_SLOT)]):
            for rt in range(4):
                row0 = cb * 512 + rt * 128
                if wslot_guard is not None:
                    ve.wait(*wslot_guard)
                    wslot_guard = None
                dtok = dequant_lpbq_tile(src, row0, t_bf[:, :], deq_guard,
                                          dst_guard=tbf_guard)
                _, tbf_guard = transpose_into_wslot(t_bf[:, :], base, rt * 128, dtok)
                deq_guard = dtok
        wtok = b.mark(ve)
        te.wait(*wtok)
        if stop_after == "dequant":
            break
        for which, (base, spdst, mcol) in enumerate([(UP_SLOT, u_sp, 2),
                                                     (GATE_SLOT, g_sp, 3)]):
            for cc in range(4):
                ch = cb * 4 + cc
                for s in range(NSL):
                    bank = psf[(s + cc) % 2]
                    for k in range(NCH):
                        te.op("matmul", bank[:, :],
                              wslot[:, base + k * 512 + cc * 128: base + k * 512 + (cc + 1) * 128],
                              arena[:, k * NSL * 512 + s * 512: k * NSL * 512 + (s + 1) * 512],
                              start=(k == 0), stop=(k == NCH - 1))
                    mtok = b.mark(te)
                    half = s % 2
                    stg = spst[:, half * 512:(half + 1) * 512]
                    sc.wait(*mtok)
                    if pst_state[half] is not None:
                        sc.wait(*pst_state[half])
                    if red_state[half] is not None:
                        sc.wait(*red_state[half])
                    sc.op("activation", stg, bank[:, :], AF.Identity, bias=0.0, scale=1.0)
                    etok = b.mark(sc)
                    te.wait(*etok)
                    ve.wait(*etok)
                    ve.op("tensor_reduce", mv[:, 3:4], stg, AX.X, ALU.max,
                          apply_absolute_value=True)
                    redtok = b.chain(ve)
                    ve.op("tensor_max", maxacc[:, mcol:mcol + 1],
                          maxacc[:, mcol:mcol + 1], mv[:, 3:4])
                    b.chain(ve)
                    sp.wait(*etok)
                    dtok = b.dma_on(sem_st[half], out=spdst[ch, s, :, :], in_=stg)
                    pst_state[half] = dtok
                    red_state[half] = redtok
                    wslot_guard = mtok
    b.barrier()

    if stop_after == "dequant":
        dbg_out("wslot0", wslot[:, 0:512], (128, 512), BF16)
        b.es.pop_all()
        return nc, b, DBG

    if stop_after in ("upgate", "upgate1"):
        dbg_out("maxug", maxacc[:, 2:4], (128, 2), F32)
        b.es.pop_all()
        return nc, b, DBG

    tok = part_reduce_to_row(maxacc[:, 2:3], 134)
    ve.wait(*tok)
    tok = part_reduce_to_row(maxacc[:, 3:4], 135)
    ve.wait(*tok)
    gp.wait(ve1, ve1.n)
    tok = collective(1, 2)
    ve.wait(*tok)
    row_mul(145, 144, 128)
    row_ts(129, 145, 1.0 / QMAX0, 1e-12, ALU.mult, ALU.max)
    row_mul(146, 144 + 1, 128)
    row_ts(130, 146, 1.0 / QMAX0, 1e-12, ALU.mult, ALU.max)
    row_recip(147, 129)
    row_mul(148, 147, 128)
    row_recip(149, 130)
    row_mul(150, 149, 128)
    row_ts(151, 130, -MAGIC, None, ALU.mult)
    bcast_row([(150, 1), (130, 2), (151, 3), (148, 4)])
    b.barrier()

    p1_iters = [(ch, s) for ch in range(NCHI) for s in range(NSL)]
    n_p1 = len(p1_iters)
    ld_g4 = [None] * 4
    st_g4 = [None] * 4
    td_rd = [None, None]
    sig_rd = [None, None]
    amark = [None] * n_p1
    bmark = [None] * n_p1
    tdbuf = (t_d, t_e)
    sigbuf = (t_a[:, 0:512], t_a[:, 512:1024])

    def p1_A(i):
        ch, s = p1_iters[i]
        q = i % 4
        p = i % 2
        stg = ldst[:, q * 512:(q + 1) * 512]
        if ld_g4[q] is not None:
            sp.wait(*ld_g4[q])
        ltok = b.dma_on(sem_ld[q % 2], out=stg, in_=g_sp[ch, s, :, :])
        ve.wait(*ltok)
        if td_rd[p] is not None:
            ve.wait(*td_rd[p])
        ve.op("tensor_scalar", tdbuf[p][:, :], stg, scal[:, 1:2], MAGIC,
              ALU.mult, ALU.add)
        amark[i] = b.mark(ve)
        ld_g4[q] = amark[i]

    def p1_B(i):
        p = i % 2
        sc.wait(*amark[i])
        if sig_rd[p] is not None:
            sc.wait(*sig_rd[p])
        sc.op("activation", sigbuf[p], tdbuf[p][:, :], AF.Sigmoid,
              bias=scal[:, 3:4], scale=scal[:, 2:3])
        bmark[i] = b.mark(sc)
        td_rd[p] = bmark[i]

    def p1_C(i):
        ch, s = p1_iters[i]
        p = i % 2
        q = i % 4
        ve.wait(*bmark[i])
        ve.op("tensor_scalar", sigbuf[p], sigbuf[p], 65536.0, 65535.0,
              ALU.mult, ALU.min)
        b.chain(ve)
        ve.op("tensor_scalar", sigbuf[p], sigbuf[p], MAGIC, MAGIC,
              ALU.add, ALU.subtract)
        b.chain(ve)
        pstg = xnslab[:, (4 + q) * 512:(5 + q) * 512]
        if st_g4[q] is not None:
            ve.wait(*st_g4[q])
        ve.op("scalar_tensor_tensor", pstg, tdbuf[p][:, :], MAGIC, sigbuf[p],
              ALU.subtract, ALU.mult)
        ptok = b.mark(ve)
        sig_rd[p] = ptok
        ve.wait(*ptok)
        ve.op("tensor_reduce", mv[:, 3:4], pstg, AX.X, ALU.max,
              apply_absolute_value=True)
        b.chain(ve)
        ve.op("tensor_max", maxacc[:, 4:5], maxacc[:, 4:5], mv[:, 3:4])
        b.chain(ve)
        sp.wait(*ptok)
        st_g4[q] = b.dma_on(sem_st[q % 2], out=p_sp[ch, s, :, :], in_=pstg)

    for i in range(n_p1):
        p1_A(i)
        p1_B(i)
        if i >= 1:
            p1_C(i - 1)
    p1_C(n_p1 - 1)
    b.barrier()

    tok = part_reduce_to_row(maxacc[:, 4:5], 134)
    ve.wait(*tok)
    gp.wait(ve1, ve1.n)
    tok = collective(2, 1)
    ve.wait(*tok)
    row_ts(152, 130, 1.0 / 65536.0, None, ALU.mult)
    row_mul(153, 144, 152)
    row_ts(131, 153, 1.0 / QMAX0, 1e-12, ALU.mult, ALU.max)
    row_recip(154, 131)
    row_mul(155, 152, 154)
    bcast_row([(155, 5)])
    b.barrier()

    ld_g4 = [None] * 4
    ldu_g4 = [None] * 4
    st_g4 = [None] * 4
    ru_rd = [None, None]
    amark = [None] * n_p1
    bmark = [None] * n_p1

    def p2_A(i):
        ch, s = p1_iters[i]
        q = i % 4
        p = i % 2
        stgp = ldst[:, q * 512:(q + 1) * 512]
        stgu = xnslab[:, q * 512:(q + 1) * 512]
        if ld_g4[q] is not None:
            sp.wait(*ld_g4[q])
        if ldu_g4[q] is not None:
            sp.wait(*ldu_g4[q])
        ltok1 = b.dma_on(sem_ld[q % 2], out=stgp, in_=p_sp[ch, s, :, :])
        ltok2 = b.dma_on(sem_ld[q % 2], out=stgu, in_=u_sp[ch, s, :, :])
        ve.wait(*ltok2)
        ve.op("tensor_scalar", tdbuf[p][:, :], stgp, scal[:, 5:6], MAGIC,
              ALU.mult, ALU.add)
        amark[i] = b.mark(ve)
        ld_g4[q] = amark[i]
        sc.wait(*ltok2)
        if ru_rd[p] is not None:
            sc.wait(*ru_rd[p])
        sc.op("activation", sigbuf[p], stgu, AF.Identity,
              bias=consts[:, 1:2], scale=scal[:, 4:5])
        bmark[i] = b.mark(sc)
        ldu_g4[q] = bmark[i]

    def p2_C(i):
        ch, s = p1_iters[i]
        p = i % 2
        q = i % 4
        ve.wait(*bmark[i])
        ve.op("tensor_scalar", sigbuf[p], sigbuf[p], MAGIC, None, ALU.subtract)
        b.chain(ve)
        pstg = xnslab[:, (4 + q) * 512:(5 + q) * 512]
        if st_g4[q] is not None:
            ve.wait(*st_g4[q])
        ve.op("scalar_tensor_tensor", pstg, tdbuf[p][:, :], MAGIC, sigbuf[p],
              ALU.subtract, ALU.mult)
        ptok = b.mark(ve)
        ru_rd[p] = ptok
        ve.wait(*ptok)
        ve.op("tensor_reduce", mv[:, 3:4], pstg, AX.X, ALU.max,
              apply_absolute_value=True)
        b.chain(ve)
        ve.op("tensor_max", maxacc[:, 5:6], maxacc[:, 5:6], mv[:, 3:4])
        b.chain(ve)
        sp.wait(*ptok)
        st_g4[q] = b.dma_on(sem_st[q % 2], out=o_sp[ch, s, :, :], in_=pstg)

    for i in range(n_p1):
        p2_A(i)
        if i >= 1:
            p2_C(i - 1)
    p2_C(n_p1 - 1)
    b.barrier()

    tok = part_reduce_to_row(maxacc[:, 5:6], 134)
    ve.wait(*tok)
    gp.wait(ve1, ve1.n)
    tok = collective(3, 1)
    ve.wait(*tok)
    row_mul(156, 131, 129)
    row_mul(157, 144, 156)
    row_ts(132, 157, 1.0 / QMAX0, 1e-12, ALU.mult, ALU.max)
    row_recip(158, 132)
    row_mul(159, 156, 158)
    bcast_row([(159, 6), (132, 7)])
    b.barrier()

    wdn_guard = deq_guard
    wdn_sl_guard = [None] * 8
    for rt in range(8):
        if wdn_guard is not None:
            sp.wait(*wdn_guard)
        tok = b.dma_on(sem_wload, out=t_a[:, :],
                       in_=dram["w_down"][rt * 128:(rt + 1) * 128, 0:1024])
        for part in range(3):
            if part > 0:
                sp.wait(*wdn_guard)
                tok = b.dma_on(sem_wload, out=t_a[:, :],
                               in_=dram["w_down"][rt * 128:(rt + 1) * 128,
                                                  part * 1024:(part + 1) * 1024])
            ve.wait(*tok)
            ve.op("tensor_reduce",
                  sc_blk[:, 0:32].rearrange("p (g o) -> p g o", o=1),
                  t_a[:, :].rearrange("p (g k) -> p g k", k=32),
                  AX.X, ALU.max, apply_absolute_value=True)
            b.chain(ve)
            ve.op("tensor_scalar", sc_blk[:, 32:64], sc_blk[:, 0:32], 1.0 / 7.0,
                  1e-12, ALU.mult, ALU.max)
            b.chain(ve)
            ve.op("reciprocal", sc_blk[:, 64:96], sc_blk[:, 32:64])
            b.chain(ve)
            ve.op("tensor_tensor", t_b[:, :].rearrange("p (g k) -> p g k", k=32),
                  t_a[:, :].rearrange("p (g k) -> p g k", k=32),
                  sc_blk[:, 64:96].rearrange("p (g o) -> p g o", o=1).broadcast_to([128, 32, 32]),
                  ALU.mult)
            b.chain(ve)
            ve.op("tensor_scalar", t_a[:, :], t_b[:, :], MAGIC, MAGIC,
                  ALU.add, ALU.subtract)
            b.chain(ve)
            ve.op("tensor_tensor", t_bf[:, :].rearrange("p (g k) -> p g k", k=32),
                  t_a[:, :].rearrange("p (g k) -> p g k", k=32),
                  sc_blk[:, 32:64].rearrange("p (g o) -> p g o", o=1).broadcast_to([128, 32, 32]),
                  ALU.mult)
            wtok = b.mark(ve)
            wdn_guard = wtok
            for k in range(NCH):
                bank = psb[k % 2]
                te.wait(*wtok)
                te.op("transpose", bank[:, 0:128], t_bf[:, k * 128:(k + 1) * 128],
                      ident_sb[:, :])
                ttok = b.mark(te)
                ve.wait(*ttok)
                if wdn_sl_guard[k] is not None:
                    ve.wait(*wdn_sl_guard[k])
                ve.op("tensor_copy",
                      spst[:, (k % 8) * 128:(k % 8 + 1) * 128], bank[:, 0:128])
                etok = b.mark(ve)
                te.wait(*etok)
                sp.wait(*etok)
                wsp_tok = b.dma_on(sem_wdnsl[k],
                                   out=wdnT_sp[part * 8 + k, :, rt * 128:(rt + 1) * 128],
                                   in_=spst[:, (k % 8) * 128:(k % 8 + 1) * 128])
                wdn_sl_guard[k] = wsp_tok
    b.barrier()

    ld_g4 = [None] * 4
    wg4 = [None] * 4
    region_rd = [None, None]
    oq_done = [None] * NSL

    def p3_quant(s):
        rb = (s % 2) * 12288
        waited_r = False
        tok = None
        for ch in range(NCHI):
            q = ch % 4
            stg = ldst[:, q * 512:(q + 1) * 512]
            if ld_g4[q] is not None:
                sp.wait(*ld_g4[q])
            ltok = b.dma_on(sem_ld[q % 2], out=stg, in_=o_sp[ch, s, :, :])
            ve.wait(*ltok)
            ve.op("tensor_scalar", t_d[:, :], stg, scal[:, 6:7], MAGIC,
                  ALU.mult, ALU.add)
            ld_g4[q] = b.mark(ve)
            ve.wait(*ld_g4[q])
            if not waited_r and region_rd[s % 2] is not None:
                ve.wait(*region_rd[s % 2])
                waited_r = True
            ve.op("tensor_scalar", arena[:, rb + ch * 512: rb + (ch + 1) * 512],
                  t_d[:, :], MAGIC, None, ALU.subtract)
            tok = b.chain(ve)
        oq_done[s] = tok

    p3_quant(0)
    for s in range(NSL):
        if s + 1 < NSL:
            p3_quant(s + 1)
        rb = (s % 2) * 12288
        te.wait(*oq_done[s])
        for hf in range(2):
            for k in range(NCHI):
                wq = k % 4
                wst = wslot[:, wq * 512:(wq + 1) * 512]
                if wg4[wq] is not None:
                    sp.wait(*wg4[wq])
                wtok = b.dma_on(sem_wslot, out=wst,
                                in_=wdnT_sp[k, :, hf * 512:(hf + 1) * 512])
                te.wait(*wtok)
                for sub in range(4):
                    te.op("matmul", psf[sub][:, :],
                          arena[:, rb + k * 512 + sub * 128: rb + k * 512 + (sub + 1) * 128],
                          wst, start=(k == 0), stop=(k == NCHI - 1))
                mtok = b.mark(te)
                wg4[wq] = mtok
            for sub in range(4):
                t = s * 4 + sub
                xsl = x_sb[:, t * D + hf * 512: t * D + (hf + 1) * 512]
                ve.wait(*mtok)
                ve.op("scalar_tensor_tensor", xsl, psf[sub][:, :], scal[:, 7:8],
                      xsl, ALU.mult, ALU.add)
                etok = b.mark(ve)
                te.wait(*etok)
        region_rd[s % 2] = mtok
    b.barrier()

    if stop_after == "ffn0":
        dbg_out("x_after_ffn0", x_sb[:, 0:D], (128, D), BF16)
        b.es.pop_all()
        return nc, b, DBG

    sem_wgp = b.dma_sem("wgp")
    wslot_rd_guard = [None, None]

    def load_wT(name, slot):
        if wslot_rd_guard[slot] is not None:
            gp.wait(*wslot_rd_guard[slot])
        base = slot * 8 * D
        tok = None
        for k in range(NCH):
            tok = b.dma_on(sem_wgp, out=wslot[:, base + k * D: base + (k + 1) * D],
                           in_=dram[name][k * 128:(k + 1) * 128, :], eng=gp)
        return tok

    arena_guard = {}

    def stream_src_slab(sp_t, sidx, base, sem):
        toks = []
        if arena_guard.get(base) is not None:
            sp.wait(*arena_guard[base])
        for k in range(NCH):
            toks.append(b.dma_on(sem, out=arena[:, base + k * 512: base + (k + 1) * 512],
                                 in_=sp_t[k, sidx, :, :]))
        return toks[-1]

    def proj_stage_from_spill(wname, slot, src_sp, dst_sp, bias_col, maxcol=None):
        wtok = load_wT(wname, slot)
        te.wait(*wtok)
        wb = slot * 8 * D
        lastmm = None
        for sidx in range(NSL):
            base = (sidx % 2) * 4096
            ltok = stream_src_slab(src_sp, sidx, base, sem_atld)
            te.wait(*ltok)
            for c in range(NCH):
                bank = psf[c % 2]
                for k in range(NCH):
                    te.op("matmul", bank[:, :],
                          wslot[:, wb + k * D + c * 128: wb + k * D + (c + 1) * 128],
                          arena[:, base + k * 512: base + (k + 1) * 512],
                          start=(k == 0), stop=(k == NCH - 1))
                mtok = b.mark(te)
                lastmm = mtok
                half = (sidx * NCH + c) % 2
                stg = spst[:, half * 512:(half + 1) * 512]
                sc.wait(*mtok)
                if pst_state[half] is not None:
                    sc.wait(*pst_state[half])
                if red_state[half] is not None:
                    sc.wait(*red_state[half])
                sc.op("activation", stg, bank[:, :], AF.Identity,
                      bias=bias_sb[:, bias_col * 8 + c: bias_col * 8 + c + 1], scale=1.0)
                etok = b.mark(sc)
                te.wait(*etok)
                if maxcol is not None:
                    ve.wait(*etok)
                    ve.op("tensor_reduce", mv[:, 3:4], stg, AX.X, ALU.max,
                          apply_absolute_value=True)
                    red_state[half] = b.chain(ve)
                    ve.op("tensor_copy", maxqk[:, maxcol * 64 + sidx * 8 + c:
                                               maxcol * 64 + sidx * 8 + c + 1], mv[:, 3:4])
                    b.chain(ve)
                sp.wait(*etok)
                dtok = b.dma_on(sem_st[half], out=dst_sp[c, sidx, :, :], in_=stg)
                pst_state[half] = dtok
            arena_guard[base] = mtok
        wslot_rd_guard[slot] = lastmm

    proj_stage_from_spill("wqkT", 0, xnfm_sp, qk_dr, 0)
    proj_stage_from_spill("wvT", 0, xnfm_sp, vin_dr, 1)
    proj_stage_from_spill("wqT", 0, qk_dr, q_dr, 2, maxcol=0)
    proj_stage_from_spill("wkT", 0, qk_dr, k_dr, 3, maxcol=1)

    ve.op("tensor_tensor", t_d[:, 0:64], maxqk[:, 0:64], maxqk[:, 64:128],
          ALU.mult)
    prodtok = b.chain(ve)
    te.wait(*prodtok)
    te.op("matmul", psf[4][0:1, 0:64], onescol_f32[:, 0:1], t_d[:, 0:64],
          start=True, stop=True)
    dottok = b.mark(te)
    sc.wait(*dottok)
    sc.op("activation", rowbuf[0:1, 0:64], psf[4][0:1, 0:64], AF.Relu,
          bias=consts[0:1, 3:4], scale=SCALE0)
    shtok = b.mark(sc)
    te.wait(*shtok)
    gp.wait(*shtok)
    t1 = b.dma_on(sem_ccd, out=ccin[0][0:1, 0:64], in_=rowbuf[0:1, 0:64], eng=gp)
    gp.wait(*t1)
    t2 = b.dma_on(sem_ccd, out=negsh_sb[:, :],
                  in_=ccin[0][0:1, 0:64].broadcast_to([128, 64]), eng=gp)
    ve.wait(*t2)
    ve.op("tensor_scalar", negsh_sb[:, :], negsh_sb[:, :], -1.0, None, ALU.mult)
    negshtok = b.chain(ve)
    sc.wait(*negshtok)

    wtok = load_wT("wvvT", 0)
    te.wait(*wtok)
    lastmm = None
    for sidx in range(NSL):
        base = (sidx % 2) * 4096
        ltok = stream_src_slab(vin_dr, sidx, base, sem_atld)
        te.wait(*ltok)
        for sub in range(4):
            for hf in range(2):
                bank = psf[(sub * 2 + hf) % 2]
                for k in range(NCH):
                    te.op("matmul", bank[:, :],
                          arena[:, base + k * 512 + sub * 128: base + k * 512 + (sub + 1) * 128],
                          wslot[:, k * D + hf * 512: k * D + (hf + 1) * 512],
                          start=(k == 0), stop=False)
                te.op("matmul", bank[:, :], onesrow_bf[0:1, :],
                      brow_sb[0:1, hf * 512:(hf + 1) * 512],
                      start=False, stop=True)
                mtok = b.mark(te)
                lastmm = mtok
                half = (sub * 2 + hf) % 2
                stg = spst[:, half * 512:(half + 1) * 512]
                sc.wait(*mtok)
                if pst_state[half] is not None:
                    sc.wait(*pst_state[half])
                if red_state[half] is not None:
                    sc.wait(*red_state[half])
                sc.op("activation", stg, bank[:, :], AF.Identity, bias=0.0, scale=1.0)
                etok = b.mark(sc)
                te.wait(*etok)
                sp.wait(*etok)
                dtok = b.dma_on(sem_st[half],
                                out=vtok_dr[sidx, :, sub * D + hf * 512: sub * D + (hf + 1) * 512],
                                in_=stg)
                pst_state[half] = dtok
        arena_guard[base] = mtok
    wslot_rd_guard[0] = lastmm
    b.barrier()

    if stop_after == "l0proj":
        dbg_out("maxqk", maxqk[:, :], (128, 128), F32)
        b.es.pop_all()
        return nc, b, DBG

    wtok = load_wT("woT", 1)
    te.wait(*wtok)
    negsh = mv
    rc_guard = [None, None]
    tbf_rd = None
    obase, ofmb = 12288, 16384
    o_wr_guard = None
    for sidx in range(NSL):
        qbase, kbase, vbase = 0, 4096, 8192
        ltq = stream_src_slab(q_dr, sidx, qbase, sem_atld)
        ltk = stream_src_slab(k_dr, sidx, kbase, sem_atld)
        if arena_guard.get(vbase) is not None:
            sp.wait(*arena_guard[vbase])
        ltv = b.dma_on(sem_atld, out=arena[:, vbase:vbase + 4 * D],
                       in_=vtok_dr[sidx, :, :])
        te.wait(*ltv)
        last_ctx = None
        last_oev = None
        for h in range(NCH):
            for sub in range(4):
                sbank = psf[sub % 2]
                te.op("matmul", sbank[:, 0:128],
                      arena[:, kbase + h * 512 + sub * 128: kbase + h * 512 + (sub + 1) * 128],
                      arena[:, qbase + h * 512 + sub * 128: qbase + h * 512 + (sub + 1) * 128],
                      start=True, stop=True)
                mtok = b.mark(te)
                ve.wait(*mtok)
                ve.op("tensor_tensor", sbank[:, 0:128], sbank[:, 0:128], segm_sb[:, :],
                      ALU.add)
                atok = b.mark(ve)
                sc.wait(*atok)
                if tbf_rd is not None:
                    sc.wait(*tbf_rd)
                sc.op("activation", t_bf[:, 0:128], sbank[:, 0:128], AF.Exp,
                      bias=negsh_sb[:, sidx * 8 + h: sidx * 8 + h + 1],
                      scale=SCALE0)
                ptok = b.mark(sc)
                te.wait(*ptok)
                cbank = psf[2 + sub % 2]
                te.op("matmul", cbank[:, 0:128], t_bf[:, 0:128],
                      arena[:, vbase + sub * D + h * 128: vbase + sub * D + (h + 1) * 128],
                      start=True, stop=False)
                te.op("matmul", cbank[:, 128:129], t_bf[:, 0:128], onescol_bf[:, 0:1],
                      start=False, stop=True)
                ctok = b.mark(te)
                tbf_rd = ctok
                last_ctx = ctok
                rcol = sub % 2
                ve.wait(*ctok)
                if rc_guard[rcol] is not None:
                    ve.wait(*rc_guard[rcol])
                ve.op("reciprocal", negsh[:, rcol:rcol + 1], cbank[:, 128:129])
                rctok = b.chain(ve)
                sc.wait(*rctok)
                if o_wr_guard is not None:
                    sc.wait(*o_wr_guard)
                sc.op("activation",
                      arena[:, obase + sub * D + h * 128: obase + sub * D + (h + 1) * 128],
                      cbank[:, 0:128], AF.Identity, bias=0.0, scale=negsh[:, rcol:rcol + 1])
                otok = b.mark(sc)
                rc_guard[rcol] = otok
                last_oev = otok
                te.wait(*otok)
        te.wait(*last_oev)
        if arena_guard.get(ofmb) is not None:
            ve.wait(*arena_guard[ofmb])
        lasttr = None
        for sub in range(4):
            for c in range(NCH):
                bank = psb[c % 2]
                te.op("transpose", bank[:, 0:128],
                      arena[:, obase + sub * D + c * 128: obase + sub * D + (c + 1) * 128],
                      ident_sb[:, :])
                ttok = b.mark(te)
                lasttr = ttok
                ve.wait(*ttok)
                ve.op("tensor_copy",
                      arena[:, ofmb + c * 512 + sub * 128: ofmb + c * 512 + (sub + 1) * 128],
                      bank[:, 0:128])
                etok = b.mark(ve)
                te.wait(*etok)
        o_wr_guard = lasttr
        te.wait(*etok)
        lastmm2 = None
        for sub in range(4):
            for hf in range(2):
                bank = psf[4 + (sub * 2 + hf) % 2]
                for k in range(NCH):
                    te.op("matmul", bank[:, :],
                          arena[:, ofmb + k * 512 + sub * 128: ofmb + k * 512 + (sub + 1) * 128],
                          wslot[:, 8 * D + k * D + hf * 512: 8 * D + k * D + (hf + 1) * 512],
                          start=(k == 0), stop=False)
                te.op("matmul", bank[:, :], onesrow_bf[0:1, :],
                      brow_sb[0:1, D + hf * 512: D + (hf + 1) * 512],
                      start=False, stop=True)
                mtok = b.mark(te)
                lastmm2 = mtok
                t = sidx * 4 + sub
                xsl = x_sb[:, t * D + hf * 512: t * D + (hf + 1) * 512]
                ve.wait(*mtok)
                ve.op("tensor_tensor", xsl, xsl, bank[:, :], ALU.add)
                etok2 = b.chain(ve)
                te.wait(*etok2)
        arena_guard[qbase] = last_ctx
        arena_guard[kbase] = last_ctx
        arena_guard[vbase] = last_ctx
        wslot_rd_guard[1] = lastmm2
        arena_guard[ofmb] = lastmm2
    b.barrier()

    if stop_after == "layer0":
        dbg_out("x_after_l0", x_sb[:, :], (128, NT * D), BF16)
        b.es.pop_all()
        return nc, b, DBG

    ln_stats("L1")
    b.barrier()

    tok = b.dma_on(sem_aux, out=rowbuf[0:1, 166:167], in_=dram["dact"][0:1, 0:1])
    sc.wait(*tok)
    sc.op("activation", rowbuf[0:1, 167:168], rowbuf[0:1, 166:167], AF.Tanh,
          bias=0.0, scale=4.0)
    b.chain(sc)
    sc.op("activation", rowbuf[0:1, 168:169], rowbuf[0:1, 167:168], AF.Exp,
          bias=consts[0:1, 4:5], scale=float(0.5 * math.log(2.0)))
    fatok = b.mark(sc)
    ve.wait(*fatok)
    bcast_row([(168, 8)])
    b.barrier()

    def mxfp4_tile(src_f32, dst_bf, fa_ap, n):
        nblk = n // 32
        ve.op("tensor_reduce",
              sc_blk[:, 0:nblk].rearrange("p (g o) -> p g o", o=1),
              src_f32[:, 0:n].rearrange("p (g k) -> p g k", k=32),
              AX.X, ALU.max, apply_absolute_value=True)
        b.chain(ve)
        ve.op("tensor_scalar", sc_blk[:, 32:32 + nblk].bitcast(U32),
              sc_blk[:, 0:nblk].bitcast(U32), 0x7F800000, None, ALU.bitwise_and)
        b.chain(ve)
        if fa_ap.shape[1] == 1:
            ve.op("tensor_scalar", sc_blk[:, 64:64 + nblk],
                  sc_blk[:, 32:32 + nblk], fa_ap, 1e-30, ALU.mult, ALU.max)
        else:
            ve.op("tensor_tensor", sc_blk[:, 64:64 + nblk],
                  sc_blk[:, 32:32 + nblk], fa_ap, ALU.mult)
            b.chain(ve)
            ve.op("tensor_scalar", sc_blk[:, 64:64 + nblk],
                  sc_blk[:, 64:64 + nblk], 1e-30, None, ALU.max)
        b.chain(ve)
        ve.op("reciprocal", sc_blk[:, 96:96 + nblk], sc_blk[:, 64:64 + nblk])
        b.chain(ve)
        ve.op("tensor_tensor", t_a[:, 0:n].rearrange("p (g k) -> p g k", k=32),
              src_f32[:, 0:n].rearrange("p (g k) -> p g k", k=32),
              sc_blk[:, 96:96 + nblk].rearrange("p (g o) -> p g o", o=1).broadcast_to([128, nblk, 32]),
              ALU.mult)
        b.chain(ve)
        ve.op("tensor_scalar", t_b[:, 0:n], t_a[:, 0:n], 6.0, -6.0, ALU.min, ALU.max)
        b.chain(ve)
        ve.op("tensor_scalar", t_a[:, 0:n].bitcast(U32), t_b[:, 0:n].bitcast(U32),
              0x7F800000, None, ALU.bitwise_and)
        b.chain(ve)
        ve.op("tensor_scalar", t_bf[:, 0:n], t_a[:, 0:n], 1.0, 4.0, ALU.max, ALU.min)
        b.chain(ve)
        ve.op("scalar_tensor_tensor", t_a[:, 0:n], t_bf[:, 0:n], MAGIC / 2.0,
              t_b[:, 0:n], ALU.mult, ALU.add)
        b.chain(ve)
        ve.op("scalar_tensor_tensor", t_b[:, 0:n], t_bf[:, 0:n], -MAGIC / 2.0,
              t_a[:, 0:n], ALU.mult, ALU.add)
        b.chain(ve)
        ve.op("tensor_tensor", dst_bf[:, 0:n].rearrange("p (g k) -> p g k", k=32),
              t_b[:, 0:n].rearrange("p (g k) -> p g k", k=32),
              sc_blk[:, 64:64 + nblk].rearrange("p (g o) -> p g o", o=1).broadcast_to([128, nblk, 32]),
              ALU.mult)
        return b.chain(ve)

    w1_guard = None
    for rt in range(NCH):
        if w1_guard is not None:
            sp.wait(*w1_guard)
        tok = b.dma_on(sem_wload, out=t_d[:, :],
                       in_=dram["w1"][rt * 128:(rt + 1) * 128, 0:512])
        tok2 = b.dma_on(sem_wload, out=t_e[:, :],
                        in_=dram["w1"][rt * 128:(rt + 1) * 128, 512:1024])
        dwt = b.dma_on(sem_wload, out=maxqk[:, 96:128],
                       in_=dram["dw"][rt * 128:(rt + 1) * 128, :])
        sc.wait(*dwt)
        sc.op("activation", maxqk[:, 64:96], maxqk[:, 96:128], AF.Tanh,
              bias=0.0, scale=4.0)
        b.chain(sc)
        sc.op("activation", maxqk[:, 96:128], maxqk[:, 64:96], AF.Exp,
              bias=consts[:, 4:5], scale=float(0.5 * math.log(2.0)))
        fwtok = b.mark(sc)
        ve.wait(*dwt)
        ve.wait(*fwtok)
        for hh, stg in enumerate((t_d, t_e)):
            qtok = mxfp4_tile(stg, t_bf, maxqk[:, 96 + hh * 16:112 + hh * 16], 512)
            for k in range(4):
                kk = hh * 4 + k
                bank = psb[kk % 2]
                te.wait(*qtok)
                te.op("transpose", bank[:, 0:128], t_bf[:, k * 128:(k + 1) * 128],
                      ident_sb[:, :])
                ttok = b.mark(te)
                ve.wait(*ttok)
                ve.op("tensor_copy",
                      wslot[:, kk * D + rt * 128: kk * D + (rt + 1) * 128],
                      bank[:, 0:128])
                etok = b.mark(ve)
                te.wait(*etok)
            w1_guard = etok
    b.barrier()

    ytr_guard = None
    for sidx in range(NSL):
        for j in range(4):
            t = sidx * 4 + j
            tok = xn_tile(t, t_bf[:, :])
            te.wait(*tok)
            for c in range(NCH):
                bank = psb[c % 2]
                te.op("transpose", bank[:, 0:128], t_bf[:, c * 128:(c + 1) * 128],
                      ident_sb[:, :])
                ttok = b.mark(te)
                ve.wait(*ttok)
                ve.op("tensor_copy", xnslab[:, c * 512 + j * 128: c * 512 + (j + 1) * 128],
                      bank[:, 0:128])
                etok = b.mark(ve)
                te.wait(*etok)
            ve.wait(*tok)
            if ytr_guard is not None:
                ve.wait(*ytr_guard)
            ytok = mxfp4_tile(t_bf, spst, scal[:, 8:9], 1024)
            sc.wait(*ytok)
            sc.wait(*ttok)
            for c in range(NCH):
                bank = psb[c % 2]
                te.wait(*ytok)
                te.op("transpose", bank[:, 0:128], spst[:, c * 128:(c + 1) * 128],
                      ident_sb[:, :])
                ttok2 = b.mark(te)
                ytr_guard = ttok2
                ve.wait(*ttok2)
                ve.op("tensor_copy",
                      arena[:, c * NSL * 512 + sidx * 512 + j * 128:
                            c * NSL * 512 + sidx * 512 + (j + 1) * 128],
                      bank[:, 0:128])
                etok2 = b.mark(ve)
                te.wait(*etok2)
        cptok = b.mark(ve)
        sp.wait(*cptok)
        ve.wait(*cptok)
        sptoks = []
        for c in range(NCH):
            sptoks.append(b.dma_on(sem_xnsp, out=xnfm_sp[c, sidx, :, :],
                                   in_=xnslab[:, c * 512:(c + 1) * 512]))
        te.wait(*sptoks[-1])
    b.barrier()

    if stop_after == "xq1":
        dbg_out("xq1", arena[:, 0:512], (128, 512), BF16)
        dbg_out("x_l1", x_sb[:, :], (128, NT * D), BF16)
        b.es.pop_all()
        return nc, b, DBG
    if stop_after == "full":
        dbg_out("x_l1", x_sb[:, :], (128, NT * D), BF16)

    for sidx in range(NSL):
        for sub in range(4):
            for hf in range(2):
                bank = psf[(sub * 2 + hf) % 4]
                for k in range(NCH):
                    te.op("matmul", bank[:, :],
                          arena[:, k * NSL * 512 + sidx * 512 + sub * 128:
                                k * NSL * 512 + sidx * 512 + (sub + 1) * 128],
                          wslot[:, k * D + hf * 512: k * D + (hf + 1) * 512],
                          start=(k == 0), stop=False)
                te.op("matmul", bank[:, :], onesrow_bf[0:1, :],
                      brow_sb[0:1, 2 * D + hf * 512: 2 * D + (hf + 1) * 512],
                      start=False, stop=True)
                mtok = b.mark(te)
                t = sidx * 4 + sub
                xsl = x_sb[:, t * D + hf * 512: t * D + (hf + 1) * 512]
                ve.wait(*mtok)
                ve.op("tensor_tensor", xsl, xsl, bank[:, :], ALU.add)
                etok = b.chain(ve)
                te.wait(*etok)
    b.barrier()

    if stop_after == "ffn1":
        dbg_out("x_after_ffn1", x_sb[:, 0:D], (128, D), BF16)
        b.es.pop_all()
        return nc, b, DBG

    out_guard = None
    for t in range(NT):
        if out_guard is not None:
            ve.wait(*out_guard)
        ve.op("tensor_copy", t_a[:, :], x_sb[:, t * D:(t + 1) * D])
        otok = b.chain(ve)
        sp.wait(*otok)
        out_guard = b.dma_on(sem_out, out=out_t[t * 128:(t + 1) * 128, :],
                             in_=t_a[:, :])
    b.barrier()

    tok1 = b.dma_on(sem_wload, out=t_a[:, 0:256], in_=dram["W_DQ"][:, :])
    tok2 = b.dma_on(sem_wload, out=t_a[:, 256:512], in_=dram["W_UQ_T"][:, :])
    tok3 = b.dma_on(sem_wload, out=t_b[:, 0:512], in_=dram["W_DKV"][:, :])
    tok4 = b.dma_on(sem_wload, out=t_d[:, :], in_=dram["W_UKV_T"][:, 0:512])
    tok4b = b.dma_on(sem_wload, out=t_e[:, :], in_=dram["W_UKV_T"][:, 512:1024])
    tok5 = b.dma_on(sem_wload, out=t_b[0:64, 512:576], in_=dram["W_QR_T"][:, :])
    tok6 = b.dma_on(sem_wload, out=t_b[0:64, 576:640], in_=dram["W_KR_T"][:, :])
    tok7 = b.dma_on(sem_wload, out=t_b[0:64, 640:704], in_=dram["W_O_T"][:, :])
    te.wait(*tok7)
    for c in range(4):
        te.op("matmul", psf[0][0:64, 0:64], t_a[:, c * 64:(c + 1) * 64],
              t_a[:, 256 + c * 64:256 + (c + 1) * 64],
              start=(c == 0), stop=(c == 3))
    mq_tok = b.mark(te)
    for c in range(NCH):
        src = t_d if c < 4 else t_e
        te.op("matmul", psf[1][0:64, 0:128], t_b[:, c * 64:(c + 1) * 64],
              src[:, (c % 4) * 128:(c % 4 + 1) * 128],
              start=(c == 0), stop=(c == 7))
    mkv_tok = b.mark(te)
    gp.op("memset", bd_sb[:, :], 0.0)
    gtok = (b.S["gp1"], b.S["gp1"].n)
    ve.wait(*gtok)
    ve.wait(*mq_tok)
    ve.op("tensor_copy", bd_sb[0:64, 0:64], psf[0][0:64, 0:64])
    b.chain(ve)
    ve.wait(*mkv_tok)
    ve.op("tensor_copy", bd_sb[0:64, 128:192], psf[1][0:64, 0:64])
    b.chain(ve)
    ve.op("tensor_copy", bd_sb[0:64, 256:320], psf[1][0:64, 64:128])
    b.chain(ve)
    ve.wait(*tok7)
    ve.op("tensor_copy", bd_sb[0:64, 384:448], t_b[0:64, 512:576])
    b.chain(ve)
    ve.op("tensor_copy", bd_sb[0:64, 512:576], t_b[0:64, 576:640])
    b.chain(ve)
    ve.op("tensor_copy", bd_sb[0:64, 640:704], t_b[0:64, 640:704])
    cptok = b.chain(ve)
    sp.wait(*cptok)
    dtoks = []
    for slot in range(5):
        dtoks.append(b.dma_on(sem_aux, out=bd_sb[64:128, slot * 128 + 64: slot * 128 + 128],
                              in_=bd_sb[0:64, slot * 128: slot * 128 + 64]))
    te.wait(*dtoks[-1])
    ve.wait(*dtoks[-1])
    gp.op("memset", bd_sb[64:65, 704:705], 1.0)
    bdtok = (b.S["gp1"], b.S["gp1"].n)
    te.wait(*bdtok)
    ve.op("tensor_scalar", qnw_sb[:, :], qnw_sb[:, :], SCALE1, None, ALU.mult)
    qnwtok = b.chain(ve)
    b.barrier()

    AXH, AQC, AQR, AKC, AKR, AVT, ACT_ = 0, 2048, 4096, 6144, 8192, 10240, 12320
    attnbuf = arena[:, 13312:13312 + 4096].bitcast(F32)
    a_guard = {}

    gp.op("memset", arena[:, AVT:AVT + 16 * 130], 1.0)
    avt_ones = (b.S["gp1"], b.S["gp1"].n)
    sc.wait(*avt_ones)

    if stop_after == "attn1":
        dbg_out("xpre", x_sb[:, :], (128, NT * D), BF16)
    UNITS = 1 if stop_after == "attn1" else 16
    for unit in range(UNITS):
        seq2, pp = divmod(unit, 8)
        if a_guard.get("qtr") is not None:
            ve.wait(*a_guard["qtr"])
            sc.wait(*a_guard["qtr"])
        if a_guard.get(AXH) is not None:
            sp.wait(*a_guard[AXH])
        xtoks = []
        for half in range(4):
            sl = seq2 * 4 + half
            xtoks.append(b.dma_on(sem_atld,
                                  out=arena[:, AXH + half * 512: AXH + (half + 1) * 512],
                                  in_=xnfm_sp[pp, sl, :, :]))
        xh_tok = xtoks[-1]
        te.wait(*xh_tok)
        last_ctx = None
        prev_qtr = a_guard.get("qtr")
        for sub in range(16):
            xh_sl = arena[:, AXH + sub * 128: AXH + (sub + 1) * 128]
            bank = psf[sub % 2]
            bank2 = psf[2 + sub % 2]
            te.op("matmul", bank[:, 0:128], xh_sl, bd_sb[:, 0:128],
                  start=True, stop=True)
            qctok = b.mark(te)
            te.op("matmul", bank2[:, 0:128], xh_sl, bd_sb[:, 128:256],
                  start=True, stop=True)
            kctok = b.mark(te)
            for which, (bk, tok_, nw, dst_base) in enumerate(
                    [(bank, qctok, qnw_sb, AQC), (bank2, kctok, knw_sb, AKC)]):
                sc.wait(*tok_)
                sc.op("activation", t_d[:, 0:128], bk[:, 0:128], AF.Square,
                      bias=0.0, scale=1.0)
                sqk = b.mark(sc)
                ve.wait(*sqk)
                ve.op("tensor_reduce", mv[:, 0:2].rearrange("p (g o) -> p g o", o=1),
                      t_d[:, 0:128].rearrange("p (g k) -> p g k", k=64),
                      AX.X, ALU.add)
                b.chain(ve)
                ve.op("tensor_scalar", mv[:, 0:2], mv[:, 0:2], 1.0 / 64.0, None,
                      ALU.mult)
                mstok = b.chain(ve)
                sc.wait(*mstok)
                sc.op("activation", mv[:, 2:4], mv[:, 0:2], AF.Sqrt,
                      bias=consts[:, 2:3], scale=1.0)
                sqtok = b.mark(sc)
                ve.wait(*sqtok)
                ve.op("reciprocal", mv[:, 0:2], mv[:, 2:4])
                b.chain(ve)
                ve.op("tensor_tensor", t_d[:, 0:128].rearrange("p (g k) -> p g k", k=64),
                      bk[:, 0:128].rearrange("p (g k) -> p g k", k=64),
                      mv[:, 0:2].rearrange("p (g o) -> p g o", o=1).broadcast_to([128, 2, 64]),
                      ALU.mult)
                b.chain(ve)
                ve.op("tensor_tensor", t_bf[:, 0:128].rearrange("p (g k) -> p g k", k=64),
                      t_d[:, 0:128].rearrange("p (g k) -> p g k", k=64),
                      nw[:, :].rearrange("p (g k) -> p g k", g=1).broadcast_to([128, 2, 64]),
                      ALU.mult)
                ntok = b.chain(ve)
                te.wait(*ntok)
                tbank = psb[which]
                te.op("transpose", tbank[:, 0:128], t_bf[:, 0:128], ident_sb[:, :])
                trtok = b.mark(te)
                ve.wait(*trtok)
                if prev_qtr is not None:
                    ve.wait(*prev_qtr)
                    prev_qtr = None
                ve.op("tensor_copy",
                      arena[:, dst_base + sub * 128: dst_base + (sub + 1) * 128],
                      tbank[:, 0:128])
                evtok = b.chain(ve)
                te.wait(*evtok)
                sc.wait(*evtok)
            te.op("matmul", bank[:, 0:128], xh_sl, bd_sb[:, 256:384],
                  start=True, stop=True)
            vtok_ = b.mark(te)
            sc.wait(*vtok_)
            sc.op("activation", arena[:, AVT + sub * 130: AVT + sub * 130 + 64],
                  bank[:, 0:64], AF.Identity, bias=0.0, scale=1.0)
            sc.op("activation", arena[:, AVT + sub * 130 + 65: AVT + sub * 130 + 129],
                  bank[:, 64:128], AF.Identity, bias=0.0, scale=1.0)
            vevtok = b.mark(sc)
            te.wait(*vevtok)

        for sl4 in range(4):
            for which, (srcb, bdofs, dstb) in enumerate(
                    [(AQC, 384, AQR), (AXH, 512, AKR)]):
                bank = psf[4 + which]
                te.wait(*evtok)
                te.op("matmul", bank[:, :], bd_sb[:, bdofs:bdofs + 128],
                      arena[:, srcb + sl4 * 512: srcb + (sl4 + 1) * 512],
                      start=True, stop=True)
                mtok = b.mark(te)
                sc.wait(*mtok)
                sc.op("activation", arena[:, dstb + sl4 * 512: dstb + (sl4 + 1) * 512],
                      bank[:, :], AF.Identity, bias=0.0, scale=1.0)
                rev = b.mark(sc)
                te.wait(*rev)
        proj_done = rev
        ve.wait(*proj_done)
        ve.op("tensor_scalar", rowbuf[0:1, 176:184], rowbuf[0:1, 176:184],
              0.0, None, ALU.mult)
        b.chain(ve)
        for pi, (b1, b2) in enumerate(((AQC, AQR), (AKC, AKR))):
            for half in range(2):
                lastmm3 = None
                for si, src in enumerate((b1, b2)):
                    if lastmm3 is not None:
                        ve.wait(*lastmm3)
                    ve.op("tensor_tensor", t_bf[:, :],
                          arena[:, src + half * 1024: src + (half + 1) * 1024],
                          arena[:, src + half * 1024: src + (half + 1) * 1024],
                          ALU.mult)
                    sqk = b.chain(ve)
                    te.wait(*sqk)
                    for h in range(2):
                        for sl5 in range(2):
                            te.op("matmul", psf[4 + h][0:1, sl5 * 256:sl5 * 256 + 256]
                                  if False else psf[4 + h][0:1, sl5 * 512 - sl5 * 512: 512]
                                  if False else psf[4 + h][0:1, 0:512],
                                  onescol_bf[64 * h:64 * h + 64, 0:1],
                                  t_bf[64 * h:64 * h + 64, sl5 * 512:(sl5 + 1) * 512],
                                  start=(si == 0 and sl5 == 0),
                                  stop=(si == 1 and sl5 == 1),
                                  skip_group_check=True)
                    lastmm3 = b.mark(te)
                for h in range(2):
                    ve.wait(*lastmm3)
                    ve.op("tensor_reduce", rowbuf[0:1, 190:191],
                          psf[4 + h][0:1, 0:512], AX.X, ALU.max)
                    b.chain(ve)
                    ve.op("tensor_max",
                          rowbuf[0:1, 176 + 2 * h + pi:177 + 2 * h + pi],
                          rowbuf[0:1, 176 + 2 * h + pi:177 + 2 * h + pi],
                          rowbuf[0:1, 190:191])
                    etok2 = b.chain(ve)
                    te.wait(*etok2)
        for h in range(2):
            ve.op("tensor_tensor", rowbuf[0:1, 170 + h:171 + h],
                  rowbuf[0:1, 176 + 2 * h:177 + 2 * h],
                  rowbuf[0:1, 177 + 2 * h:178 + 2 * h], ALU.mult)
            b.chain(ve)
        vtk = (b.S["ve1"], b.S["ve1"].n)
        sc.wait(*vtk)
        sc.op("activation", rowbuf[0:1, 174:176], rowbuf[0:1, 170:172], AF.Sqrt,
              bias=0.0, scale=1.0)
        b.chain(sc)
        sc.op("activation", rowbuf[0:1, 172:174], rowbuf[0:1, 174:176], AF.Relu,
              bias=consts[0:1, 3:4], scale=1.0)
        shtok = b.mark(sc)
        gp.wait(*shtok)
        t1 = b.dma_on(sem_ccd, out=scr_row[0:1, 0:2], in_=rowbuf[0:1, 172:174], eng=gp)
        gp.wait(*t1)
        if a_guard.get("negsh") is not None:
            gp.wait(*a_guard["negsh"])
        t2 = b.dma_on(sem_ccd, out=mv[:, 6:8],
                      in_=scr_row[0:1, 0:2].broadcast_to([128, 2]), eng=gp)
        ve.wait(*t2)
        ve.op("tensor_scalar", mv[:, 4:6], mv[:, 6:8], -1.0, None, ALU.mult)
        nstok = b.chain(ve)
        exp_rd = None
        for h in range(2):
            qcn_h = arena[:, AQC:AQC + 2048]
            for qb in range(4):
                cb = psf[5]
                nkt = 4 * qb + 4
                for kt in range(nkt):
                    sbank = psf[kt % 2]
                    te.op("matmul", sbank[:, :],
                          arena[64 * h:64 * h + 64, AKC + kt * 128: AKC + (kt + 1) * 128],
                          arena[64 * h:64 * h + 64, AQC + qb * 512: AQC + (qb + 1) * 512],
                          start=True, stop=False)
                    te.op("matmul", sbank[:, :],
                          arena[64 * h:64 * h + 64, AKR + kt * 128: AKR + (kt + 1) * 128],
                          arena[64 * h:64 * h + 64, AQR + qb * 512: AQR + (qb + 1) * 512],
                          start=False, stop=True)
                    smtok = b.mark(te)
                    j = kt - 4 * qb
                    if j >= 0:
                        ve.wait(*smtok)
                        ve.op("tensor_tensor", sbank[:, :], sbank[:, :],
                              cmask_sb[:, j * 512:(j + 1) * 512], ALU.add)
                        amtok = b.chain(ve)
                    else:
                        amtok = smtok
                    sc.wait(*amtok)
                    if exp_rd is not None:
                        sc.wait(*exp_rd)
                    sc.op("activation", t_bf[:, 0:512], sbank[:, :], AF.Exp,
                          bias=mv[:, 4 + h:5 + h], scale=1.0)
                    ptok = b.mark(sc)
                    te.wait(*ptok)
                    te.op("matmul", cb[0:65, :],
                          arena[:, AVT + kt * 130 + 65 * h: AVT + kt * 130 + 65 * h + 65],
                          t_bf[:, 0:512], start=(kt == 0), stop=(kt == nkt - 1))
                    exp_rd = b.mark(te)
                last_ctx = exp_rd
                sc.wait(*last_ctx)
                if a_guard.get("ctxT") is not None:
                    sc.wait(*a_guard["ctxT"])
                sc.op("activation", arena[0:65, ACT_:ACT_ + 512], cb[0:65, :],
                      AF.Identity, bias=0.0, scale=1.0)
                cevtok = b.mark(sc)
                te.wait(*cevtok)
                for sub in range(4):
                    ob = psf[2 + sub % 2]
                    te.op("matmul", ob[:, 0:65],
                          arena[0:65, ACT_ + sub * 128: ACT_ + (sub + 1) * 128],
                          bd_sb[0:65, 640:705], start=True, stop=True)
                    omtok = b.mark(te)
                    ve.wait(*omtok)
                    if a_guard.get("rcp2") is not None:
                        ve.wait(*a_guard["rcp2"])
                    ve.op("reciprocal", t_d[:, 1:2], ob[:, 64:65])
                    rctok = b.chain(ve)
                    tl = qb * 4 + sub
                    asl = attnbuf[:, tl * 128 + h * 64: tl * 128 + h * 64 + 64]
                    ve.op("tensor_scalar", asl, ob[:, 0:64], t_d[:, 1:2], None,
                          ALU.mult)
                    fintok = b.chain(ve)
                    te.wait(*fintok)
                    a_guard["rcp2"] = fintok
                a_guard["ctxT"] = omtok
        flv = fintok
        gp.wait(*flv)
        if a_guard.get("flush") is not None:
            gp.wait(*a_guard["flush"])
        gp.op("dma_start",
              out=out_t[seq2 * 2048:(seq2 + 1) * 2048, pp * 128:(pp + 1) * 128]
              .rearrange("(t p) m -> p t m", p=128),
              in_=attnbuf[:, :].rearrange("p (t m) -> p t m", m=128),
              accum_op=ALU.add, inc=[(sem_out, 16)])
        ftok = (sem_out, sem_out.n)
        a_guard["flush"] = ftok
        ve.wait(*ftok)
        a_guard[AXH] = last_ctx
        a_guard["qtr"] = last_ctx
        a_guard["negsh"] = ptok
    b.barrier()
    if stop_after == "attn1":
        dbg_out("mvd", mv[:, :], (128, 8), F32)
        dbg_out("rowb2", rowbuf[0:1, :], (1, 256), F32)
        dbg_out("tdd", t_d[:, 0:2], (128, 2), F32)
        dbg_out("qcn", arena[:, AQC:AQC + 2048], (128, 2048), BF16)
        dbg_out("qr", arena[:, AQR:AQR + 2048], (128, 2048), BF16)
        dbg_out("kcn", arena[:, AKC:AKC + 2048], (128, 2048), BF16)
        dbg_out("kr", arena[:, AKR:AKR + 2048], (128, 2048), BF16)
        dbg_out("vt", arena[:, AVT:AVT + 2048], (128, 2048), BF16)
        dbg_out("ctxT", arena[:, ACT_:ACT_ + 512], (128, 512), BF16)
        dbg_out("xout", x_sb[:, :], (128, NT * D), BF16)
        dbg_out("xl1", xnfm_sp[0, 0, :, :], (128, 512), BF16)
        b.es.pop_all()
        return nc, b, DBG

    b.barrier()
    b.es.pop_all()
    return nc, b, DBG


def _layer1(nc, b, dram, L):
    pass



def kernel(**inputs):
    W = host_prep(inputs)
    nc, b, DBG = emit_program(stop_after=None)
    _finalize(nc, b)
    in_maps = []
    for c in range(NCORES):
        m = {k: v for k, v in W.items() if k != "x"}
        m["x"] = np.ascontiguousarray(W["x"][2 * c:2 * c + 2].reshape(TOK, D))
        in_maps.append(m)
    res = run_bass_kernel_spmd(nc, in_maps, core_ids=list(range(NCORES)))
    outs = [r["out"] for r in res.results]
    full = np.concatenate([np.asarray(o).reshape(2, SEQ, D) for o in outs], axis=0)
    return full.astype(np.float32, copy=False)


def _finalize(nc, b):
    with nc.Block() as block:
        @block.sync
        def _(e):
            b.sp.replay(e)

        @block.gpsimd
        def _(e):
            b.gp.replay(e)

        @block.vector
        def _(e):
            b.ve.replay(e)

        @block.scalar
        def _(e):
            b.sc.replay(e)

        @block.tensor
        def _(e):
            b.te.replay(e)


# version 07
# speedup vs baseline: 1.2684x; 1.0889x over previous
import math
import contextlib
import numpy as np
import concourse.bass as bass
import concourse.mybir as mybir
from concourse.bass_utils import run_bass_kernel_spmd

F32 = mybir.dt.float32
BF16 = mybir.dt.bfloat16
U32 = mybir.dt.uint32
AF = mybir.ActivationFunctionType
ALU = mybir.AluOpType
AX = mybir.AxisListType

MAGIC = float(np.float32(3.0 * 2.0 ** 22))
NCORES = 8
TOK = 4096
SEQ = 2048
D = 1024
DI = 3072
NT = 32
NSL = 8
NCH = 8
NCHI = 24
QMAX0 = 32767.0
LN_EPS = 1e-5
RMS_EPS = 1.1920929e-07
HEADROOM = 30.0
SCALE0 = 1.0 / math.sqrt(128.0)
SCALE1 = 64.0 ** (-0.5)
EXPMASK = float(np.float32(np.frombuffer(np.uint32(0x7F800000).tobytes(), dtype=np.float32)[0]))


class Sem:
    __slots__ = ("h", "n")

    def __init__(self, h):
        self.h = h
        self.n = 0


class Eng:
    def __init__(self, name):
        self.name = name
        self.items = []
        self.auto_sem = None

    def wait(self, sem, val=None):
        if isinstance(sem, tuple):
            sem, val = sem
        h = sem.h if isinstance(sem, Sem) else sem
        self.items.append(("w", h, int(val)))

    def op(self, method, *args, inc=None, **kwargs):
        pairs = []
        if inc:
            for s, k in inc:
                s.n += k
                pairs.append((s.h, k))
        if (self.auto_sem is not None and not pairs
                and not method.startswith("dma")
                and method not in ("collective_compute",)):
            self.auto_sem.n += 1
            pairs.append((self.auto_sem.h, 1))
        self.items.append(("o", method, args, kwargs, pairs))

    def replay(self, eng):
        for it in self.items:
            if it[0] == "w":
                eng.wait_ge(it[1], it[2])
            else:
                _, method, args, kwargs, pairs = it
                ins = getattr(eng, method)(*args, **kwargs)
                for h, k in pairs:
                    ins = ins.then_inc(h, k)


class B:
    def __init__(self, nc):
        self.nc = nc
        self.es = contextlib.ExitStack()
        self.sp = Eng("sync")
        self.gp = Eng("gpsimd")
        self.ve = Eng("vector")
        self.sc = Eng("scalar")
        self.te = Eng("tensor")
        self.engines = [self.sp, self.gp, self.ve, self.sc, self.te]
        self.S = {}
        for name in ("sp16", "gp16", "gp1", "cc1", "ve1", "sc1", "te1", "bar"):
            self.S[name] = Sem(self.es.enter_context(nc.semaphore(name)))
        self.esem = {id(self.sp): self.S["sp16"], id(self.gp): self.S["gp1"],
                     id(self.ve): self.S["ve1"], id(self.sc): self.S["sc1"],
                     id(self.te): self.S["te1"]}
        self.gp.auto_sem = self.S["gp1"]
        self.dma_sems = []

    def sb(self, name, shape, dtype):
        return self.es.enter_context(self.nc.sbuf_tensor(name, shape, dtype))

    def ps(self, name, shape, dtype):
        return self.es.enter_context(self.nc.psum_tensor(name, shape, dtype))

    def mark(self, E):
        sem = self.esem[id(E)]
        last = E.items[-1]
        assert last[0] == "o", "mark needs an op"
        if last[4]:
            h, k = last[4][0]
            assert h is sem.h or h == sem.h, "op already marked with another sem"
            return (sem, sem.n)
        sem.n += 1
        last[4].append((sem.h, 1))
        return (sem, sem.n)

    def chain(self, E):
        tok = self.mark(E)
        E.wait(*tok)
        return tok

    def dma_sem(self, name):
        sem = Sem(self.es.enter_context(self.nc.semaphore(name)))
        self.dma_sems.append((name, sem))
        return sem

    def dma_on(self, sem, out, in_, eng=None):
        E = eng or self.sp
        E.op("dma_start", out=out, in_=in_, inc=[(sem, 16)])
        return (sem, sem.n)

    def dma(self, out, in_):
        s = self.S["sp16"]
        self.sp.op("dma_start", out=out, in_=in_, inc=[(s, 16)])
        return (s, s.n)

    def dma_gp(self, out, in_):
        s = self.S["gp16"]
        self.gp.op("dma_start", out=out, in_=in_, inc=[(s, 16)])
        return (s, s.n)

    def barrier(self):
        bar = self.S["bar"]
        self.sp.wait(self.S["sp16"], self.S["sp16"].n)
        self.gp.wait(self.S["gp16"], self.S["gp16"].n)
        self.gp.wait(self.S["cc1"], self.S["cc1"].n)
        self.gp.wait(self.S["gp1"], self.S["gp1"].n)
        for _, sem in self.dma_sems:
            self.sp.wait(sem, sem.n)
        for e in self.engines:
            e.op("nop", inc=[(bar, 1)])
        tgt = bar.n
        for e in self.engines:
            e.wait(bar, tgt)


def host_prep(inputs):
    ins = {k: np.asarray(v) for k, v in inputs.items()}
    f = np.float32
    W = {}
    W["x"] = np.ascontiguousarray(ins["L0_attn_x"].astype(f, copy=False))
    W["wqkT"] = np.ascontiguousarray(ins["L0_attn_proj_qk_w"].T.astype(f))
    W["wvT"] = np.ascontiguousarray(ins["L0_attn_proj_v_w"].T.astype(f))
    wq, wk, wv = np.split(ins["L0_attn_in_proj_w"].astype(f), 3, axis=0)
    W["wqT"] = np.ascontiguousarray(wq.T)
    W["wkT"] = np.ascontiguousarray(wk.T)
    W["wvvT"] = np.ascontiguousarray(wv.T)
    W["woT"] = np.ascontiguousarray(ins["L0_attn_out_proj_w"].T.astype(f))
    bq, bk, bv = np.split(ins["L0_attn_in_proj_b"].astype(f), 3)

    def chunked(v):
        return np.ascontiguousarray(np.asarray(v, f).reshape(8, 128).T)

    W["b_qk"] = chunked(ins["L0_attn_proj_qk_b"])
    W["b_v"] = chunked(ins["L0_attn_proj_v_b"])
    W["b_q"] = chunked(bq)
    W["b_k"] = chunked(bk)
    W["b_vv_row"] = np.ascontiguousarray(bv.reshape(1, D))
    W["b_o_row"] = np.ascontiguousarray(ins["L0_attn_out_proj_b"].astype(f).reshape(1, D))
    W["w_gate"] = np.ascontiguousarray(ins["L0_ffn_w_gate"].astype(f))
    W["w_up"] = np.ascontiguousarray(ins["L0_ffn_w_up"].astype(f))
    W["w_down"] = np.ascontiguousarray(ins["L0_ffn_w_down"].astype(f))
    def pchunk(a):
        c = a.shape[0] // 128
        return np.ascontiguousarray(
            a.reshape(c, 128, a.shape[1]).transpose(1, 0, 2).reshape(128, -1).astype(f))

    W["W_DQ"] = pchunk(ins["L1_attn_W_DQ"].astype(f))
    W["W_UQ_T"] = pchunk(ins["L1_attn_W_UQ"].T.astype(f))
    W["W_DKV"] = pchunk(ins["L1_attn_W_DKV"].astype(f))
    W["W_UKV_T"] = pchunk(np.ascontiguousarray(ins["L1_attn_W_UKV"].T.astype(f)))
    W["W_QR_T"] = np.ascontiguousarray(ins["L1_attn_W_QR"].T.astype(f))
    W["W_KR_T"] = np.ascontiguousarray(ins["L1_attn_W_KR"].T.astype(f))
    W["W_O_T"] = np.ascontiguousarray(ins["L1_attn_W_O"].T.astype(f))
    W["qnw"] = np.ascontiguousarray(np.broadcast_to(ins["L1_attn_q_norm_w"].astype(f), (128, 64)))
    W["knw"] = np.ascontiguousarray(np.broadcast_to(ins["L1_attn_k_norm_w"].astype(f), (128, 64)))
    W["w1"] = np.ascontiguousarray(ins["L1_ffn_weight"].astype(f))
    W["b1_row"] = np.ascontiguousarray(ins["L1_ffn_bias"].astype(f).reshape(1, D))
    W["dw"] = np.ascontiguousarray(ins["L1_ffn_delta_w"].astype(f))
    W["dact"] = np.ascontiguousarray(ins["L1_ffn_delta_act"].astype(f).reshape(1, 1))
    W["ident"] = np.eye(128, dtype=f)
    k_ = np.arange(128)[:, None]
    q_ = np.arange(128)[None, :]
    seg = ((k_ // 4) == (q_ // 4)) & (k_ != q_)
    W["seg_mask"] = np.where(seg, f(0.0), f(-1e9)).astype(f)
    cm = np.zeros((4, 128, 512), dtype=f)
    qq = np.arange(512)[None, :]
    for j in range(4):
        cm[j] = np.where((k_ + 128 * j) <= qq, f(0.0), f(-1e9))
    W["causal_masks"] = np.ascontiguousarray(cm.transpose(1, 0, 2).reshape(128, 4 * 512))
    W["ones_row"] = np.ones((1, 128), dtype=f)
    return W



def emit_program(stop_after=None):
    nc = bass.Bass("TRN2", target_bir_lowering=False)
    dram = {}

    def din(name, shape, dtype=F32):
        dram[name] = nc.dram_tensor(name, list(shape), dtype, kind="ExternalInput")
        return dram[name]

    def dint(name, shape, dtype=BF16):
        dram[name] = nc.dram_tensor(name, list(shape), dtype)
        return dram[name]

    x_in = din("x", (TOK, D))
    out_t = nc.dram_tensor("out", [TOK, D], F32, kind="ExternalOutput")
    for nm, shp in [("wqkT", (D, D)), ("wvT", (D, D)), ("wqT", (D, D)),
                    ("wkT", (D, D)), ("wvvT", (D, D)), ("woT", (D, D)),
                    ("b_qk", (128, 8)), ("b_v", (128, 8)), ("b_q", (128, 8)),
                    ("b_k", (128, 8)), ("b_vv_row", (1, D)), ("b_o_row", (1, D)),
                    ("w_gate", (DI, D)), ("w_up", (DI, D)), ("w_down", (D, DI)),
                    ("W_DQ", (128, 256)), ("W_UQ_T", (128, 256)), ("W_DKV", (128, 512)),
                    ("W_UKV_T", (128, 1024)), ("W_QR_T", (64, 64)), ("W_KR_T", (64, 64)),
                    ("W_O_T", (64, 64)), ("qnw", (128, 64)), ("knw", (128, 64)),
                    ("w1", (D, D)), ("b1_row", (1, D)), ("dw", (D, 32)),
                    ("dact", (1, 1)), ("ident", (128, 128)), ("seg_mask", (128, 128)),
                    ("causal_masks", (128, 4 * 512)), ("ones_row", (1, 128))]:
        din(nm, shp)
    xnfm_sp = dint("xnfm_sp", (NCH, NSL, 128, 512))
    u_sp = dint("u_sp", (NCHI, NSL, 128, 512))
    g_sp = dint("g_sp", (NCHI, NSL, 128, 512))
    p_sp = dint("p_sp", (NCHI, NSL, 128, 512))
    o_sp = dint("o_sp", (NCHI, NSL, 128, 512))
    wdnT_sp = dint("wdnT_sp", (NCHI, 128, D))
    qk_dr = dint("qk_dr", (NCH, NSL, 128, 512))
    vin_dr = dint("vin_dr", (NCH, NSL, 128, 512))
    q_dr = dint("q_dr", (NCH, NSL, 128, 512))
    k_dr = dint("k_dr", (NCH, NSL, 128, 512))
    vtok_dr = dint("vtok_dr", (NSL, 128, 4 * D))
    ccin = [nc.dram_tensor(f"ccin{i}", [1, 64], F32) for i in range(4)]
    scr_row = nc.dram_tensor("scr_row", [1, 16], F32)
    ccout = [nc.dram_tensor(f"ccout{i}", [1, 64], F32) for i in range(4)]

    b = B(nc)
    S = b.S
    sem_xnsp = b.dma_sem("xnsp")
    sem_wload = b.dma_sem("wload")
    sem_wslot = b.dma_sem("wslotld")
    sem_st = [b.dma_sem("st0"), b.dma_sem("st1")]
    sem_ld = [b.dma_sem("ld0"), b.dma_sem("ld1")]
    sem_aux = b.dma_sem("auxd")
    sem_ccd = b.dma_sem("ccd")
    sem_atld = b.dma_sem("atld")
    sem_out = b.dma_sem("outd")
    sem_wdnsl = [b.dma_sem(f"wdnsl{i}") for i in range(8)]
    sp16, gp16, gp1, cc1, ve1, sc1, te1 = (S["sp16"], S["gp16"], S["gp1"],
                                           S["cc1"], S["ve1"], S["sc1"], S["te1"])

    x_sb = b.sb("x_sb", [128, NT * D], BF16)
    arena = b.sb("arena", [128, NT * D], BF16)
    wslot = b.sb("wslot", [128, 2 * 8 * D], BF16)
    xnslab = b.sb("xnslab", [128, NCH * 512], BF16)
    stats_sb = b.sb("stats_sb", [128, NT * 2], F32)
    t_a = b.sb("t_a", [128, 1024], F32)
    t_b = b.sb("t_b", [128, 1024], F32)
    t_d = b.sb("t_d", [128, 512], F32)
    t_e = b.sb("t_e", [128, 512], F32)
    t_f = b.sb("t_f", [128, 512], F32)
    t_bf = b.sb("t_bf", [128, 1024], BF16)
    stat12 = b.sb("stat12", [128, 12], F32)
    mv = b.sb("mv", [128, 8], F32)
    maxacc = b.sb("maxacc", [128, 8], F32)
    maxqk = b.sb("maxqk", [128, 128], F32)
    rowbuf = b.sb("rowbuf", [1, 256], F32)
    scal = b.sb("scal", [128, 32], F32)
    consts = b.sb("consts", [128, 8], F32)
    ident_sb = b.sb("ident_sb", [128, 128], BF16)
    segm_sb = b.sb("segm_sb", [128, 128], F32)
    cmask_sb = b.sb("cmask_sb", [128, 4 * 512], BF16)
    onesrow_sb = b.sb("onesrow_sb", [1, 128], F32)
    onesrow_bf = b.sb("onesrow_bf", [1, 128], BF16)
    onescol_bf = b.sb("onescol_bf", [128, 1], BF16)
    onescol_f32 = b.sb("onescol_f32", [128, 1], F32)
    negsh_sb = b.sb("negsh_sb", [128, 64], F32)
    bias_sb = b.sb("bias_sb", [128, 4 * 8], F32)
    brow_sb = b.sb("brow_sb", [1, 3 * D], BF16)
    qnw_sb = b.sb("qnw_sb", [128, 64], F32)
    knw_sb = b.sb("knw_sb", [128, 64], F32)
    bd_sb = b.sb("bd_sb", [128, 6 * 128], BF16)
    sc_blk = b.sb("sc_blk", [128, 4 * 32], F32)
    spst = b.sb("spst", [128, 2 * 512], BF16)
    ldst = b.sb("ldst", [128, 4 * 512], BF16)
    psf = [b.ps(f"psf{i}", [128, 512], F32) for i in range(6)]
    psb = [b.ps(f"psb{i}", [128, 1024], BF16) for i in range(2)]

    DBG = {}

    def dbg_out(tag, ap, shape, dtype):
        t = nc.dram_tensor("dbg_" + tag, list(shape), dtype, kind="ExternalOutput")
        b.barrier()
        idx = tuple(slice(None) for _ in shape)
        b.dma(out=t[idx], in_=ap)
        b.barrier()
        DBG[tag] = True

    sp, gp, ve, sc, te = b.sp, b.gp, b.ve, b.sc, b.te

    for t in range(NT):
        b.dma_gp(out=x_sb[:, t * D:(t + 1) * D], in_=x_in[t * 128:(t + 1) * 128, :])
    b.dma_gp(out=ident_sb[:, :], in_=dram["ident"][:, :])
    b.dma(out=segm_sb[:, :], in_=dram["seg_mask"][:, :])
    b.dma_gp(out=cmask_sb[:, :], in_=dram["causal_masks"][:, :])
    b.dma(out=onesrow_sb[:, :], in_=dram["ones_row"][:, :])
    b.dma_gp(out=onesrow_bf[:, :], in_=dram["ones_row"][:, :])
    for i, nm in enumerate(("b_qk", "b_v", "b_q", "b_k")):
        b.dma(out=bias_sb[:, i * 8:(i + 1) * 8], in_=dram[nm][:, :])
    for i, nm in enumerate(("b_vv_row", "b_o_row", "b1_row")):
        b.dma_gp(out=brow_sb[:, i * D:(i + 1) * D], in_=dram[nm][:, :])
    b.dma(out=qnw_sb[:, :], in_=dram["qnw"][:, :])
    b.dma(out=knw_sb[:, :], in_=dram["knw"][:, :])
    gp.op("memset", onescol_bf[:, :], 1.0)
    gp.op("memset", onescol_f32[:, :], 1.0)
    gp.op("memset", rowbuf[0:1, :], 0.0)
    gp.op("memset", consts[:, 0:1], LN_EPS)
    gp.op("memset", consts[:, 1:2], MAGIC)
    gp.op("memset", consts[:, 2:3], RMS_EPS)
    gp.op("memset", consts[:, 3:4], -HEADROOM)
    gp.op("memset", consts[:, 4:5], float(np.log(0.25)))
    gp.op("memset", maxacc[:, :], 0.0)
    gp.op("memset", maxqk[:, :], 0.0)
    b.barrier()

    if stop_after == "prologue":
        dbg_out("x0", x_sb[:, 0:D], (128, D), BF16)
        b.es.pop_all()
        return nc, b, DBG


    def ln_stats(tag):
        for t in range(NT):
            xt = x_sb[:, t * D:(t + 1) * D]
            ve.op("bn_stats", stat12[:, 0:6], xt[:, 0:512])
            ve.op("bn_stats", stat12[:, 6:12], xt[:, 512:1024])
            b.chain(ve)
            ve.op("bn_aggr", mv[:, 0:2], stat12[:, 0:12])
            tok = b.mark(ve)
            sc.wait(*tok)
            sc.op("activation", mv[:, 2:3], mv[:, 1:2], AF.Sqrt,
                  bias=consts[:, 0:1], scale=1.0)
            tok = b.mark(sc)
            ve.wait(*tok)
            ve.op("reciprocal", stats_sb[:, 2 * t:2 * t + 1], mv[:, 2:3])
            b.chain(ve)
            ve.op("tensor_scalar", stats_sb[:, 2 * t + 1:2 * t + 2], mv[:, 0:1],
                  stats_sb[:, 2 * t:2 * t + 1], -1.0, ALU.mult, ALU.mult)
            b.chain(ve)

    def xn_tile(t, dst):
        sc.op("activation", dst, x_sb[:, t * D:(t + 1) * D], AF.Identity,
              bias=stats_sb[:, 2 * t + 1:2 * t + 2],
              scale=stats_sb[:, 2 * t:2 * t + 1])
        return b.mark(sc)

    ln_stats("L0")
    b.barrier()

    for s in range(NSL):
        for j in range(4):
            t = s * 4 + j
            tok = xn_tile(t, t_bf[:, :])
            ve.wait(*tok)
            ve.op("tensor_reduce", mv[:, 3:4], t_bf[:, :], AX.X, ALU.max,
                  apply_absolute_value=True)
            b.chain(ve)
            ve.op("tensor_max", maxacc[:, 0:1], maxacc[:, 0:1], mv[:, 3:4])
            b.chain(ve)
            te.wait(*tok)
            ttok = None
            for c in range(NCH):
                bank = psb[c % 2]
                te.op("transpose", bank[:, 0:128], t_bf[:, c * 128:(c + 1) * 128],
                      ident_sb[:, :])
                ttok = b.mark(te)
                ve.wait(*ttok)
                ve.op("tensor_copy", xnslab[:, c * 512 + j * 128: c * 512 + (j + 1) * 128],
                      bank[:, 0:128])
                etok = b.mark(ve)
                te.wait(*etok)
            sc.wait(*ttok)
            sc.wait(*etok)
        cptok = b.mark(ve)
        sp.wait(*cptok)
        ve.wait(*cptok)
        sptoks = []
        for c in range(NCH):
            ve.op("tensor_copy", arena[:, c * NSL * 512 + s * 512:
                                       c * NSL * 512 + (s + 1) * 512],
                  xnslab[:, c * 512:(c + 1) * 512])
            sptoks.append(b.dma_on(sem_xnsp, out=xnfm_sp[c, s, :, :],
                                   in_=xnslab[:, c * 512:(c + 1) * 512]))
        b.chain(ve)
        te.wait(*sptoks[-1])
    b.barrier()

    if stop_after == "xn":
        dbg_out("xnfm0", arena[:, 0:512], (128, 512), BF16)
        dbg_out("stats", stats_sb[:, :], (128, NT * 2), F32)
        dbg_out("maxacc", maxacc[:, 0:1], (128, 1), F32)
        b.es.pop_all()
        return nc, b, DBG


    prr_guard = [None]

    def part_reduce_to_row(col_ap, slot):
        if prr_guard[0] is not None:
            sp.wait(*prr_guard[0])
        tok = b.dma_on(sem_aux, out=rowbuf[0:1, 0:128], in_=col_ap)
        ve.wait(*tok)
        ve.op("tensor_reduce", rowbuf[0:1, slot:slot + 1], rowbuf[0:1, 0:128],
              AX.X, ALU.max)
        prr_guard[0] = b.mark(ve)
        return prr_guard[0]

    def bcast_row(cols):
        srcs = [c[0] for c in cols]
        dsts = [c[1] for c in cols]
        k = len(cols)
        assert dsts == list(range(dsts[0], dsts[0] + k))
        for j, src in enumerate(srcs):
            ve.op("tensor_copy", rowbuf[0:1, 200 + j:201 + j],
                  rowbuf[0:1, src:src + 1])
        vtok = b.mark(ve)
        gp.wait(*vtok)
        t1 = b.dma_on(sem_ccd, out=scr_row[0:1, 0:k],
                      in_=rowbuf[0:1, 200:200 + k], eng=gp)
        gp.wait(*t1)
        t2 = b.dma_on(sem_ccd, out=scal[:, dsts[0]:dsts[0] + k],
                      in_=scr_row[0:1, 0:k].broadcast_to([128, k]), eng=gp)
        return t2

    def collective(round_idx, n):
        tok = b.dma_on(sem_ccd, out=ccin[round_idx][0:1, 0:64],
                       in_=rowbuf[0:1, 134:134 + 64], eng=gp)
        gp.wait(*tok)
        gp.op("collective_compute", "AllReduce", ALU.max,
              replica_groups=[list(range(NCORES))],
              ins=[ccin[round_idx][:, :].opt()],
              outs=[ccout[round_idx][:, :].opt()], inc=[(cc1, 1)])
        gp.wait(cc1, cc1.n)
        tok = b.dma_on(sem_ccd, out=rowbuf[0:1, 144:144 + n],
                       in_=ccout[round_idx][0:1, 0:n], eng=gp)
        return tok

    def row_ts(dst, src, s1, s2, op0, op1=ALU.bypass):
        ve.op("tensor_scalar", rowbuf[0:1, dst:dst + 1], rowbuf[0:1, src:src + 1],
              s1, s2, op0, op1)
        b.chain(ve)

    def row_recip(dst, src):
        ve.op("reciprocal", rowbuf[0:1, dst:dst + 1], rowbuf[0:1, src:src + 1])
        b.chain(ve)

    def row_mul(dst, a, bcol):
        ve.op("tensor_tensor", rowbuf[0:1, dst:dst + 1], rowbuf[0:1, a:a + 1],
              rowbuf[0:1, bcol:bcol + 1], ALU.mult)
        b.chain(ve)

    tok = part_reduce_to_row(maxacc[:, 0:1], 134)
    ve.wait(*tok)
    gp.wait(ve1, ve1.n)
    tok = collective(0, 1)
    ve.wait(*tok)
    row_ts(128, 144, 1.0 / QMAX0, 1e-12, ALU.mult, ALU.max)
    row_recip(133, 128)
    bcast_row([(133, 0)])
    b.barrier()

    if stop_after == "ar1":
        dbg_out("scal0", scal[:, 0:8], (128, 8), F32)
        dbg_out("rowb", rowbuf[0:1, :], (1, 256), F32)
        b.es.pop_all()
        return nc, b, DBG

    for c in range(NCH):
        for s in range(NSL):
            seg = arena[:, c * NSL * 512 + s * 512: c * NSL * 512 + (s + 1) * 512]
            ve.op("tensor_scalar", t_d[:, :], seg, scal[:, 0:1], MAGIC,
                  ALU.mult, ALU.add)
            b.chain(ve)
            ve.op("tensor_scalar", seg, t_d[:, :], MAGIC, None, ALU.subtract)
            b.chain(ve)
    b.barrier()

    if stop_after == "xq":
        dbg_out("xq0", arena[:, 0:512], (128, 512), BF16)
        b.es.pop_all()
        return nc, b, DBG

    def dequant_lpbq_tile(src_dram, row0, dst_bf, guard, dst_guard=None):
        if guard is not None:
            sp.wait(*guard)
        tok = b.dma_on(sem_wload, out=t_a[:, :], in_=src_dram[row0:row0 + 128, :])
        ve.wait(*tok)
        if dst_guard is not None:
            ve.wait(*dst_guard)
        ve.op("tensor_reduce",
              sc_blk[:, 0:32].rearrange("p (g o) -> p g o", o=1),
              t_a[:, :].rearrange("p (g k) -> p g k", k=32),
              AX.X, ALU.max, apply_absolute_value=True)
        b.chain(ve)
        ve.op("tensor_scalar", sc_blk[:, 32:64], sc_blk[:, 0:32], 1.0 / 7.0,
              1e-12, ALU.mult, ALU.max)
        b.chain(ve)
        ve.op("reciprocal", sc_blk[:, 64:96], sc_blk[:, 32:64])
        b.chain(ve)
        ve.op("tensor_tensor", t_b[:, :].rearrange("p (g k) -> p g k", k=32),
              t_a[:, :].rearrange("p (g k) -> p g k", k=32),
              sc_blk[:, 64:96].rearrange("p (g o) -> p g o", o=1).broadcast_to([128, 32, 32]),
              ALU.mult)
        b.chain(ve)
        ve.op("tensor_scalar", t_a[:, :], t_b[:, :], MAGIC, MAGIC, ALU.add, ALU.subtract)
        b.chain(ve)
        ve.op("tensor_tensor", dst_bf.rearrange("p (g k) -> p g k", k=32),
              t_a[:, :].rearrange("p (g k) -> p g k", k=32),
              sc_blk[:, 32:64].rearrange("p (g o) -> p g o", o=1).broadcast_to([128, 32, 32]),
              ALU.mult)
        return b.mark(ve)

    def transpose_into_wslot(src_bf_tok, slot_base, out_col, wtok):
        last = None
        ttok = None
        for k in range(NCH):
            bank = psb[k % 2]
            te.wait(*wtok)
            te.op("transpose", bank[:, 0:128], src_bf_tok[:, k * 128:(k + 1) * 128],
                  ident_sb[:, :])
            ttok = b.mark(te)
            ve.wait(*ttok)
            ve.op("tensor_copy",
                  wslot[:, slot_base + k * 512 + out_col: slot_base + k * 512 + out_col + 128],
                  bank[:, 0:128])
            last = b.mark(ve)
            te.wait(*last)
        return last, ttok

    UP_SLOT, GATE_SLOT = 0, 8 * 512
    pst_state = [None, None]
    red_state = [None, None]

    deq_guard = None
    wslot_guard = None
    tbf_guard = None
    NCB = 1 if stop_after in ("upgate1", "dequant") else 6
    for cb in range(NCB):
        for which, (src, base) in enumerate([(dram["w_up"], UP_SLOT),
                                             (dram["w_gate"], GATE_SLOT)]):
            for rt in range(4):
                row0 = cb * 512 + rt * 128
                if wslot_guard is not None:
                    ve.wait(*wslot_guard)
                    wslot_guard = None
                dtok = dequant_lpbq_tile(src, row0, t_bf[:, :], deq_guard,
                                          dst_guard=tbf_guard)
                _, tbf_guard = transpose_into_wslot(t_bf[:, :], base, rt * 128, dtok)
                deq_guard = dtok
        wtok = b.mark(ve)
        te.wait(*wtok)
        if stop_after == "dequant":
            break
        for which, (base, spdst, mcol) in enumerate([(UP_SLOT, u_sp, 2),
                                                     (GATE_SLOT, g_sp, 3)]):
            for cc in range(4):
                ch = cb * 4 + cc
                for s in range(NSL):
                    bank = psf[(s + cc) % 2]
                    for k in range(NCH):
                        te.op("matmul", bank[:, :],
                              wslot[:, base + k * 512 + cc * 128: base + k * 512 + (cc + 1) * 128],
                              arena[:, k * NSL * 512 + s * 512: k * NSL * 512 + (s + 1) * 512],
                              start=(k == 0), stop=(k == NCH - 1))
                    mtok = b.mark(te)
                    half = s % 2
                    stg = spst[:, half * 512:(half + 1) * 512]
                    sc.wait(*mtok)
                    if pst_state[half] is not None:
                        sc.wait(*pst_state[half])
                    if red_state[half] is not None:
                        sc.wait(*red_state[half])
                    sc.op("activation", stg, bank[:, :], AF.Identity, bias=0.0, scale=1.0)
                    etok = b.mark(sc)
                    te.wait(*etok)
                    ve.wait(*etok)
                    ve.op("tensor_reduce", mv[:, 3:4], stg, AX.X, ALU.max,
                          apply_absolute_value=True)
                    redtok = b.chain(ve)
                    ve.op("tensor_max", maxacc[:, mcol:mcol + 1],
                          maxacc[:, mcol:mcol + 1], mv[:, 3:4])
                    b.chain(ve)
                    gp.wait(*etok)
                    dtok = b.dma_on(sem_st[half], out=spdst[ch, s, :, :],
                                    in_=stg, eng=gp)
                    pst_state[half] = dtok
                    red_state[half] = redtok
                    wslot_guard = mtok
    b.barrier()

    if stop_after == "dequant":
        dbg_out("wslot0", wslot[:, 0:512], (128, 512), BF16)
        b.es.pop_all()
        return nc, b, DBG

    if stop_after in ("upgate", "upgate1"):
        dbg_out("maxug", maxacc[:, 2:4], (128, 2), F32)
        b.es.pop_all()
        return nc, b, DBG

    tok = part_reduce_to_row(maxacc[:, 2:3], 134)
    ve.wait(*tok)
    tok = part_reduce_to_row(maxacc[:, 3:4], 135)
    ve.wait(*tok)
    gp.wait(ve1, ve1.n)
    tok = collective(1, 2)
    ve.wait(*tok)
    row_mul(145, 144, 128)
    row_ts(129, 145, 1.0 / QMAX0, 1e-12, ALU.mult, ALU.max)
    row_mul(146, 144 + 1, 128)
    row_ts(130, 146, 1.0 / QMAX0, 1e-12, ALU.mult, ALU.max)
    row_recip(147, 129)
    row_mul(148, 147, 128)
    row_recip(149, 130)
    row_mul(150, 149, 128)
    row_ts(151, 130, -MAGIC, None, ALU.mult)
    bcast_row([(150, 1), (130, 2), (151, 3), (148, 4)])
    b.barrier()

    p1_iters = [(ch, s) for ch in range(NCHI) for s in range(NSL)]
    n_p1 = len(p1_iters)
    ld_g4 = [None] * 4
    st_g4 = [None] * 4
    td_rd = [None, None]
    sig_rd = [None, None]
    amark = [None] * n_p1
    bmark = [None] * n_p1
    tdbuf = (t_d, t_e)
    sigbuf = (t_a[:, 0:512], t_a[:, 512:1024])

    def p1_A(i):
        ch, s = p1_iters[i]
        q = i % 4
        p = i % 2
        stg = ldst[:, q * 512:(q + 1) * 512]
        if ld_g4[q] is not None:
            sp.wait(*ld_g4[q])
        ltok = b.dma_on(sem_ld[q % 2], out=stg, in_=g_sp[ch, s, :, :])
        ve.wait(*ltok)
        if td_rd[p] is not None:
            ve.wait(*td_rd[p])
        ve.op("tensor_scalar", tdbuf[p][:, :], stg, scal[:, 1:2], MAGIC,
              ALU.mult, ALU.add)
        amark[i] = b.mark(ve)
        ld_g4[q] = amark[i]

    def p1_B(i):
        p = i % 2
        sc.wait(*amark[i])
        if sig_rd[p] is not None:
            sc.wait(*sig_rd[p])
        sc.op("activation", sigbuf[p], tdbuf[p][:, :], AF.Sigmoid,
              bias=scal[:, 3:4], scale=scal[:, 2:3])
        bmark[i] = b.mark(sc)
        td_rd[p] = bmark[i]

    def p1_C(i):
        ch, s = p1_iters[i]
        p = i % 2
        q = i % 4
        ve.wait(*bmark[i])
        ve.op("tensor_scalar", sigbuf[p], sigbuf[p], 65536.0, 65535.0,
              ALU.mult, ALU.min)
        b.chain(ve)
        ve.op("tensor_scalar", sigbuf[p], sigbuf[p], MAGIC, MAGIC,
              ALU.add, ALU.subtract)
        b.chain(ve)
        pstg = xnslab[:, (4 + q) * 512:(5 + q) * 512]
        if st_g4[q] is not None:
            ve.wait(*st_g4[q])
        ve.op("scalar_tensor_tensor", pstg, tdbuf[p][:, :], MAGIC, sigbuf[p],
              ALU.subtract, ALU.mult)
        ptok = b.mark(ve)
        sig_rd[p] = ptok
        ve.wait(*ptok)
        ve.op("tensor_reduce", mv[:, 3:4], pstg, AX.X, ALU.max,
              apply_absolute_value=True)
        b.chain(ve)
        ve.op("tensor_max", maxacc[:, 4:5], maxacc[:, 4:5], mv[:, 3:4])
        b.chain(ve)
        gp.wait(*ptok)
        st_g4[q] = b.dma_on(sem_st[q % 2], out=p_sp[ch, s, :, :], in_=pstg,
                            eng=gp)

    for i in range(n_p1):
        p1_A(i)
        p1_B(i)
        if i >= 1:
            p1_C(i - 1)
    p1_C(n_p1 - 1)
    b.barrier()

    tok = part_reduce_to_row(maxacc[:, 4:5], 134)
    ve.wait(*tok)
    gp.wait(ve1, ve1.n)
    tok = collective(2, 1)
    ve.wait(*tok)
    row_ts(152, 130, 1.0 / 65536.0, None, ALU.mult)
    row_mul(153, 144, 152)
    row_ts(131, 153, 1.0 / QMAX0, 1e-12, ALU.mult, ALU.max)
    row_recip(154, 131)
    row_mul(155, 152, 154)
    bcast_row([(155, 5)])
    b.barrier()

    ld_g4 = [None] * 4
    ldu_g4 = [None] * 4
    st_g4 = [None] * 4
    ru_rd = [None, None]
    amark = [None] * n_p1
    bmark = [None] * n_p1

    def p2_A(i):
        ch, s = p1_iters[i]
        q = i % 4
        p = i % 2
        stgp = ldst[:, q * 512:(q + 1) * 512]
        stgu = xnslab[:, q * 512:(q + 1) * 512]
        if ld_g4[q] is not None:
            sp.wait(*ld_g4[q])
        if ldu_g4[q] is not None:
            sp.wait(*ldu_g4[q])
        ltok1 = b.dma_on(sem_ld[q % 2], out=stgp, in_=p_sp[ch, s, :, :])
        ltok2 = b.dma_on(sem_ld[q % 2], out=stgu, in_=u_sp[ch, s, :, :])
        ve.wait(*ltok2)
        ve.op("tensor_scalar", tdbuf[p][:, :], stgp, scal[:, 5:6], MAGIC,
              ALU.mult, ALU.add)
        amark[i] = b.mark(ve)
        ld_g4[q] = amark[i]
        sc.wait(*ltok2)
        if ru_rd[p] is not None:
            sc.wait(*ru_rd[p])
        sc.op("activation", sigbuf[p], stgu, AF.Identity,
              bias=consts[:, 1:2], scale=scal[:, 4:5])
        bmark[i] = b.mark(sc)
        ldu_g4[q] = bmark[i]

    def p2_C(i):
        ch, s = p1_iters[i]
        p = i % 2
        q = i % 4
        ve.wait(*bmark[i])
        ve.op("tensor_scalar", sigbuf[p], sigbuf[p], MAGIC, None, ALU.subtract)
        b.chain(ve)
        pstg = xnslab[:, (4 + q) * 512:(5 + q) * 512]
        if st_g4[q] is not None:
            ve.wait(*st_g4[q])
        ve.op("scalar_tensor_tensor", pstg, tdbuf[p][:, :], MAGIC, sigbuf[p],
              ALU.subtract, ALU.mult)
        ptok = b.mark(ve)
        ru_rd[p] = ptok
        ve.wait(*ptok)
        ve.op("tensor_reduce", mv[:, 3:4], pstg, AX.X, ALU.max,
              apply_absolute_value=True)
        b.chain(ve)
        ve.op("tensor_max", maxacc[:, 5:6], maxacc[:, 5:6], mv[:, 3:4])
        b.chain(ve)
        gp.wait(*ptok)
        st_g4[q] = b.dma_on(sem_st[q % 2], out=o_sp[ch, s, :, :], in_=pstg,
                            eng=gp)

    for i in range(n_p1):
        p2_A(i)
        if i >= 1:
            p2_C(i - 1)
    p2_C(n_p1 - 1)
    b.barrier()

    tok = part_reduce_to_row(maxacc[:, 5:6], 134)
    ve.wait(*tok)
    gp.wait(ve1, ve1.n)
    tok = collective(3, 1)
    ve.wait(*tok)
    row_mul(156, 131, 129)
    row_mul(157, 144, 156)
    row_ts(132, 157, 1.0 / QMAX0, 1e-12, ALU.mult, ALU.max)
    row_recip(158, 132)
    row_mul(159, 156, 158)
    bcast_row([(159, 6), (132, 7)])
    b.barrier()

    wdn_guard = deq_guard
    wdn_sl_guard = [None] * 8
    for rt in range(8):
        if wdn_guard is not None:
            sp.wait(*wdn_guard)
        tok = b.dma_on(sem_wload, out=t_a[:, :],
                       in_=dram["w_down"][rt * 128:(rt + 1) * 128, 0:1024])
        for part in range(3):
            if part > 0:
                sp.wait(*wdn_guard)
                tok = b.dma_on(sem_wload, out=t_a[:, :],
                               in_=dram["w_down"][rt * 128:(rt + 1) * 128,
                                                  part * 1024:(part + 1) * 1024])
            ve.wait(*tok)
            ve.op("tensor_reduce",
                  sc_blk[:, 0:32].rearrange("p (g o) -> p g o", o=1),
                  t_a[:, :].rearrange("p (g k) -> p g k", k=32),
                  AX.X, ALU.max, apply_absolute_value=True)
            b.chain(ve)
            ve.op("tensor_scalar", sc_blk[:, 32:64], sc_blk[:, 0:32], 1.0 / 7.0,
                  1e-12, ALU.mult, ALU.max)
            b.chain(ve)
            ve.op("reciprocal", sc_blk[:, 64:96], sc_blk[:, 32:64])
            b.chain(ve)
            ve.op("tensor_tensor", t_b[:, :].rearrange("p (g k) -> p g k", k=32),
                  t_a[:, :].rearrange("p (g k) -> p g k", k=32),
                  sc_blk[:, 64:96].rearrange("p (g o) -> p g o", o=1).broadcast_to([128, 32, 32]),
                  ALU.mult)
            b.chain(ve)
            ve.op("tensor_scalar", t_a[:, :], t_b[:, :], MAGIC, MAGIC,
                  ALU.add, ALU.subtract)
            b.chain(ve)
            ve.op("tensor_tensor", t_bf[:, :].rearrange("p (g k) -> p g k", k=32),
                  t_a[:, :].rearrange("p (g k) -> p g k", k=32),
                  sc_blk[:, 32:64].rearrange("p (g o) -> p g o", o=1).broadcast_to([128, 32, 32]),
                  ALU.mult)
            wtok = b.mark(ve)
            wdn_guard = wtok
            for k in range(NCH):
                bank = psb[k % 2]
                te.wait(*wtok)
                te.op("transpose", bank[:, 0:128], t_bf[:, k * 128:(k + 1) * 128],
                      ident_sb[:, :])
                ttok = b.mark(te)
                ve.wait(*ttok)
                if wdn_sl_guard[k] is not None:
                    ve.wait(*wdn_sl_guard[k])
                ve.op("tensor_copy",
                      spst[:, (k % 8) * 128:(k % 8 + 1) * 128], bank[:, 0:128])
                etok = b.mark(ve)
                te.wait(*etok)
                gp.wait(*etok)
                wsp_tok = b.dma_on(sem_wdnsl[k],
                                   out=wdnT_sp[part * 8 + k, :, rt * 128:(rt + 1) * 128],
                                   in_=spst[:, (k % 8) * 128:(k % 8 + 1) * 128],
                                   eng=gp)
                wdn_sl_guard[k] = wsp_tok
    b.barrier()

    ld_g4 = [None] * 4
    wg4 = [None] * 4
    region_rd = [None, None]
    oq_done = [None] * NSL

    def p3_quant(s):
        rb = (s % 2) * 12288
        waited_r = False
        tok = None
        for ch in range(NCHI):
            q = ch % 4
            stg = ldst[:, q * 512:(q + 1) * 512]
            if ld_g4[q] is not None:
                sp.wait(*ld_g4[q])
            ltok = b.dma_on(sem_ld[q % 2], out=stg, in_=o_sp[ch, s, :, :])
            ve.wait(*ltok)
            ve.op("tensor_scalar", t_d[:, :], stg, scal[:, 6:7], MAGIC,
                  ALU.mult, ALU.add)
            ld_g4[q] = b.mark(ve)
            ve.wait(*ld_g4[q])
            if not waited_r and region_rd[s % 2] is not None:
                ve.wait(*region_rd[s % 2])
                waited_r = True
            ve.op("tensor_scalar", arena[:, rb + ch * 512: rb + (ch + 1) * 512],
                  t_d[:, :], MAGIC, None, ALU.subtract)
            tok = b.chain(ve)
        oq_done[s] = tok

    p3_quant(0)
    for s in range(NSL):
        if s + 1 < NSL:
            p3_quant(s + 1)
        rb = (s % 2) * 12288
        te.wait(*oq_done[s])
        for hf in range(2):
            for k in range(NCHI):
                wq = k % 4
                wst = wslot[:, wq * 512:(wq + 1) * 512]
                if wg4[wq] is not None:
                    sp.wait(*wg4[wq])
                wtok = b.dma_on(sem_wslot, out=wst,
                                in_=wdnT_sp[k, :, hf * 512:(hf + 1) * 512])
                te.wait(*wtok)
                for sub in range(4):
                    te.op("matmul", psf[sub][:, :],
                          arena[:, rb + k * 512 + sub * 128: rb + k * 512 + (sub + 1) * 128],
                          wst, start=(k == 0), stop=(k == NCHI - 1))
                mtok = b.mark(te)
                wg4[wq] = mtok
            for sub in range(4):
                t = s * 4 + sub
                xsl = x_sb[:, t * D + hf * 512: t * D + (hf + 1) * 512]
                ve.wait(*mtok)
                ve.op("scalar_tensor_tensor", xsl, psf[sub][:, :], scal[:, 7:8],
                      xsl, ALU.mult, ALU.add)
                etok = b.mark(ve)
                te.wait(*etok)
        region_rd[s % 2] = mtok
    b.barrier()

    if stop_after == "ffn0":
        dbg_out("x_after_ffn0", x_sb[:, 0:D], (128, D), BF16)
        b.es.pop_all()
        return nc, b, DBG

    sem_wgp = b.dma_sem("wgp")
    wslot_rd_guard = [None, None]

    def load_wT(name, slot):
        if wslot_rd_guard[slot] is not None:
            gp.wait(*wslot_rd_guard[slot])
        base = slot * 8 * D
        tok = None
        for k in range(NCH):
            tok = b.dma_on(sem_wgp, out=wslot[:, base + k * D: base + (k + 1) * D],
                           in_=dram[name][k * 128:(k + 1) * 128, :], eng=gp)
        return tok

    arena_guard = {}

    def stream_src_slab(sp_t, sidx, base, sem):
        toks = []
        if arena_guard.get(base) is not None:
            sp.wait(*arena_guard[base])
        for k in range(NCH):
            toks.append(b.dma_on(sem, out=arena[:, base + k * 512: base + (k + 1) * 512],
                                 in_=sp_t[k, sidx, :, :]))
        return toks[-1]

    def proj_stage_from_spill(wname, slot, src_sp, dst_sp, bias_col, maxcol=None):
        wtok = load_wT(wname, slot)
        te.wait(*wtok)
        wb = slot * 8 * D
        lastmm = None
        for sidx in range(NSL):
            base = (sidx % 2) * 4096
            ltok = stream_src_slab(src_sp, sidx, base, sem_atld)
            te.wait(*ltok)
            for c in range(NCH):
                bank = psf[c % 2]
                for k in range(NCH):
                    te.op("matmul", bank[:, :],
                          wslot[:, wb + k * D + c * 128: wb + k * D + (c + 1) * 128],
                          arena[:, base + k * 512: base + (k + 1) * 512],
                          start=(k == 0), stop=(k == NCH - 1))
                mtok = b.mark(te)
                lastmm = mtok
                half = (sidx * NCH + c) % 2
                stg = spst[:, half * 512:(half + 1) * 512]
                sc.wait(*mtok)
                if pst_state[half] is not None:
                    sc.wait(*pst_state[half])
                if red_state[half] is not None:
                    sc.wait(*red_state[half])
                sc.op("activation", stg, bank[:, :], AF.Identity,
                      bias=bias_sb[:, bias_col * 8 + c: bias_col * 8 + c + 1], scale=1.0)
                etok = b.mark(sc)
                te.wait(*etok)
                if maxcol is not None:
                    ve.wait(*etok)
                    ve.op("tensor_reduce", mv[:, 3:4], stg, AX.X, ALU.max,
                          apply_absolute_value=True)
                    red_state[half] = b.chain(ve)
                    ve.op("tensor_copy", maxqk[:, maxcol * 64 + sidx * 8 + c:
                                               maxcol * 64 + sidx * 8 + c + 1], mv[:, 3:4])
                    b.chain(ve)
                gp.wait(*etok)
                dtok = b.dma_on(sem_st[half], out=dst_sp[c, sidx, :, :],
                                in_=stg, eng=gp)
                pst_state[half] = dtok
            arena_guard[base] = mtok
        wslot_rd_guard[slot] = lastmm

    proj_stage_from_spill("wqkT", 0, xnfm_sp, qk_dr, 0)
    proj_stage_from_spill("wvT", 0, xnfm_sp, vin_dr, 1)
    proj_stage_from_spill("wqT", 0, qk_dr, q_dr, 2, maxcol=0)
    proj_stage_from_spill("wkT", 0, qk_dr, k_dr, 3, maxcol=1)

    ve.op("tensor_tensor", t_d[:, 0:64], maxqk[:, 0:64], maxqk[:, 64:128],
          ALU.mult)
    prodtok = b.chain(ve)
    te.wait(*prodtok)
    te.op("matmul", psf[4][0:1, 0:64], onescol_f32[:, 0:1], t_d[:, 0:64],
          start=True, stop=True)
    dottok = b.mark(te)
    sc.wait(*dottok)
    sc.op("activation", rowbuf[0:1, 0:64], psf[4][0:1, 0:64], AF.Relu,
          bias=consts[0:1, 3:4], scale=SCALE0)
    shtok = b.mark(sc)
    te.wait(*shtok)
    gp.wait(*shtok)
    t1 = b.dma_on(sem_ccd, out=ccin[0][0:1, 0:64], in_=rowbuf[0:1, 0:64], eng=gp)
    gp.wait(*t1)
    t2 = b.dma_on(sem_ccd, out=negsh_sb[:, :],
                  in_=ccin[0][0:1, 0:64].broadcast_to([128, 64]), eng=gp)
    ve.wait(*t2)
    ve.op("tensor_scalar", negsh_sb[:, :], negsh_sb[:, :], -1.0, None, ALU.mult)
    negshtok = b.chain(ve)
    sc.wait(*negshtok)

    wtok = load_wT("wvvT", 0)
    te.wait(*wtok)
    lastmm = None
    for sidx in range(NSL):
        base = (sidx % 2) * 4096
        ltok = stream_src_slab(vin_dr, sidx, base, sem_atld)
        te.wait(*ltok)
        for sub in range(4):
            for hf in range(2):
                bank = psf[(sub * 2 + hf) % 2]
                for k in range(NCH):
                    te.op("matmul", bank[:, :],
                          arena[:, base + k * 512 + sub * 128: base + k * 512 + (sub + 1) * 128],
                          wslot[:, k * D + hf * 512: k * D + (hf + 1) * 512],
                          start=(k == 0), stop=False)
                te.op("matmul", bank[:, :], onesrow_bf[0:1, :],
                      brow_sb[0:1, hf * 512:(hf + 1) * 512],
                      start=False, stop=True)
                mtok = b.mark(te)
                lastmm = mtok
                half = (sub * 2 + hf) % 2
                stg = spst[:, half * 512:(half + 1) * 512]
                sc.wait(*mtok)
                if pst_state[half] is not None:
                    sc.wait(*pst_state[half])
                if red_state[half] is not None:
                    sc.wait(*red_state[half])
                sc.op("activation", stg, bank[:, :], AF.Identity, bias=0.0, scale=1.0)
                etok = b.mark(sc)
                te.wait(*etok)
                gp.wait(*etok)
                dtok = b.dma_on(sem_st[half],
                                out=vtok_dr[sidx, :, sub * D + hf * 512: sub * D + (hf + 1) * 512],
                                in_=stg, eng=gp)
                pst_state[half] = dtok
        arena_guard[base] = mtok
    wslot_rd_guard[0] = lastmm
    b.barrier()

    if stop_after == "l0proj":
        dbg_out("maxqk", maxqk[:, :], (128, 128), F32)
        b.es.pop_all()
        return nc, b, DBG

    wtok = load_wT("woT", 1)
    te.wait(*wtok)
    negsh = mv
    rc_guard = [None, None]
    tbf_rd = None
    obase, ofmb = 12288, 16384
    o_wr_guard = None
    for sidx in range(NSL):
        qbase, kbase, vbase = 0, 4096, 8192
        ltq = stream_src_slab(q_dr, sidx, qbase, sem_atld)
        ltk = stream_src_slab(k_dr, sidx, kbase, sem_atld)
        if arena_guard.get(vbase) is not None:
            sp.wait(*arena_guard[vbase])
        ltv = b.dma_on(sem_atld, out=arena[:, vbase:vbase + 4 * D],
                       in_=vtok_dr[sidx, :, :])
        te.wait(*ltv)
        last_ctx = None
        last_oev = None
        for h in range(NCH):
            for sub in range(4):
                sbank = psf[sub % 2]
                te.op("matmul", sbank[:, 0:128],
                      arena[:, kbase + h * 512 + sub * 128: kbase + h * 512 + (sub + 1) * 128],
                      arena[:, qbase + h * 512 + sub * 128: qbase + h * 512 + (sub + 1) * 128],
                      start=True, stop=True)
                mtok = b.mark(te)
                ve.wait(*mtok)
                ve.op("tensor_tensor", sbank[:, 0:128], sbank[:, 0:128], segm_sb[:, :],
                      ALU.add)
                atok = b.mark(ve)
                sc.wait(*atok)
                if tbf_rd is not None:
                    sc.wait(*tbf_rd)
                sc.op("activation", t_bf[:, 0:128], sbank[:, 0:128], AF.Exp,
                      bias=negsh_sb[:, sidx * 8 + h: sidx * 8 + h + 1],
                      scale=SCALE0)
                ptok = b.mark(sc)
                te.wait(*ptok)
                cbank = psf[2 + sub % 2]
                te.op("matmul", cbank[:, 0:128], t_bf[:, 0:128],
                      arena[:, vbase + sub * D + h * 128: vbase + sub * D + (h + 1) * 128],
                      start=True, stop=False)
                te.op("matmul", cbank[:, 128:129], t_bf[:, 0:128], onescol_bf[:, 0:1],
                      start=False, stop=True)
                ctok = b.mark(te)
                tbf_rd = ctok
                last_ctx = ctok
                rcol = sub % 2
                ve.wait(*ctok)
                if rc_guard[rcol] is not None:
                    ve.wait(*rc_guard[rcol])
                ve.op("reciprocal", negsh[:, rcol:rcol + 1], cbank[:, 128:129])
                rctok = b.chain(ve)
                sc.wait(*rctok)
                if o_wr_guard is not None:
                    sc.wait(*o_wr_guard)
                sc.op("activation",
                      arena[:, obase + sub * D + h * 128: obase + sub * D + (h + 1) * 128],
                      cbank[:, 0:128], AF.Identity, bias=0.0, scale=negsh[:, rcol:rcol + 1])
                otok = b.mark(sc)
                rc_guard[rcol] = otok
                last_oev = otok
                te.wait(*otok)
        te.wait(*last_oev)
        if arena_guard.get(ofmb) is not None:
            ve.wait(*arena_guard[ofmb])
        lasttr = None
        for sub in range(4):
            for c in range(NCH):
                bank = psb[c % 2]
                te.op("transpose", bank[:, 0:128],
                      arena[:, obase + sub * D + c * 128: obase + sub * D + (c + 1) * 128],
                      ident_sb[:, :])
                ttok = b.mark(te)
                lasttr = ttok
                ve.wait(*ttok)
                ve.op("tensor_copy",
                      arena[:, ofmb + c * 512 + sub * 128: ofmb + c * 512 + (sub + 1) * 128],
                      bank[:, 0:128])
                etok = b.mark(ve)
                te.wait(*etok)
        o_wr_guard = lasttr
        te.wait(*etok)
        lastmm2 = None
        for sub in range(4):
            for hf in range(2):
                bank = psf[4 + (sub * 2 + hf) % 2]
                for k in range(NCH):
                    te.op("matmul", bank[:, :],
                          arena[:, ofmb + k * 512 + sub * 128: ofmb + k * 512 + (sub + 1) * 128],
                          wslot[:, 8 * D + k * D + hf * 512: 8 * D + k * D + (hf + 1) * 512],
                          start=(k == 0), stop=False)
                te.op("matmul", bank[:, :], onesrow_bf[0:1, :],
                      brow_sb[0:1, D + hf * 512: D + (hf + 1) * 512],
                      start=False, stop=True)
                mtok = b.mark(te)
                lastmm2 = mtok
                t = sidx * 4 + sub
                xsl = x_sb[:, t * D + hf * 512: t * D + (hf + 1) * 512]
                ve.wait(*mtok)
                ve.op("tensor_tensor", xsl, xsl, bank[:, :], ALU.add)
                etok2 = b.chain(ve)
                te.wait(*etok2)
        arena_guard[qbase] = last_ctx
        arena_guard[kbase] = last_ctx
        arena_guard[vbase] = last_ctx
        wslot_rd_guard[1] = lastmm2
        arena_guard[ofmb] = lastmm2
    b.barrier()

    if stop_after == "layer0":
        dbg_out("x_after_l0", x_sb[:, :], (128, NT * D), BF16)
        b.es.pop_all()
        return nc, b, DBG

    ln_stats("L1")
    b.barrier()

    tok = b.dma_on(sem_aux, out=rowbuf[0:1, 166:167], in_=dram["dact"][0:1, 0:1])
    sc.wait(*tok)
    sc.op("activation", rowbuf[0:1, 167:168], rowbuf[0:1, 166:167], AF.Tanh,
          bias=0.0, scale=4.0)
    b.chain(sc)
    sc.op("activation", rowbuf[0:1, 168:169], rowbuf[0:1, 167:168], AF.Exp,
          bias=consts[0:1, 4:5], scale=float(0.5 * math.log(2.0)))
    fatok = b.mark(sc)
    ve.wait(*fatok)
    bcast_row([(168, 8)])
    b.barrier()

    def mxfp4_tile(src_f32, dst_bf, fa_ap, n):
        nblk = n // 32
        ve.op("tensor_reduce",
              sc_blk[:, 0:nblk].rearrange("p (g o) -> p g o", o=1),
              src_f32[:, 0:n].rearrange("p (g k) -> p g k", k=32),
              AX.X, ALU.max, apply_absolute_value=True)
        b.chain(ve)
        ve.op("tensor_scalar", sc_blk[:, 32:32 + nblk].bitcast(U32),
              sc_blk[:, 0:nblk].bitcast(U32), 0x7F800000, None, ALU.bitwise_and)
        b.chain(ve)
        if fa_ap.shape[1] == 1:
            ve.op("tensor_scalar", sc_blk[:, 64:64 + nblk],
                  sc_blk[:, 32:32 + nblk], fa_ap, 1e-30, ALU.mult, ALU.max)
        else:
            ve.op("tensor_tensor", sc_blk[:, 64:64 + nblk],
                  sc_blk[:, 32:32 + nblk], fa_ap, ALU.mult)
            b.chain(ve)
            ve.op("tensor_scalar", sc_blk[:, 64:64 + nblk],
                  sc_blk[:, 64:64 + nblk], 1e-30, None, ALU.max)
        b.chain(ve)
        ve.op("reciprocal", sc_blk[:, 96:96 + nblk], sc_blk[:, 64:64 + nblk])
        b.chain(ve)
        ve.op("tensor_tensor", t_a[:, 0:n].rearrange("p (g k) -> p g k", k=32),
              src_f32[:, 0:n].rearrange("p (g k) -> p g k", k=32),
              sc_blk[:, 96:96 + nblk].rearrange("p (g o) -> p g o", o=1).broadcast_to([128, nblk, 32]),
              ALU.mult)
        b.chain(ve)
        ve.op("tensor_scalar", t_b[:, 0:n], t_a[:, 0:n], 6.0, -6.0, ALU.min, ALU.max)
        b.chain(ve)
        ve.op("tensor_scalar", t_a[:, 0:n].bitcast(U32), t_b[:, 0:n].bitcast(U32),
              0x7F800000, None, ALU.bitwise_and)
        b.chain(ve)
        ve.op("tensor_scalar", t_bf[:, 0:n], t_a[:, 0:n], 1.0, 4.0, ALU.max, ALU.min)
        b.chain(ve)
        ve.op("scalar_tensor_tensor", t_a[:, 0:n], t_bf[:, 0:n], MAGIC / 2.0,
              t_b[:, 0:n], ALU.mult, ALU.add)
        b.chain(ve)
        ve.op("scalar_tensor_tensor", t_b[:, 0:n], t_bf[:, 0:n], -MAGIC / 2.0,
              t_a[:, 0:n], ALU.mult, ALU.add)
        b.chain(ve)
        ve.op("tensor_tensor", dst_bf[:, 0:n].rearrange("p (g k) -> p g k", k=32),
              t_b[:, 0:n].rearrange("p (g k) -> p g k", k=32),
              sc_blk[:, 64:64 + nblk].rearrange("p (g o) -> p g o", o=1).broadcast_to([128, nblk, 32]),
              ALU.mult)
        return b.chain(ve)

    w1_guard = None
    for rt in range(NCH):
        if w1_guard is not None:
            sp.wait(*w1_guard)
        tok = b.dma_on(sem_wload, out=t_d[:, :],
                       in_=dram["w1"][rt * 128:(rt + 1) * 128, 0:512])
        tok2 = b.dma_on(sem_wload, out=t_e[:, :],
                        in_=dram["w1"][rt * 128:(rt + 1) * 128, 512:1024])
        dwt = b.dma_on(sem_wload, out=maxqk[:, 96:128],
                       in_=dram["dw"][rt * 128:(rt + 1) * 128, :])
        sc.wait(*dwt)
        sc.op("activation", maxqk[:, 64:96], maxqk[:, 96:128], AF.Tanh,
              bias=0.0, scale=4.0)
        b.chain(sc)
        sc.op("activation", maxqk[:, 96:128], maxqk[:, 64:96], AF.Exp,
              bias=consts[:, 4:5], scale=float(0.5 * math.log(2.0)))
        fwtok = b.mark(sc)
        ve.wait(*dwt)
        ve.wait(*fwtok)
        for hh, stg in enumerate((t_d, t_e)):
            qtok = mxfp4_tile(stg, t_bf, maxqk[:, 96 + hh * 16:112 + hh * 16], 512)
            for k in range(4):
                kk = hh * 4 + k
                bank = psb[kk % 2]
                te.wait(*qtok)
                te.op("transpose", bank[:, 0:128], t_bf[:, k * 128:(k + 1) * 128],
                      ident_sb[:, :])
                ttok = b.mark(te)
                ve.wait(*ttok)
                ve.op("tensor_copy",
                      wslot[:, kk * D + rt * 128: kk * D + (rt + 1) * 128],
                      bank[:, 0:128])
                etok = b.mark(ve)
                te.wait(*etok)
            w1_guard = etok
    b.barrier()

    ytr_guard = None
    for sidx in range(NSL):
        for j in range(4):
            t = sidx * 4 + j
            tok = xn_tile(t, t_bf[:, :])
            te.wait(*tok)
            for c in range(NCH):
                bank = psb[c % 2]
                te.op("transpose", bank[:, 0:128], t_bf[:, c * 128:(c + 1) * 128],
                      ident_sb[:, :])
                ttok = b.mark(te)
                ve.wait(*ttok)
                ve.op("tensor_copy", xnslab[:, c * 512 + j * 128: c * 512 + (j + 1) * 128],
                      bank[:, 0:128])
                etok = b.mark(ve)
                te.wait(*etok)
            ve.wait(*tok)
            if ytr_guard is not None:
                ve.wait(*ytr_guard)
            ytok = mxfp4_tile(t_bf, spst, scal[:, 8:9], 1024)
            sc.wait(*ytok)
            sc.wait(*ttok)
            for c in range(NCH):
                bank = psb[c % 2]
                te.wait(*ytok)
                te.op("transpose", bank[:, 0:128], spst[:, c * 128:(c + 1) * 128],
                      ident_sb[:, :])
                ttok2 = b.mark(te)
                ytr_guard = ttok2
                ve.wait(*ttok2)
                ve.op("tensor_copy",
                      arena[:, c * NSL * 512 + sidx * 512 + j * 128:
                            c * NSL * 512 + sidx * 512 + (j + 1) * 128],
                      bank[:, 0:128])
                etok2 = b.mark(ve)
                te.wait(*etok2)
        cptok = b.mark(ve)
        sp.wait(*cptok)
        ve.wait(*cptok)
        sptoks = []
        for c in range(NCH):
            sptoks.append(b.dma_on(sem_xnsp, out=xnfm_sp[c, sidx, :, :],
                                   in_=xnslab[:, c * 512:(c + 1) * 512]))
        te.wait(*sptoks[-1])
    b.barrier()

    if stop_after == "xq1":
        dbg_out("xq1", arena[:, 0:512], (128, 512), BF16)
        dbg_out("x_l1", x_sb[:, :], (128, NT * D), BF16)
        b.es.pop_all()
        return nc, b, DBG
    if stop_after == "full":
        dbg_out("x_l1", x_sb[:, :], (128, NT * D), BF16)

    for sidx in range(NSL):
        for sub in range(4):
            for hf in range(2):
                bank = psf[(sub * 2 + hf) % 4]
                for k in range(NCH):
                    te.op("matmul", bank[:, :],
                          arena[:, k * NSL * 512 + sidx * 512 + sub * 128:
                                k * NSL * 512 + sidx * 512 + (sub + 1) * 128],
                          wslot[:, k * D + hf * 512: k * D + (hf + 1) * 512],
                          start=(k == 0), stop=False)
                te.op("matmul", bank[:, :], onesrow_bf[0:1, :],
                      brow_sb[0:1, 2 * D + hf * 512: 2 * D + (hf + 1) * 512],
                      start=False, stop=True)
                mtok = b.mark(te)
                t = sidx * 4 + sub
                xsl = x_sb[:, t * D + hf * 512: t * D + (hf + 1) * 512]
                ve.wait(*mtok)
                ve.op("tensor_tensor", xsl, xsl, bank[:, :], ALU.add)
                etok = b.chain(ve)
                te.wait(*etok)
    b.barrier()

    if stop_after == "ffn1":
        dbg_out("x_after_ffn1", x_sb[:, 0:D], (128, D), BF16)
        b.es.pop_all()
        return nc, b, DBG

    out_guard = None
    for t in range(NT):
        if out_guard is not None:
            ve.wait(*out_guard)
        ve.op("tensor_copy", t_a[:, :], x_sb[:, t * D:(t + 1) * D])
        otok = b.chain(ve)
        sp.wait(*otok)
        out_guard = b.dma_on(sem_out, out=out_t[t * 128:(t + 1) * 128, :],
                             in_=t_a[:, :])
    b.barrier()

    tok1 = b.dma_on(sem_wload, out=t_a[:, 0:256], in_=dram["W_DQ"][:, :])
    tok2 = b.dma_on(sem_wload, out=t_a[:, 256:512], in_=dram["W_UQ_T"][:, :])
    tok3 = b.dma_on(sem_wload, out=t_b[:, 0:512], in_=dram["W_DKV"][:, :])
    tok4 = b.dma_on(sem_wload, out=t_d[:, :], in_=dram["W_UKV_T"][:, 0:512])
    tok4b = b.dma_on(sem_wload, out=t_e[:, :], in_=dram["W_UKV_T"][:, 512:1024])
    tok5 = b.dma_on(sem_wload, out=t_b[0:64, 512:576], in_=dram["W_QR_T"][:, :])
    tok6 = b.dma_on(sem_wload, out=t_b[0:64, 576:640], in_=dram["W_KR_T"][:, :])
    tok7 = b.dma_on(sem_wload, out=t_b[0:64, 640:704], in_=dram["W_O_T"][:, :])
    te.wait(*tok7)
    for c in range(4):
        te.op("matmul", psf[0][0:64, 0:64], t_a[:, c * 64:(c + 1) * 64],
              t_a[:, 256 + c * 64:256 + (c + 1) * 64],
              start=(c == 0), stop=(c == 3))
    mq_tok = b.mark(te)
    for c in range(NCH):
        src = t_d if c < 4 else t_e
        te.op("matmul", psf[1][0:64, 0:128], t_b[:, c * 64:(c + 1) * 64],
              src[:, (c % 4) * 128:(c % 4 + 1) * 128],
              start=(c == 0), stop=(c == 7))
    mkv_tok = b.mark(te)
    gp.op("memset", bd_sb[:, :], 0.0)
    gtok = (b.S["gp1"], b.S["gp1"].n)
    ve.wait(*gtok)
    ve.wait(*mq_tok)
    ve.op("tensor_copy", bd_sb[0:64, 0:64], psf[0][0:64, 0:64])
    b.chain(ve)
    ve.wait(*mkv_tok)
    ve.op("tensor_copy", bd_sb[0:64, 128:192], psf[1][0:64, 0:64])
    b.chain(ve)
    ve.op("tensor_copy", bd_sb[0:64, 256:320], psf[1][0:64, 64:128])
    b.chain(ve)
    ve.wait(*tok7)
    ve.op("tensor_copy", bd_sb[0:64, 384:448], t_b[0:64, 512:576])
    b.chain(ve)
    ve.op("tensor_copy", bd_sb[0:64, 512:576], t_b[0:64, 576:640])
    b.chain(ve)
    ve.op("tensor_copy", bd_sb[0:64, 640:704], t_b[0:64, 640:704])
    cptok = b.chain(ve)
    sp.wait(*cptok)
    dtoks = []
    for slot in range(5):
        dtoks.append(b.dma_on(sem_aux, out=bd_sb[64:128, slot * 128 + 64: slot * 128 + 128],
                              in_=bd_sb[0:64, slot * 128: slot * 128 + 64]))
    te.wait(*dtoks[-1])
    ve.wait(*dtoks[-1])
    gp.op("memset", bd_sb[64:65, 704:705], 1.0)
    bdtok = (b.S["gp1"], b.S["gp1"].n)
    te.wait(*bdtok)
    ve.op("tensor_scalar", qnw_sb[:, :], qnw_sb[:, :], SCALE1, None, ALU.mult)
    qnwtok = b.chain(ve)
    b.barrier()

    AXH, AQC, AQR, AKC, AKR, AVT, ACT_ = 0, 2048, 4096, 6144, 8192, 10240, 12320
    attnbuf = arena[:, 13312:13312 + 4096].bitcast(F32)
    a_guard = {}

    gp.op("memset", arena[:, AVT:AVT + 16 * 130], 1.0)
    avt_ones = (b.S["gp1"], b.S["gp1"].n)
    sc.wait(*avt_ones)

    if stop_after == "attn1":
        dbg_out("xpre", x_sb[:, :], (128, NT * D), BF16)
    UNITS = 1 if stop_after == "attn1" else 16
    for unit in range(UNITS):
        seq2, pp = divmod(unit, 8)
        if a_guard.get("qtr") is not None:
            ve.wait(*a_guard["qtr"])
            sc.wait(*a_guard["qtr"])
        if a_guard.get(AXH) is not None:
            sp.wait(*a_guard[AXH])
        xtoks = []
        for half in range(4):
            sl = seq2 * 4 + half
            xtoks.append(b.dma_on(sem_atld,
                                  out=arena[:, AXH + half * 512: AXH + (half + 1) * 512],
                                  in_=xnfm_sp[pp, sl, :, :]))
        xh_tok = xtoks[-1]
        te.wait(*xh_tok)
        last_ctx = None
        prev_qtr = a_guard.get("qtr")
        for sub in range(16):
            xh_sl = arena[:, AXH + sub * 128: AXH + (sub + 1) * 128]
            bank = psf[sub % 2]
            bank2 = psf[2 + sub % 2]
            te.op("matmul", bank[:, 0:128], xh_sl, bd_sb[:, 0:128],
                  start=True, stop=True)
            qctok = b.mark(te)
            te.op("matmul", bank2[:, 0:128], xh_sl, bd_sb[:, 128:256],
                  start=True, stop=True)
            kctok = b.mark(te)
            for which, (bk, tok_, nw, dst_base) in enumerate(
                    [(bank, qctok, qnw_sb, AQC), (bank2, kctok, knw_sb, AKC)]):
                sc.wait(*tok_)
                sc.op("activation", t_d[:, 0:128], bk[:, 0:128], AF.Square,
                      bias=0.0, scale=1.0)
                sqk = b.mark(sc)
                ve.wait(*sqk)
                ve.op("tensor_reduce", mv[:, 0:2].rearrange("p (g o) -> p g o", o=1),
                      t_d[:, 0:128].rearrange("p (g k) -> p g k", k=64),
                      AX.X, ALU.add)
                b.chain(ve)
                ve.op("tensor_scalar", mv[:, 0:2], mv[:, 0:2], 1.0 / 64.0, None,
                      ALU.mult)
                mstok = b.chain(ve)
                sc.wait(*mstok)
                sc.op("activation", mv[:, 2:4], mv[:, 0:2], AF.Sqrt,
                      bias=consts[:, 2:3], scale=1.0)
                sqtok = b.mark(sc)
                ve.wait(*sqtok)
                ve.op("reciprocal", mv[:, 0:2], mv[:, 2:4])
                b.chain(ve)
                ve.op("tensor_tensor", t_d[:, 0:128].rearrange("p (g k) -> p g k", k=64),
                      bk[:, 0:128].rearrange("p (g k) -> p g k", k=64),
                      mv[:, 0:2].rearrange("p (g o) -> p g o", o=1).broadcast_to([128, 2, 64]),
                      ALU.mult)
                b.chain(ve)
                ve.op("tensor_tensor", t_bf[:, 0:128].rearrange("p (g k) -> p g k", k=64),
                      t_d[:, 0:128].rearrange("p (g k) -> p g k", k=64),
                      nw[:, :].rearrange("p (g k) -> p g k", g=1).broadcast_to([128, 2, 64]),
                      ALU.mult)
                ntok = b.chain(ve)
                te.wait(*ntok)
                tbank = psb[which]
                te.op("transpose", tbank[:, 0:128], t_bf[:, 0:128], ident_sb[:, :])
                trtok = b.mark(te)
                ve.wait(*trtok)
                if prev_qtr is not None:
                    ve.wait(*prev_qtr)
                    prev_qtr = None
                ve.op("tensor_copy",
                      arena[:, dst_base + sub * 128: dst_base + (sub + 1) * 128],
                      tbank[:, 0:128])
                evtok = b.chain(ve)
                te.wait(*evtok)
                sc.wait(*evtok)
            te.op("matmul", bank[:, 0:128], xh_sl, bd_sb[:, 256:384],
                  start=True, stop=True)
            vtok_ = b.mark(te)
            sc.wait(*vtok_)
            sc.op("activation", arena[:, AVT + sub * 130: AVT + sub * 130 + 64],
                  bank[:, 0:64], AF.Identity, bias=0.0, scale=1.0)
            sc.op("activation", arena[:, AVT + sub * 130 + 65: AVT + sub * 130 + 129],
                  bank[:, 64:128], AF.Identity, bias=0.0, scale=1.0)
            vevtok = b.mark(sc)
            te.wait(*vevtok)

        for sl4 in range(4):
            for which, (srcb, bdofs, dstb) in enumerate(
                    [(AQC, 384, AQR), (AXH, 512, AKR)]):
                bank = psf[4 + which]
                te.wait(*evtok)
                te.op("matmul", bank[:, :], bd_sb[:, bdofs:bdofs + 128],
                      arena[:, srcb + sl4 * 512: srcb + (sl4 + 1) * 512],
                      start=True, stop=True)
                mtok = b.mark(te)
                sc.wait(*mtok)
                sc.op("activation", arena[:, dstb + sl4 * 512: dstb + (sl4 + 1) * 512],
                      bank[:, :], AF.Identity, bias=0.0, scale=1.0)
                rev = b.mark(sc)
                te.wait(*rev)
        proj_done = rev
        ve.wait(*proj_done)
        ve.op("tensor_scalar", rowbuf[0:1, 176:184], rowbuf[0:1, 176:184],
              0.0, None, ALU.mult)
        b.chain(ve)
        for pi, (b1, b2) in enumerate(((AQC, AQR), (AKC, AKR))):
            for half in range(2):
                lastmm3 = None
                for si, src in enumerate((b1, b2)):
                    if lastmm3 is not None:
                        ve.wait(*lastmm3)
                    ve.op("tensor_tensor", t_bf[:, :],
                          arena[:, src + half * 1024: src + (half + 1) * 1024],
                          arena[:, src + half * 1024: src + (half + 1) * 1024],
                          ALU.mult)
                    sqk = b.chain(ve)
                    te.wait(*sqk)
                    for h in range(2):
                        for sl5 in range(2):
                            te.op("matmul", psf[4 + h][0:1, sl5 * 256:sl5 * 256 + 256]
                                  if False else psf[4 + h][0:1, sl5 * 512 - sl5 * 512: 512]
                                  if False else psf[4 + h][0:1, 0:512],
                                  onescol_bf[64 * h:64 * h + 64, 0:1],
                                  t_bf[64 * h:64 * h + 64, sl5 * 512:(sl5 + 1) * 512],
                                  start=(si == 0 and sl5 == 0),
                                  stop=(si == 1 and sl5 == 1),
                                  skip_group_check=True)
                    lastmm3 = b.mark(te)
                for h in range(2):
                    ve.wait(*lastmm3)
                    ve.op("tensor_reduce", rowbuf[0:1, 190:191],
                          psf[4 + h][0:1, 0:512], AX.X, ALU.max)
                    b.chain(ve)
                    ve.op("tensor_max",
                          rowbuf[0:1, 176 + 2 * h + pi:177 + 2 * h + pi],
                          rowbuf[0:1, 176 + 2 * h + pi:177 + 2 * h + pi],
                          rowbuf[0:1, 190:191])
                    etok2 = b.chain(ve)
                    te.wait(*etok2)
        for h in range(2):
            ve.op("tensor_tensor", rowbuf[0:1, 170 + h:171 + h],
                  rowbuf[0:1, 176 + 2 * h:177 + 2 * h],
                  rowbuf[0:1, 177 + 2 * h:178 + 2 * h], ALU.mult)
            b.chain(ve)
        vtk = (b.S["ve1"], b.S["ve1"].n)
        sc.wait(*vtk)
        sc.op("activation", rowbuf[0:1, 174:176], rowbuf[0:1, 170:172], AF.Sqrt,
              bias=0.0, scale=1.0)
        b.chain(sc)
        sc.op("activation", rowbuf[0:1, 172:174], rowbuf[0:1, 174:176], AF.Relu,
              bias=consts[0:1, 3:4], scale=1.0)
        shtok = b.mark(sc)
        gp.wait(*shtok)
        t1 = b.dma_on(sem_ccd, out=scr_row[0:1, 0:2], in_=rowbuf[0:1, 172:174], eng=gp)
        gp.wait(*t1)
        if a_guard.get("negsh") is not None:
            gp.wait(*a_guard["negsh"])
        t2 = b.dma_on(sem_ccd, out=mv[:, 6:8],
                      in_=scr_row[0:1, 0:2].broadcast_to([128, 2]), eng=gp)
        ve.wait(*t2)
        ve.op("tensor_scalar", mv[:, 4:6], mv[:, 6:8], -1.0, None, ALU.mult)
        nstok = b.chain(ve)
        exp_rd2 = [None, None]
        exp_rd = None
        ktg = 0
        for h in range(2):
            qcn_h = arena[:, AQC:AQC + 2048]
            for qb in range(4):
                cb = psf[5]
                nkt = 4 * qb + 4
                for kt in range(nkt):
                    pb = ktg % 2
                    ktg += 1
                    tbf_h = t_bf[:, pb * 512:(pb + 1) * 512]
                    sbank = psf[kt % 2]
                    te.op("matmul", sbank[:, :],
                          arena[64 * h:64 * h + 64, AKC + kt * 128: AKC + (kt + 1) * 128],
                          arena[64 * h:64 * h + 64, AQC + qb * 512: AQC + (qb + 1) * 512],
                          start=True, stop=False)
                    te.op("matmul", sbank[:, :],
                          arena[64 * h:64 * h + 64, AKR + kt * 128: AKR + (kt + 1) * 128],
                          arena[64 * h:64 * h + 64, AQR + qb * 512: AQR + (qb + 1) * 512],
                          start=False, stop=True)
                    smtok = b.mark(te)
                    j = kt - 4 * qb
                    if j >= 0:
                        ve.wait(*smtok)
                        ve.op("tensor_tensor", sbank[:, :], sbank[:, :],
                              cmask_sb[:, j * 512:(j + 1) * 512], ALU.add)
                        amtok = b.chain(ve)
                    else:
                        amtok = smtok
                    sc.wait(*amtok)
                    if exp_rd2[pb] is not None:
                        sc.wait(*exp_rd2[pb])
                    sc.op("activation", tbf_h, sbank[:, :], AF.Exp,
                          bias=mv[:, 4 + h:5 + h], scale=1.0)
                    ptok = b.mark(sc)
                    te.wait(*ptok)
                    te.op("matmul", cb[0:65, :],
                          arena[:, AVT + kt * 130 + 65 * h: AVT + kt * 130 + 65 * h + 65],
                          tbf_h, start=(kt == 0), stop=(kt == nkt - 1))
                    exp_rd = b.mark(te)
                    exp_rd2[pb] = exp_rd
                last_ctx = exp_rd
                sc.wait(*last_ctx)
                if a_guard.get("ctxT") is not None:
                    sc.wait(*a_guard["ctxT"])
                sc.op("activation", arena[0:65, ACT_:ACT_ + 512], cb[0:65, :],
                      AF.Identity, bias=0.0, scale=1.0)
                cevtok = b.mark(sc)
                te.wait(*cevtok)
                for sub in range(4):
                    ob = psf[2 + sub % 2]
                    te.op("matmul", ob[:, 0:65],
                          arena[0:65, ACT_ + sub * 128: ACT_ + (sub + 1) * 128],
                          bd_sb[0:65, 640:705], start=True, stop=True)
                    omtok = b.mark(te)
                    ve.wait(*omtok)
                    if a_guard.get("rcp2") is not None:
                        ve.wait(*a_guard["rcp2"])
                    ve.op("reciprocal", t_d[:, 1:2], ob[:, 64:65])
                    rctok = b.chain(ve)
                    tl = qb * 4 + sub
                    asl = attnbuf[:, tl * 128 + h * 64: tl * 128 + h * 64 + 64]
                    ve.op("tensor_scalar", asl, ob[:, 0:64], t_d[:, 1:2], None,
                          ALU.mult)
                    fintok = b.chain(ve)
                    te.wait(*fintok)
                    a_guard["rcp2"] = fintok
                a_guard["ctxT"] = omtok
        flv = fintok
        gp.wait(*flv)
        if a_guard.get("flush") is not None:
            gp.wait(*a_guard["flush"])
        gp.op("dma_start",
              out=out_t[seq2 * 2048:(seq2 + 1) * 2048, pp * 128:(pp + 1) * 128]
              .rearrange("(t p) m -> p t m", p=128),
              in_=attnbuf[:, :].rearrange("p (t m) -> p t m", m=128),
              accum_op=ALU.add, inc=[(sem_out, 16)])
        ftok = (sem_out, sem_out.n)
        a_guard["flush"] = ftok
        ve.wait(*ftok)
        a_guard[AXH] = last_ctx
        a_guard["qtr"] = last_ctx
        a_guard["negsh"] = ptok
    b.barrier()
    if stop_after == "attn1":
        dbg_out("mvd", mv[:, :], (128, 8), F32)
        dbg_out("rowb2", rowbuf[0:1, :], (1, 256), F32)
        dbg_out("tdd", t_d[:, 0:2], (128, 2), F32)
        dbg_out("qcn", arena[:, AQC:AQC + 2048], (128, 2048), BF16)
        dbg_out("qr", arena[:, AQR:AQR + 2048], (128, 2048), BF16)
        dbg_out("kcn", arena[:, AKC:AKC + 2048], (128, 2048), BF16)
        dbg_out("kr", arena[:, AKR:AKR + 2048], (128, 2048), BF16)
        dbg_out("vt", arena[:, AVT:AVT + 2048], (128, 2048), BF16)
        dbg_out("ctxT", arena[:, ACT_:ACT_ + 512], (128, 512), BF16)
        dbg_out("xout", x_sb[:, :], (128, NT * D), BF16)
        dbg_out("xl1", xnfm_sp[0, 0, :, :], (128, 512), BF16)
        b.es.pop_all()
        return nc, b, DBG

    b.barrier()
    b.es.pop_all()
    return nc, b, DBG


def _layer1(nc, b, dram, L):
    pass



def kernel(**inputs):
    W = host_prep(inputs)
    nc, b, DBG = emit_program(stop_after=None)
    _finalize(nc, b)
    in_maps = []
    for c in range(NCORES):
        m = {k: v for k, v in W.items() if k != "x"}
        m["x"] = np.ascontiguousarray(W["x"][2 * c:2 * c + 2].reshape(TOK, D))
        in_maps.append(m)
    res = run_bass_kernel_spmd(nc, in_maps, core_ids=list(range(NCORES)))
    outs = [r["out"] for r in res.results]
    full = np.concatenate([np.asarray(o).reshape(2, SEQ, D) for o in outs], axis=0)
    return full.astype(np.float32, copy=False)


def _finalize(nc, b):
    with nc.Block() as block:
        @block.sync
        def _(e):
            b.sp.replay(e)

        @block.gpsimd
        def _(e):
            b.gp.replay(e)

        @block.vector
        def _(e):
            b.ve.replay(e)

        @block.scalar
        def _(e):
            b.sc.replay(e)

        @block.tensor
        def _(e):
            b.te.replay(e)


# version 08
# speedup vs baseline: 1.2982x; 1.0235x over previous
import math
import contextlib
import numpy as np
import concourse.bass as bass
import concourse.mybir as mybir
from concourse.bass_utils import run_bass_kernel_spmd

F32 = mybir.dt.float32
BF16 = mybir.dt.bfloat16
U32 = mybir.dt.uint32
AF = mybir.ActivationFunctionType
ALU = mybir.AluOpType
AX = mybir.AxisListType

MAGIC = float(np.float32(3.0 * 2.0 ** 22))
NCORES = 8
TOK = 4096
SEQ = 2048
D = 1024
DI = 3072
NT = 32
NSL = 8
NCH = 8
NCHI = 24
QMAX0 = 32767.0
LN_EPS = 1e-5
RMS_EPS = 1.1920929e-07
HEADROOM = 30.0
SCALE0 = 1.0 / math.sqrt(128.0)
SCALE1 = 64.0 ** (-0.5)
EXPMASK = float(np.float32(np.frombuffer(np.uint32(0x7F800000).tobytes(), dtype=np.float32)[0]))


class Sem:
    __slots__ = ("h", "n")

    def __init__(self, h):
        self.h = h
        self.n = 0


class Eng:
    def __init__(self, name):
        self.name = name
        self.items = []
        self.auto_sem = None

    def wait(self, sem, val=None):
        if isinstance(sem, tuple):
            sem, val = sem
        h = sem.h if isinstance(sem, Sem) else sem
        self.items.append(("w", h, int(val)))

    def op(self, method, *args, inc=None, **kwargs):
        pairs = []
        if inc:
            for s, k in inc:
                s.n += k
                pairs.append((s.h, k))
        if (self.auto_sem is not None and not pairs
                and not method.startswith("dma")
                and method not in ("collective_compute",)):
            self.auto_sem.n += 1
            pairs.append((self.auto_sem.h, 1))
        self.items.append(("o", method, args, kwargs, pairs))

    def replay(self, eng):
        for it in self.items:
            if it[0] == "w":
                eng.wait_ge(it[1], it[2])
            else:
                _, method, args, kwargs, pairs = it
                ins = getattr(eng, method)(*args, **kwargs)
                for h, k in pairs:
                    ins = ins.then_inc(h, k)


class B:
    def __init__(self, nc):
        self.nc = nc
        self.es = contextlib.ExitStack()
        self.sp = Eng("sync")
        self.gp = Eng("gpsimd")
        self.ve = Eng("vector")
        self.sc = Eng("scalar")
        self.te = Eng("tensor")
        self.engines = [self.sp, self.gp, self.ve, self.sc, self.te]
        self.S = {}
        for name in ("sp16", "gp16", "gp1", "cc1", "ve1", "sc1", "te1", "bar"):
            self.S[name] = Sem(self.es.enter_context(nc.semaphore(name)))
        self.esem = {id(self.sp): self.S["sp16"], id(self.gp): self.S["gp1"],
                     id(self.ve): self.S["ve1"], id(self.sc): self.S["sc1"],
                     id(self.te): self.S["te1"]}
        self.gp.auto_sem = self.S["gp1"]
        self.dma_sems = []

    def sb(self, name, shape, dtype):
        return self.es.enter_context(self.nc.sbuf_tensor(name, shape, dtype))

    def ps(self, name, shape, dtype):
        return self.es.enter_context(self.nc.psum_tensor(name, shape, dtype))

    def mark(self, E):
        sem = self.esem[id(E)]
        last = E.items[-1]
        assert last[0] == "o", "mark needs an op"
        if last[4]:
            h, k = last[4][0]
            assert h is sem.h or h == sem.h, "op already marked with another sem"
            return (sem, sem.n)
        sem.n += 1
        last[4].append((sem.h, 1))
        return (sem, sem.n)

    def chain(self, E):
        tok = self.mark(E)
        E.wait(*tok)
        return tok

    def dma_sem(self, name):
        sem = Sem(self.es.enter_context(self.nc.semaphore(name)))
        self.dma_sems.append((name, sem))
        return sem

    def dma_on(self, sem, out, in_, eng=None):
        E = eng or self.sp
        E.op("dma_start", out=out, in_=in_, inc=[(sem, 16)])
        return (sem, sem.n)

    def dma(self, out, in_):
        s = self.S["sp16"]
        self.sp.op("dma_start", out=out, in_=in_, inc=[(s, 16)])
        return (s, s.n)

    def dma_gp(self, out, in_):
        s = self.S["gp16"]
        self.gp.op("dma_start", out=out, in_=in_, inc=[(s, 16)])
        return (s, s.n)

    def barrier(self):
        bar = self.S["bar"]
        self.sp.wait(self.S["sp16"], self.S["sp16"].n)
        self.gp.wait(self.S["gp16"], self.S["gp16"].n)
        self.gp.wait(self.S["cc1"], self.S["cc1"].n)
        self.gp.wait(self.S["gp1"], self.S["gp1"].n)
        for _, sem in self.dma_sems:
            self.sp.wait(sem, sem.n)
        for e in self.engines:
            e.op("nop", inc=[(bar, 1)])
        tgt = bar.n
        for e in self.engines:
            e.wait(bar, tgt)


def host_prep(inputs):
    ins = {k: np.asarray(v) for k, v in inputs.items()}
    f = np.float32
    W = {}
    W["x"] = np.ascontiguousarray(ins["L0_attn_x"].astype(f, copy=False))
    W["wqkT"] = np.ascontiguousarray(ins["L0_attn_proj_qk_w"].T.astype(f))
    W["wvT"] = np.ascontiguousarray(ins["L0_attn_proj_v_w"].T.astype(f))
    wq, wk, wv = np.split(ins["L0_attn_in_proj_w"].astype(f), 3, axis=0)
    W["wqT"] = np.ascontiguousarray(wq.T)
    W["wkT"] = np.ascontiguousarray(wk.T)
    W["wvvT"] = np.ascontiguousarray(wv.T)
    W["woT"] = np.ascontiguousarray(ins["L0_attn_out_proj_w"].T.astype(f))
    bq, bk, bv = np.split(ins["L0_attn_in_proj_b"].astype(f), 3)

    def chunked(v):
        return np.ascontiguousarray(np.asarray(v, f).reshape(8, 128).T)

    W["b_qk"] = chunked(ins["L0_attn_proj_qk_b"])
    W["b_v"] = chunked(ins["L0_attn_proj_v_b"])
    W["b_q"] = chunked(bq)
    W["b_k"] = chunked(bk)
    W["b_vv_row"] = np.ascontiguousarray(bv.reshape(1, D))
    W["b_o_row"] = np.ascontiguousarray(ins["L0_attn_out_proj_b"].astype(f).reshape(1, D))
    W["w_gate"] = np.ascontiguousarray(ins["L0_ffn_w_gate"].astype(f))
    W["w_up"] = np.ascontiguousarray(ins["L0_ffn_w_up"].astype(f))
    W["w_down"] = np.ascontiguousarray(ins["L0_ffn_w_down"].astype(f))
    def pchunk(a):
        c = a.shape[0] // 128
        return np.ascontiguousarray(
            a.reshape(c, 128, a.shape[1]).transpose(1, 0, 2).reshape(128, -1).astype(f))

    W["W_DQ"] = pchunk(ins["L1_attn_W_DQ"].astype(f))
    W["W_UQ_T"] = pchunk(ins["L1_attn_W_UQ"].T.astype(f))
    W["W_DKV"] = pchunk(ins["L1_attn_W_DKV"].astype(f))
    W["W_UKV_T"] = pchunk(np.ascontiguousarray(ins["L1_attn_W_UKV"].T.astype(f)))
    W["W_QR_T"] = np.ascontiguousarray(ins["L1_attn_W_QR"].T.astype(f))
    W["W_KR_T"] = np.ascontiguousarray(ins["L1_attn_W_KR"].T.astype(f))
    W["W_O_T"] = np.ascontiguousarray(ins["L1_attn_W_O"].T.astype(f))
    W["qnw"] = np.ascontiguousarray(np.broadcast_to(ins["L1_attn_q_norm_w"].astype(f), (128, 64)))
    W["knw"] = np.ascontiguousarray(np.broadcast_to(ins["L1_attn_k_norm_w"].astype(f), (128, 64)))
    W["w1"] = np.ascontiguousarray(ins["L1_ffn_weight"].astype(f))
    W["b1_row"] = np.ascontiguousarray(ins["L1_ffn_bias"].astype(f).reshape(1, D))
    W["dw"] = np.ascontiguousarray(ins["L1_ffn_delta_w"].astype(f))
    W["dact"] = np.ascontiguousarray(ins["L1_ffn_delta_act"].astype(f).reshape(1, 1))
    W["ident"] = np.eye(128, dtype=f)
    k_ = np.arange(128)[:, None]
    q_ = np.arange(128)[None, :]
    seg = ((k_ // 4) == (q_ // 4)) & (k_ != q_)
    W["seg_mask"] = np.where(seg, f(0.0), f(-1e9)).astype(f)
    cm = np.zeros((4, 128, 512), dtype=f)
    qq = np.arange(512)[None, :]
    for j in range(4):
        cm[j] = np.where((k_ + 128 * j) <= qq, f(0.0), f(-1e9))
    W["causal_masks"] = np.ascontiguousarray(cm.transpose(1, 0, 2).reshape(128, 4 * 512))
    W["ones_row"] = np.ones((1, 128), dtype=f)
    return W



def emit_program(stop_after=None):
    nc = bass.Bass("TRN2", target_bir_lowering=False)
    dram = {}

    def din(name, shape, dtype=F32):
        dram[name] = nc.dram_tensor(name, list(shape), dtype, kind="ExternalInput")
        return dram[name]

    def dint(name, shape, dtype=BF16):
        dram[name] = nc.dram_tensor(name, list(shape), dtype)
        return dram[name]

    x_in = din("x", (TOK, D))
    out_t = nc.dram_tensor("out", [TOK, D], F32, kind="ExternalOutput")
    for nm, shp in [("wqkT", (D, D)), ("wvT", (D, D)), ("wqT", (D, D)),
                    ("wkT", (D, D)), ("wvvT", (D, D)), ("woT", (D, D)),
                    ("b_qk", (128, 8)), ("b_v", (128, 8)), ("b_q", (128, 8)),
                    ("b_k", (128, 8)), ("b_vv_row", (1, D)), ("b_o_row", (1, D)),
                    ("w_gate", (DI, D)), ("w_up", (DI, D)), ("w_down", (D, DI)),
                    ("W_DQ", (128, 256)), ("W_UQ_T", (128, 256)), ("W_DKV", (128, 512)),
                    ("W_UKV_T", (128, 1024)), ("W_QR_T", (64, 64)), ("W_KR_T", (64, 64)),
                    ("W_O_T", (64, 64)), ("qnw", (128, 64)), ("knw", (128, 64)),
                    ("w1", (D, D)), ("b1_row", (1, D)), ("dw", (D, 32)),
                    ("dact", (1, 1)), ("ident", (128, 128)), ("seg_mask", (128, 128)),
                    ("causal_masks", (128, 4 * 512)), ("ones_row", (1, 128))]:
        din(nm, shp)
    xnfm_sp = dint("xnfm_sp", (NCH, NSL, 128, 512))
    u_sp = dint("u_sp", (NCHI, NSL, 128, 512))
    g_sp = dint("g_sp", (NCHI, NSL, 128, 512))
    p_sp = dint("p_sp", (NCHI, NSL, 128, 512))
    o_sp = dint("o_sp", (NCHI, NSL, 128, 512))
    wdnT_sp = dint("wdnT_sp", (NCHI, 128, D))
    qk_dr = dint("qk_dr", (NCH, NSL, 128, 512))
    vin_dr = dint("vin_dr", (NCH, NSL, 128, 512))
    q_dr = dint("q_dr", (NCH, NSL, 128, 512))
    k_dr = dint("k_dr", (NCH, NSL, 128, 512))
    vtok_dr = dint("vtok_dr", (NSL, 128, 4 * D))
    ccin = [nc.dram_tensor(f"ccin{i}", [1, 64], F32) for i in range(4)]
    scr_row = nc.dram_tensor("scr_row", [1, 16], F32)
    ccout = [nc.dram_tensor(f"ccout{i}", [1, 64], F32) for i in range(4)]

    b = B(nc)
    S = b.S
    sem_xnsp = b.dma_sem("xnsp")
    sem_wload = b.dma_sem("wload")
    sem_wslot = b.dma_sem("wslotld")
    sem_st = [b.dma_sem("st0"), b.dma_sem("st1")]
    sem_ld = [b.dma_sem("ld0"), b.dma_sem("ld1")]
    sem_aux = b.dma_sem("auxd")
    sem_ccd = b.dma_sem("ccd")
    sem_atld = b.dma_sem("atld")
    sem_out = b.dma_sem("outd")
    sem_wdnsl = [b.dma_sem(f"wdnsl{i}") for i in range(8)]
    sp16, gp16, gp1, cc1, ve1, sc1, te1 = (S["sp16"], S["gp16"], S["gp1"],
                                           S["cc1"], S["ve1"], S["sc1"], S["te1"])

    x_sb = b.sb("x_sb", [128, NT * D], BF16)
    arena = b.sb("arena", [128, NT * D], BF16)
    wslot = b.sb("wslot", [128, 2 * 8 * D], BF16)
    xnslab = b.sb("xnslab", [128, NCH * 512], BF16)
    stats_sb = b.sb("stats_sb", [128, NT * 2], F32)
    t_a = b.sb("t_a", [128, 1024], F32)
    t_b = b.sb("t_b", [128, 1024], F32)
    t_d = b.sb("t_d", [128, 512], F32)
    t_e = b.sb("t_e", [128, 512], F32)
    t_f = b.sb("t_f", [128, 512], F32)
    t_bf = b.sb("t_bf", [128, 1024], BF16)
    stat12 = b.sb("stat12", [128, 12], F32)
    mv = b.sb("mv", [128, 8], F32)
    maxacc = b.sb("maxacc", [128, 8], F32)
    maxqk = b.sb("maxqk", [128, 128], F32)
    rowbuf = b.sb("rowbuf", [1, 256], F32)
    scal = b.sb("scal", [128, 32], F32)
    consts = b.sb("consts", [128, 8], F32)
    ident_sb = b.sb("ident_sb", [128, 128], BF16)
    segm_sb = b.sb("segm_sb", [128, 128], F32)
    cmask_sb = b.sb("cmask_sb", [128, 4 * 512], BF16)
    onesrow_sb = b.sb("onesrow_sb", [1, 128], F32)
    onesrow_bf = b.sb("onesrow_bf", [1, 128], BF16)
    onescol_bf = b.sb("onescol_bf", [128, 1], BF16)
    onescol_f32 = b.sb("onescol_f32", [128, 1], F32)
    negsh_sb = b.sb("negsh_sb", [128, 64], F32)
    bias_sb = b.sb("bias_sb", [128, 4 * 8], F32)
    brow_sb = b.sb("brow_sb", [1, 3 * D], BF16)
    qnw_sb = b.sb("qnw_sb", [128, 64], F32)
    knw_sb = b.sb("knw_sb", [128, 64], F32)
    bd_sb = b.sb("bd_sb", [128, 6 * 128], BF16)
    sc_blk = b.sb("sc_blk", [128, 4 * 32], F32)
    spst = b.sb("spst", [128, 2 * 512], BF16)
    ldst = b.sb("ldst", [128, 4 * 512], BF16)
    psf = [b.ps(f"psf{i}", [128, 512], F32) for i in range(6)]
    psb = [b.ps(f"psb{i}", [128, 1024], BF16) for i in range(2)]

    DBG = {}

    def dbg_out(tag, ap, shape, dtype):
        t = nc.dram_tensor("dbg_" + tag, list(shape), dtype, kind="ExternalOutput")
        b.barrier()
        idx = tuple(slice(None) for _ in shape)
        b.dma(out=t[idx], in_=ap)
        b.barrier()
        DBG[tag] = True

    sp, gp, ve, sc, te = b.sp, b.gp, b.ve, b.sc, b.te

    for t in range(NT):
        b.dma_gp(out=x_sb[:, t * D:(t + 1) * D], in_=x_in[t * 128:(t + 1) * 128, :])
    b.dma_gp(out=ident_sb[:, :], in_=dram["ident"][:, :])
    b.dma(out=segm_sb[:, :], in_=dram["seg_mask"][:, :])
    b.dma_gp(out=cmask_sb[:, :], in_=dram["causal_masks"][:, :])
    b.dma(out=onesrow_sb[:, :], in_=dram["ones_row"][:, :])
    b.dma_gp(out=onesrow_bf[:, :], in_=dram["ones_row"][:, :])
    for i, nm in enumerate(("b_qk", "b_v", "b_q", "b_k")):
        b.dma(out=bias_sb[:, i * 8:(i + 1) * 8], in_=dram[nm][:, :])
    for i, nm in enumerate(("b_vv_row", "b_o_row", "b1_row")):
        b.dma_gp(out=brow_sb[:, i * D:(i + 1) * D], in_=dram[nm][:, :])
    b.dma(out=qnw_sb[:, :], in_=dram["qnw"][:, :])
    b.dma(out=knw_sb[:, :], in_=dram["knw"][:, :])
    gp.op("memset", onescol_bf[:, :], 1.0)
    gp.op("memset", onescol_f32[:, :], 1.0)
    gp.op("memset", rowbuf[0:1, :], 0.0)
    gp.op("memset", consts[:, 0:1], LN_EPS)
    gp.op("memset", consts[:, 1:2], MAGIC)
    gp.op("memset", consts[:, 2:3], RMS_EPS)
    gp.op("memset", consts[:, 3:4], -HEADROOM)
    gp.op("memset", consts[:, 4:5], float(np.log(0.25)))
    gp.op("memset", maxacc[:, :], 0.0)
    gp.op("memset", maxqk[:, :], 0.0)
    b.barrier()

    if stop_after == "prologue":
        dbg_out("x0", x_sb[:, 0:D], (128, D), BF16)
        b.es.pop_all()
        return nc, b, DBG


    def ln_stats(tag):
        for t in range(NT):
            xt = x_sb[:, t * D:(t + 1) * D]
            ve.op("bn_stats", stat12[:, 0:6], xt[:, 0:512])
            ve.op("bn_stats", stat12[:, 6:12], xt[:, 512:1024])
            b.chain(ve)
            ve.op("bn_aggr", mv[:, 0:2], stat12[:, 0:12])
            tok = b.mark(ve)
            sc.wait(*tok)
            sc.op("activation", mv[:, 2:3], mv[:, 1:2], AF.Sqrt,
                  bias=consts[:, 0:1], scale=1.0)
            tok = b.mark(sc)
            ve.wait(*tok)
            ve.op("reciprocal", stats_sb[:, 2 * t:2 * t + 1], mv[:, 2:3])
            b.chain(ve)
            ve.op("tensor_scalar", stats_sb[:, 2 * t + 1:2 * t + 2], mv[:, 0:1],
                  stats_sb[:, 2 * t:2 * t + 1], -1.0, ALU.mult, ALU.mult)
            b.chain(ve)

    def xn_tile(t, dst):
        sc.op("activation", dst, x_sb[:, t * D:(t + 1) * D], AF.Identity,
              bias=stats_sb[:, 2 * t + 1:2 * t + 2],
              scale=stats_sb[:, 2 * t:2 * t + 1])
        return b.mark(sc)

    ln_stats("L0")
    b.barrier()

    for s in range(NSL):
        for j in range(4):
            t = s * 4 + j
            tok = xn_tile(t, t_bf[:, :])
            ve.wait(*tok)
            ve.op("tensor_reduce", mv[:, 3:4], t_bf[:, :], AX.X, ALU.max,
                  apply_absolute_value=True)
            b.chain(ve)
            ve.op("tensor_max", maxacc[:, 0:1], maxacc[:, 0:1], mv[:, 3:4])
            b.chain(ve)
            te.wait(*tok)
            ttok = None
            for c in range(NCH):
                bank = psb[c % 2]
                te.op("transpose", bank[:, 0:128], t_bf[:, c * 128:(c + 1) * 128],
                      ident_sb[:, :])
                ttok = b.mark(te)
                ve.wait(*ttok)
                ve.op("tensor_copy", xnslab[:, c * 512 + j * 128: c * 512 + (j + 1) * 128],
                      bank[:, 0:128])
                etok = b.mark(ve)
                te.wait(*etok)
            sc.wait(*ttok)
            sc.wait(*etok)
        cptok = b.mark(ve)
        sp.wait(*cptok)
        ve.wait(*cptok)
        sptoks = []
        for c in range(NCH):
            ve.op("tensor_copy", arena[:, c * NSL * 512 + s * 512:
                                       c * NSL * 512 + (s + 1) * 512],
                  xnslab[:, c * 512:(c + 1) * 512])
            sptoks.append(b.dma_on(sem_xnsp, out=xnfm_sp[c, s, :, :],
                                   in_=xnslab[:, c * 512:(c + 1) * 512]))
        b.chain(ve)
        te.wait(*sptoks[-1])
    b.barrier()

    if stop_after == "xn":
        dbg_out("xnfm0", arena[:, 0:512], (128, 512), BF16)
        dbg_out("stats", stats_sb[:, :], (128, NT * 2), F32)
        dbg_out("maxacc", maxacc[:, 0:1], (128, 1), F32)
        b.es.pop_all()
        return nc, b, DBG


    prr_guard = [None]

    def part_reduce_to_row(col_ap, slot):
        if prr_guard[0] is not None:
            sp.wait(*prr_guard[0])
        tok = b.dma_on(sem_aux, out=rowbuf[0:1, 0:128], in_=col_ap)
        ve.wait(*tok)
        ve.op("tensor_reduce", rowbuf[0:1, slot:slot + 1], rowbuf[0:1, 0:128],
              AX.X, ALU.max)
        prr_guard[0] = b.mark(ve)
        return prr_guard[0]

    def bcast_row(cols):
        srcs = [c[0] for c in cols]
        dsts = [c[1] for c in cols]
        k = len(cols)
        assert dsts == list(range(dsts[0], dsts[0] + k))
        for j, src in enumerate(srcs):
            ve.op("tensor_copy", rowbuf[0:1, 200 + j:201 + j],
                  rowbuf[0:1, src:src + 1])
        vtok = b.mark(ve)
        gp.wait(*vtok)
        t1 = b.dma_on(sem_ccd, out=scr_row[0:1, 0:k],
                      in_=rowbuf[0:1, 200:200 + k], eng=gp)
        gp.wait(*t1)
        t2 = b.dma_on(sem_ccd, out=scal[:, dsts[0]:dsts[0] + k],
                      in_=scr_row[0:1, 0:k].broadcast_to([128, k]), eng=gp)
        return t2

    def collective(round_idx, n):
        tok = b.dma_on(sem_ccd, out=ccin[round_idx][0:1, 0:64],
                       in_=rowbuf[0:1, 134:134 + 64], eng=gp)
        gp.wait(*tok)
        gp.op("collective_compute", "AllReduce", ALU.max,
              replica_groups=[list(range(NCORES))],
              ins=[ccin[round_idx][:, :].opt()],
              outs=[ccout[round_idx][:, :].opt()], inc=[(cc1, 1)])
        gp.wait(cc1, cc1.n)
        tok = b.dma_on(sem_ccd, out=rowbuf[0:1, 144:144 + n],
                       in_=ccout[round_idx][0:1, 0:n], eng=gp)
        return tok

    def row_ts(dst, src, s1, s2, op0, op1=ALU.bypass):
        ve.op("tensor_scalar", rowbuf[0:1, dst:dst + 1], rowbuf[0:1, src:src + 1],
              s1, s2, op0, op1)
        b.chain(ve)

    def row_recip(dst, src):
        ve.op("reciprocal", rowbuf[0:1, dst:dst + 1], rowbuf[0:1, src:src + 1])
        b.chain(ve)

    def row_mul(dst, a, bcol):
        ve.op("tensor_tensor", rowbuf[0:1, dst:dst + 1], rowbuf[0:1, a:a + 1],
              rowbuf[0:1, bcol:bcol + 1], ALU.mult)
        b.chain(ve)

    tok = part_reduce_to_row(maxacc[:, 0:1], 134)
    ve.wait(*tok)
    gp.wait(ve1, ve1.n)
    tok = collective(0, 1)
    ve.wait(*tok)
    row_ts(128, 144, 1.0 / QMAX0, 1e-12, ALU.mult, ALU.max)
    row_recip(133, 128)
    bcast_row([(133, 0)])
    b.barrier()

    if stop_after == "ar1":
        dbg_out("scal0", scal[:, 0:8], (128, 8), F32)
        dbg_out("rowb", rowbuf[0:1, :], (1, 256), F32)
        b.es.pop_all()
        return nc, b, DBG

    for c in range(NCH):
        for s in range(NSL):
            seg = arena[:, c * NSL * 512 + s * 512: c * NSL * 512 + (s + 1) * 512]
            ve.op("tensor_scalar", t_d[:, :], seg, scal[:, 0:1], MAGIC,
                  ALU.mult, ALU.add)
            b.chain(ve)
            ve.op("tensor_scalar", seg, t_d[:, :], MAGIC, None, ALU.subtract)
            b.chain(ve)
    b.barrier()

    if stop_after == "xq":
        dbg_out("xq0", arena[:, 0:512], (128, 512), BF16)
        b.es.pop_all()
        return nc, b, DBG

    def dequant_lpbq_tile(src_dram, row0, dst_bf, guard, dst_guard=None):
        if guard is not None:
            sp.wait(*guard)
        tok = b.dma_on(sem_wload, out=t_a[:, :], in_=src_dram[row0:row0 + 128, :])
        ve.wait(*tok)
        if dst_guard is not None:
            ve.wait(*dst_guard)
        ve.op("tensor_reduce",
              sc_blk[:, 0:32].rearrange("p (g o) -> p g o", o=1),
              t_a[:, :].rearrange("p (g k) -> p g k", k=32),
              AX.X, ALU.max, apply_absolute_value=True)
        b.chain(ve)
        ve.op("tensor_scalar", sc_blk[:, 32:64], sc_blk[:, 0:32], 1.0 / 7.0,
              1e-12, ALU.mult, ALU.max)
        b.chain(ve)
        ve.op("reciprocal", sc_blk[:, 64:96], sc_blk[:, 32:64])
        b.chain(ve)
        ve.op("tensor_tensor", t_b[:, :].rearrange("p (g k) -> p g k", k=32),
              t_a[:, :].rearrange("p (g k) -> p g k", k=32),
              sc_blk[:, 64:96].rearrange("p (g o) -> p g o", o=1).broadcast_to([128, 32, 32]),
              ALU.mult)
        b.chain(ve)
        ve.op("tensor_scalar", t_a[:, :], t_b[:, :], MAGIC, MAGIC, ALU.add, ALU.subtract)
        b.chain(ve)
        ve.op("tensor_tensor", dst_bf.rearrange("p (g k) -> p g k", k=32),
              t_a[:, :].rearrange("p (g k) -> p g k", k=32),
              sc_blk[:, 32:64].rearrange("p (g o) -> p g o", o=1).broadcast_to([128, 32, 32]),
              ALU.mult)
        return b.mark(ve)

    def transpose_into_wslot(src_bf_tok, slot_base, out_col, wtok):
        last = None
        ttok = None
        for k in range(NCH):
            bank = psb[k % 2]
            te.wait(*wtok)
            te.op("transpose", bank[:, 0:128], src_bf_tok[:, k * 128:(k + 1) * 128],
                  ident_sb[:, :])
            ttok = b.mark(te)
            ve.wait(*ttok)
            ve.op("tensor_copy",
                  wslot[:, slot_base + k * 512 + out_col: slot_base + k * 512 + out_col + 128],
                  bank[:, 0:128])
            last = b.mark(ve)
            te.wait(*last)
        return last, ttok

    UP_SLOT, GATE_SLOT = 0, 8 * 512
    pst_state = [None, None]
    red_state = [None, None]

    deq_guard = None
    wslot_guard = None
    tbf_guard = None
    NCB = 1 if stop_after in ("upgate1", "dequant") else 6
    for cb in range(NCB):
        for which, (src, base) in enumerate([(dram["w_up"], UP_SLOT),
                                             (dram["w_gate"], GATE_SLOT)]):
            for rt in range(4):
                row0 = cb * 512 + rt * 128
                if wslot_guard is not None:
                    ve.wait(*wslot_guard)
                    wslot_guard = None
                dtok = dequant_lpbq_tile(src, row0, t_bf[:, :], deq_guard,
                                          dst_guard=tbf_guard)
                _, tbf_guard = transpose_into_wslot(t_bf[:, :], base, rt * 128, dtok)
                deq_guard = dtok
        wtok = b.mark(ve)
        te.wait(*wtok)
        if stop_after == "dequant":
            break
        for which, (base, spdst, mcol) in enumerate([(UP_SLOT, u_sp, 2),
                                                     (GATE_SLOT, g_sp, 3)]):
            for cc in range(4):
                ch = cb * 4 + cc
                for s in range(NSL):
                    bank = psf[(s + cc) % 2]
                    for k in range(NCH):
                        te.op("matmul", bank[:, :],
                              wslot[:, base + k * 512 + cc * 128: base + k * 512 + (cc + 1) * 128],
                              arena[:, k * NSL * 512 + s * 512: k * NSL * 512 + (s + 1) * 512],
                              start=(k == 0), stop=(k == NCH - 1))
                    mtok = b.mark(te)
                    half = s % 2
                    stg = spst[:, half * 512:(half + 1) * 512]
                    sc.wait(*mtok)
                    if pst_state[half] is not None:
                        sc.wait(*pst_state[half])
                    if red_state[half] is not None:
                        sc.wait(*red_state[half])
                    sc.op("activation", stg, bank[:, :], AF.Identity, bias=0.0, scale=1.0)
                    etok = b.mark(sc)
                    te.wait(*etok)
                    ve.wait(*etok)
                    ve.op("tensor_reduce", mv[:, 3:4], stg, AX.X, ALU.max,
                          apply_absolute_value=True)
                    redtok = b.chain(ve)
                    ve.op("tensor_max", maxacc[:, mcol:mcol + 1],
                          maxacc[:, mcol:mcol + 1], mv[:, 3:4])
                    b.chain(ve)
                    gp.wait(*etok)
                    dtok = b.dma_on(sem_st[half], out=spdst[ch, s, :, :],
                                    in_=stg, eng=gp)
                    pst_state[half] = dtok
                    red_state[half] = redtok
                    wslot_guard = mtok
    b.barrier()

    if stop_after == "dequant":
        dbg_out("wslot0", wslot[:, 0:512], (128, 512), BF16)
        b.es.pop_all()
        return nc, b, DBG

    if stop_after in ("upgate", "upgate1"):
        dbg_out("maxug", maxacc[:, 2:4], (128, 2), F32)
        b.es.pop_all()
        return nc, b, DBG

    tok = part_reduce_to_row(maxacc[:, 2:3], 134)
    ve.wait(*tok)
    tok = part_reduce_to_row(maxacc[:, 3:4], 135)
    ve.wait(*tok)
    gp.wait(ve1, ve1.n)
    tok = collective(1, 2)
    ve.wait(*tok)
    row_mul(145, 144, 128)
    row_ts(129, 145, 1.0 / QMAX0, 1e-12, ALU.mult, ALU.max)
    row_mul(146, 144 + 1, 128)
    row_ts(130, 146, 1.0 / QMAX0, 1e-12, ALU.mult, ALU.max)
    row_recip(147, 129)
    row_mul(148, 147, 128)
    row_recip(149, 130)
    row_mul(150, 149, 128)
    row_ts(151, 130, -MAGIC, None, ALU.mult)
    bcast_row([(150, 1), (130, 2), (151, 3), (148, 4)])
    b.barrier()

    p1_iters = [(ch, s) for ch in range(NCHI) for s in range(NSL)]
    n_p1 = len(p1_iters)
    ld_g4 = [None] * 4
    st_g4 = [None] * 4
    td_rd = [None, None]
    sig_rd = [None, None]
    amark = [None] * n_p1
    bmark = [None] * n_p1
    tdbuf = (t_d, t_e)
    sigbuf = (t_a[:, 0:512], t_a[:, 512:1024])

    def p1_A(i):
        ch, s = p1_iters[i]
        q = i % 4
        p = i % 2
        stg = ldst[:, q * 512:(q + 1) * 512]
        if ld_g4[q] is not None:
            sp.wait(*ld_g4[q])
        ltok = b.dma_on(sem_ld[q % 2], out=stg, in_=g_sp[ch, s, :, :])
        ve.wait(*ltok)
        if td_rd[p] is not None:
            ve.wait(*td_rd[p])
        ve.op("tensor_scalar", tdbuf[p][:, :], stg, scal[:, 1:2], MAGIC,
              ALU.mult, ALU.add)
        amark[i] = b.mark(ve)
        ld_g4[q] = amark[i]

    def p1_B(i):
        p = i % 2
        sc.wait(*amark[i])
        if sig_rd[p] is not None:
            sc.wait(*sig_rd[p])
        sc.op("activation", sigbuf[p], tdbuf[p][:, :], AF.Sigmoid,
              bias=scal[:, 3:4], scale=scal[:, 2:3])
        bmark[i] = b.mark(sc)
        td_rd[p] = bmark[i]

    def p1_C(i):
        ch, s = p1_iters[i]
        p = i % 2
        q = i % 4
        ve.wait(*bmark[i])
        ve.op("tensor_scalar", sigbuf[p], sigbuf[p], 65536.0, 65535.0,
              ALU.mult, ALU.min)
        b.chain(ve)
        ve.op("tensor_scalar", sigbuf[p], sigbuf[p], MAGIC, MAGIC,
              ALU.add, ALU.subtract)
        b.chain(ve)
        pstg = xnslab[:, (4 + q) * 512:(5 + q) * 512]
        if st_g4[q] is not None:
            ve.wait(*st_g4[q])
        ve.op("scalar_tensor_tensor", pstg, tdbuf[p][:, :], MAGIC, sigbuf[p],
              ALU.subtract, ALU.mult)
        ptok = b.mark(ve)
        sig_rd[p] = ptok
        ve.wait(*ptok)
        ve.op("tensor_reduce", mv[:, 3:4], pstg, AX.X, ALU.max,
              apply_absolute_value=True)
        b.chain(ve)
        ve.op("tensor_max", maxacc[:, 4:5], maxacc[:, 4:5], mv[:, 3:4])
        b.chain(ve)
        gp.wait(*ptok)
        st_g4[q] = b.dma_on(sem_st[q % 2], out=p_sp[ch, s, :, :], in_=pstg,
                            eng=gp)

    for i in range(n_p1):
        p1_A(i)
        p1_B(i)
        if i >= 1:
            p1_C(i - 1)
    p1_C(n_p1 - 1)
    b.barrier()

    tok = part_reduce_to_row(maxacc[:, 4:5], 134)
    ve.wait(*tok)
    gp.wait(ve1, ve1.n)
    tok = collective(2, 1)
    ve.wait(*tok)
    row_ts(152, 130, 1.0 / 65536.0, None, ALU.mult)
    row_mul(153, 144, 152)
    row_ts(131, 153, 1.0 / QMAX0, 1e-12, ALU.mult, ALU.max)
    row_recip(154, 131)
    row_mul(155, 152, 154)
    bcast_row([(155, 5)])
    b.barrier()

    ld_g4 = [None] * 4
    ldu_g4 = [None] * 4
    st_g4 = [None] * 4
    ru_rd = [None, None]
    amark = [None] * n_p1
    bmark = [None] * n_p1

    def p2_A(i):
        ch, s = p1_iters[i]
        q = i % 4
        p = i % 2
        stgp = ldst[:, q * 512:(q + 1) * 512]
        stgu = xnslab[:, q * 512:(q + 1) * 512]
        if ld_g4[q] is not None:
            sp.wait(*ld_g4[q])
        if ldu_g4[q] is not None:
            sp.wait(*ldu_g4[q])
        ltok1 = b.dma_on(sem_ld[q % 2], out=stgp, in_=p_sp[ch, s, :, :])
        ltok2 = b.dma_on(sem_ld[q % 2], out=stgu, in_=u_sp[ch, s, :, :])
        ve.wait(*ltok2)
        ve.op("tensor_scalar", tdbuf[p][:, :], stgp, scal[:, 5:6], MAGIC,
              ALU.mult, ALU.add)
        amark[i] = b.mark(ve)
        ld_g4[q] = amark[i]
        sc.wait(*ltok2)
        if ru_rd[p] is not None:
            sc.wait(*ru_rd[p])
        sc.op("activation", sigbuf[p], stgu, AF.Identity,
              bias=consts[:, 1:2], scale=scal[:, 4:5])
        bmark[i] = b.mark(sc)
        ldu_g4[q] = bmark[i]

    def p2_C(i):
        ch, s = p1_iters[i]
        p = i % 2
        q = i % 4
        ve.wait(*bmark[i])
        ve.op("tensor_scalar", sigbuf[p], sigbuf[p], MAGIC, None, ALU.subtract)
        b.chain(ve)
        pstg = xnslab[:, (4 + q) * 512:(5 + q) * 512]
        if st_g4[q] is not None:
            ve.wait(*st_g4[q])
        ve.op("scalar_tensor_tensor", pstg, tdbuf[p][:, :], MAGIC, sigbuf[p],
              ALU.subtract, ALU.mult)
        ptok = b.mark(ve)
        ru_rd[p] = ptok
        ve.wait(*ptok)
        ve.op("tensor_reduce", mv[:, 3:4], pstg, AX.X, ALU.max,
              apply_absolute_value=True)
        b.chain(ve)
        ve.op("tensor_max", maxacc[:, 5:6], maxacc[:, 5:6], mv[:, 3:4])
        b.chain(ve)
        gp.wait(*ptok)
        st_g4[q] = b.dma_on(sem_st[q % 2], out=o_sp[ch, s, :, :], in_=pstg,
                            eng=gp)

    for i in range(n_p1):
        p2_A(i)
        if i >= 1:
            p2_C(i - 1)
    p2_C(n_p1 - 1)
    b.barrier()

    tok = part_reduce_to_row(maxacc[:, 5:6], 134)
    ve.wait(*tok)
    gp.wait(ve1, ve1.n)
    tok = collective(3, 1)
    ve.wait(*tok)
    row_mul(156, 131, 129)
    row_mul(157, 144, 156)
    row_ts(132, 157, 1.0 / QMAX0, 1e-12, ALU.mult, ALU.max)
    row_recip(158, 132)
    row_mul(159, 156, 158)
    bcast_row([(159, 6), (132, 7)])
    b.barrier()

    wdn_guard = deq_guard
    wdn_sl_guard = [None] * 8
    for rt in range(8):
        if wdn_guard is not None:
            sp.wait(*wdn_guard)
        tok = b.dma_on(sem_wload, out=t_a[:, :],
                       in_=dram["w_down"][rt * 128:(rt + 1) * 128, 0:1024])
        for part in range(3):
            if part > 0:
                sp.wait(*wdn_guard)
                tok = b.dma_on(sem_wload, out=t_a[:, :],
                               in_=dram["w_down"][rt * 128:(rt + 1) * 128,
                                                  part * 1024:(part + 1) * 1024])
            ve.wait(*tok)
            ve.op("tensor_reduce",
                  sc_blk[:, 0:32].rearrange("p (g o) -> p g o", o=1),
                  t_a[:, :].rearrange("p (g k) -> p g k", k=32),
                  AX.X, ALU.max, apply_absolute_value=True)
            b.chain(ve)
            ve.op("tensor_scalar", sc_blk[:, 32:64], sc_blk[:, 0:32], 1.0 / 7.0,
                  1e-12, ALU.mult, ALU.max)
            b.chain(ve)
            ve.op("reciprocal", sc_blk[:, 64:96], sc_blk[:, 32:64])
            b.chain(ve)
            ve.op("tensor_tensor", t_b[:, :].rearrange("p (g k) -> p g k", k=32),
                  t_a[:, :].rearrange("p (g k) -> p g k", k=32),
                  sc_blk[:, 64:96].rearrange("p (g o) -> p g o", o=1).broadcast_to([128, 32, 32]),
                  ALU.mult)
            b.chain(ve)
            ve.op("tensor_scalar", t_a[:, :], t_b[:, :], MAGIC, MAGIC,
                  ALU.add, ALU.subtract)
            b.chain(ve)
            ve.op("tensor_tensor", t_bf[:, :].rearrange("p (g k) -> p g k", k=32),
                  t_a[:, :].rearrange("p (g k) -> p g k", k=32),
                  sc_blk[:, 32:64].rearrange("p (g o) -> p g o", o=1).broadcast_to([128, 32, 32]),
                  ALU.mult)
            wtok = b.mark(ve)
            wdn_guard = wtok
            for k in range(NCH):
                bank = psb[k % 2]
                te.wait(*wtok)
                te.op("transpose", bank[:, 0:128], t_bf[:, k * 128:(k + 1) * 128],
                      ident_sb[:, :])
                ttok = b.mark(te)
                ve.wait(*ttok)
                if wdn_sl_guard[k] is not None:
                    ve.wait(*wdn_sl_guard[k])
                ve.op("tensor_copy",
                      spst[:, (k % 8) * 128:(k % 8 + 1) * 128], bank[:, 0:128])
                etok = b.mark(ve)
                te.wait(*etok)
                gp.wait(*etok)
                wsp_tok = b.dma_on(sem_wdnsl[k],
                                   out=wdnT_sp[part * 8 + k, :, rt * 128:(rt + 1) * 128],
                                   in_=spst[:, (k % 8) * 128:(k % 8 + 1) * 128],
                                   eng=gp)
                wdn_sl_guard[k] = wsp_tok
    b.barrier()

    ld_g4 = [None] * 4
    wg4 = [None] * 4
    region_rd = [None, None]
    oq_done = [None] * NSL

    def p3_quant(s):
        rb = (s % 2) * 12288
        waited_r = False
        tok = None
        for ch in range(NCHI):
            q = ch % 4
            stg = ldst[:, q * 512:(q + 1) * 512]
            if ld_g4[q] is not None:
                sp.wait(*ld_g4[q])
            ltok = b.dma_on(sem_ld[q % 2], out=stg, in_=o_sp[ch, s, :, :])
            ve.wait(*ltok)
            ve.op("tensor_scalar", t_d[:, :], stg, scal[:, 6:7], MAGIC,
                  ALU.mult, ALU.add)
            ld_g4[q] = b.mark(ve)
            ve.wait(*ld_g4[q])
            if not waited_r and region_rd[s % 2] is not None:
                ve.wait(*region_rd[s % 2])
                waited_r = True
            ve.op("tensor_scalar", arena[:, rb + ch * 512: rb + (ch + 1) * 512],
                  t_d[:, :], MAGIC, None, ALU.subtract)
            tok = b.chain(ve)
        oq_done[s] = tok

    p3_quant(0)
    for s in range(NSL):
        if s + 1 < NSL:
            p3_quant(s + 1)
        rb = (s % 2) * 12288
        te.wait(*oq_done[s])
        for hf in range(2):
            for k in range(NCHI):
                wq = k % 4
                wst = wslot[:, wq * 512:(wq + 1) * 512]
                if wg4[wq] is not None:
                    sp.wait(*wg4[wq])
                wtok = b.dma_on(sem_wslot, out=wst,
                                in_=wdnT_sp[k, :, hf * 512:(hf + 1) * 512])
                te.wait(*wtok)
                for sub in range(4):
                    te.op("matmul", psf[sub][:, :],
                          arena[:, rb + k * 512 + sub * 128: rb + k * 512 + (sub + 1) * 128],
                          wst, start=(k == 0), stop=(k == NCHI - 1))
                mtok = b.mark(te)
                wg4[wq] = mtok
            for sub in range(4):
                t = s * 4 + sub
                xsl = x_sb[:, t * D + hf * 512: t * D + (hf + 1) * 512]
                ve.wait(*mtok)
                ve.op("scalar_tensor_tensor", xsl, psf[sub][:, :], scal[:, 7:8],
                      xsl, ALU.mult, ALU.add)
                etok = b.mark(ve)
                te.wait(*etok)
        region_rd[s % 2] = mtok
    b.barrier()

    if stop_after == "ffn0":
        dbg_out("x_after_ffn0", x_sb[:, 0:D], (128, D), BF16)
        b.es.pop_all()
        return nc, b, DBG

    sem_wgp = b.dma_sem("wgp")
    wslot_rd_guard = [None, None]

    def load_wT(name, slot):
        if wslot_rd_guard[slot] is not None:
            gp.wait(*wslot_rd_guard[slot])
        base = slot * 8 * D
        tok = None
        for k in range(NCH):
            tok = b.dma_on(sem_wgp, out=wslot[:, base + k * D: base + (k + 1) * D],
                           in_=dram[name][k * 128:(k + 1) * 128, :], eng=gp)
        return tok

    arena_guard = {}

    def stream_src_slab(sp_t, sidx, base, sem):
        toks = []
        if arena_guard.get(base) is not None:
            sp.wait(*arena_guard[base])
        for k in range(NCH):
            toks.append(b.dma_on(sem, out=arena[:, base + k * 512: base + (k + 1) * 512],
                                 in_=sp_t[k, sidx, :, :]))
        return toks[-1]

    def proj_stage_from_spill(wname, slot, src_sp, dst_sp, bias_col, maxcol=None):
        wtok = load_wT(wname, slot)
        te.wait(*wtok)
        wb = slot * 8 * D
        lastmm = None
        for sidx in range(NSL):
            base = (sidx % 2) * 4096
            ltok = stream_src_slab(src_sp, sidx, base, sem_atld)
            te.wait(*ltok)
            for c in range(NCH):
                bank = psf[c % 2]
                for k in range(NCH):
                    te.op("matmul", bank[:, :],
                          wslot[:, wb + k * D + c * 128: wb + k * D + (c + 1) * 128],
                          arena[:, base + k * 512: base + (k + 1) * 512],
                          start=(k == 0), stop=(k == NCH - 1))
                mtok = b.mark(te)
                lastmm = mtok
                half = (sidx * NCH + c) % 2
                stg = spst[:, half * 512:(half + 1) * 512]
                sc.wait(*mtok)
                if pst_state[half] is not None:
                    sc.wait(*pst_state[half])
                if red_state[half] is not None:
                    sc.wait(*red_state[half])
                sc.op("activation", stg, bank[:, :], AF.Identity,
                      bias=bias_sb[:, bias_col * 8 + c: bias_col * 8 + c + 1], scale=1.0)
                etok = b.mark(sc)
                te.wait(*etok)
                if maxcol is not None:
                    ve.wait(*etok)
                    ve.op("tensor_reduce", mv[:, 3:4], stg, AX.X, ALU.max,
                          apply_absolute_value=True)
                    red_state[half] = b.chain(ve)
                    ve.op("tensor_copy", maxqk[:, maxcol * 64 + sidx * 8 + c:
                                               maxcol * 64 + sidx * 8 + c + 1], mv[:, 3:4])
                    b.chain(ve)
                gp.wait(*etok)
                dtok = b.dma_on(sem_st[half], out=dst_sp[c, sidx, :, :],
                                in_=stg, eng=gp)
                pst_state[half] = dtok
            arena_guard[base] = mtok
        wslot_rd_guard[slot] = lastmm

    proj_stage_from_spill("wqkT", 0, xnfm_sp, qk_dr, 0)
    proj_stage_from_spill("wvT", 0, xnfm_sp, vin_dr, 1)
    proj_stage_from_spill("wqT", 0, qk_dr, q_dr, 2, maxcol=0)
    proj_stage_from_spill("wkT", 0, qk_dr, k_dr, 3, maxcol=1)

    ve.op("tensor_tensor", t_d[:, 0:64], maxqk[:, 0:64], maxqk[:, 64:128],
          ALU.mult)
    prodtok = b.chain(ve)
    te.wait(*prodtok)
    te.op("matmul", psf[4][0:1, 0:64], onescol_f32[:, 0:1], t_d[:, 0:64],
          start=True, stop=True)
    dottok = b.mark(te)
    sc.wait(*dottok)
    sc.op("activation", rowbuf[0:1, 0:64], psf[4][0:1, 0:64], AF.Relu,
          bias=consts[0:1, 3:4], scale=SCALE0)
    shtok = b.mark(sc)
    te.wait(*shtok)
    gp.wait(*shtok)
    t1 = b.dma_on(sem_ccd, out=ccin[0][0:1, 0:64], in_=rowbuf[0:1, 0:64], eng=gp)
    gp.wait(*t1)
    t2 = b.dma_on(sem_ccd, out=negsh_sb[:, :],
                  in_=ccin[0][0:1, 0:64].broadcast_to([128, 64]), eng=gp)
    ve.wait(*t2)
    ve.op("tensor_scalar", negsh_sb[:, :], negsh_sb[:, :], -1.0, None, ALU.mult)
    negshtok = b.chain(ve)
    sc.wait(*negshtok)

    wtok = load_wT("wvvT", 0)
    te.wait(*wtok)
    lastmm = None
    for sidx in range(NSL):
        base = (sidx % 2) * 4096
        ltok = stream_src_slab(vin_dr, sidx, base, sem_atld)
        te.wait(*ltok)
        for sub in range(4):
            for hf in range(2):
                bank = psf[(sub * 2 + hf) % 2]
                for k in range(NCH):
                    te.op("matmul", bank[:, :],
                          arena[:, base + k * 512 + sub * 128: base + k * 512 + (sub + 1) * 128],
                          wslot[:, k * D + hf * 512: k * D + (hf + 1) * 512],
                          start=(k == 0), stop=False)
                te.op("matmul", bank[:, :], onesrow_bf[0:1, :],
                      brow_sb[0:1, hf * 512:(hf + 1) * 512],
                      start=False, stop=True)
                mtok = b.mark(te)
                lastmm = mtok
                half = (sub * 2 + hf) % 2
                stg = spst[:, half * 512:(half + 1) * 512]
                sc.wait(*mtok)
                if pst_state[half] is not None:
                    sc.wait(*pst_state[half])
                if red_state[half] is not None:
                    sc.wait(*red_state[half])
                sc.op("activation", stg, bank[:, :], AF.Identity, bias=0.0, scale=1.0)
                etok = b.mark(sc)
                te.wait(*etok)
                gp.wait(*etok)
                dtok = b.dma_on(sem_st[half],
                                out=vtok_dr[sidx, :, sub * D + hf * 512: sub * D + (hf + 1) * 512],
                                in_=stg, eng=gp)
                pst_state[half] = dtok
        arena_guard[base] = mtok
    wslot_rd_guard[0] = lastmm
    b.barrier()

    if stop_after == "l0proj":
        dbg_out("maxqk", maxqk[:, :], (128, 128), F32)
        b.es.pop_all()
        return nc, b, DBG

    wtok = load_wT("woT", 1)
    te.wait(*wtok)
    negsh = mv
    rc_guard = [None, None]
    tbf_rd = None
    obase, ofmb = 12288, 16384
    o_wr_guard = None
    for sidx in range(NSL):
        qbase, kbase, vbase = 0, 4096, 8192
        ltq = stream_src_slab(q_dr, sidx, qbase, sem_atld)
        ltk = stream_src_slab(k_dr, sidx, kbase, sem_atld)
        if arena_guard.get(vbase) is not None:
            sp.wait(*arena_guard[vbase])
        ltv = b.dma_on(sem_atld, out=arena[:, vbase:vbase + 4 * D],
                       in_=vtok_dr[sidx, :, :])
        te.wait(*ltv)
        last_ctx = None
        last_oev = None
        for h in range(NCH):
            for sub in range(4):
                sbank = psf[sub % 2]
                te.op("matmul", sbank[:, 0:128],
                      arena[:, kbase + h * 512 + sub * 128: kbase + h * 512 + (sub + 1) * 128],
                      arena[:, qbase + h * 512 + sub * 128: qbase + h * 512 + (sub + 1) * 128],
                      start=True, stop=True)
                mtok = b.mark(te)
                ve.wait(*mtok)
                ve.op("tensor_tensor", sbank[:, 0:128], sbank[:, 0:128], segm_sb[:, :],
                      ALU.add)
                atok = b.mark(ve)
                sc.wait(*atok)
                if tbf_rd is not None:
                    sc.wait(*tbf_rd)
                sc.op("activation", t_bf[:, 0:128], sbank[:, 0:128], AF.Exp,
                      bias=negsh_sb[:, sidx * 8 + h: sidx * 8 + h + 1],
                      scale=SCALE0)
                ptok = b.mark(sc)
                te.wait(*ptok)
                cbank = psf[2 + sub % 2]
                te.op("matmul", cbank[:, 0:128], t_bf[:, 0:128],
                      arena[:, vbase + sub * D + h * 128: vbase + sub * D + (h + 1) * 128],
                      start=True, stop=False)
                te.op("matmul", cbank[:, 128:129], t_bf[:, 0:128], onescol_bf[:, 0:1],
                      start=False, stop=True)
                ctok = b.mark(te)
                tbf_rd = ctok
                last_ctx = ctok
                rcol = sub % 2
                ve.wait(*ctok)
                if rc_guard[rcol] is not None:
                    ve.wait(*rc_guard[rcol])
                ve.op("reciprocal", negsh[:, rcol:rcol + 1], cbank[:, 128:129])
                rctok = b.chain(ve)
                sc.wait(*rctok)
                if o_wr_guard is not None:
                    sc.wait(*o_wr_guard)
                sc.op("activation",
                      arena[:, obase + sub * D + h * 128: obase + sub * D + (h + 1) * 128],
                      cbank[:, 0:128], AF.Identity, bias=0.0, scale=negsh[:, rcol:rcol + 1])
                otok = b.mark(sc)
                rc_guard[rcol] = otok
                last_oev = otok
                te.wait(*otok)
        te.wait(*last_oev)
        if arena_guard.get(ofmb) is not None:
            ve.wait(*arena_guard[ofmb])
        lasttr = None
        for sub in range(4):
            for c in range(NCH):
                bank = psb[c % 2]
                te.op("transpose", bank[:, 0:128],
                      arena[:, obase + sub * D + c * 128: obase + sub * D + (c + 1) * 128],
                      ident_sb[:, :])
                ttok = b.mark(te)
                lasttr = ttok
                ve.wait(*ttok)
                ve.op("tensor_copy",
                      arena[:, ofmb + c * 512 + sub * 128: ofmb + c * 512 + (sub + 1) * 128],
                      bank[:, 0:128])
                etok = b.mark(ve)
                te.wait(*etok)
        o_wr_guard = lasttr
        te.wait(*etok)
        lastmm2 = None
        for sub in range(4):
            for hf in range(2):
                bank = psf[4 + (sub * 2 + hf) % 2]
                for k in range(NCH):
                    te.op("matmul", bank[:, :],
                          arena[:, ofmb + k * 512 + sub * 128: ofmb + k * 512 + (sub + 1) * 128],
                          wslot[:, 8 * D + k * D + hf * 512: 8 * D + k * D + (hf + 1) * 512],
                          start=(k == 0), stop=False)
                te.op("matmul", bank[:, :], onesrow_bf[0:1, :],
                      brow_sb[0:1, D + hf * 512: D + (hf + 1) * 512],
                      start=False, stop=True)
                mtok = b.mark(te)
                lastmm2 = mtok
                t = sidx * 4 + sub
                xsl = x_sb[:, t * D + hf * 512: t * D + (hf + 1) * 512]
                ve.wait(*mtok)
                ve.op("tensor_tensor", xsl, xsl, bank[:, :], ALU.add)
                etok2 = b.chain(ve)
                te.wait(*etok2)
        arena_guard[qbase] = last_ctx
        arena_guard[kbase] = last_ctx
        arena_guard[vbase] = last_ctx
        wslot_rd_guard[1] = lastmm2
        arena_guard[ofmb] = lastmm2
    b.barrier()

    if stop_after == "layer0":
        dbg_out("x_after_l0", x_sb[:, :], (128, NT * D), BF16)
        b.es.pop_all()
        return nc, b, DBG

    ln_stats("L1")
    b.barrier()

    tok = b.dma_on(sem_aux, out=rowbuf[0:1, 166:167], in_=dram["dact"][0:1, 0:1])
    sc.wait(*tok)
    sc.op("activation", rowbuf[0:1, 167:168], rowbuf[0:1, 166:167], AF.Tanh,
          bias=0.0, scale=4.0)
    b.chain(sc)
    sc.op("activation", rowbuf[0:1, 168:169], rowbuf[0:1, 167:168], AF.Exp,
          bias=consts[0:1, 4:5], scale=float(0.5 * math.log(2.0)))
    fatok = b.mark(sc)
    ve.wait(*fatok)
    bcast_row([(168, 8)])
    b.barrier()

    def mxfp4_tile(src_f32, dst_bf, fa_ap, n):
        nblk = n // 32
        ve.op("tensor_reduce",
              sc_blk[:, 0:nblk].rearrange("p (g o) -> p g o", o=1),
              src_f32[:, 0:n].rearrange("p (g k) -> p g k", k=32),
              AX.X, ALU.max, apply_absolute_value=True)
        b.chain(ve)
        ve.op("tensor_scalar", sc_blk[:, 32:32 + nblk].bitcast(U32),
              sc_blk[:, 0:nblk].bitcast(U32), 0x7F800000, None, ALU.bitwise_and)
        b.chain(ve)
        if fa_ap.shape[1] == 1:
            ve.op("tensor_scalar", sc_blk[:, 64:64 + nblk],
                  sc_blk[:, 32:32 + nblk], fa_ap, 1e-30, ALU.mult, ALU.max)
        else:
            ve.op("tensor_tensor", sc_blk[:, 64:64 + nblk],
                  sc_blk[:, 32:32 + nblk], fa_ap, ALU.mult)
            b.chain(ve)
            ve.op("tensor_scalar", sc_blk[:, 64:64 + nblk],
                  sc_blk[:, 64:64 + nblk], 1e-30, None, ALU.max)
        b.chain(ve)
        ve.op("reciprocal", sc_blk[:, 96:96 + nblk], sc_blk[:, 64:64 + nblk])
        b.chain(ve)
        ve.op("tensor_tensor", t_a[:, 0:n].rearrange("p (g k) -> p g k", k=32),
              src_f32[:, 0:n].rearrange("p (g k) -> p g k", k=32),
              sc_blk[:, 96:96 + nblk].rearrange("p (g o) -> p g o", o=1).broadcast_to([128, nblk, 32]),
              ALU.mult)
        b.chain(ve)
        ve.op("tensor_scalar", t_b[:, 0:n], t_a[:, 0:n], 6.0, -6.0, ALU.min, ALU.max)
        b.chain(ve)
        ve.op("tensor_scalar", t_a[:, 0:n].bitcast(U32), t_b[:, 0:n].bitcast(U32),
              0x7F800000, None, ALU.bitwise_and)
        b.chain(ve)
        ve.op("tensor_scalar", t_bf[:, 0:n], t_a[:, 0:n], 1.0, 4.0, ALU.max, ALU.min)
        b.chain(ve)
        ve.op("scalar_tensor_tensor", t_a[:, 0:n], t_bf[:, 0:n], MAGIC / 2.0,
              t_b[:, 0:n], ALU.mult, ALU.add)
        b.chain(ve)
        ve.op("scalar_tensor_tensor", t_b[:, 0:n], t_bf[:, 0:n], -MAGIC / 2.0,
              t_a[:, 0:n], ALU.mult, ALU.add)
        b.chain(ve)
        ve.op("tensor_tensor", dst_bf[:, 0:n].rearrange("p (g k) -> p g k", k=32),
              t_b[:, 0:n].rearrange("p (g k) -> p g k", k=32),
              sc_blk[:, 64:64 + nblk].rearrange("p (g o) -> p g o", o=1).broadcast_to([128, nblk, 32]),
              ALU.mult)
        return b.chain(ve)

    w1_guard = None
    for rt in range(NCH):
        if w1_guard is not None:
            sp.wait(*w1_guard)
        tok = b.dma_on(sem_wload, out=t_d[:, :],
                       in_=dram["w1"][rt * 128:(rt + 1) * 128, 0:512])
        tok2 = b.dma_on(sem_wload, out=t_e[:, :],
                        in_=dram["w1"][rt * 128:(rt + 1) * 128, 512:1024])
        dwt = b.dma_on(sem_wload, out=maxqk[:, 96:128],
                       in_=dram["dw"][rt * 128:(rt + 1) * 128, :])
        sc.wait(*dwt)
        sc.op("activation", maxqk[:, 64:96], maxqk[:, 96:128], AF.Tanh,
              bias=0.0, scale=4.0)
        b.chain(sc)
        sc.op("activation", maxqk[:, 96:128], maxqk[:, 64:96], AF.Exp,
              bias=consts[:, 4:5], scale=float(0.5 * math.log(2.0)))
        fwtok = b.mark(sc)
        ve.wait(*dwt)
        ve.wait(*fwtok)
        for hh, stg in enumerate((t_d, t_e)):
            qtok = mxfp4_tile(stg, t_bf, maxqk[:, 96 + hh * 16:112 + hh * 16], 512)
            for k in range(4):
                kk = hh * 4 + k
                bank = psb[kk % 2]
                te.wait(*qtok)
                te.op("transpose", bank[:, 0:128], t_bf[:, k * 128:(k + 1) * 128],
                      ident_sb[:, :])
                ttok = b.mark(te)
                ve.wait(*ttok)
                ve.op("tensor_copy",
                      wslot[:, kk * D + rt * 128: kk * D + (rt + 1) * 128],
                      bank[:, 0:128])
                etok = b.mark(ve)
                te.wait(*etok)
            w1_guard = etok
    b.barrier()

    ytr_guard = None
    for sidx in range(NSL):
        for j in range(4):
            t = sidx * 4 + j
            tok = xn_tile(t, t_bf[:, :])
            te.wait(*tok)
            for c in range(NCH):
                bank = psb[c % 2]
                te.op("transpose", bank[:, 0:128], t_bf[:, c * 128:(c + 1) * 128],
                      ident_sb[:, :])
                ttok = b.mark(te)
                ve.wait(*ttok)
                ve.op("tensor_copy", xnslab[:, c * 512 + j * 128: c * 512 + (j + 1) * 128],
                      bank[:, 0:128])
                etok = b.mark(ve)
                te.wait(*etok)
            ve.wait(*tok)
            if ytr_guard is not None:
                ve.wait(*ytr_guard)
            ytok = mxfp4_tile(t_bf, spst, scal[:, 8:9], 1024)
            sc.wait(*ytok)
            sc.wait(*ttok)
            for c in range(NCH):
                bank = psb[c % 2]
                te.wait(*ytok)
                te.op("transpose", bank[:, 0:128], spst[:, c * 128:(c + 1) * 128],
                      ident_sb[:, :])
                ttok2 = b.mark(te)
                ytr_guard = ttok2
                ve.wait(*ttok2)
                ve.op("tensor_copy",
                      arena[:, c * NSL * 512 + sidx * 512 + j * 128:
                            c * NSL * 512 + sidx * 512 + (j + 1) * 128],
                      bank[:, 0:128])
                etok2 = b.mark(ve)
                te.wait(*etok2)
        cptok = b.mark(ve)
        sp.wait(*cptok)
        ve.wait(*cptok)
        sptoks = []
        for c in range(NCH):
            sptoks.append(b.dma_on(sem_xnsp, out=xnfm_sp[c, sidx, :, :],
                                   in_=xnslab[:, c * 512:(c + 1) * 512]))
        te.wait(*sptoks[-1])
    b.barrier()

    if stop_after == "xq1":
        dbg_out("xq1", arena[:, 0:512], (128, 512), BF16)
        dbg_out("x_l1", x_sb[:, :], (128, NT * D), BF16)
        b.es.pop_all()
        return nc, b, DBG
    if stop_after == "full":
        dbg_out("x_l1", x_sb[:, :], (128, NT * D), BF16)

    for sidx in range(NSL):
        for sub in range(4):
            for hf in range(2):
                bank = psf[(sub * 2 + hf) % 4]
                for k in range(NCH):
                    te.op("matmul", bank[:, :],
                          arena[:, k * NSL * 512 + sidx * 512 + sub * 128:
                                k * NSL * 512 + sidx * 512 + (sub + 1) * 128],
                          wslot[:, k * D + hf * 512: k * D + (hf + 1) * 512],
                          start=(k == 0), stop=False)
                te.op("matmul", bank[:, :], onesrow_bf[0:1, :],
                      brow_sb[0:1, 2 * D + hf * 512: 2 * D + (hf + 1) * 512],
                      start=False, stop=True)
                mtok = b.mark(te)
                t = sidx * 4 + sub
                xsl = x_sb[:, t * D + hf * 512: t * D + (hf + 1) * 512]
                ve.wait(*mtok)
                ve.op("tensor_tensor", xsl, xsl, bank[:, :], ALU.add)
                etok = b.chain(ve)
                te.wait(*etok)
    b.barrier()

    if stop_after == "ffn1":
        dbg_out("x_after_ffn1", x_sb[:, 0:D], (128, D), BF16)
        b.es.pop_all()
        return nc, b, DBG

    out_guard = None
    for t in range(NT):
        if out_guard is not None:
            ve.wait(*out_guard)
        ve.op("tensor_copy", t_a[:, :], x_sb[:, t * D:(t + 1) * D])
        otok = b.chain(ve)
        sp.wait(*otok)
        out_guard = b.dma_on(sem_out, out=out_t[t * 128:(t + 1) * 128, :],
                             in_=t_a[:, :])
    b.barrier()

    tok1 = b.dma_on(sem_wload, out=t_a[:, 0:256], in_=dram["W_DQ"][:, :])
    tok2 = b.dma_on(sem_wload, out=t_a[:, 256:512], in_=dram["W_UQ_T"][:, :])
    tok3 = b.dma_on(sem_wload, out=t_b[:, 0:512], in_=dram["W_DKV"][:, :])
    tok4 = b.dma_on(sem_wload, out=t_d[:, :], in_=dram["W_UKV_T"][:, 0:512])
    tok4b = b.dma_on(sem_wload, out=t_e[:, :], in_=dram["W_UKV_T"][:, 512:1024])
    tok5 = b.dma_on(sem_wload, out=t_b[0:64, 512:576], in_=dram["W_QR_T"][:, :])
    tok6 = b.dma_on(sem_wload, out=t_b[0:64, 576:640], in_=dram["W_KR_T"][:, :])
    tok7 = b.dma_on(sem_wload, out=t_b[0:64, 640:704], in_=dram["W_O_T"][:, :])
    te.wait(*tok7)
    for c in range(4):
        te.op("matmul", psf[0][0:64, 0:64], t_a[:, c * 64:(c + 1) * 64],
              t_a[:, 256 + c * 64:256 + (c + 1) * 64],
              start=(c == 0), stop=(c == 3))
    mq_tok = b.mark(te)
    for c in range(NCH):
        src = t_d if c < 4 else t_e
        te.op("matmul", psf[1][0:64, 0:128], t_b[:, c * 64:(c + 1) * 64],
              src[:, (c % 4) * 128:(c % 4 + 1) * 128],
              start=(c == 0), stop=(c == 7))
    mkv_tok = b.mark(te)
    gp.op("memset", bd_sb[:, :], 0.0)
    gtok = (b.S["gp1"], b.S["gp1"].n)
    ve.wait(*gtok)
    ve.wait(*mq_tok)
    ve.op("tensor_copy", bd_sb[0:64, 0:64], psf[0][0:64, 0:64])
    b.chain(ve)
    ve.wait(*mkv_tok)
    ve.op("tensor_copy", bd_sb[0:64, 128:192], psf[1][0:64, 0:64])
    b.chain(ve)
    ve.op("tensor_copy", bd_sb[0:64, 256:320], psf[1][0:64, 64:128])
    b.chain(ve)
    ve.wait(*tok7)
    ve.op("tensor_copy", bd_sb[0:64, 384:448], t_b[0:64, 512:576])
    b.chain(ve)
    ve.op("tensor_copy", bd_sb[0:64, 512:576], t_b[0:64, 576:640])
    b.chain(ve)
    ve.op("tensor_copy", bd_sb[0:64, 640:704], t_b[0:64, 640:704])
    cptok = b.chain(ve)
    sp.wait(*cptok)
    dtoks = []
    for slot in range(5):
        dtoks.append(b.dma_on(sem_aux, out=bd_sb[64:128, slot * 128 + 64: slot * 128 + 128],
                              in_=bd_sb[0:64, slot * 128: slot * 128 + 64]))
    te.wait(*dtoks[-1])
    ve.wait(*dtoks[-1])
    gp.op("memset", bd_sb[64:65, 704:705], 1.0)
    bdtok = (b.S["gp1"], b.S["gp1"].n)
    te.wait(*bdtok)
    ve.op("tensor_scalar", qnw_sb[:, :], qnw_sb[:, :], SCALE1, None, ALU.mult)
    qnwtok = b.chain(ve)
    b.barrier()

    AXH, AQC, AQR, AKC, AKR, AVT, ACT_ = 0, 2048, 4096, 6144, 8192, 10240, 12320
    attnbuf = arena[:, 13312:13312 + 4096].bitcast(F32)
    a_guard = {}

    gp.op("memset", arena[:, AVT:AVT + 16 * 130], 1.0)
    avt_ones = (b.S["gp1"], b.S["gp1"].n)
    sc.wait(*avt_ones)

    if stop_after == "attn1":
        dbg_out("xpre", x_sb[:, :], (128, NT * D), BF16)
    UNITS = 1 if stop_after == "attn1" else 16
    ntok_prev = [None, None]
    trtok_prev = [None, None]
    for unit in range(UNITS):
        seq2, pp = divmod(unit, 8)
        if a_guard.get("qtr") is not None:
            ve.wait(*a_guard["qtr"])
            sc.wait(*a_guard["qtr"])
        if a_guard.get(AXH) is not None:
            sp.wait(*a_guard[AXH])
        xtoks = []
        for half in range(4):
            sl = seq2 * 4 + half
            xtoks.append(b.dma_on(sem_atld,
                                  out=arena[:, AXH + half * 512: AXH + (half + 1) * 512],
                                  in_=xnfm_sp[pp, sl, :, :]))
        xh_tok = xtoks[-1]
        te.wait(*xh_tok)
        last_ctx = None
        prev_qtr = a_guard.get("qtr")
        for sub in range(16):
            xh_sl = arena[:, AXH + sub * 128: AXH + (sub + 1) * 128]
            bank = psf[sub % 2]
            bank2 = psf[2 + sub % 2]
            te.op("matmul", bank[:, 0:128], xh_sl, bd_sb[:, 0:128],
                  start=True, stop=True)
            qctok = b.mark(te)
            te.op("matmul", bank2[:, 0:128], xh_sl, bd_sb[:, 128:256],
                  start=True, stop=True)
            kctok = b.mark(te)
            rms_sets = [(bank, qctok, qnw_sb, AQC, t_d[:, 0:128],
                         t_bf[:, 0:128], mv),
                        (bank2, kctok, knw_sb, AKC, t_d[:, 128:256],
                         t_bf[:, 128:256], stat12)]
            sqk2 = [None, None]
            mstok2 = [None, None]
            sqtok2 = [None, None]
            ntok2 = [None, None]
            trtok2 = [None, None]
            for W, (bk, tok_, nw, dst_base, sq, ob, st) in enumerate(rms_sets):
                sc.wait(*tok_)
                if ntok_prev[W] is not None:
                    sc.wait(*ntok_prev[W])
                sc.op("activation", sq, bk[:, 0:128], AF.Square,
                      bias=0.0, scale=1.0)
                sqk2[W] = b.mark(sc)
            for W, (bk, tok_, nw, dst_base, sq, ob, st) in enumerate(rms_sets):
                ve.wait(*sqk2[W])
                ve.op("tensor_reduce", st[:, 0:2].rearrange("p (g o) -> p g o", o=1),
                      sq.rearrange("p (g k) -> p g k", k=64),
                      AX.X, ALU.add)
                b.chain(ve)
                ve.op("tensor_scalar", st[:, 0:2], st[:, 0:2], 1.0 / 64.0, None,
                      ALU.mult)
                mstok2[W] = b.chain(ve)
            for W, (bk, tok_, nw, dst_base, sq, ob, st) in enumerate(rms_sets):
                sc.wait(*mstok2[W])
                sc.op("activation", st[:, 2:4], st[:, 0:2], AF.Sqrt,
                      bias=consts[:, 2:3], scale=1.0)
                sqtok2[W] = b.mark(sc)
            for W, (bk, tok_, nw, dst_base, sq, ob, st) in enumerate(rms_sets):
                ve.wait(*sqtok2[W])
                ve.op("reciprocal", st[:, 0:2], st[:, 2:4])
                b.chain(ve)
                ve.op("tensor_tensor", sq.rearrange("p (g k) -> p g k", k=64),
                      bk[:, 0:128].rearrange("p (g k) -> p g k", k=64),
                      st[:, 0:2].rearrange("p (g o) -> p g o", o=1).broadcast_to([128, 2, 64]),
                      ALU.mult)
                b.chain(ve)
                if trtok_prev[W] is not None:
                    ve.wait(*trtok_prev[W])
                ve.op("tensor_tensor", ob.rearrange("p (g k) -> p g k", k=64),
                      sq.rearrange("p (g k) -> p g k", k=64),
                      nw[:, :].rearrange("p (g k) -> p g k", g=1).broadcast_to([128, 2, 64]),
                      ALU.mult)
                ntok2[W] = b.chain(ve)
                ntok_prev[W] = ntok2[W]
            for W, (bk, tok_, nw, dst_base, sq, ob, st) in enumerate(rms_sets):
                te.wait(*ntok2[W])
                te.op("transpose", psb[W][:, 0:128], ob, ident_sb[:, :])
                trtok2[W] = b.mark(te)
                trtok_prev[W] = trtok2[W]
            for W, (bk, tok_, nw, dst_base, sq, ob, st) in enumerate(rms_sets):
                ve.wait(*trtok2[W])
                if prev_qtr is not None:
                    ve.wait(*prev_qtr)
                    prev_qtr = None
                ve.op("tensor_copy",
                      arena[:, dst_base + sub * 128: dst_base + (sub + 1) * 128],
                      psb[W][:, 0:128])
                evtok = b.chain(ve)
                te.wait(*evtok)
            te.op("matmul", bank[:, 0:128], xh_sl, bd_sb[:, 256:384],
                  start=True, stop=True)
            vtok_ = b.mark(te)
            sc.wait(*vtok_)
            sc.op("activation", arena[:, AVT + sub * 130: AVT + sub * 130 + 64],
                  bank[:, 0:64], AF.Identity, bias=0.0, scale=1.0)
            sc.op("activation", arena[:, AVT + sub * 130 + 65: AVT + sub * 130 + 129],
                  bank[:, 64:128], AF.Identity, bias=0.0, scale=1.0)
            vevtok = b.mark(sc)
            te.wait(*vevtok)

        for sl4 in range(4):
            for which, (srcb, bdofs, dstb) in enumerate(
                    [(AQC, 384, AQR), (AXH, 512, AKR)]):
                bank = psf[4 + which]
                te.wait(*evtok)
                te.op("matmul", bank[:, :], bd_sb[:, bdofs:bdofs + 128],
                      arena[:, srcb + sl4 * 512: srcb + (sl4 + 1) * 512],
                      start=True, stop=True)
                mtok = b.mark(te)
                sc.wait(*mtok)
                sc.op("activation", arena[:, dstb + sl4 * 512: dstb + (sl4 + 1) * 512],
                      bank[:, :], AF.Identity, bias=0.0, scale=1.0)
                rev = b.mark(sc)
                te.wait(*rev)
        proj_done = rev
        ve.wait(*proj_done)
        ve.op("tensor_scalar", rowbuf[0:1, 176:184], rowbuf[0:1, 176:184],
              0.0, None, ALU.mult)
        b.chain(ve)
        for pi, (b1, b2) in enumerate(((AQC, AQR), (AKC, AKR))):
            for half in range(2):
                lastmm3 = None
                for si, src in enumerate((b1, b2)):
                    if lastmm3 is not None:
                        ve.wait(*lastmm3)
                    ve.op("tensor_tensor", t_bf[:, :],
                          arena[:, src + half * 1024: src + (half + 1) * 1024],
                          arena[:, src + half * 1024: src + (half + 1) * 1024],
                          ALU.mult)
                    sqk = b.chain(ve)
                    te.wait(*sqk)
                    for h in range(2):
                        for sl5 in range(2):
                            te.op("matmul", psf[4 + h][0:1, sl5 * 256:sl5 * 256 + 256]
                                  if False else psf[4 + h][0:1, sl5 * 512 - sl5 * 512: 512]
                                  if False else psf[4 + h][0:1, 0:512],
                                  onescol_bf[64 * h:64 * h + 64, 0:1],
                                  t_bf[64 * h:64 * h + 64, sl5 * 512:(sl5 + 1) * 512],
                                  start=(si == 0 and sl5 == 0),
                                  stop=(si == 1 and sl5 == 1),
                                  skip_group_check=True)
                    lastmm3 = b.mark(te)
                for h in range(2):
                    ve.wait(*lastmm3)
                    ve.op("tensor_reduce", rowbuf[0:1, 190:191],
                          psf[4 + h][0:1, 0:512], AX.X, ALU.max)
                    b.chain(ve)
                    ve.op("tensor_max",
                          rowbuf[0:1, 176 + 2 * h + pi:177 + 2 * h + pi],
                          rowbuf[0:1, 176 + 2 * h + pi:177 + 2 * h + pi],
                          rowbuf[0:1, 190:191])
                    etok2 = b.chain(ve)
                    te.wait(*etok2)
        for h in range(2):
            ve.op("tensor_tensor", rowbuf[0:1, 170 + h:171 + h],
                  rowbuf[0:1, 176 + 2 * h:177 + 2 * h],
                  rowbuf[0:1, 177 + 2 * h:178 + 2 * h], ALU.mult)
            b.chain(ve)
        vtk = (b.S["ve1"], b.S["ve1"].n)
        sc.wait(*vtk)
        sc.op("activation", rowbuf[0:1, 174:176], rowbuf[0:1, 170:172], AF.Sqrt,
              bias=0.0, scale=1.0)
        b.chain(sc)
        sc.op("activation", rowbuf[0:1, 172:174], rowbuf[0:1, 174:176], AF.Relu,
              bias=consts[0:1, 3:4], scale=1.0)
        shtok = b.mark(sc)
        gp.wait(*shtok)
        t1 = b.dma_on(sem_ccd, out=scr_row[0:1, 0:2], in_=rowbuf[0:1, 172:174], eng=gp)
        gp.wait(*t1)
        if a_guard.get("negsh") is not None:
            gp.wait(*a_guard["negsh"])
        t2 = b.dma_on(sem_ccd, out=mv[:, 6:8],
                      in_=scr_row[0:1, 0:2].broadcast_to([128, 2]), eng=gp)
        ve.wait(*t2)
        ve.op("tensor_scalar", mv[:, 4:6], mv[:, 6:8], -1.0, None, ALU.mult)
        nstok = b.chain(ve)
        exp_rd2 = [None, None]
        exp_rd = None
        ktg = 0
        for h in range(2):
            qcn_h = arena[:, AQC:AQC + 2048]
            for qb in range(4):
                cb = psf[5]
                nkt = 4 * qb + 4
                for kt in range(nkt):
                    pb = ktg % 2
                    ktg += 1
                    tbf_h = t_bf[:, pb * 512:(pb + 1) * 512]
                    sbank = psf[kt % 2]
                    te.op("matmul", sbank[:, :],
                          arena[64 * h:64 * h + 64, AKC + kt * 128: AKC + (kt + 1) * 128],
                          arena[64 * h:64 * h + 64, AQC + qb * 512: AQC + (qb + 1) * 512],
                          start=True, stop=False)
                    te.op("matmul", sbank[:, :],
                          arena[64 * h:64 * h + 64, AKR + kt * 128: AKR + (kt + 1) * 128],
                          arena[64 * h:64 * h + 64, AQR + qb * 512: AQR + (qb + 1) * 512],
                          start=False, stop=True)
                    smtok = b.mark(te)
                    j = kt - 4 * qb
                    if j >= 0:
                        ve.wait(*smtok)
                        ve.op("tensor_tensor", sbank[:, :], sbank[:, :],
                              cmask_sb[:, j * 512:(j + 1) * 512], ALU.add)
                        amtok = b.chain(ve)
                    else:
                        amtok = smtok
                    sc.wait(*amtok)
                    if exp_rd2[pb] is not None:
                        sc.wait(*exp_rd2[pb])
                    sc.op("activation", tbf_h, sbank[:, :], AF.Exp,
                          bias=mv[:, 4 + h:5 + h], scale=1.0)
                    ptok = b.mark(sc)
                    te.wait(*ptok)
                    te.op("matmul", cb[0:65, :],
                          arena[:, AVT + kt * 130 + 65 * h: AVT + kt * 130 + 65 * h + 65],
                          tbf_h, start=(kt == 0), stop=(kt == nkt - 1))
                    exp_rd = b.mark(te)
                    exp_rd2[pb] = exp_rd
                last_ctx = exp_rd
                sc.wait(*last_ctx)
                if a_guard.get("ctxT") is not None:
                    sc.wait(*a_guard["ctxT"])
                sc.op("activation", arena[0:65, ACT_:ACT_ + 512], cb[0:65, :],
                      AF.Identity, bias=0.0, scale=1.0)
                cevtok = b.mark(sc)
                te.wait(*cevtok)
                for sub in range(4):
                    ob = psf[2 + sub % 2]
                    te.op("matmul", ob[:, 0:65],
                          arena[0:65, ACT_ + sub * 128: ACT_ + (sub + 1) * 128],
                          bd_sb[0:65, 640:705], start=True, stop=True)
                    omtok = b.mark(te)
                    ve.wait(*omtok)
                    if a_guard.get("rcp2") is not None:
                        ve.wait(*a_guard["rcp2"])
                    ve.op("reciprocal", t_d[:, 1:2], ob[:, 64:65])
                    rctok = b.chain(ve)
                    tl = qb * 4 + sub
                    asl = attnbuf[:, tl * 128 + h * 64: tl * 128 + h * 64 + 64]
                    ve.op("tensor_scalar", asl, ob[:, 0:64], t_d[:, 1:2], None,
                          ALU.mult)
                    fintok = b.chain(ve)
                    te.wait(*fintok)
                    a_guard["rcp2"] = fintok
                a_guard["ctxT"] = omtok
        flv = fintok
        gp.wait(*flv)
        if a_guard.get("flush") is not None:
            gp.wait(*a_guard["flush"])
        gp.op("dma_start",
              out=out_t[seq2 * 2048:(seq2 + 1) * 2048, pp * 128:(pp + 1) * 128]
              .rearrange("(t p) m -> p t m", p=128),
              in_=attnbuf[:, :].rearrange("p (t m) -> p t m", m=128),
              accum_op=ALU.add, inc=[(sem_out, 16)])
        ftok = (sem_out, sem_out.n)
        a_guard["flush"] = ftok
        ve.wait(*ftok)
        a_guard[AXH] = last_ctx
        a_guard["qtr"] = last_ctx
        a_guard["negsh"] = ptok
    b.barrier()
    if stop_after == "attn1":
        dbg_out("mvd", mv[:, :], (128, 8), F32)
        dbg_out("rowb2", rowbuf[0:1, :], (1, 256), F32)
        dbg_out("tdd", t_d[:, 0:2], (128, 2), F32)
        dbg_out("qcn", arena[:, AQC:AQC + 2048], (128, 2048), BF16)
        dbg_out("qr", arena[:, AQR:AQR + 2048], (128, 2048), BF16)
        dbg_out("kcn", arena[:, AKC:AKC + 2048], (128, 2048), BF16)
        dbg_out("kr", arena[:, AKR:AKR + 2048], (128, 2048), BF16)
        dbg_out("vt", arena[:, AVT:AVT + 2048], (128, 2048), BF16)
        dbg_out("ctxT", arena[:, ACT_:ACT_ + 512], (128, 512), BF16)
        dbg_out("xout", x_sb[:, :], (128, NT * D), BF16)
        dbg_out("xl1", xnfm_sp[0, 0, :, :], (128, 512), BF16)
        b.es.pop_all()
        return nc, b, DBG

    b.barrier()
    b.es.pop_all()
    return nc, b, DBG


def _layer1(nc, b, dram, L):
    pass



def kernel(**inputs):
    W = host_prep(inputs)
    nc, b, DBG = emit_program(stop_after=None)
    _finalize(nc, b)
    in_maps = []
    for c in range(NCORES):
        m = {k: v for k, v in W.items() if k != "x"}
        m["x"] = np.ascontiguousarray(W["x"][2 * c:2 * c + 2].reshape(TOK, D))
        in_maps.append(m)
    res = run_bass_kernel_spmd(nc, in_maps, core_ids=list(range(NCORES)))
    outs = [r["out"] for r in res.results]
    full = np.concatenate([np.asarray(o).reshape(2, SEQ, D) for o in outs], axis=0)
    return full.astype(np.float32, copy=False)


def _finalize(nc, b):
    with nc.Block() as block:
        @block.sync
        def _(e):
            b.sp.replay(e)

        @block.gpsimd
        def _(e):
            b.gp.replay(e)

        @block.vector
        def _(e):
            b.ve.replay(e)

        @block.scalar
        def _(e):
            b.sc.replay(e)

        @block.tensor
        def _(e):
            b.te.replay(e)


# version 09
# speedup vs baseline: 1.3118x; 1.0105x over previous
import math
import contextlib
import numpy as np
import concourse.bass as bass
import concourse.mybir as mybir
from concourse.bass_utils import run_bass_kernel_spmd

F32 = mybir.dt.float32
BF16 = mybir.dt.bfloat16
U32 = mybir.dt.uint32
AF = mybir.ActivationFunctionType
ALU = mybir.AluOpType
AX = mybir.AxisListType

MAGIC = float(np.float32(3.0 * 2.0 ** 22))
NCORES = 8
TOK = 4096
SEQ = 2048
D = 1024
DI = 3072
NT = 32
NSL = 8
NCH = 8
NCHI = 24
QMAX0 = 32767.0
LN_EPS = 1e-5
RMS_EPS = 1.1920929e-07
HEADROOM = 30.0
SCALE0 = 1.0 / math.sqrt(128.0)
SCALE1 = 64.0 ** (-0.5)
EXPMASK = float(np.float32(np.frombuffer(np.uint32(0x7F800000).tobytes(), dtype=np.float32)[0]))


class Sem:
    __slots__ = ("h", "n")

    def __init__(self, h):
        self.h = h
        self.n = 0


class Eng:
    def __init__(self, name):
        self.name = name
        self.items = []
        self.auto_sem = None

    def wait(self, sem, val=None):
        if isinstance(sem, tuple):
            sem, val = sem
        h = sem.h if isinstance(sem, Sem) else sem
        self.items.append(("w", h, int(val)))

    def op(self, method, *args, inc=None, **kwargs):
        pairs = []
        if inc:
            for s, k in inc:
                s.n += k
                pairs.append((s.h, k))
        if (self.auto_sem is not None and not pairs
                and not method.startswith("dma")
                and method not in ("collective_compute",)):
            self.auto_sem.n += 1
            pairs.append((self.auto_sem.h, 1))
        self.items.append(("o", method, args, kwargs, pairs))

    def replay(self, eng):
        for it in self.items:
            if it[0] == "w":
                eng.wait_ge(it[1], it[2])
            else:
                _, method, args, kwargs, pairs = it
                ins = getattr(eng, method)(*args, **kwargs)
                for h, k in pairs:
                    ins = ins.then_inc(h, k)


class B:
    def __init__(self, nc):
        self.nc = nc
        self.es = contextlib.ExitStack()
        self.sp = Eng("sync")
        self.gp = Eng("gpsimd")
        self.ve = Eng("vector")
        self.sc = Eng("scalar")
        self.te = Eng("tensor")
        self.engines = [self.sp, self.gp, self.ve, self.sc, self.te]
        self.S = {}
        for name in ("sp16", "gp16", "gp1", "cc1", "ve1", "sc1", "te1", "bar"):
            self.S[name] = Sem(self.es.enter_context(nc.semaphore(name)))
        self.esem = {id(self.sp): self.S["sp16"], id(self.gp): self.S["gp1"],
                     id(self.ve): self.S["ve1"], id(self.sc): self.S["sc1"],
                     id(self.te): self.S["te1"]}
        self.gp.auto_sem = self.S["gp1"]
        self.dma_sems = []

    def sb(self, name, shape, dtype):
        return self.es.enter_context(self.nc.sbuf_tensor(name, shape, dtype))

    def ps(self, name, shape, dtype):
        return self.es.enter_context(self.nc.psum_tensor(name, shape, dtype))

    def mark(self, E):
        sem = self.esem[id(E)]
        last = E.items[-1]
        assert last[0] == "o", "mark needs an op"
        if last[4]:
            h, k = last[4][0]
            assert h is sem.h or h == sem.h, "op already marked with another sem"
            return (sem, sem.n)
        sem.n += 1
        last[4].append((sem.h, 1))
        return (sem, sem.n)

    def chain(self, E):
        tok = self.mark(E)
        E.wait(*tok)
        return tok

    def dma_sem(self, name):
        sem = Sem(self.es.enter_context(self.nc.semaphore(name)))
        self.dma_sems.append((name, sem))
        return sem

    def dma_on(self, sem, out, in_, eng=None):
        E = eng or self.sp
        E.op("dma_start", out=out, in_=in_, inc=[(sem, 16)])
        return (sem, sem.n)

    def dma(self, out, in_):
        s = self.S["sp16"]
        self.sp.op("dma_start", out=out, in_=in_, inc=[(s, 16)])
        return (s, s.n)

    def dma_gp(self, out, in_):
        s = self.S["gp16"]
        self.gp.op("dma_start", out=out, in_=in_, inc=[(s, 16)])
        return (s, s.n)

    def barrier(self):
        bar = self.S["bar"]
        self.sp.wait(self.S["sp16"], self.S["sp16"].n)
        self.gp.wait(self.S["gp16"], self.S["gp16"].n)
        self.gp.wait(self.S["cc1"], self.S["cc1"].n)
        self.gp.wait(self.S["gp1"], self.S["gp1"].n)
        for _, sem in self.dma_sems:
            self.sp.wait(sem, sem.n)
        for e in self.engines:
            e.op("nop", inc=[(bar, 1)])
        tgt = bar.n
        for e in self.engines:
            e.wait(bar, tgt)


def host_prep(inputs):
    ins = {k: np.asarray(v) for k, v in inputs.items()}
    f = np.float32
    W = {}
    W["x"] = np.ascontiguousarray(ins["L0_attn_x"].astype(f, copy=False))
    W["wqkT"] = np.ascontiguousarray(ins["L0_attn_proj_qk_w"].T.astype(f))
    W["wvT"] = np.ascontiguousarray(ins["L0_attn_proj_v_w"].T.astype(f))
    wq, wk, wv = np.split(ins["L0_attn_in_proj_w"].astype(f), 3, axis=0)
    W["wqT"] = np.ascontiguousarray(wq.T)
    W["wkT"] = np.ascontiguousarray(wk.T)
    W["wvvT"] = np.ascontiguousarray(wv.T)
    W["woT"] = np.ascontiguousarray(ins["L0_attn_out_proj_w"].T.astype(f))
    bq, bk, bv = np.split(ins["L0_attn_in_proj_b"].astype(f), 3)

    def chunked(v):
        return np.ascontiguousarray(np.asarray(v, f).reshape(8, 128).T)

    W["b_qk"] = chunked(ins["L0_attn_proj_qk_b"])
    W["b_v"] = chunked(ins["L0_attn_proj_v_b"])
    W["b_q"] = chunked(bq)
    W["b_k"] = chunked(bk)
    W["b_vv_row"] = np.ascontiguousarray(bv.reshape(1, D))
    W["b_o_row"] = np.ascontiguousarray(ins["L0_attn_out_proj_b"].astype(f).reshape(1, D))
    W["w_gate"] = np.ascontiguousarray(ins["L0_ffn_w_gate"].astype(f))
    W["w_up"] = np.ascontiguousarray(ins["L0_ffn_w_up"].astype(f))
    W["w_down"] = np.ascontiguousarray(ins["L0_ffn_w_down"].astype(f))
    def pchunk(a):
        c = a.shape[0] // 128
        return np.ascontiguousarray(
            a.reshape(c, 128, a.shape[1]).transpose(1, 0, 2).reshape(128, -1).astype(f))

    W["W_DQ"] = pchunk(ins["L1_attn_W_DQ"].astype(f))
    W["W_UQ_T"] = pchunk(ins["L1_attn_W_UQ"].T.astype(f))
    W["W_DKV"] = pchunk(ins["L1_attn_W_DKV"].astype(f))
    W["W_UKV_T"] = pchunk(np.ascontiguousarray(ins["L1_attn_W_UKV"].T.astype(f)))
    W["W_QR_T"] = np.ascontiguousarray(ins["L1_attn_W_QR"].T.astype(f))
    W["W_KR_T"] = np.ascontiguousarray(ins["L1_attn_W_KR"].T.astype(f))
    W["W_O_T"] = np.ascontiguousarray(ins["L1_attn_W_O"].T.astype(f))
    W["qnw"] = np.ascontiguousarray(np.broadcast_to(ins["L1_attn_q_norm_w"].astype(f), (128, 64)))
    W["knw"] = np.ascontiguousarray(np.broadcast_to(ins["L1_attn_k_norm_w"].astype(f), (128, 64)))
    W["w1"] = np.ascontiguousarray(ins["L1_ffn_weight"].astype(f))
    W["b1_row"] = np.ascontiguousarray(ins["L1_ffn_bias"].astype(f).reshape(1, D))
    W["dw"] = np.ascontiguousarray(ins["L1_ffn_delta_w"].astype(f))
    W["dact"] = np.ascontiguousarray(ins["L1_ffn_delta_act"].astype(f).reshape(1, 1))
    W["ident"] = np.eye(128, dtype=f)
    k_ = np.arange(128)[:, None]
    q_ = np.arange(128)[None, :]
    seg = ((k_ // 4) == (q_ // 4)) & (k_ != q_)
    W["seg_mask"] = np.where(seg, f(0.0), f(-1e9)).astype(f)
    cm = np.zeros((4, 128, 512), dtype=f)
    qq = np.arange(512)[None, :]
    for j in range(4):
        cm[j] = np.where((k_ + 128 * j) <= qq, f(0.0), f(-1e9))
    W["causal_masks"] = np.ascontiguousarray(cm.transpose(1, 0, 2).reshape(128, 4 * 512))
    W["ones_row"] = np.ones((1, 128), dtype=f)
    return W



def emit_program(stop_after=None):
    nc = bass.Bass("TRN2", target_bir_lowering=False)
    dram = {}

    def din(name, shape, dtype=F32):
        dram[name] = nc.dram_tensor(name, list(shape), dtype, kind="ExternalInput")
        return dram[name]

    def dint(name, shape, dtype=BF16):
        dram[name] = nc.dram_tensor(name, list(shape), dtype)
        return dram[name]

    x_in = din("x", (TOK, D))
    out_t = nc.dram_tensor("out", [TOK, D], F32, kind="ExternalOutput")
    for nm, shp in [("wqkT", (D, D)), ("wvT", (D, D)), ("wqT", (D, D)),
                    ("wkT", (D, D)), ("wvvT", (D, D)), ("woT", (D, D)),
                    ("b_qk", (128, 8)), ("b_v", (128, 8)), ("b_q", (128, 8)),
                    ("b_k", (128, 8)), ("b_vv_row", (1, D)), ("b_o_row", (1, D)),
                    ("w_gate", (DI, D)), ("w_up", (DI, D)), ("w_down", (D, DI)),
                    ("W_DQ", (128, 256)), ("W_UQ_T", (128, 256)), ("W_DKV", (128, 512)),
                    ("W_UKV_T", (128, 1024)), ("W_QR_T", (64, 64)), ("W_KR_T", (64, 64)),
                    ("W_O_T", (64, 64)), ("qnw", (128, 64)), ("knw", (128, 64)),
                    ("w1", (D, D)), ("b1_row", (1, D)), ("dw", (D, 32)),
                    ("dact", (1, 1)), ("ident", (128, 128)), ("seg_mask", (128, 128)),
                    ("causal_masks", (128, 4 * 512)), ("ones_row", (1, 128))]:
        din(nm, shp)
    xnfm_sp = dint("xnfm_sp", (NCH, NSL, 128, 512))
    u_sp = dint("u_sp", (NCHI, NSL, 128, 512))
    g_sp = dint("g_sp", (NCHI, NSL, 128, 512))
    p_sp = dint("p_sp", (NCHI, NSL, 128, 512))
    o_sp = dint("o_sp", (NCHI, NSL, 128, 512))
    wdnT_sp = dint("wdnT_sp", (NCHI, 128, D))
    qk_dr = dint("qk_dr", (NCH, NSL, 128, 512))
    vin_dr = dint("vin_dr", (NCH, NSL, 128, 512))
    q_dr = dint("q_dr", (NCH, NSL, 128, 512))
    k_dr = dint("k_dr", (NCH, NSL, 128, 512))
    vtok_dr = dint("vtok_dr", (NSL, 128, 4 * D))
    ccin = [nc.dram_tensor(f"ccin{i}", [1, 64], F32) for i in range(4)]
    scr_row = nc.dram_tensor("scr_row", [1, 16], F32)
    ccout = [nc.dram_tensor(f"ccout{i}", [1, 64], F32) for i in range(4)]

    b = B(nc)
    S = b.S
    sem_xnsp = b.dma_sem("xnsp")
    sem_wload = b.dma_sem("wload")
    sem_wslot = b.dma_sem("wslotld")
    sem_st = [b.dma_sem("st0"), b.dma_sem("st1")]
    sem_ld = [b.dma_sem("ld0"), b.dma_sem("ld1")]
    sem_aux = b.dma_sem("auxd")
    sem_ccd = b.dma_sem("ccd")
    sem_atld = b.dma_sem("atld")
    sem_out = b.dma_sem("outd")
    sem_wdnsl = [b.dma_sem(f"wdnsl{i}") for i in range(8)]
    sp16, gp16, gp1, cc1, ve1, sc1, te1 = (S["sp16"], S["gp16"], S["gp1"],
                                           S["cc1"], S["ve1"], S["sc1"], S["te1"])

    x_sb = b.sb("x_sb", [128, NT * D], BF16)
    arena = b.sb("arena", [128, NT * D], BF16)
    wslot = b.sb("wslot", [128, 2 * 8 * D], BF16)
    xnslab = b.sb("xnslab", [128, NCH * 512], BF16)
    stats_sb = b.sb("stats_sb", [128, NT * 2], F32)
    t_a = b.sb("t_a", [128, 1024], F32)
    t_b = b.sb("t_b", [128, 1024], F32)
    t_d = b.sb("t_d", [128, 512], F32)
    t_e = b.sb("t_e", [128, 512], F32)
    t_f = b.sb("t_f", [128, 512], F32)
    t_bf = b.sb("t_bf", [128, 1024], BF16)
    stat12 = b.sb("stat12", [128, 12], F32)
    mv = b.sb("mv", [128, 8], F32)
    maxacc = b.sb("maxacc", [128, 8], F32)
    maxqk = b.sb("maxqk", [128, 128], F32)
    rowbuf = b.sb("rowbuf", [1, 256], F32)
    scal = b.sb("scal", [128, 32], F32)
    consts = b.sb("consts", [128, 8], F32)
    ident_sb = b.sb("ident_sb", [128, 128], BF16)
    segm_sb = b.sb("segm_sb", [128, 128], F32)
    cmask_sb = b.sb("cmask_sb", [128, 4 * 512], BF16)
    onesrow_sb = b.sb("onesrow_sb", [1, 128], F32)
    onesrow_bf = b.sb("onesrow_bf", [1, 128], BF16)
    onescol_bf = b.sb("onescol_bf", [128, 1], BF16)
    onescol_f32 = b.sb("onescol_f32", [128, 1], F32)
    negsh_sb = b.sb("negsh_sb", [128, 64], F32)
    bias_sb = b.sb("bias_sb", [128, 4 * 8], F32)
    brow_sb = b.sb("brow_sb", [1, 3 * D], BF16)
    qnw_sb = b.sb("qnw_sb", [128, 64], F32)
    knw_sb = b.sb("knw_sb", [128, 64], F32)
    bd_sb = b.sb("bd_sb", [128, 6 * 128], BF16)
    sc_blk = b.sb("sc_blk", [128, 4 * 32], F32)
    spst = b.sb("spst", [128, 2 * 512], BF16)
    ldst = b.sb("ldst", [128, 4 * 512], BF16)
    psf = [b.ps(f"psf{i}", [128, 512], F32) for i in range(6)]
    psb = [b.ps(f"psb{i}", [128, 1024], BF16) for i in range(2)]

    DBG = {}

    def dbg_out(tag, ap, shape, dtype):
        t = nc.dram_tensor("dbg_" + tag, list(shape), dtype, kind="ExternalOutput")
        b.barrier()
        idx = tuple(slice(None) for _ in shape)
        b.dma(out=t[idx], in_=ap)
        b.barrier()
        DBG[tag] = True

    sp, gp, ve, sc, te = b.sp, b.gp, b.ve, b.sc, b.te

    for t in range(NT):
        b.dma_gp(out=x_sb[:, t * D:(t + 1) * D], in_=x_in[t * 128:(t + 1) * 128, :])
    b.dma_gp(out=ident_sb[:, :], in_=dram["ident"][:, :])
    b.dma(out=segm_sb[:, :], in_=dram["seg_mask"][:, :])
    b.dma_gp(out=cmask_sb[:, :], in_=dram["causal_masks"][:, :])
    b.dma(out=onesrow_sb[:, :], in_=dram["ones_row"][:, :])
    b.dma_gp(out=onesrow_bf[:, :], in_=dram["ones_row"][:, :])
    for i, nm in enumerate(("b_qk", "b_v", "b_q", "b_k")):
        b.dma(out=bias_sb[:, i * 8:(i + 1) * 8], in_=dram[nm][:, :])
    for i, nm in enumerate(("b_vv_row", "b_o_row", "b1_row")):
        b.dma_gp(out=brow_sb[:, i * D:(i + 1) * D], in_=dram[nm][:, :])
    b.dma(out=qnw_sb[:, :], in_=dram["qnw"][:, :])
    b.dma(out=knw_sb[:, :], in_=dram["knw"][:, :])
    gp.op("memset", onescol_bf[:, :], 1.0)
    gp.op("memset", onescol_f32[:, :], 1.0)
    gp.op("memset", rowbuf[0:1, :], 0.0)
    gp.op("memset", consts[:, 0:1], LN_EPS)
    gp.op("memset", consts[:, 1:2], MAGIC)
    gp.op("memset", consts[:, 2:3], RMS_EPS)
    gp.op("memset", consts[:, 3:4], -HEADROOM)
    gp.op("memset", consts[:, 4:5], float(np.log(0.25)))
    gp.op("memset", maxacc[:, :], 0.0)
    gp.op("memset", maxqk[:, :], 0.0)
    b.barrier()

    if stop_after == "prologue":
        dbg_out("x0", x_sb[:, 0:D], (128, D), BF16)
        b.es.pop_all()
        return nc, b, DBG


    def ln_stats(tag):
        for t in range(NT):
            xt = x_sb[:, t * D:(t + 1) * D]
            ve.op("bn_stats", stat12[:, 0:6], xt[:, 0:512])
            ve.op("bn_stats", stat12[:, 6:12], xt[:, 512:1024])
            b.chain(ve)
            ve.op("bn_aggr", mv[:, 0:2], stat12[:, 0:12])
            tok = b.mark(ve)
            sc.wait(*tok)
            sc.op("activation", mv[:, 2:3], mv[:, 1:2], AF.Sqrt,
                  bias=consts[:, 0:1], scale=1.0)
            tok = b.mark(sc)
            ve.wait(*tok)
            ve.op("reciprocal", stats_sb[:, 2 * t:2 * t + 1], mv[:, 2:3])
            b.chain(ve)
            ve.op("tensor_scalar", stats_sb[:, 2 * t + 1:2 * t + 2], mv[:, 0:1],
                  stats_sb[:, 2 * t:2 * t + 1], -1.0, ALU.mult, ALU.mult)
            b.chain(ve)

    def xn_tile(t, dst):
        sc.op("activation", dst, x_sb[:, t * D:(t + 1) * D], AF.Identity,
              bias=stats_sb[:, 2 * t + 1:2 * t + 2],
              scale=stats_sb[:, 2 * t:2 * t + 1])
        return b.mark(sc)

    ln_stats("L0")
    b.barrier()

    for s in range(NSL):
        for j in range(4):
            t = s * 4 + j
            tok = xn_tile(t, t_bf[:, :])
            ve.wait(*tok)
            ve.op("tensor_reduce", mv[:, 3:4], t_bf[:, :], AX.X, ALU.max,
                  apply_absolute_value=True)
            b.chain(ve)
            ve.op("tensor_max", maxacc[:, 0:1], maxacc[:, 0:1], mv[:, 3:4])
            b.chain(ve)
            te.wait(*tok)
            ttok = None
            for c in range(NCH):
                bank = psb[c % 2]
                te.op("transpose", bank[:, 0:128], t_bf[:, c * 128:(c + 1) * 128],
                      ident_sb[:, :])
                ttok = b.mark(te)
                ve.wait(*ttok)
                ve.op("tensor_copy", xnslab[:, c * 512 + j * 128: c * 512 + (j + 1) * 128],
                      bank[:, 0:128])
                etok = b.mark(ve)
                te.wait(*etok)
            sc.wait(*ttok)
            sc.wait(*etok)
        cptok = b.mark(ve)
        sp.wait(*cptok)
        ve.wait(*cptok)
        sptoks = []
        for c in range(NCH):
            ve.op("tensor_copy", arena[:, c * NSL * 512 + s * 512:
                                       c * NSL * 512 + (s + 1) * 512],
                  xnslab[:, c * 512:(c + 1) * 512])
            sptoks.append(b.dma_on(sem_xnsp, out=xnfm_sp[c, s, :, :],
                                   in_=xnslab[:, c * 512:(c + 1) * 512]))
        b.chain(ve)
        te.wait(*sptoks[-1])
    b.barrier()

    if stop_after == "xn":
        dbg_out("xnfm0", arena[:, 0:512], (128, 512), BF16)
        dbg_out("stats", stats_sb[:, :], (128, NT * 2), F32)
        dbg_out("maxacc", maxacc[:, 0:1], (128, 1), F32)
        b.es.pop_all()
        return nc, b, DBG


    prr_guard = [None]

    def part_reduce_to_row(col_ap, slot):
        if prr_guard[0] is not None:
            sp.wait(*prr_guard[0])
        tok = b.dma_on(sem_aux, out=rowbuf[0:1, 0:128], in_=col_ap)
        ve.wait(*tok)
        ve.op("tensor_reduce", rowbuf[0:1, slot:slot + 1], rowbuf[0:1, 0:128],
              AX.X, ALU.max)
        prr_guard[0] = b.mark(ve)
        return prr_guard[0]

    def bcast_row(cols):
        srcs = [c[0] for c in cols]
        dsts = [c[1] for c in cols]
        k = len(cols)
        assert dsts == list(range(dsts[0], dsts[0] + k))
        for j, src in enumerate(srcs):
            ve.op("tensor_copy", rowbuf[0:1, 200 + j:201 + j],
                  rowbuf[0:1, src:src + 1])
        vtok = b.mark(ve)
        gp.wait(*vtok)
        t1 = b.dma_on(sem_ccd, out=scr_row[0:1, 0:k],
                      in_=rowbuf[0:1, 200:200 + k], eng=gp)
        gp.wait(*t1)
        t2 = b.dma_on(sem_ccd, out=scal[:, dsts[0]:dsts[0] + k],
                      in_=scr_row[0:1, 0:k].broadcast_to([128, k]), eng=gp)
        return t2

    def collective(round_idx, n):
        tok = b.dma_on(sem_ccd, out=ccin[round_idx][0:1, 0:64],
                       in_=rowbuf[0:1, 134:134 + 64], eng=gp)
        gp.wait(*tok)
        gp.op("collective_compute", "AllReduce", ALU.max,
              replica_groups=[list(range(NCORES))],
              ins=[ccin[round_idx][:, :].opt()],
              outs=[ccout[round_idx][:, :].opt()], inc=[(cc1, 1)])
        gp.wait(cc1, cc1.n)
        tok = b.dma_on(sem_ccd, out=rowbuf[0:1, 144:144 + n],
                       in_=ccout[round_idx][0:1, 0:n], eng=gp)
        return tok

    def row_ts(dst, src, s1, s2, op0, op1=ALU.bypass):
        ve.op("tensor_scalar", rowbuf[0:1, dst:dst + 1], rowbuf[0:1, src:src + 1],
              s1, s2, op0, op1)
        b.chain(ve)

    def row_recip(dst, src):
        ve.op("reciprocal", rowbuf[0:1, dst:dst + 1], rowbuf[0:1, src:src + 1])
        b.chain(ve)

    def row_mul(dst, a, bcol):
        ve.op("tensor_tensor", rowbuf[0:1, dst:dst + 1], rowbuf[0:1, a:a + 1],
              rowbuf[0:1, bcol:bcol + 1], ALU.mult)
        b.chain(ve)

    tok = part_reduce_to_row(maxacc[:, 0:1], 134)
    ve.wait(*tok)
    gp.wait(ve1, ve1.n)
    tok = collective(0, 1)
    ve.wait(*tok)
    row_ts(128, 144, 1.0 / QMAX0, 1e-12, ALU.mult, ALU.max)
    row_recip(133, 128)
    bcast_row([(133, 0)])
    b.barrier()

    if stop_after == "ar1":
        dbg_out("scal0", scal[:, 0:8], (128, 8), F32)
        dbg_out("rowb", rowbuf[0:1, :], (1, 256), F32)
        b.es.pop_all()
        return nc, b, DBG

    for c in range(NCH):
        for s in range(NSL):
            seg = arena[:, c * NSL * 512 + s * 512: c * NSL * 512 + (s + 1) * 512]
            ve.op("tensor_scalar", t_d[:, :], seg, scal[:, 0:1], MAGIC,
                  ALU.mult, ALU.add)
            b.chain(ve)
            ve.op("tensor_scalar", seg, t_d[:, :], MAGIC, None, ALU.subtract)
            b.chain(ve)
    b.barrier()

    if stop_after == "xq":
        dbg_out("xq0", arena[:, 0:512], (128, 512), BF16)
        b.es.pop_all()
        return nc, b, DBG

    def dequant_lpbq_tile(src_dram, row0, dst_bf, guard, dst_guard=None):
        if guard is not None:
            sp.wait(*guard)
        tok = b.dma_on(sem_wload, out=t_a[:, :], in_=src_dram[row0:row0 + 128, :])
        ve.wait(*tok)
        if dst_guard is not None:
            ve.wait(*dst_guard)
        ve.op("tensor_reduce",
              sc_blk[:, 0:32].rearrange("p (g o) -> p g o", o=1),
              t_a[:, :].rearrange("p (g k) -> p g k", k=32),
              AX.X, ALU.max, apply_absolute_value=True)
        b.chain(ve)
        ve.op("tensor_scalar", sc_blk[:, 32:64], sc_blk[:, 0:32], 1.0 / 7.0,
              1e-12, ALU.mult, ALU.max)
        b.chain(ve)
        ve.op("reciprocal", sc_blk[:, 64:96], sc_blk[:, 32:64])
        b.chain(ve)
        ve.op("tensor_tensor", t_b[:, :].rearrange("p (g k) -> p g k", k=32),
              t_a[:, :].rearrange("p (g k) -> p g k", k=32),
              sc_blk[:, 64:96].rearrange("p (g o) -> p g o", o=1).broadcast_to([128, 32, 32]),
              ALU.mult)
        b.chain(ve)
        ve.op("tensor_scalar", t_a[:, :], t_b[:, :], MAGIC, MAGIC, ALU.add, ALU.subtract)
        b.chain(ve)
        ve.op("tensor_tensor", dst_bf.rearrange("p (g k) -> p g k", k=32),
              t_a[:, :].rearrange("p (g k) -> p g k", k=32),
              sc_blk[:, 32:64].rearrange("p (g o) -> p g o", o=1).broadcast_to([128, 32, 32]),
              ALU.mult)
        return b.mark(ve)

    def transpose_into_wslot(src_bf_tok, slot_base, out_col, wtok):
        last = None
        ttok = None
        for k in range(NCH):
            bank = psb[k % 2]
            te.wait(*wtok)
            te.op("transpose", bank[:, 0:128], src_bf_tok[:, k * 128:(k + 1) * 128],
                  ident_sb[:, :])
            ttok = b.mark(te)
            ve.wait(*ttok)
            ve.op("tensor_copy",
                  wslot[:, slot_base + k * 512 + out_col: slot_base + k * 512 + out_col + 128],
                  bank[:, 0:128])
            last = b.mark(ve)
            te.wait(*last)
        return last, ttok

    UP_SLOT, GATE_SLOT = 0, 8 * 512
    pst_state = [None, None]
    red_state = [None, None]

    deq_guard = None
    wslot_guard = None
    tbf_guard = None
    NCB = 1 if stop_after in ("upgate1", "dequant") else 6
    for cb in range(NCB):
        for which, (src, base) in enumerate([(dram["w_up"], UP_SLOT),
                                             (dram["w_gate"], GATE_SLOT)]):
            for rt in range(4):
                row0 = cb * 512 + rt * 128
                if wslot_guard is not None:
                    ve.wait(*wslot_guard)
                    wslot_guard = None
                dtok = dequant_lpbq_tile(src, row0, t_bf[:, :], deq_guard,
                                          dst_guard=tbf_guard)
                _, tbf_guard = transpose_into_wslot(t_bf[:, :], base, rt * 128, dtok)
                deq_guard = dtok
        wtok = b.mark(ve)
        te.wait(*wtok)
        if stop_after == "dequant":
            break
        for which, (base, spdst, mcol) in enumerate([(UP_SLOT, u_sp, 2),
                                                     (GATE_SLOT, g_sp, 3)]):
            for cc in range(4):
                ch = cb * 4 + cc
                for s in range(NSL):
                    bank = psf[(s + cc) % 2]
                    for k in range(NCH):
                        te.op("matmul", bank[:, :],
                              wslot[:, base + k * 512 + cc * 128: base + k * 512 + (cc + 1) * 128],
                              arena[:, k * NSL * 512 + s * 512: k * NSL * 512 + (s + 1) * 512],
                              start=(k == 0), stop=(k == NCH - 1))
                    mtok = b.mark(te)
                    half = s % 2
                    stg = spst[:, half * 512:(half + 1) * 512]
                    sc.wait(*mtok)
                    if pst_state[half] is not None:
                        sc.wait(*pst_state[half])
                    if red_state[half] is not None:
                        sc.wait(*red_state[half])
                    sc.op("activation", stg, bank[:, :], AF.Identity, bias=0.0, scale=1.0)
                    etok = b.mark(sc)
                    te.wait(*etok)
                    ve.wait(*etok)
                    ve.op("tensor_reduce", mv[:, 3:4], stg, AX.X, ALU.max,
                          apply_absolute_value=True)
                    redtok = b.chain(ve)
                    ve.op("tensor_max", maxacc[:, mcol:mcol + 1],
                          maxacc[:, mcol:mcol + 1], mv[:, 3:4])
                    b.chain(ve)
                    gp.wait(*etok)
                    dtok = b.dma_on(sem_st[half], out=spdst[ch, s, :, :],
                                    in_=stg, eng=gp)
                    pst_state[half] = dtok
                    red_state[half] = redtok
                    wslot_guard = mtok
    b.barrier()

    if stop_after == "dequant":
        dbg_out("wslot0", wslot[:, 0:512], (128, 512), BF16)
        b.es.pop_all()
        return nc, b, DBG

    if stop_after in ("upgate", "upgate1"):
        dbg_out("maxug", maxacc[:, 2:4], (128, 2), F32)
        b.es.pop_all()
        return nc, b, DBG

    tok = part_reduce_to_row(maxacc[:, 2:3], 134)
    ve.wait(*tok)
    tok = part_reduce_to_row(maxacc[:, 3:4], 135)
    ve.wait(*tok)
    gp.wait(ve1, ve1.n)
    tok = collective(1, 2)
    ve.wait(*tok)
    row_mul(145, 144, 128)
    row_ts(129, 145, 1.0 / QMAX0, 1e-12, ALU.mult, ALU.max)
    row_mul(146, 144 + 1, 128)
    row_ts(130, 146, 1.0 / QMAX0, 1e-12, ALU.mult, ALU.max)
    row_recip(147, 129)
    row_mul(148, 147, 128)
    row_recip(149, 130)
    row_mul(150, 149, 128)
    row_ts(151, 130, -MAGIC, None, ALU.mult)
    bcast_row([(150, 1), (130, 2), (151, 3), (148, 4)])
    b.barrier()

    p1_iters = [(ch, s) for ch in range(NCHI) for s in range(NSL)]
    n_p1 = len(p1_iters)
    ld_g4 = [None] * 4
    st_g4 = [None] * 4
    td_rd = [None, None]
    sig_rd = [None, None]
    amark = [None] * n_p1
    bmark = [None] * n_p1
    tdbuf = (t_d, t_e)
    sigbuf = (t_a[:, 0:512], t_a[:, 512:1024])

    def p1_A(i):
        ch, s = p1_iters[i]
        q = i % 4
        p = i % 2
        stg = ldst[:, q * 512:(q + 1) * 512]
        if ld_g4[q] is not None:
            sp.wait(*ld_g4[q])
        ltok = b.dma_on(sem_ld[q % 2], out=stg, in_=g_sp[ch, s, :, :])
        ve.wait(*ltok)
        if td_rd[p] is not None:
            ve.wait(*td_rd[p])
        ve.op("tensor_scalar", tdbuf[p][:, :], stg, scal[:, 1:2], MAGIC,
              ALU.mult, ALU.add)
        amark[i] = b.mark(ve)
        ld_g4[q] = amark[i]

    def p1_B(i):
        p = i % 2
        sc.wait(*amark[i])
        if sig_rd[p] is not None:
            sc.wait(*sig_rd[p])
        sc.op("activation", sigbuf[p], tdbuf[p][:, :], AF.Sigmoid,
              bias=scal[:, 3:4], scale=scal[:, 2:3])
        bmark[i] = b.mark(sc)
        td_rd[p] = bmark[i]

    def p1_C(i):
        ch, s = p1_iters[i]
        p = i % 2
        q = i % 4
        ve.wait(*bmark[i])
        ve.op("tensor_scalar", sigbuf[p], sigbuf[p], 65536.0, 65535.0,
              ALU.mult, ALU.min)
        b.chain(ve)
        ve.op("tensor_scalar", sigbuf[p], sigbuf[p], MAGIC, MAGIC,
              ALU.add, ALU.subtract)
        b.chain(ve)
        pstg = xnslab[:, (4 + q) * 512:(5 + q) * 512]
        if st_g4[q] is not None:
            ve.wait(*st_g4[q])
        ve.op("scalar_tensor_tensor", pstg, tdbuf[p][:, :], MAGIC, sigbuf[p],
              ALU.subtract, ALU.mult)
        ptok = b.mark(ve)
        sig_rd[p] = ptok
        ve.wait(*ptok)
        ve.op("tensor_reduce", mv[:, 3:4], pstg, AX.X, ALU.max,
              apply_absolute_value=True)
        b.chain(ve)
        ve.op("tensor_max", maxacc[:, 4:5], maxacc[:, 4:5], mv[:, 3:4])
        b.chain(ve)
        gp.wait(*ptok)
        st_g4[q] = b.dma_on(sem_st[q % 2], out=p_sp[ch, s, :, :], in_=pstg,
                            eng=gp)

    for i in range(n_p1):
        p1_A(i)
        p1_B(i)
        if i >= 1:
            p1_C(i - 1)
    p1_C(n_p1 - 1)
    b.barrier()

    tok = part_reduce_to_row(maxacc[:, 4:5], 134)
    ve.wait(*tok)
    gp.wait(ve1, ve1.n)
    tok = collective(2, 1)
    ve.wait(*tok)
    row_ts(152, 130, 1.0 / 65536.0, None, ALU.mult)
    row_mul(153, 144, 152)
    row_ts(131, 153, 1.0 / QMAX0, 1e-12, ALU.mult, ALU.max)
    row_recip(154, 131)
    row_mul(155, 152, 154)
    bcast_row([(155, 5)])
    b.barrier()

    ld_g4 = [None] * 4
    ldu_g4 = [None] * 4
    st_g4 = [None] * 4
    ru_rd = [None, None]
    amark = [None] * n_p1
    bmark = [None] * n_p1

    def p2_A(i):
        ch, s = p1_iters[i]
        q = i % 4
        p = i % 2
        stgp = ldst[:, q * 512:(q + 1) * 512]
        stgu = xnslab[:, q * 512:(q + 1) * 512]
        if ld_g4[q] is not None:
            sp.wait(*ld_g4[q])
        if ldu_g4[q] is not None:
            sp.wait(*ldu_g4[q])
        ltok1 = b.dma_on(sem_ld[q % 2], out=stgp, in_=p_sp[ch, s, :, :])
        ltok2 = b.dma_on(sem_ld[q % 2], out=stgu, in_=u_sp[ch, s, :, :])
        ve.wait(*ltok2)
        ve.op("tensor_scalar", tdbuf[p][:, :], stgp, scal[:, 5:6], MAGIC,
              ALU.mult, ALU.add)
        amark[i] = b.mark(ve)
        ld_g4[q] = amark[i]
        sc.wait(*ltok2)
        if ru_rd[p] is not None:
            sc.wait(*ru_rd[p])
        sc.op("activation", sigbuf[p], stgu, AF.Identity,
              bias=consts[:, 1:2], scale=scal[:, 4:5])
        bmark[i] = b.mark(sc)
        ldu_g4[q] = bmark[i]

    def p2_C(i):
        ch, s = p1_iters[i]
        p = i % 2
        q = i % 4
        ve.wait(*bmark[i])
        ve.op("tensor_scalar", sigbuf[p], sigbuf[p], MAGIC, None, ALU.subtract)
        b.chain(ve)
        pstg = xnslab[:, (4 + q) * 512:(5 + q) * 512]
        if st_g4[q] is not None:
            ve.wait(*st_g4[q])
        ve.op("scalar_tensor_tensor", pstg, tdbuf[p][:, :], MAGIC, sigbuf[p],
              ALU.subtract, ALU.mult)
        ptok = b.mark(ve)
        ru_rd[p] = ptok
        ve.wait(*ptok)
        ve.op("tensor_reduce", mv[:, 3:4], pstg, AX.X, ALU.max,
              apply_absolute_value=True)
        b.chain(ve)
        ve.op("tensor_max", maxacc[:, 5:6], maxacc[:, 5:6], mv[:, 3:4])
        b.chain(ve)
        gp.wait(*ptok)
        st_g4[q] = b.dma_on(sem_st[q % 2], out=o_sp[ch, s, :, :], in_=pstg,
                            eng=gp)

    for i in range(n_p1):
        p2_A(i)
        if i >= 1:
            p2_C(i - 1)
    p2_C(n_p1 - 1)
    b.barrier()

    tok = part_reduce_to_row(maxacc[:, 5:6], 134)
    ve.wait(*tok)
    gp.wait(ve1, ve1.n)
    tok = collective(3, 1)
    ve.wait(*tok)
    row_mul(156, 131, 129)
    row_mul(157, 144, 156)
    row_ts(132, 157, 1.0 / QMAX0, 1e-12, ALU.mult, ALU.max)
    row_recip(158, 132)
    row_mul(159, 156, 158)
    bcast_row([(159, 6), (132, 7)])
    b.barrier()

    wdn_guard = deq_guard
    wdn_sl_guard = [None] * 8
    for rt in range(8):
        if wdn_guard is not None:
            sp.wait(*wdn_guard)
        tok = b.dma_on(sem_wload, out=t_a[:, :],
                       in_=dram["w_down"][rt * 128:(rt + 1) * 128, 0:1024])
        for part in range(3):
            if part > 0:
                sp.wait(*wdn_guard)
                tok = b.dma_on(sem_wload, out=t_a[:, :],
                               in_=dram["w_down"][rt * 128:(rt + 1) * 128,
                                                  part * 1024:(part + 1) * 1024])
            ve.wait(*tok)
            ve.op("tensor_reduce",
                  sc_blk[:, 0:32].rearrange("p (g o) -> p g o", o=1),
                  t_a[:, :].rearrange("p (g k) -> p g k", k=32),
                  AX.X, ALU.max, apply_absolute_value=True)
            b.chain(ve)
            ve.op("tensor_scalar", sc_blk[:, 32:64], sc_blk[:, 0:32], 1.0 / 7.0,
                  1e-12, ALU.mult, ALU.max)
            b.chain(ve)
            ve.op("reciprocal", sc_blk[:, 64:96], sc_blk[:, 32:64])
            b.chain(ve)
            ve.op("tensor_tensor", t_b[:, :].rearrange("p (g k) -> p g k", k=32),
                  t_a[:, :].rearrange("p (g k) -> p g k", k=32),
                  sc_blk[:, 64:96].rearrange("p (g o) -> p g o", o=1).broadcast_to([128, 32, 32]),
                  ALU.mult)
            b.chain(ve)
            ve.op("tensor_scalar", t_a[:, :], t_b[:, :], MAGIC, MAGIC,
                  ALU.add, ALU.subtract)
            b.chain(ve)
            ve.op("tensor_tensor", t_bf[:, :].rearrange("p (g k) -> p g k", k=32),
                  t_a[:, :].rearrange("p (g k) -> p g k", k=32),
                  sc_blk[:, 32:64].rearrange("p (g o) -> p g o", o=1).broadcast_to([128, 32, 32]),
                  ALU.mult)
            wtok = b.mark(ve)
            wdn_guard = wtok
            for k in range(NCH):
                bank = psb[k % 2]
                te.wait(*wtok)
                te.op("transpose", bank[:, 0:128], t_bf[:, k * 128:(k + 1) * 128],
                      ident_sb[:, :])
                ttok = b.mark(te)
                ve.wait(*ttok)
                if wdn_sl_guard[k] is not None:
                    ve.wait(*wdn_sl_guard[k])
                ve.op("tensor_copy",
                      spst[:, (k % 8) * 128:(k % 8 + 1) * 128], bank[:, 0:128])
                etok = b.mark(ve)
                te.wait(*etok)
                gp.wait(*etok)
                wsp_tok = b.dma_on(sem_wdnsl[k],
                                   out=wdnT_sp[part * 8 + k, :, rt * 128:(rt + 1) * 128],
                                   in_=spst[:, (k % 8) * 128:(k % 8 + 1) * 128],
                                   eng=gp)
                wdn_sl_guard[k] = wsp_tok
    b.barrier()

    ld_g4 = [None] * 4
    wg4 = [None] * 4
    region_rd = [None, None]
    oq_done = [None] * NSL

    def p3_quant(s):
        rb = (s % 2) * 12288
        waited_r = False
        tok = None
        for ch in range(NCHI):
            q = ch % 4
            stg = ldst[:, q * 512:(q + 1) * 512]
            if ld_g4[q] is not None:
                sp.wait(*ld_g4[q])
            ltok = b.dma_on(sem_ld[q % 2], out=stg, in_=o_sp[ch, s, :, :])
            ve.wait(*ltok)
            ve.op("tensor_scalar", t_d[:, :], stg, scal[:, 6:7], MAGIC,
                  ALU.mult, ALU.add)
            ld_g4[q] = b.mark(ve)
            ve.wait(*ld_g4[q])
            if not waited_r and region_rd[s % 2] is not None:
                ve.wait(*region_rd[s % 2])
                waited_r = True
            ve.op("tensor_scalar", arena[:, rb + ch * 512: rb + (ch + 1) * 512],
                  t_d[:, :], MAGIC, None, ALU.subtract)
            tok = b.chain(ve)
        oq_done[s] = tok

    p3_quant(0)
    for s in range(NSL):
        if s + 1 < NSL:
            p3_quant(s + 1)
        rb = (s % 2) * 12288
        te.wait(*oq_done[s])
        for hf in range(2):
            for k in range(NCHI):
                wq = k % 4
                wst = wslot[:, wq * 512:(wq + 1) * 512]
                if wg4[wq] is not None:
                    sp.wait(*wg4[wq])
                wtok = b.dma_on(sem_wslot, out=wst,
                                in_=wdnT_sp[k, :, hf * 512:(hf + 1) * 512])
                te.wait(*wtok)
                for sub in range(4):
                    te.op("matmul", psf[sub][:, :],
                          arena[:, rb + k * 512 + sub * 128: rb + k * 512 + (sub + 1) * 128],
                          wst, start=(k == 0), stop=(k == NCHI - 1))
                mtok = b.mark(te)
                wg4[wq] = mtok
            for sub in range(4):
                t = s * 4 + sub
                xsl = x_sb[:, t * D + hf * 512: t * D + (hf + 1) * 512]
                ve.wait(*mtok)
                ve.op("scalar_tensor_tensor", xsl, psf[sub][:, :], scal[:, 7:8],
                      xsl, ALU.mult, ALU.add)
                etok = b.mark(ve)
                te.wait(*etok)
        region_rd[s % 2] = mtok
    b.barrier()

    if stop_after == "ffn0":
        dbg_out("x_after_ffn0", x_sb[:, 0:D], (128, D), BF16)
        b.es.pop_all()
        return nc, b, DBG

    sem_wgp = b.dma_sem("wgp")
    wslot_rd_guard = [None, None]

    def load_wT(name, slot):
        if wslot_rd_guard[slot] is not None:
            gp.wait(*wslot_rd_guard[slot])
        base = slot * 8 * D
        tok = None
        for k in range(NCH):
            tok = b.dma_on(sem_wgp, out=wslot[:, base + k * D: base + (k + 1) * D],
                           in_=dram[name][k * 128:(k + 1) * 128, :], eng=gp)
        return tok

    arena_guard = {}

    def stream_src_slab(sp_t, sidx, base, sem):
        toks = []
        if arena_guard.get(base) is not None:
            sp.wait(*arena_guard[base])
        for k in range(NCH):
            toks.append(b.dma_on(sem, out=arena[:, base + k * 512: base + (k + 1) * 512],
                                 in_=sp_t[k, sidx, :, :]))
        return toks[-1]

    def proj_stage_from_spill(wname, slot, src_sp, dst_sp, bias_col, maxcol=None):
        wtok = load_wT(wname, slot)
        te.wait(*wtok)
        wb = slot * 8 * D
        lastmm = None
        for sidx in range(NSL):
            base = (sidx % 2) * 4096
            ltok = stream_src_slab(src_sp, sidx, base, sem_atld)
            te.wait(*ltok)
            for c in range(NCH):
                bank = psf[c % 2]
                for k in range(NCH):
                    te.op("matmul", bank[:, :],
                          wslot[:, wb + k * D + c * 128: wb + k * D + (c + 1) * 128],
                          arena[:, base + k * 512: base + (k + 1) * 512],
                          start=(k == 0), stop=(k == NCH - 1))
                mtok = b.mark(te)
                lastmm = mtok
                half = (sidx * NCH + c) % 2
                stg = spst[:, half * 512:(half + 1) * 512]
                sc.wait(*mtok)
                if pst_state[half] is not None:
                    sc.wait(*pst_state[half])
                if red_state[half] is not None:
                    sc.wait(*red_state[half])
                sc.op("activation", stg, bank[:, :], AF.Identity,
                      bias=bias_sb[:, bias_col * 8 + c: bias_col * 8 + c + 1], scale=1.0)
                etok = b.mark(sc)
                te.wait(*etok)
                if maxcol is not None:
                    ve.wait(*etok)
                    ve.op("tensor_reduce", mv[:, 3:4], stg, AX.X, ALU.max,
                          apply_absolute_value=True)
                    red_state[half] = b.chain(ve)
                    ve.op("tensor_copy", maxqk[:, maxcol * 64 + sidx * 8 + c:
                                               maxcol * 64 + sidx * 8 + c + 1], mv[:, 3:4])
                    b.chain(ve)
                gp.wait(*etok)
                dtok = b.dma_on(sem_st[half], out=dst_sp[c, sidx, :, :],
                                in_=stg, eng=gp)
                pst_state[half] = dtok
            arena_guard[base] = mtok
        wslot_rd_guard[slot] = lastmm

    proj_stage_from_spill("wqkT", 0, xnfm_sp, qk_dr, 0)
    proj_stage_from_spill("wvT", 0, xnfm_sp, vin_dr, 1)
    proj_stage_from_spill("wqT", 0, qk_dr, q_dr, 2, maxcol=0)
    proj_stage_from_spill("wkT", 0, qk_dr, k_dr, 3, maxcol=1)

    ve.op("tensor_tensor", t_d[:, 0:64], maxqk[:, 0:64], maxqk[:, 64:128],
          ALU.mult)
    prodtok = b.chain(ve)
    te.wait(*prodtok)
    te.op("matmul", psf[4][0:1, 0:64], onescol_f32[:, 0:1], t_d[:, 0:64],
          start=True, stop=True)
    dottok = b.mark(te)
    sc.wait(*dottok)
    sc.op("activation", rowbuf[0:1, 0:64], psf[4][0:1, 0:64], AF.Relu,
          bias=consts[0:1, 3:4], scale=SCALE0)
    shtok = b.mark(sc)
    te.wait(*shtok)
    gp.wait(*shtok)
    t1 = b.dma_on(sem_ccd, out=ccin[0][0:1, 0:64], in_=rowbuf[0:1, 0:64], eng=gp)
    gp.wait(*t1)
    t2 = b.dma_on(sem_ccd, out=negsh_sb[:, :],
                  in_=ccin[0][0:1, 0:64].broadcast_to([128, 64]), eng=gp)
    ve.wait(*t2)
    ve.op("tensor_scalar", negsh_sb[:, :], negsh_sb[:, :], -1.0, None, ALU.mult)
    negshtok = b.chain(ve)
    sc.wait(*negshtok)

    wtok = load_wT("wvvT", 0)
    te.wait(*wtok)
    lastmm = None
    for sidx in range(NSL):
        base = (sidx % 2) * 4096
        ltok = stream_src_slab(vin_dr, sidx, base, sem_atld)
        te.wait(*ltok)
        for sub in range(4):
            for hf in range(2):
                bank = psf[(sub * 2 + hf) % 2]
                for k in range(NCH):
                    te.op("matmul", bank[:, :],
                          arena[:, base + k * 512 + sub * 128: base + k * 512 + (sub + 1) * 128],
                          wslot[:, k * D + hf * 512: k * D + (hf + 1) * 512],
                          start=(k == 0), stop=False)
                te.op("matmul", bank[:, :], onesrow_bf[0:1, :],
                      brow_sb[0:1, hf * 512:(hf + 1) * 512],
                      start=False, stop=True)
                mtok = b.mark(te)
                lastmm = mtok
                half = (sub * 2 + hf) % 2
                stg = spst[:, half * 512:(half + 1) * 512]
                sc.wait(*mtok)
                if pst_state[half] is not None:
                    sc.wait(*pst_state[half])
                if red_state[half] is not None:
                    sc.wait(*red_state[half])
                sc.op("activation", stg, bank[:, :], AF.Identity, bias=0.0, scale=1.0)
                etok = b.mark(sc)
                te.wait(*etok)
                gp.wait(*etok)
                dtok = b.dma_on(sem_st[half],
                                out=vtok_dr[sidx, :, sub * D + hf * 512: sub * D + (hf + 1) * 512],
                                in_=stg, eng=gp)
                pst_state[half] = dtok
        arena_guard[base] = mtok
    wslot_rd_guard[0] = lastmm
    b.barrier()

    if stop_after == "l0proj":
        dbg_out("maxqk", maxqk[:, :], (128, 128), F32)
        b.es.pop_all()
        return nc, b, DBG

    wtok = load_wT("woT", 1)
    te.wait(*wtok)
    negsh = mv
    rc_guard = [None, None]
    tbf_rd = None
    obase, ofmb = 12288, 16384
    o_wr_guard = None
    for sidx in range(NSL):
        qbase, kbase, vbase = 0, 4096, 8192
        ltq = stream_src_slab(q_dr, sidx, qbase, sem_atld)
        ltk = stream_src_slab(k_dr, sidx, kbase, sem_atld)
        if arena_guard.get(vbase) is not None:
            sp.wait(*arena_guard[vbase])
        ltv = b.dma_on(sem_atld, out=arena[:, vbase:vbase + 4 * D],
                       in_=vtok_dr[sidx, :, :])
        te.wait(*ltv)
        last_ctx = None
        last_oev = None
        for h in range(NCH):
            for sub in range(4):
                sbank = psf[sub % 2]
                te.op("matmul", sbank[:, 0:128],
                      arena[:, kbase + h * 512 + sub * 128: kbase + h * 512 + (sub + 1) * 128],
                      arena[:, qbase + h * 512 + sub * 128: qbase + h * 512 + (sub + 1) * 128],
                      start=True, stop=True)
                mtok = b.mark(te)
                ve.wait(*mtok)
                ve.op("tensor_tensor", sbank[:, 0:128], sbank[:, 0:128], segm_sb[:, :],
                      ALU.add)
                atok = b.mark(ve)
                sc.wait(*atok)
                if tbf_rd is not None:
                    sc.wait(*tbf_rd)
                sc.op("activation", t_bf[:, 0:128], sbank[:, 0:128], AF.Exp,
                      bias=negsh_sb[:, sidx * 8 + h: sidx * 8 + h + 1],
                      scale=SCALE0)
                ptok = b.mark(sc)
                te.wait(*ptok)
                cbank = psf[2 + sub % 2]
                te.op("matmul", cbank[:, 0:128], t_bf[:, 0:128],
                      arena[:, vbase + sub * D + h * 128: vbase + sub * D + (h + 1) * 128],
                      start=True, stop=False)
                te.op("matmul", cbank[:, 128:129], t_bf[:, 0:128], onescol_bf[:, 0:1],
                      start=False, stop=True)
                ctok = b.mark(te)
                tbf_rd = ctok
                last_ctx = ctok
                rcol = sub % 2
                ve.wait(*ctok)
                if rc_guard[rcol] is not None:
                    ve.wait(*rc_guard[rcol])
                ve.op("reciprocal", negsh[:, rcol:rcol + 1], cbank[:, 128:129])
                rctok = b.chain(ve)
                sc.wait(*rctok)
                if o_wr_guard is not None:
                    sc.wait(*o_wr_guard)
                sc.op("activation",
                      arena[:, obase + sub * D + h * 128: obase + sub * D + (h + 1) * 128],
                      cbank[:, 0:128], AF.Identity, bias=0.0, scale=negsh[:, rcol:rcol + 1])
                otok = b.mark(sc)
                rc_guard[rcol] = otok
                last_oev = otok
                te.wait(*otok)
        te.wait(*last_oev)
        if arena_guard.get(ofmb) is not None:
            ve.wait(*arena_guard[ofmb])
        lasttr = None
        for sub in range(4):
            for c in range(NCH):
                bank = psb[c % 2]
                te.op("transpose", bank[:, 0:128],
                      arena[:, obase + sub * D + c * 128: obase + sub * D + (c + 1) * 128],
                      ident_sb[:, :])
                ttok = b.mark(te)
                lasttr = ttok
                ve.wait(*ttok)
                ve.op("tensor_copy",
                      arena[:, ofmb + c * 512 + sub * 128: ofmb + c * 512 + (sub + 1) * 128],
                      bank[:, 0:128])
                etok = b.mark(ve)
                te.wait(*etok)
        o_wr_guard = lasttr
        te.wait(*etok)
        lastmm2 = None
        for sub in range(4):
            for hf in range(2):
                bank = psf[4 + (sub * 2 + hf) % 2]
                for k in range(NCH):
                    te.op("matmul", bank[:, :],
                          arena[:, ofmb + k * 512 + sub * 128: ofmb + k * 512 + (sub + 1) * 128],
                          wslot[:, 8 * D + k * D + hf * 512: 8 * D + k * D + (hf + 1) * 512],
                          start=(k == 0), stop=False)
                te.op("matmul", bank[:, :], onesrow_bf[0:1, :],
                      brow_sb[0:1, D + hf * 512: D + (hf + 1) * 512],
                      start=False, stop=True)
                mtok = b.mark(te)
                lastmm2 = mtok
                t = sidx * 4 + sub
                xsl = x_sb[:, t * D + hf * 512: t * D + (hf + 1) * 512]
                ve.wait(*mtok)
                ve.op("tensor_tensor", xsl, xsl, bank[:, :], ALU.add)
                etok2 = b.chain(ve)
                te.wait(*etok2)
        arena_guard[qbase] = last_ctx
        arena_guard[kbase] = last_ctx
        arena_guard[vbase] = last_ctx
        wslot_rd_guard[1] = lastmm2
        arena_guard[ofmb] = lastmm2
    b.barrier()

    if stop_after == "layer0":
        dbg_out("x_after_l0", x_sb[:, :], (128, NT * D), BF16)
        b.es.pop_all()
        return nc, b, DBG

    ln_stats("L1")
    b.barrier()

    tok = b.dma_on(sem_aux, out=rowbuf[0:1, 166:167], in_=dram["dact"][0:1, 0:1])
    sc.wait(*tok)
    sc.op("activation", rowbuf[0:1, 167:168], rowbuf[0:1, 166:167], AF.Tanh,
          bias=0.0, scale=4.0)
    b.chain(sc)
    sc.op("activation", rowbuf[0:1, 168:169], rowbuf[0:1, 167:168], AF.Exp,
          bias=consts[0:1, 4:5], scale=float(0.5 * math.log(2.0)))
    fatok = b.mark(sc)
    ve.wait(*fatok)
    bcast_row([(168, 8)])
    b.barrier()

    def mxfp4_tile(src_f32, dst_bf, fa_ap, n):
        nblk = n // 32
        ve.op("tensor_reduce",
              sc_blk[:, 0:nblk].rearrange("p (g o) -> p g o", o=1),
              src_f32[:, 0:n].rearrange("p (g k) -> p g k", k=32),
              AX.X, ALU.max, apply_absolute_value=True)
        b.chain(ve)
        ve.op("tensor_scalar", sc_blk[:, 32:32 + nblk].bitcast(U32),
              sc_blk[:, 0:nblk].bitcast(U32), 0x7F800000, None, ALU.bitwise_and)
        b.chain(ve)
        if fa_ap.shape[1] == 1:
            ve.op("tensor_scalar", sc_blk[:, 64:64 + nblk],
                  sc_blk[:, 32:32 + nblk], fa_ap, 1e-30, ALU.mult, ALU.max)
        else:
            ve.op("tensor_tensor", sc_blk[:, 64:64 + nblk],
                  sc_blk[:, 32:32 + nblk], fa_ap, ALU.mult)
            b.chain(ve)
            ve.op("tensor_scalar", sc_blk[:, 64:64 + nblk],
                  sc_blk[:, 64:64 + nblk], 1e-30, None, ALU.max)
        b.chain(ve)
        ve.op("reciprocal", sc_blk[:, 96:96 + nblk], sc_blk[:, 64:64 + nblk])
        b.chain(ve)
        ve.op("tensor_tensor", t_a[:, 0:n].rearrange("p (g k) -> p g k", k=32),
              src_f32[:, 0:n].rearrange("p (g k) -> p g k", k=32),
              sc_blk[:, 96:96 + nblk].rearrange("p (g o) -> p g o", o=1).broadcast_to([128, nblk, 32]),
              ALU.mult)
        b.chain(ve)
        ve.op("tensor_scalar", t_b[:, 0:n], t_a[:, 0:n], 6.0, -6.0, ALU.min, ALU.max)
        b.chain(ve)
        ve.op("tensor_scalar", t_a[:, 0:n].bitcast(U32), t_b[:, 0:n].bitcast(U32),
              0x7F800000, None, ALU.bitwise_and)
        b.chain(ve)
        ve.op("tensor_scalar", t_bf[:, 0:n], t_a[:, 0:n], 1.0, 4.0, ALU.max, ALU.min)
        b.chain(ve)
        ve.op("scalar_tensor_tensor", t_a[:, 0:n], t_bf[:, 0:n], MAGIC / 2.0,
              t_b[:, 0:n], ALU.mult, ALU.add)
        b.chain(ve)
        ve.op("scalar_tensor_tensor", t_b[:, 0:n], t_bf[:, 0:n], -MAGIC / 2.0,
              t_a[:, 0:n], ALU.mult, ALU.add)
        b.chain(ve)
        ve.op("tensor_tensor", dst_bf[:, 0:n].rearrange("p (g k) -> p g k", k=32),
              t_b[:, 0:n].rearrange("p (g k) -> p g k", k=32),
              sc_blk[:, 64:64 + nblk].rearrange("p (g o) -> p g o", o=1).broadcast_to([128, nblk, 32]),
              ALU.mult)
        return b.chain(ve)

    w1_guard = None
    for rt in range(NCH):
        if w1_guard is not None:
            sp.wait(*w1_guard)
        tok = b.dma_on(sem_wload, out=t_d[:, :],
                       in_=dram["w1"][rt * 128:(rt + 1) * 128, 0:512])
        tok2 = b.dma_on(sem_wload, out=t_e[:, :],
                        in_=dram["w1"][rt * 128:(rt + 1) * 128, 512:1024])
        dwt = b.dma_on(sem_wload, out=maxqk[:, 96:128],
                       in_=dram["dw"][rt * 128:(rt + 1) * 128, :])
        sc.wait(*dwt)
        sc.op("activation", maxqk[:, 64:96], maxqk[:, 96:128], AF.Tanh,
              bias=0.0, scale=4.0)
        b.chain(sc)
        sc.op("activation", maxqk[:, 96:128], maxqk[:, 64:96], AF.Exp,
              bias=consts[:, 4:5], scale=float(0.5 * math.log(2.0)))
        fwtok = b.mark(sc)
        ve.wait(*dwt)
        ve.wait(*fwtok)
        for hh, stg in enumerate((t_d, t_e)):
            qtok = mxfp4_tile(stg, t_bf, maxqk[:, 96 + hh * 16:112 + hh * 16], 512)
            for k in range(4):
                kk = hh * 4 + k
                bank = psb[kk % 2]
                te.wait(*qtok)
                te.op("transpose", bank[:, 0:128], t_bf[:, k * 128:(k + 1) * 128],
                      ident_sb[:, :])
                ttok = b.mark(te)
                ve.wait(*ttok)
                ve.op("tensor_copy",
                      wslot[:, kk * D + rt * 128: kk * D + (rt + 1) * 128],
                      bank[:, 0:128])
                etok = b.mark(ve)
                te.wait(*etok)
            w1_guard = etok
    b.barrier()

    ytr_guard = None
    for sidx in range(NSL):
        for j in range(4):
            t = sidx * 4 + j
            tok = xn_tile(t, t_bf[:, :])
            te.wait(*tok)
            for c in range(NCH):
                bank = psb[c % 2]
                te.op("transpose", bank[:, 0:128], t_bf[:, c * 128:(c + 1) * 128],
                      ident_sb[:, :])
                ttok = b.mark(te)
                ve.wait(*ttok)
                ve.op("tensor_copy", xnslab[:, c * 512 + j * 128: c * 512 + (j + 1) * 128],
                      bank[:, 0:128])
                etok = b.mark(ve)
                te.wait(*etok)
            ve.wait(*tok)
            if ytr_guard is not None:
                ve.wait(*ytr_guard)
            ytok = mxfp4_tile(t_bf, spst, scal[:, 8:9], 1024)
            sc.wait(*ytok)
            sc.wait(*ttok)
            for c in range(NCH):
                bank = psb[c % 2]
                te.wait(*ytok)
                te.op("transpose", bank[:, 0:128], spst[:, c * 128:(c + 1) * 128],
                      ident_sb[:, :])
                ttok2 = b.mark(te)
                ytr_guard = ttok2
                ve.wait(*ttok2)
                ve.op("tensor_copy",
                      arena[:, c * NSL * 512 + sidx * 512 + j * 128:
                            c * NSL * 512 + sidx * 512 + (j + 1) * 128],
                      bank[:, 0:128])
                etok2 = b.mark(ve)
                te.wait(*etok2)
        cptok = b.mark(ve)
        sp.wait(*cptok)
        ve.wait(*cptok)
        sptoks = []
        for c in range(NCH):
            sptoks.append(b.dma_on(sem_xnsp, out=xnfm_sp[c, sidx, :, :],
                                   in_=xnslab[:, c * 512:(c + 1) * 512]))
        te.wait(*sptoks[-1])
    b.barrier()

    if stop_after == "xq1":
        dbg_out("xq1", arena[:, 0:512], (128, 512), BF16)
        dbg_out("x_l1", x_sb[:, :], (128, NT * D), BF16)
        b.es.pop_all()
        return nc, b, DBG
    if stop_after == "full":
        dbg_out("x_l1", x_sb[:, :], (128, NT * D), BF16)

    for sidx in range(NSL):
        for sub in range(4):
            for hf in range(2):
                bank = psf[(sub * 2 + hf) % 4]
                for k in range(NCH):
                    te.op("matmul", bank[:, :],
                          arena[:, k * NSL * 512 + sidx * 512 + sub * 128:
                                k * NSL * 512 + sidx * 512 + (sub + 1) * 128],
                          wslot[:, k * D + hf * 512: k * D + (hf + 1) * 512],
                          start=(k == 0), stop=False)
                te.op("matmul", bank[:, :], onesrow_bf[0:1, :],
                      brow_sb[0:1, 2 * D + hf * 512: 2 * D + (hf + 1) * 512],
                      start=False, stop=True)
                mtok = b.mark(te)
                t = sidx * 4 + sub
                xsl = x_sb[:, t * D + hf * 512: t * D + (hf + 1) * 512]
                ve.wait(*mtok)
                ve.op("tensor_tensor", xsl, xsl, bank[:, :], ALU.add)
                etok = b.chain(ve)
                te.wait(*etok)
    b.barrier()

    if stop_after == "ffn1":
        dbg_out("x_after_ffn1", x_sb[:, 0:D], (128, D), BF16)
        b.es.pop_all()
        return nc, b, DBG

    out_guard = None
    for t in range(NT):
        if out_guard is not None:
            ve.wait(*out_guard)
        ve.op("tensor_copy", t_a[:, :], x_sb[:, t * D:(t + 1) * D])
        otok = b.chain(ve)
        sp.wait(*otok)
        out_guard = b.dma_on(sem_out, out=out_t[t * 128:(t + 1) * 128, :],
                             in_=t_a[:, :])
    b.barrier()

    tok1 = b.dma_on(sem_wload, out=t_a[:, 0:256], in_=dram["W_DQ"][:, :])
    tok2 = b.dma_on(sem_wload, out=t_a[:, 256:512], in_=dram["W_UQ_T"][:, :])
    tok3 = b.dma_on(sem_wload, out=t_b[:, 0:512], in_=dram["W_DKV"][:, :])
    tok4 = b.dma_on(sem_wload, out=t_d[:, :], in_=dram["W_UKV_T"][:, 0:512])
    tok4b = b.dma_on(sem_wload, out=t_e[:, :], in_=dram["W_UKV_T"][:, 512:1024])
    tok5 = b.dma_on(sem_wload, out=t_b[0:64, 512:576], in_=dram["W_QR_T"][:, :])
    tok6 = b.dma_on(sem_wload, out=t_b[0:64, 576:640], in_=dram["W_KR_T"][:, :])
    tok7 = b.dma_on(sem_wload, out=t_b[0:64, 640:704], in_=dram["W_O_T"][:, :])
    te.wait(*tok7)
    for c in range(4):
        te.op("matmul", psf[0][0:64, 0:64], t_a[:, c * 64:(c + 1) * 64],
              t_a[:, 256 + c * 64:256 + (c + 1) * 64],
              start=(c == 0), stop=(c == 3))
    mq_tok = b.mark(te)
    for c in range(NCH):
        src = t_d if c < 4 else t_e
        te.op("matmul", psf[1][0:64, 0:128], t_b[:, c * 64:(c + 1) * 64],
              src[:, (c % 4) * 128:(c % 4 + 1) * 128],
              start=(c == 0), stop=(c == 7))
    mkv_tok = b.mark(te)
    gp.op("memset", bd_sb[:, :], 0.0)
    gtok = (b.S["gp1"], b.S["gp1"].n)
    ve.wait(*gtok)
    ve.wait(*mq_tok)
    ve.op("tensor_copy", bd_sb[0:64, 0:64], psf[0][0:64, 0:64])
    b.chain(ve)
    ve.wait(*mkv_tok)
    ve.op("tensor_copy", bd_sb[0:64, 128:192], psf[1][0:64, 0:64])
    b.chain(ve)
    ve.op("tensor_copy", bd_sb[0:64, 256:320], psf[1][0:64, 64:128])
    b.chain(ve)
    ve.wait(*tok7)
    ve.op("tensor_copy", bd_sb[0:64, 384:448], t_b[0:64, 512:576])
    b.chain(ve)
    ve.op("tensor_copy", bd_sb[0:64, 512:576], t_b[0:64, 576:640])
    b.chain(ve)
    ve.op("tensor_copy", bd_sb[0:64, 640:704], t_b[0:64, 640:704])
    cptok = b.chain(ve)
    sp.wait(*cptok)
    dtoks = []
    for slot in range(5):
        dtoks.append(b.dma_on(sem_aux, out=bd_sb[64:128, slot * 128 + 64: slot * 128 + 128],
                              in_=bd_sb[0:64, slot * 128: slot * 128 + 64]))
    te.wait(*dtoks[-1])
    ve.wait(*dtoks[-1])
    gp.op("memset", bd_sb[64:65, 704:705], 1.0)
    bdtok = (b.S["gp1"], b.S["gp1"].n)
    te.wait(*bdtok)
    ve.op("tensor_scalar", qnw_sb[:, :], qnw_sb[:, :], SCALE1, None, ALU.mult)
    qnwtok = b.chain(ve)
    b.barrier()

    AXH, AQC, AQR, AKC, AKR, AVT, ACT_ = 0, 2048, 4096, 6144, 8192, 10240, 12320
    attnbuf = arena[:, 13312:13312 + 4096].bitcast(F32)
    a_guard = {}

    gp.op("memset", arena[:, AVT:AVT + 16 * 130], 1.0)
    avt_ones = (b.S["gp1"], b.S["gp1"].n)
    sc.wait(*avt_ones)

    if stop_after == "attn1":
        dbg_out("xpre", x_sb[:, :], (128, NT * D), BF16)
    UNITS = 1 if stop_after == "attn1" else 16
    ntok_prev = [None, None]
    trtok_prev = [None, None]
    for unit in range(UNITS):
        seq2, pp = divmod(unit, 8)
        if a_guard.get("qtr") is not None:
            ve.wait(*a_guard["qtr"])
            sc.wait(*a_guard["qtr"])
        if a_guard.get(AXH) is not None:
            sp.wait(*a_guard[AXH])
        xtoks = []
        for half in range(4):
            sl = seq2 * 4 + half
            xtoks.append(b.dma_on(sem_atld,
                                  out=arena[:, AXH + half * 512: AXH + (half + 1) * 512],
                                  in_=xnfm_sp[pp, sl, :, :]))
        xh_tok = xtoks[-1]
        te.wait(*xh_tok)
        last_ctx = None
        prev_qtr = a_guard.get("qtr")
        for sub in range(16):
            xh_sl = arena[:, AXH + sub * 128: AXH + (sub + 1) * 128]
            bank = psf[sub % 2]
            bank2 = psf[2 + sub % 2]
            te.op("matmul", bank[:, 0:128], xh_sl, bd_sb[:, 0:128],
                  start=True, stop=True)
            qctok = b.mark(te)
            te.op("matmul", bank2[:, 0:128], xh_sl, bd_sb[:, 128:256],
                  start=True, stop=True)
            kctok = b.mark(te)
            rms_sets = [(bank, qctok, qnw_sb, AQC, t_d[:, 0:128],
                         t_bf[:, 0:128], mv),
                        (bank2, kctok, knw_sb, AKC, t_d[:, 128:256],
                         t_bf[:, 128:256], stat12)]
            sqk2 = [None, None]
            mstok2 = [None, None]
            sqtok2 = [None, None]
            ntok2 = [None, None]
            trtok2 = [None, None]
            for W, (bk, tok_, nw, dst_base, sq, ob, st) in enumerate(rms_sets):
                sc.wait(*tok_)
                if ntok_prev[W] is not None:
                    sc.wait(*ntok_prev[W])
                sc.op("activation", sq, bk[:, 0:128], AF.Square,
                      bias=0.0, scale=1.0)
                sqk2[W] = b.mark(sc)
            for W, (bk, tok_, nw, dst_base, sq, ob, st) in enumerate(rms_sets):
                ve.wait(*sqk2[W])
                ve.op("tensor_reduce", st[:, 0:2].rearrange("p (g o) -> p g o", o=1),
                      sq.rearrange("p (g k) -> p g k", k=64),
                      AX.X, ALU.add)
                b.chain(ve)
                ve.op("tensor_scalar", st[:, 0:2], st[:, 0:2], 1.0 / 64.0, None,
                      ALU.mult)
                mstok2[W] = b.chain(ve)
            for W, (bk, tok_, nw, dst_base, sq, ob, st) in enumerate(rms_sets):
                sc.wait(*mstok2[W])
                sc.op("activation", st[:, 2:4], st[:, 0:2], AF.Sqrt,
                      bias=consts[:, 2:3], scale=1.0)
                sqtok2[W] = b.mark(sc)
            for W, (bk, tok_, nw, dst_base, sq, ob, st) in enumerate(rms_sets):
                ve.wait(*sqtok2[W])
                ve.op("reciprocal", st[:, 0:2], st[:, 2:4])
                b.chain(ve)
                ve.op("tensor_tensor", sq.rearrange("p (g k) -> p g k", k=64),
                      bk[:, 0:128].rearrange("p (g k) -> p g k", k=64),
                      st[:, 0:2].rearrange("p (g o) -> p g o", o=1).broadcast_to([128, 2, 64]),
                      ALU.mult)
                b.chain(ve)
                if trtok_prev[W] is not None:
                    ve.wait(*trtok_prev[W])
                ve.op("tensor_tensor", ob.rearrange("p (g k) -> p g k", k=64),
                      sq.rearrange("p (g k) -> p g k", k=64),
                      nw[:, :].rearrange("p (g k) -> p g k", g=1).broadcast_to([128, 2, 64]),
                      ALU.mult)
                ntok2[W] = b.chain(ve)
                ntok_prev[W] = ntok2[W]
            for W, (bk, tok_, nw, dst_base, sq, ob, st) in enumerate(rms_sets):
                te.wait(*ntok2[W])
                te.op("transpose", psb[W][:, 0:128], ob, ident_sb[:, :])
                trtok2[W] = b.mark(te)
                trtok_prev[W] = trtok2[W]
            for W, (bk, tok_, nw, dst_base, sq, ob, st) in enumerate(rms_sets):
                ve.wait(*trtok2[W])
                if prev_qtr is not None:
                    ve.wait(*prev_qtr)
                    prev_qtr = None
                ve.op("tensor_copy",
                      arena[:, dst_base + sub * 128: dst_base + (sub + 1) * 128],
                      psb[W][:, 0:128])
                evtok = b.chain(ve)
                te.wait(*evtok)
            te.op("matmul", bank[:, 0:128], xh_sl, bd_sb[:, 256:384],
                  start=True, stop=True)
            vtok_ = b.mark(te)
            sc.wait(*vtok_)
            sc.op("activation", arena[:, AVT + sub * 130: AVT + sub * 130 + 64],
                  bank[:, 0:64], AF.Identity, bias=0.0, scale=1.0)
            sc.op("activation", arena[:, AVT + sub * 130 + 65: AVT + sub * 130 + 129],
                  bank[:, 64:128], AF.Identity, bias=0.0, scale=1.0)
            vevtok = b.mark(sc)
            te.wait(*vevtok)

        for sl4 in range(4):
            for which, (srcb, bdofs, dstb) in enumerate(
                    [(AQC, 384, AQR), (AXH, 512, AKR)]):
                bank = psf[4 + which]
                te.wait(*evtok)
                te.op("matmul", bank[:, :], bd_sb[:, bdofs:bdofs + 128],
                      arena[:, srcb + sl4 * 512: srcb + (sl4 + 1) * 512],
                      start=True, stop=True)
                mtok = b.mark(te)
                sc.wait(*mtok)
                sc.op("activation", arena[:, dstb + sl4 * 512: dstb + (sl4 + 1) * 512],
                      bank[:, :], AF.Identity, bias=0.0, scale=1.0)
                rev = b.mark(sc)
                te.wait(*rev)
        proj_done = rev
        ve.wait(*proj_done)
        ve.op("tensor_scalar", rowbuf[0:1, 176:184], rowbuf[0:1, 176:184],
              0.0, None, ALU.mult)
        b.chain(ve)
        for pi, (b1, b2) in enumerate(((AQC, AQR), (AKC, AKR))):
            for half in range(2):
                lastmm3 = None
                for si, src in enumerate((b1, b2)):
                    if lastmm3 is not None:
                        ve.wait(*lastmm3)
                    ve.op("tensor_tensor", t_bf[:, :],
                          arena[:, src + half * 1024: src + (half + 1) * 1024],
                          arena[:, src + half * 1024: src + (half + 1) * 1024],
                          ALU.mult)
                    sqk = b.chain(ve)
                    te.wait(*sqk)
                    for h in range(2):
                        for sl5 in range(2):
                            te.op("matmul", psf[4 + h][0:1, sl5 * 256:sl5 * 256 + 256]
                                  if False else psf[4 + h][0:1, sl5 * 512 - sl5 * 512: 512]
                                  if False else psf[4 + h][0:1, 0:512],
                                  onescol_bf[64 * h:64 * h + 64, 0:1],
                                  t_bf[64 * h:64 * h + 64, sl5 * 512:(sl5 + 1) * 512],
                                  start=(si == 0 and sl5 == 0),
                                  stop=(si == 1 and sl5 == 1),
                                  skip_group_check=True)
                    lastmm3 = b.mark(te)
                for h in range(2):
                    ve.wait(*lastmm3)
                    ve.op("tensor_reduce", rowbuf[0:1, 190:191],
                          psf[4 + h][0:1, 0:512], AX.X, ALU.max)
                    b.chain(ve)
                    ve.op("tensor_max",
                          rowbuf[0:1, 176 + 2 * h + pi:177 + 2 * h + pi],
                          rowbuf[0:1, 176 + 2 * h + pi:177 + 2 * h + pi],
                          rowbuf[0:1, 190:191])
                    etok2 = b.chain(ve)
                    te.wait(*etok2)
        for h in range(2):
            ve.op("tensor_tensor", rowbuf[0:1, 170 + h:171 + h],
                  rowbuf[0:1, 176 + 2 * h:177 + 2 * h],
                  rowbuf[0:1, 177 + 2 * h:178 + 2 * h], ALU.mult)
            b.chain(ve)
        vtk = (b.S["ve1"], b.S["ve1"].n)
        sc.wait(*vtk)
        sc.op("activation", rowbuf[0:1, 174:176], rowbuf[0:1, 170:172], AF.Sqrt,
              bias=0.0, scale=1.0)
        b.chain(sc)
        sc.op("activation", rowbuf[0:1, 172:174], rowbuf[0:1, 174:176], AF.Relu,
              bias=consts[0:1, 3:4], scale=1.0)
        shtok = b.mark(sc)
        gp.wait(*shtok)
        t1 = b.dma_on(sem_ccd, out=scr_row[0:1, 0:2], in_=rowbuf[0:1, 172:174], eng=gp)
        gp.wait(*t1)
        if a_guard.get("negsh") is not None:
            gp.wait(*a_guard["negsh"])
        t2 = b.dma_on(sem_ccd, out=mv[:, 6:8],
                      in_=scr_row[0:1, 0:2].broadcast_to([128, 2]), eng=gp)
        ve.wait(*t2)
        ve.op("tensor_scalar", mv[:, 4:6], mv[:, 6:8], -1.0, None, ALU.mult)
        nstok = b.chain(ve)
        exp_rd2 = [None, None]
        exp_rd = None
        ktg = 0
        for h in range(2):
            qcn_h = arena[:, AQC:AQC + 2048]
            for qb in range(4):
                cb = psf[5]
                nkt = 4 * qb + 4
                for kt in range(nkt):
                    pb = ktg % 2
                    ktg += 1
                    tbf_h = t_bf[:, pb * 512:(pb + 1) * 512]
                    sbank = psf[kt % 2]
                    te.op("matmul", sbank[:, :],
                          arena[64 * h:64 * h + 64, AKC + kt * 128: AKC + (kt + 1) * 128],
                          arena[64 * h:64 * h + 64, AQC + qb * 512: AQC + (qb + 1) * 512],
                          start=True, stop=False)
                    te.op("matmul", sbank[:, :],
                          arena[64 * h:64 * h + 64, AKR + kt * 128: AKR + (kt + 1) * 128],
                          arena[64 * h:64 * h + 64, AQR + qb * 512: AQR + (qb + 1) * 512],
                          start=False, stop=True)
                    smtok = b.mark(te)
                    j = kt - 4 * qb
                    if j >= 0:
                        ve.wait(*smtok)
                        ve.op("tensor_tensor", sbank[:, :], sbank[:, :],
                              cmask_sb[:, j * 512:(j + 1) * 512], ALU.add)
                        amtok = b.chain(ve)
                    else:
                        amtok = smtok
                    sc.wait(*amtok)
                    if exp_rd2[pb] is not None:
                        sc.wait(*exp_rd2[pb])
                    sc.op("activation", tbf_h, sbank[:, :], AF.Exp,
                          bias=mv[:, 4 + h:5 + h], scale=1.0)
                    ptok = b.mark(sc)
                    te.wait(*ptok)
                    te.op("matmul", cb[0:65, :],
                          arena[:, AVT + kt * 130 + 65 * h: AVT + kt * 130 + 65 * h + 65],
                          tbf_h, start=(kt == 0), stop=(kt == nkt - 1))
                    exp_rd = b.mark(te)
                    exp_rd2[pb] = exp_rd
                last_ctx = exp_rd
                sc.wait(*last_ctx)
                if a_guard.get("ctxT") is not None:
                    sc.wait(*a_guard["ctxT"])
                sc.op("activation", arena[0:65, ACT_:ACT_ + 512], cb[0:65, :],
                      AF.Identity, bias=0.0, scale=1.0)
                cevtok = b.mark(sc)
                te.wait(*cevtok)
                for sub in range(4):
                    ob = psf[sub]
                    te.op("matmul", ob[:, 0:65],
                          arena[0:65, ACT_ + sub * 128: ACT_ + (sub + 1) * 128],
                          bd_sb[0:65, 640:705], start=True, stop=True)
                    omtok = b.mark(te)
                    ve.wait(*omtok)
                    ve.op("reciprocal", t_d[:, 1:2], ob[:, 64:65])
                    rctok = b.chain(ve)
                    tl = qb * 4 + sub
                    asl = attnbuf[:, tl * 128 + h * 64: tl * 128 + h * 64 + 64]
                    ve.op("tensor_scalar", asl, ob[:, 0:64], t_d[:, 1:2], None,
                          ALU.mult)
                    fintok = b.chain(ve)
                te.wait(*fintok)
                a_guard["ctxT"] = omtok
        flv = fintok
        gp.wait(*flv)
        if a_guard.get("flush") is not None:
            gp.wait(*a_guard["flush"])
        gp.op("dma_start",
              out=out_t[seq2 * 2048:(seq2 + 1) * 2048, pp * 128:(pp + 1) * 128]
              .rearrange("(t p) m -> p t m", p=128),
              in_=attnbuf[:, :].rearrange("p (t m) -> p t m", m=128),
              accum_op=ALU.add, inc=[(sem_out, 16)])
        ftok = (sem_out, sem_out.n)
        a_guard["flush"] = ftok
        ve.wait(*ftok)
        a_guard[AXH] = last_ctx
        a_guard["qtr"] = last_ctx
        a_guard["negsh"] = ptok
    b.barrier()
    if stop_after == "attn1":
        dbg_out("mvd", mv[:, :], (128, 8), F32)
        dbg_out("rowb2", rowbuf[0:1, :], (1, 256), F32)
        dbg_out("tdd", t_d[:, 0:2], (128, 2), F32)
        dbg_out("qcn", arena[:, AQC:AQC + 2048], (128, 2048), BF16)
        dbg_out("qr", arena[:, AQR:AQR + 2048], (128, 2048), BF16)
        dbg_out("kcn", arena[:, AKC:AKC + 2048], (128, 2048), BF16)
        dbg_out("kr", arena[:, AKR:AKR + 2048], (128, 2048), BF16)
        dbg_out("vt", arena[:, AVT:AVT + 2048], (128, 2048), BF16)
        dbg_out("ctxT", arena[:, ACT_:ACT_ + 512], (128, 512), BF16)
        dbg_out("xout", x_sb[:, :], (128, NT * D), BF16)
        dbg_out("xl1", xnfm_sp[0, 0, :, :], (128, 512), BF16)
        b.es.pop_all()
        return nc, b, DBG

    b.barrier()
    b.es.pop_all()
    return nc, b, DBG


def _layer1(nc, b, dram, L):
    pass



def kernel(**inputs):
    W = host_prep(inputs)
    nc, b, DBG = emit_program(stop_after=None)
    _finalize(nc, b)
    in_maps = []
    for c in range(NCORES):
        m = {k: v for k, v in W.items() if k != "x"}
        m["x"] = np.ascontiguousarray(W["x"][2 * c:2 * c + 2].reshape(TOK, D))
        in_maps.append(m)
    res = run_bass_kernel_spmd(nc, in_maps, core_ids=list(range(NCORES)))
    outs = [r["out"] for r in res.results]
    full = np.concatenate([np.asarray(o).reshape(2, SEQ, D) for o in outs], axis=0)
    return full.astype(np.float32, copy=False)


def _finalize(nc, b):
    with nc.Block() as block:
        @block.sync
        def _(e):
            b.sp.replay(e)

        @block.gpsimd
        def _(e):
            b.gp.replay(e)

        @block.vector
        def _(e):
            b.ve.replay(e)

        @block.scalar
        def _(e):
            b.sc.replay(e)

        @block.tensor
        def _(e):
            b.te.replay(e)
